# version 01
# baseline (speedup 1.0000x reference)
import numpy as np
import ml_dtypes

import concourse.bass as bass
import concourse.mybir as mybir
import concourse.library_config as libcfg
from concourse.bass import IndirectOffsetOnAxis
from concourse.bass_utils import run_bass_kernel_spmd
from concourse.library_overlay import lower_extended_insts

F32 = mybir.dt.float32
BF16 = mybir.dt.bfloat16
U32 = mybir.dt.uint32
I32 = mybir.dt.int32
I16 = mybir.dt.int16
AF = mybir.ActivationFunctionType
ALU = mybir.AluOpType
AXX = mybir.AxisListType.X

P, D, DCH, S, SEQ = 128, 1024, 8, 2048, 2
T, HALF, N, KSEL, DTRANK, FF, NH, HD = 4096, 1024, 16, 256, 64, 4096, 16, 64
EPS_LN, EPS_RMS, NEG, RSC = 1e-5, 1e-6, -1.0e9, 0.125
TWO_PI = 6.283185307179586
MAGIC = 12582912.0
PHASE_LIMIT = 99
DEBUG_HBM = False


class Ctr:
    def __init__(self, nc, name):
        self.h = nc.alloc_semaphore(name)
        self.n = 0

    def dma(self, instr):
        instr.then_inc(self.h, 16)
        self.n += 16

    def one(self, instr):
        instr.then_inc(self.h, 1)
        self.n += 1

    def seq(self, eng):
        eng.sem_inc(self.h, 1)
        self.n += 1


def build_nc():
    nc = bass.Bass("TRN2", target_bir_lowering=False)

    xin = nc.dram_tensor("xin", [D, T], F32, kind="ExternalInput").ap()
    w_in_h = nc.dram_tensor("w_in", [D, D], F32, kind="ExternalInput").ap()
    w_x_h = nc.dram_tensor("w_x", [D, 96], F32, kind="ExternalInput").ap()
    w_dt_h = nc.dram_tensor("w_dt", [DTRANK, D], F32, kind="ExternalInput").ap()
    wq_h = nc.dram_tensor("wq", [D, D], F32, kind="ExternalInput").ap()
    wk_h = nc.dram_tensor("wk", [D, D], F32, kind="ExternalInput").ap()
    wv_h = nc.dram_tensor("wv", [D, D], F32, kind="ExternalInput").ap()
    wo_h = nc.dram_tensor("wo", [D, D], F32, kind="ExternalInput").ap()
    w1_h = nc.dram_tensor("w1", [D, FF], F32, kind="ExternalInput").ap()
    w2_h = nc.dram_tensor("w2", [FF, D], F32, kind="ExternalInput").ap()
    a_perm = nc.dram_tensor("a_perm", [P, P], F32, kind="ExternalInput").ap()
    bdt8 = nc.dram_tensor("bdt8", [P, DCH], F32, kind="ExternalInput").ap()
    dp8 = nc.dram_tensor("dp8", [P, DCH], F32, kind="ExternalInput").ap()
    ln18 = nc.dram_tensor("ln18", [P, DCH], F32, kind="ExternalInput").ap()
    ln28 = nc.dram_tensor("ln28", [P, DCH], F32, kind="ExternalInput").ap()
    rw8 = nc.dram_tensor("rw8", [P, DCH], F32, kind="ExternalInput").ap()
    rb1 = nc.dram_tensor("rb1", [1, 1], F32, kind="ExternalInput").ap()
    mask_h = [nc.dram_tensor("mask0", [S, S], F32, kind="ExternalInput").ap(),
              nc.dram_tensor("mask1", [S, S], F32, kind="ExternalInput").ap()]
    pos2 = nc.dram_tensor("pos2", [SEQ, S], I32, kind="ExternalInput").ap()
    cpack = nc.dram_tensor("cpack", [P, 257], F32, kind="ExternalInput").ap()
    out_fm = nc.dram_tensor("out_fm", [D, T], F32, kind="ExternalOutput").ap()

    KIND = "ExternalOutput" if DEBUG_HBM else "Internal"
    h_wrow = nc.dram_tensor("h_wrow", [SEQ, 2, S], F32, kind=KIND).ap()
    h_midx = nc.dram_tensor("h_midx", [SEQ, S], F32, kind="Internal").ap()
    h_selw = nc.dram_tensor("h_selw", [SEQ, 256], F32, kind="Internal").ap()
    h_seli = nc.dram_tensor("h_seli", [SEQ, 16, 16], mybir.dt.uint16, kind="Internal").ap()
    h_rows = nc.dram_tensor("h_rows", [SEQ, 6, KSEL], F32, kind=KIND).ap()
    h_g = nc.dram_tensor("h_g", [N, T], F32, kind=KIND).ap()
    h_cc = nc.dram_tensor("h_cc", [N, T], BF16, kind=KIND).ap()
    h_dtl = nc.dram_tensor("h_dtl", [P, S], BF16, kind=KIND).ap()
    h_corr = nc.dram_tensor("h_corr", [SEQ, P, 2048], BF16, kind=KIND).ap()
    h_dbg = nc.dram_tensor("h_dbg", [P, 16384], F32, kind=KIND).ap() if DEBUG_HBM else None

    AR_A = nc.alloc_sbuf_tensor("AR_A", [P, 16384], F32)
    AR_B = nc.alloc_sbuf_tensor("AR_B", [P, 16384], F32)
    AR_C = nc.alloc_sbuf_tensor("AR_C", [P, 16384], F32)
    SMALL = nc.alloc_sbuf_tensor("SMALL", [P, 1792], F32)
    WST = nc.alloc_sbuf_tensor("WST", [P, 1024], F32)
    CORR = nc.alloc_sbuf_tensor("CORR", [P, 2048], BF16)
    SEL = nc.alloc_sbuf_tensor("SEL", [P, 96], mybir.dt.uint16)

    aA, aB, aC = AR_A.ap(), AR_B.ap(), AR_C.ap()
    aAb, aBb, aCb = aA.bitcast(BF16), aB.bitcast(BF16), aC.bitcast(BF16)
    aCi16 = aC.bitcast(I16)
    r1 = aB[0:32, 0:2048]
    r2 = aB[0:32, 2048:4096]
    r2i = r2.bitcast(I32)
    sm = SMALL.ap()
    smb = sm.bitcast(BF16)
    wst = WST.ap()
    corr = CORR.ap()
    sil = SEL.ap()
    slw = aB[0:32, 4096:5632]

    A_f = sm[:, 0:128]
    A099 = sm[:, 128:256]
    A001 = sm[:, 256:384]
    A_bf = smb[:, 768:896]
    bdt_s = sm[:, 448:456]
    dp_s = sm[:, 456:464]
    ln1_s = sm[:, 464:472]
    ln2_s = sm[:, 472:480]
    rw_s = sm[:, 480:488]
    thr2 = sm[:, 488:489]
    rb2 = sm[:, 489:490]
    nRcol = sm[:, 490:491]
    Rtmp = sm[:, 491:499]
    kth_o = sm[:, 500:504]
    w128t = [sm[:, 504:520], sm[:, 520:536]]
    ridf = sm[:, 536:540]
    ridu = sm.bitcast(U32)[:, 540:544]
    psel_raw = sm.bitcast(I32)[:, 544:548]
    psel_f = sm[:, 548:552]
    wsel_raw = sm[:, 552:556]
    inv32c = sm[:, 556:557]
    epsln_c = sm[:, 557:558]
    epsrms_c = sm[:, 558:559]
    identf_st = sm[:, 560:688]
    ones_f = sm[:, 688:816]
    ident_bf = smb[:, 1632:1760]
    ones_bf = smb[:, 1760:1888]
    hstate = sm[:, 944:1200]
    wdt_bf = smb[:, 2400:3424]
    nopv = sm[0:1, 1712:1713]
    nops = sm[0:1, 1713:1714]

    PS1 = nc.alloc_psum_tensor("PS1", [P, 2048], F32)
    PS2 = nc.alloc_psum_tensor("PS2", [P, 2048], F32)
    ps1, ps2 = PS1.ap(), PS2.ap()
    psTb = ps2[:, 1536:2048].bitcast(BF16)

    dX = Ctr(nc, "dX")
    dCst = Ctr(nc, "dCst")
    dXp = [Ctr(nc, "dXp0"), Ctr(nc, "dXp1")]
    dGp = [Ctr(nc, "dGp0"), Ctr(nc, "dGp1")]
    dHp = [Ctr(nc, "dHp0"), Ctr(nc, "dHp1")]
    dOp = [Ctr(nc, "dOp0"), Ctr(nc, "dOp1")]
    dO = Ctr(nc, "dO")
    dGm = Ctr(nc, "dGm")
    cV = Ctr(nc, "cV")
    cS = Ctr(nc, "cS")
    cT = Ctr(nc, "cT")
    cG = Ctr(nc, "cG")

    def xn_t(c):
        return aAb[:, c * 4096:(c + 1) * 4096]

    def u_t(c):
        return aBb[:, c * 4096:(c + 1) * 4096]

    def pslice(sl):
        return (ps1 if sl < 4 else ps2)[:, (sl % 4) * 512:(sl % 4 + 1) * 512]

    blk = nc.Block("main")
    blk.__enter__()

    def _emit_finish():
        @blk.vector
        def _(v):
            v.wait_ge(cS.h, cS.n)
            v.wait_ge(cT.h, cT.n)
            v.wait_ge(cG.h, cG.n)
            v.memset(aC[:, 0:4096], 0.0)
            cV.one(v.drain())

        @blk.sync
        def _(sy):
            sy.wait_ge(dX.h, dX.n)
            sy.wait_ge(cV.h, cV.n)
            sy.wait_ge(dCst.h, dCst.n)
            sy.wait_ge(dGm.h, dGm.n)
            for p in (dXp + dGp + dHp):
                sy.wait_ge(p.h, p.n)
            for c in range(DCH):
                dO.dma(sy.dma_start(out=out_fm[c * P:(c + 1) * P, :], in_=aC[:, 0:4096]))
            sy.wait_ge(dO.h, dO.n)
        blk.__exit__(None, None, None)


    xstage = [aC[:, 0:4096], aC[:, 4096:8192]]
    racc = aC[:, 8192:12288]
    sqb = [aCb[:, 24576:26624], aCb[:, 26624:28672]]

    @blk.sync
    def _(sy):
        for dst, src in [(A_f, a_perm), (bdt_s, bdt8), (dp_s, dp8), (ln1_s, ln18),
                         (ln2_s, ln28), (rw_s, rw8), (rb2[0:1, :], rb1), (rb2[1:2, :], rb1),
                         (identf_st, cpack[:, 0:128]), (ones_f, cpack[:, 128:256]),
                         (inv32c[0:32, :], cpack[0:32, 256:257])]:
            dCst.dma(sy.dma_start(out=dst, in_=src, allow_slow_non_contiguous=True))
    ph1_small = dCst.n

    @blk.gpsimd
    def _(g):
        cG.one(g.iota(r2i[0:2, :], pattern=[[1, 2048]], base=1, channel_multiplier=0))
    iota_done = cG.n

    @blk.scalar
    def _(s):
        s.wait_ge(dCst.h, ph1_small)
        cS.one(s.activation(out=A_f, in_=A_f, func=AF.Exp, scale=1.0))
    aexp = cS.n

    if PHASE_LIMIT <= -1:
        _emit_finish()
        return nc

    vbase_x = cV.n
    for c in range(DCH):
        @blk.sync
        def _(sy, c=c):
            if c >= 2:
                sy.wait_ge(cV.h, vbase_x + (c - 1))
            dXp[c % 2].dma(sy.dma_start(out=xstage[c % 2], in_=xin[c * P:(c + 1) * P, :]))
        need = dXp[c % 2].n

        @blk.vector
        def _(v, c=c, need=need):
            v.wait_ge(dXp[c % 2].h, need)
            if c == 0:
                v.wait_ge(dCst.h, ph1_small)
                v.wait_ge(cG.h, iota_done)
                v.wait_ge(cS.h, aexp)
                v.tensor_copy(out=ident_bf, in_=identf_st)
                v.tensor_copy(out=ones_bf, in_=ones_f)
                v.memset(epsln_c, EPS_LN)
                v.memset(epsrms_c, EPS_RMS)
                v.tensor_copy(out=r2[0:2, :], in_=r2i[0:2, :])
                v.tensor_scalar(out=A_f, in0=A_f, scalar1=-1.0, scalar2=None, op0=ALU.mult)
                v.drain()
                v.tensor_scalar(out=A099, in0=A_f, scalar1=0.099, scalar2=None, op0=ALU.mult)
                v.tensor_scalar(out=A001, in0=A_f, scalar1=0.001, scalar2=None, op0=ALU.mult)
                v.tensor_copy(out=A_bf, in_=A_f)
            v.tensor_copy(out=xn_t(c), in_=xstage[c % 2])
            if c == 0:
                v.scalar_tensor_tensor(out=racc, in0=xstage[0], scalar=rw_s[:, 0:1],
                                       in1=xstage[0], op0=ALU.mult, op1=ALU.bypass)
            else:
                v.scalar_tensor_tensor(out=racc, in0=xstage[c % 2], scalar=rw_s[:, c:c + 1],
                                       in1=racc, op0=ALU.mult, op1=ALU.add)
            cV.one(v.drain())
    xstream = cV.n


    if PHASE_LIMIT <= 0:
        _emit_finish()
        return nc

    m_b = aC[:, 0:2048]
    r_b = aC[:, 2048:4096]
    m_bf = aCb[:, 8192:10240]
    r_bf = aCb[:, 10240:12288]
    msq = aC[:, 6144:8192]
    w_bc = aC[:, 14336:16384]

    for s_ in range(SEQ):
        col0 = s_ * S
        for c in range(DCH):
            @blk.scalar
            def _(s, c=c, s_=s_):
                if s_ == 0 and c == 0:
                    s.wait_ge(cV.h, xstream)
                if not (s_ == 0 and c in (0, 1)):
                    s.wait_ge(cT.h, cT.n)
                s.activation(out=sqb[c % 2], in_=xn_t(c)[:, col0:col0 + S],
                             func=AF.Square, scale=1.0)
                cS.one(s.drain())
            need = cS.n

            @blk.tensor
            def _(t, c=c, s_=s_, need=need):
                t.wait_ge(cS.h, need)
                if s_ == 1 and c == 0:
                    t.wait_ge(cV.h, cV.n)
                for sl in range(4):
                    t.matmul(ps1[:, sl * 512:(sl + 1) * 512], ones_bf,
                             xn_t(c)[:, col0 + sl * 512:col0 + (sl + 1) * 512],
                             start=(c == 0), stop=(c == DCH - 1))
                for sl in range(4):
                    _mm = t.matmul(ps2[:, sl * 512:(sl + 1) * 512], ones_bf,
                             sqb[c % 2][:, sl * 512:(sl + 1) * 512],
                             start=(c == 0), stop=(c == DCH - 1))
                cT.one(_mm)
        smm = cT.n

        if PHASE_LIMIT <= 0.2:
            _emit_finish()
            return nc

        @blk.vector
        def _(v, need=smm):
            v.wait_ge(cT.h, need)
            v.tensor_scalar(out=m_b, in0=ps1[:, 0:2048], scalar1=1.0 / D, scalar2=None,
                            op0=ALU.mult)
            cV.one(v.drain())
        mdone = cV.n

        if DEBUG_HBM:
            @blk.sync
            def _(sy, s_=s_, need=mdone):
                sy.wait_ge(cV.h, need)
                if s_ == 0:
                    dX.dma(sy.dma_start(out=h_dbg[:, 12416:14464], in_=m_b))
                sy.wait_ge(dX.h, dX.n)

        @blk.scalar
        def _(s, need=mdone):
            s.wait_ge(cV.h, need)
            if DEBUG_HBM:
                s.wait_ge(dX.h, dX.n)
            cS.one(s.activation(out=msq, in_=m_b, func=AF.Square, scale=1.0))
        msqd = cS.n

        @blk.vector
        def _(v, need=msqd):
            v.wait_ge(cS.h, need)
            v.scalar_tensor_tensor(out=r_b, in0=ps2[:, 0:2048], scalar=1.0 / D, in1=msq,
                                   op0=ALU.mult, op1=ALU.subtract)
            cV.one(v.drain())
        vard = cV.n

        @blk.scalar
        def _(s, need=vard):
            s.wait_ge(cV.h, need)
            s.activation(out=r_b, in_=r_b, func=AF.Ln, bias=epsln_c[:, 0:1], scale=1.0)
            cS.one(s.activation(out=r_b, in_=r_b, func=AF.Exp, scale=-0.5))
        rstdd = cS.n

        if DEBUG_HBM:
            @blk.sync
            def _(sy, s_=s_, need=rstdd):
                sy.wait_ge(cS.h, need)
                dX.dma(sy.dma_start(out=h_dbg[:, 8192 + s_ * 2048:8192 + (s_ + 1) * 2048],
                                    in_=r_b))

        @blk.vector
        def _(v, s_=s_, need=rstdd):
            v.wait_ge(cS.h, need)
            if DEBUG_HBM:
                v.wait_ge(dX.h, dX.n)
            v.tensor_copy(out=m_bf, in_=m_b)
            v.tensor_copy(out=r_bf, in_=r_b)
            for c in range(DCH):
                xseg = xn_t(c)[:, col0:col0 + S]
                v.tensor_sub(out=sqb[0], in0=xseg, in1=m_bf)
                v.tensor_mul(out=xseg, in0=sqb[0], in1=r_bf)
            cV.one(v.drain())
        xnd = cV.n

        if PHASE_LIMIT <= 0.4:
            _emit_finish()
            return nc

        @blk.tensor
        def _(t, s_=s_, need=xnd):
            t.wait_ge(cV.h, need)
            t.matmul(ps2[:, 0:8], ones_f, rw_s, start=True, stop=True)
            for sl in range(4):
                _mm = t.matmul(ps1[:, sl * 512:(sl + 1) * 512], ones_f,
                         racc[:, col0 + sl * 512:col0 + (sl + 1) * 512],
                         start=True, stop=True)
            cT.one(_mm)
        rxd = cT.n

        @blk.vector
        def _(v, s_=s_, need=rxd):
            v.wait_ge(cT.h, need)
            if DEBUG_HBM:
                v.tensor_copy(out=msq, in_=ps1[:, 0:2048])
            v.tensor_reduce(out=Rtmp[:, 0:1], in_=ps2[:, 0:8], axis=AXX, op=ALU.add)
            v.drain()
            v.tensor_scalar(out=nRcol, in0=Rtmp[:, 0:1], scalar1=-1.0, scalar2=None,
                            op0=ALU.mult)
            v.drain()
            v.scalar_tensor_tensor(out=w_bc, in0=m_b, scalar=nRcol[:, 0:1],
                                   in1=ps1[:, 0:2048], op0=ALU.mult, op1=ALU.add)
            v.drain()
            v.tensor_mul(out=w_bc, in0=w_bc, in1=r_b)
            cV.one(v.drain())
        wd = cV.n

        if DEBUG_HBM:
            @blk.sync
            def _(sy, s_=s_, need=wd):
                sy.wait_ge(cV.h, need)
                dX.dma(sy.dma_start(out=h_dbg[:, 4096 + s_ * 2048:4096 + (s_ + 1) * 2048],
                                    in_=msq))
                if s_ == 0:
                    dX.dma(sy.dma_start(out=h_dbg[:, 0:2048], in_=w_bc))
                    dX.dma(sy.dma_start(out=h_dbg[:, 2048:2056], in_=sm[:, 488:496]))
                    dX.dma(sy.dma_start(out=h_dbg[:, 12288:12416], in_=ones_f))
                sy.wait_ge(dX.h, dX.n)

        if PHASE_LIMIT <= 0.6:
            _emit_finish()
            return nc

        @blk.sync
        def _(sy, s_=s_, need=wd):
            sy.wait_ge(cV.h, need)
            dX.dma(sy.dma_start(out=h_wrow[s_, 0, :], in_=w_bc[0:1, :]))
            sy.wait_ge(dX.h, dX.n)
            dX.dma(sy.dma_start(out=w128t[s_],
                                in_=h_wrow[s_, 0, :].rearrange("(p f) -> p f", p=P)))
    wwrap = dX.n

    if PHASE_LIMIT <= 0.8:
        _emit_finish()
        return nc

    @blk.gpsimd
    def _(g, need=wwrap):
        g.wait_ge(dX.h, need)
        g.load_library(libcfg.attn)
        q = 1.0 - 254.5 / (S - 1)
        g.kth_largest(kth_o[0:1, 0:2], w128t[0], n_per_lane=S // P,
                      k=KSEL + 2, quantile=q)
        cG.one(g.kth_largest(kth_o[0:1, 2:4], w128t[1], n_per_lane=S // P,
                             k=KSEL + 2, quantile=q))
    kthd = cG.n

    @blk.sync
    def _(sy, need=kthd):
        sy.wait_ge(cG.h, need)
        dX.dma(sy.dma_start(out=thr2[0:1, :], in_=kth_o[0:1, 1:2]))
        dX.dma(sy.dma_start(out=thr2[1:2, :], in_=kth_o[0:1, 3:4]))
        dX.dma(sy.dma_start(out=r1[0:2, :], in_=h_wrow[:, 0, :]))
    thrd = dX.n

    @blk.vector
    def _(v, need=thrd):
        v.wait_ge(dX.h, need)
        v.scalar_tensor_tensor(out=r2[0:2, :], in0=r1[0:2, :], scalar=thr2[0:2, :],
                               in1=r2[0:2, :], op0=ALU.is_gt, op1=ALU.mult)
        v.tensor_scalar(out=r2[0:2, :], in0=r2[0:2, :], scalar1=1.0, scalar2=None,
                        op0=ALU.subtract)
        v.scalar_tensor_tensor(out=r1[0:2, :], in0=r1[0:2, :], scalar=rb2[0:2, :],
                               in1=r1[0:2, :], op0=ALU.add, op1=ALU.bypass)
        cV.one(v.drain())
    midxd = cV.n

    @blk.sync
    def _(sy, need=midxd):
        sy.wait_ge(cV.h, need)
        dX.dma(sy.dma_start(out=h_midx, in_=r2[0:2, :]))
        dX.dma(sy.dma_start(out=h_wrow[0, 1, :], in_=r1[0:1, :]))
        dX.dma(sy.dma_start(out=h_wrow[1, 1, :], in_=r1[1:2, :]))
    mbounce = dX.n

    gp_sparse_base = cG.n
    for s_ in range(SEQ):
        @blk.sync
        def _(sy, s_=s_, need=mbounce):
            sy.wait_ge(dX.h, need)
            if s_ == 1:
                sy.wait_ge(cG.h, gp_sparse_base + 1)
            dX.dma(sy.dma_start(out=slw[0:16, 0:128],
                                in_=h_midx[s_].rearrange("(f p) -> p f", p=16),
                                allow_slow_non_contiguous=True))
        wdone = dX.n

        @blk.gpsimd
        def _(g, s_=s_, need=wdone):
            if s_ == 0:
                g.load_library(libcfg.sparse_gather)
            g.wait_ge(dX.h, need)
            nf_u = sil.bitcast(U32)[0:1, 44:45]
            cG.one(g.sparse_gather(out=slw[0:16, 128 + 16 * s_:144 + 16 * s_],
                                   in_=slw[0:16, 0:128], num_found=nf_u))
    sparsed = cG.n

    @blk.vector
    def _(v, need=sparsed):
        v.wait_ge(cG.h, need)
        v.tensor_scalar(out=slw[0:16, 128:160], in0=slw[0:16, 128:160], scalar1=0.0,
                        scalar2=2047.0, op0=ALU.max, op1=ALU.min)
        v.drain()
        v.tensor_copy(out=sil[0:16, 0:32], in_=slw[0:16, 128:160])
        cV.one(v.drain())
    selwd = cV.n

    @blk.sync
    def _(sy, need=selwd):
        sy.wait_ge(cV.h, need)
        for s_ in range(SEQ):
            dX.dma(sy.dma_start(out=h_selw[s_].rearrange("(f p) -> p f", p=16),
                                in_=slw[0:16, 128 + 16 * s_:144 + 16 * s_],
                                allow_slow_non_contiguous=True))
            dX.dma(sy.dma_start(out=h_seli[s_], in_=sil[0:16, 16 * s_:16 * (s_ + 1)]))
        sy.wait_ge(dX.h, dX.n)
        dX.dma(sy.dma_start(out=r2[0:2, 0:256], in_=h_selw))
        for s_ in range(SEQ):
            for grp in range(8):
                dX.dma(sy.dma_start(out=sil[grp * 16:(grp + 1) * 16,
                                            32 + 16 * s_:48 + 16 * s_],
                                    in_=h_seli[s_]))
    lind = dX.n

    @blk.vector
    def _(v, need=lind):
        v.wait_ge(dX.h, need)
        v.memset(r2[0:2, 255:256], -1.0)
        v.drain()
        v.tensor_scalar(out=r2[0:2, 0:256], in0=r2[0:2, 0:256], scalar1=-1.0,
                        scalar2=2047.0, op0=ALU.max, op1=ALU.min)
        v.drain()
        v.tensor_scalar(out=r2[0:2, 512:768], in0=r2[0:2, 0:256], scalar1=1024.0,
                        scalar2=None, op0=ALU.subtract)
        v.tensor_scalar(out=r2[0:2, 768:1024], in0=r2[0:2, 0:256], scalar1=1024.0,
                        scalar2=None, op0=ALU.is_ge)
        v.drain()
        v.scalar_tensor_tensor(out=r2[0:2, 768:1024], in0=r2[0:2, 768:1024],
                               scalar=-2048.0, in1=r2[0:2, 0:256],
                               op0=ALU.mult, op1=ALU.add)
        v.tensor_scalar(out=r2[0:2, 1024:1280], in0=r2[0:2, 0:256], scalar1=0.0,
                        scalar2=None, op0=ALU.max)
        cV.one(v.drain())
    halvd = cV.n

    @blk.sync
    def _(sy, need=halvd):
        sy.wait_ge(cV.h, need)
        for s_ in range(SEQ):
            dX.dma(sy.dma_start(out=h_rows[s_, 0, :], in_=r2[s_:s_ + 1, 0:256]))
            dX.dma(sy.dma_start(out=h_rows[s_, 1, :], in_=r2[s_:s_ + 1, 768:1024]))
            dX.dma(sy.dma_start(out=h_rows[s_, 2, :], in_=r2[s_:s_ + 1, 512:768]))
            dX.dma(sy.dma_start(out=h_rows[s_, 3, :], in_=r2[s_:s_ + 1, 1024:1280]))
        sy.wait_ge(dX.h, dX.n)
        for s_ in range(SEQ):
            for b_ in range(2):
                cc = 2 * s_ + b_
                dX.dma(sy.dma_start(out=ridf[:, cc:cc + 1],
                                    in_=h_rows[s_, 3, b_ * 128:(b_ + 1) * 128].unsqueeze(-1)))
    ridd = dX.n

    @blk.vector
    def _(v, need=ridd):
        v.wait_ge(dX.h, need)
        v.tensor_copy(out=ridu, in_=ridf)
        cV.one(v.drain())
    ridc = cV.n

    @blk.gpsimd
    def _(g, need=ridc):
        g.wait_ge(cV.h, need)
        for s_ in range(SEQ):
            for b_ in range(2):
                cc = 2 * s_ + b_
                dGm.dma(g.indirect_dma_start(
                    out=psel_raw[:, cc:cc + 1], out_offset=None,
                    in_=pos2.rearrange("s t -> (s t)").unsqueeze(-1),
                    in_offset=IndirectOffsetOnAxis(ap=ridu[:, cc:cc + 1], axis=0),
                    element_offset=s_ * S))
                dGm.dma(g.indirect_dma_start(
                    out=wsel_raw[:, cc:cc + 1], out_offset=None,
                    in_=h_wrow.rearrange("s two t -> (s two t)").unsqueeze(-1),
                    in_offset=IndirectOffsetOnAxis(ap=ridu[:, cc:cc + 1], axis=0),
                    element_offset=(s_ * 2 + 1) * S))
        cG.seq(g)
    pseld = cG.n
    pseld_dma = dGm.n

    @blk.vector
    def _(v, need=pseld, needd=pseld_dma):
        v.wait_ge(cG.h, need)
        v.wait_ge(dGm.h, needd)
        v.tensor_copy(out=psel_f, in_=psel_raw)
        cV.one(v.drain())
    pselc = cV.n

    @blk.sync
    def _(sy, need=pselc):
        sy.wait_ge(cV.h, need)
        for s_ in range(SEQ):
            for b_ in range(2):
                cc = 2 * s_ + b_
                dX.dma(sy.dma_start(out=h_rows[s_, 4, b_ * 128:(b_ + 1) * 128].unsqueeze(-1),
                                    in_=psel_f[:, cc:cc + 1]))
                dX.dma(sy.dma_start(out=h_rows[s_, 5, b_ * 128:(b_ + 1) * 128].unsqueeze(-1),
                                    in_=wsel_raw[:, cc:cc + 1]))
    rows_done = dX.n

    if PHASE_LIMIT <= 1:
        _emit_finish()
        return nc

    w_in_bf = aCb[:, 0:8192]

    def winbf(k):
        return w_in_bf[:, k * 1024:(k + 1) * 1024]

    for k in range(DCH):
        @blk.sync
        def _(sy, k=k):
            sy.wait_ge(cV.h, cV.n)
            if k == 0:
                sy.wait_ge(dX.h, rows_done)
            dX.dma(sy.dma_start(out=wst, in_=w_in_h[k * P:(k + 1) * P, :]))
        need = dX.n

        @blk.vector
        def _(v, k=k, need=need):
            v.wait_ge(dX.h, need)
            v.tensor_copy(out=winbf(k), in_=wst)
            cV.one(v.drain())
    winc = cV.n

    for e in range(DCH):
        @blk.tensor
        def _(t, e=e, need=winc):
            t.wait_ge(cV.h, need)
            if e > 0:
                t.wait_ge(cS.h, cS.n)
            for k in range(DCH):
                for sl in range(8):
                    _mm = t.matmul(pslice(sl), winbf(k)[:, e * 128:(e + 1) * 128],
                             xn_t(k)[:, sl * 512:(sl + 1) * 512],
                             start=(k == 0), stop=(k == DCH - 1))
            cT.one(_mm)
        need = cT.n

        @blk.scalar
        def _(s, e=e, need=need):
            s.wait_ge(cT.h, need)
            for sl in range(8):
                s.copy(out=u_t(e)[:, sl * 512:(sl + 1) * 512], in_=pslice(sl))
            cS.one(s.drain())
    xind = cS.n

    if PHASE_LIMIT <= 2:
        _emit_finish()
        return nc

    B0 = aC[0:16, 0:4096]
    C0 = aC[32:48, 0:4096]
    zac = aC[0:16, 8192:12288]
    g0 = B0
    sp_c = aC[:, 4096:8192]
    dt_c = sp_c
    du_c = aCb[:, 24576:28672]
    dtl2 = aCb[:, 28672:30720]
    Cbf = aCb[32:48, 8192:12288]
    wxs = wst.bitcast(BF16)[:, 0:768]

    @blk.sync
    def _(sy, need=xind):
        sy.wait_ge(cS.h, need)
        sy.wait_ge(cV.h, cV.n)
        dX.dma(sy.dma_start(out=wst[0:64, 0:1024], in_=w_dt_h))
        dX.dma(sy.dma_start(out=wst[64:128, 0:1024], in_=w_dt_h))
    need = dX.n

    @blk.vector
    def _(v, need=need):
        v.wait_ge(dX.h, need)
        v.tensor_copy(out=wdt_bf, in_=wst[:, 0:1024])
        cV.one(v.drain())
    wdtc = cV.n

    for k in range(DCH):
        @blk.sync
        def _(sy, k=k):
            sy.wait_ge(cV.h, cV.n)
            dX.dma(sy.dma_start(out=wst[:, 512:608], in_=w_x_h[k * P:(k + 1) * P, :]))
        need = dX.n

        @blk.vector
        def _(v, k=k, need=need):
            v.wait_ge(dX.h, need)
            v.tensor_copy(out=wxs[:, k * 96:(k + 1) * 96], in_=wst[:, 512:608])
            cV.one(v.drain())
    wxc = cV.n

    def proj(mlo, mhi, evac):
        @blk.tensor
        def _(t, mlo=mlo, mhi=mhi):
            t.wait_ge(cV.h, cV.n)
            mm = mhi - mlo
            for k in range(DCH):
                for sl in range(8):
                    _mm = t.matmul(pslice(sl)[0:mm, :], wxs[:, k * 96 + mlo:k * 96 + mhi],
                             u_t(k)[:, sl * 512:(sl + 1) * 512],
                             start=(k == 0), stop=(k == DCH - 1))
            cT.one(_mm)
        need = cT.n

        @blk.vector
        def _(v, evac=evac, need=need):
            v.wait_ge(cT.h, need)
            evac(v)
            cV.one(v.drain())

    def ev_dt(v):
        for sl in range(8):
            s2 = sl // 4
            cb = (sl % 4) * 512
            v.tensor_copy(out=dtl2[64 * s2:64 * (s2 + 1),
                                   (sl % 4) * 512:(sl % 4 + 1) * 512]
                          if False else dtl2[64 * (sl // 4):64 * (sl // 4) + 64,
                                             (sl % 4) * 512:(sl % 4) * 512 + 512],
                          in_=pslice(sl)[0:64, :])

    def ev_B(v):
        for sl in range(8):
            v.tensor_copy(out=B0[:, sl * 512:(sl + 1) * 512], in_=pslice(sl)[0:16, :])

    def ev_C(v):
        for sl in range(8):
            v.tensor_copy(out=C0[:, sl * 512:(sl + 1) * 512], in_=pslice(sl)[0:16, :])

    proj(0, 64, ev_dt)
    proj(64, 80, ev_B)
    proj(80, 96, ev_C)
    projd = cV.n

    @blk.sync
    def _(sy, need=projd):
        sy.wait_ge(cV.h, need)
        dX.dma(sy.dma_start(out=h_dtl, in_=dtl2))
    dtl_saved = dX.n

    for c in range(DCH):
        @blk.tensor
        def _(t, c=c):
            t.wait_ge(cV.h, cV.n)
            t.wait_ge(cS.h, cS.n)
            for sl in range(8):
                rb_ = 64 * (sl // 4)
                _mm = t.matmul(pslice(sl), wdt_bf[rb_:rb_ + 64, c * 128:(c + 1) * 128],
                         dtl2[rb_:rb_ + 64,
                              (sl % 4) * 512:(sl % 4) * 512 + 512],
                         start=True, stop=True)
            cT.one(_mm)
        need = cT.n

        @blk.scalar
        def _(s, c=c, need=need):
            s.wait_ge(cT.h, need)
            for sl in range(8):
                s.activation(out=sp_c[:, sl * 512:(sl + 1) * 512], in_=pslice(sl),
                             func=AF.Exp, bias=bdt_s[:, c:c + 1], scale=1.0)
            s.activation(out=sp_c, in_=sp_c, func=AF.Ln, bias=1.0, scale=1.0)
            cS.one(s.drain())
        need = cS.n

        @blk.vector
        def _(v, c=c, need=need):
            v.wait_ge(cS.h, need)
            v.tensor_scalar(out=dt_c, in0=sp_c, scalar1=0.099, scalar2=0.001,
                            op0=ALU.mult, op1=ALU.add)
            v.tensor_mul(out=du_c, in0=dt_c, in1=u_t(c))
            cV.one(v.drain())
        need = cV.n

        @blk.tensor
        def _(t, c=c, need=need):
            t.wait_ge(cV.h, need)
            for sl in range(8):
                _mm = t.matmul(pslice(sl)[0:16, :], A_bf[:, c * 16:(c + 1) * 16],
                         du_c[:, sl * 512:(sl + 1) * 512], start=True, stop=True)
            cT.one(_mm)
        need = cT.n

        @blk.vector
        def _(v, c=c, need=need):
            v.wait_ge(cT.h, need)
            for half in range(2):
                src = (ps1 if half == 0 else ps2)[0:16, 0:2048]
                if c == 0:
                    v.tensor_copy(out=zac[:, half * 2048:(half + 1) * 2048], in_=src)
                else:
                    v.tensor_add(out=zac[:, half * 2048:(half + 1) * 2048],
                                 in0=zac[:, half * 2048:(half + 1) * 2048], in1=src)
            v.scalar_tensor_tensor(out=u_t(c), in0=u_t(c), scalar=dp_s[:, c:c + 1],
                                   in1=xn_t(c), op0=ALU.mult, op1=ALU.add)
            cV.one(v.drain())
    zdone = cV.n

    @blk.vector
    def _(v, need=zdone):
        v.wait_ge(cV.h, need)
        v.tensor_mul(out=g0, in0=B0, in1=zac)
        v.tensor_copy(out=Cbf, in_=C0)
        cV.one(v.drain())
    gdone = cV.n

    @blk.sync
    def _(sy, need=gdone):
        sy.wait_ge(cV.h, need)
        dX.dma(sy.dma_start(out=h_g, in_=g0))
        dX.dma(sy.dma_start(out=h_cc, in_=Cbf))
    gsaved = dX.n

    if PHASE_LIMIT <= 3:
        _emit_finish()
        return nc

    def xsel_t(c):
        return aAb[:, c * 512:(c + 1) * 512]

    @blk.gpsimd
    def _(g, need=gdone):
        g.wait_ge(cV.h, need)
        g.wait_ge(dX.h, rows_done)
        last = None
        for s_ in range(SEQ):
            for c in range(DCH):
                last = g.indirect_copy(
                    out=xsel_t(c)[:, s_ * 256:(s_ + 1) * 256].unsqueeze(-1),
                    data=xn_t(c)[:, s_ * S:(s_ + 1) * S].unsqueeze(-1),
                    idxs=sil[:, 32 + 16 * s_:48 + 16 * s_],
                    i_know_ap_gather_is_preferred=True)
        cG.one(last)
    xseld = cG.n

    mrow = aC[:, 0:2048]
    mcolf = aC[:, 2048:2304]

    def msel_t(s_, b_):
        i = 2 * s_ + b_
        return aCb[:, 13312 + i * 256:13312 + (i + 1) * 256]

    for s_ in range(SEQ):
        for b_ in range(2):
            cc = 2 * s_ + b_

            @blk.gpsimd
            def _(g, s_=s_, b_=b_, cc=cc, gsaved=gsaved):
                g.wait_ge(cV.h, cV.n)
                g.wait_ge(dX.h, gsaved)
                dGm.dma(g.indirect_dma_start(
                    out=mrow, out_offset=None,
                    in_=mask_h[s_][:, 0:S],
                    in_offset=IndirectOffsetOnAxis(ap=ridu[:, cc:cc + 1], axis=0)))
                g.wait_ge(dGm.h, dGm.n)
                cG.one(g.indirect_copy(
                    out=mcolf.unsqueeze(-1), data=mrow.unsqueeze(-1),
                    idxs=sil[:, 32 + 16 * s_:48 + 16 * s_],
                    i_know_ap_gather_is_preferred=True))
            need = cG.n

            @blk.vector
            def _(v, s_=s_, b_=b_, need=need):
                v.wait_ge(cG.h, need)
                v.tensor_copy(out=msel_t(s_, b_), in_=mcolf)
                v.memset(msel_t(s_, b_)[:, 255:256], NEG)
                cV.one(v.drain())
    mseld = cV.n

    if PHASE_LIMIT <= 4:
        _emit_finish()
        return nc

    hn_r = lambda c: aAb[:, 4096 + c * 512:4096 + (c + 1) * 512]
    q_r = lambda c: aAb[:, 8192 + c * 512:8192 + (c + 1) * 512]
    k_r = lambda c: aAb[:, 12288 + c * 512:12288 + (c + 1) * 512]
    v_r = lambda i: aAb[:, 16384 + i * 1024:16384 + (i + 1) * 1024]
    scr2 = lambda i: aAb[:, 20480 + i * 512:20480 + (i + 1) * 512]
    cs_r = lambda i: aA[:, 11264 + i * 256:11264 + (i + 1) * 256]
    x1_r = lambda c: aAb[:, 24576 + c * 512:24576 + (c + 1) * 512]
    hn2_r = lambda c: aAb[:, 28672 + c * 512:28672 + (c + 1) * 512]
    h1_r = lambda f, s_: aAb[:, 4096 + (f * 2 + s_) * 256:4096 + (f * 2 + s_ + 1) * 256]
    wselb = lambda s_: aA[:, 10752 + s_ * 256:10752 + (s_ + 1) * 256]
    rms_b = aC[:, 4096:4352]
    sqs = aCb[:, 8704:8960]
    sc_f = lambda qb: aC[:, 4608 + qb * 256:4608 + (qb + 1) * 256]

    for s_ in range(SEQ):
        for c in range(DCH):
            @blk.scalar
            def _(s, c=c, s_=s_):
                if c == 0 and s_ == 0:
                    s.wait_ge(cV.h, mseld)
                    s.wait_ge(cG.h, xseld)
                else:
                    s.wait_ge(cT.h, cT.n)
                s.activation(out=sqs, in_=xsel_t(c)[:, s_ * 256:(s_ + 1) * 256],
                             func=AF.Square, scale=1.0)
                cS.one(s.drain())
            need = cS.n

            @blk.tensor
            def _(t, c=c, need=need):
                t.wait_ge(cS.h, need)
                _mm = t.matmul(ps1[:, 0:256], ones_bf, sqs, start=(c == 0), stop=(c == DCH - 1))
                cT.one(_mm)
        need = cT.n

        @blk.scalar
        def _(s, need=need):
            s.wait_ge(cT.h, need)
            s.activation(out=rms_b, in_=ps1[:, 0:256], func=AF.Ln, bias=epsrms_c[:, 0:1],
                         scale=1.0 / D)
            s.drain()
            cS.one(s.activation(out=rms_b, in_=rms_b, func=AF.Exp, scale=-0.5))
        need = cS.n

        @blk.vector
        def _(v, s_=s_, need=need):
            v.wait_ge(cS.h, need)
            for c in range(DCH):
                v.scalar_tensor_tensor(out=hn_r(c)[:, s_ * 256:(s_ + 1) * 256],
                                       in0=xsel_t(c)[:, s_ * 256:(s_ + 1) * 256],
                                       scalar=ln1_s[:, c:c + 1], in1=rms_b,
                                       op0=ALU.mult, op1=ALU.mult)
            cV.one(v.drain())
    hnd = cV.n

    wcur = aCb[:, 0:8192]

    def wcur_k(k):
        return wcur[:, k * 1024:(k + 1) * 1024]

    def load_dd(w_hbm):
        for k in range(DCH):
            @blk.sync
            def _(sy, k=k):
                sy.wait_ge(cV.h, cV.n)
                sy.wait_ge(cT.h, cT.n)
                dX.dma(sy.dma_start(out=wst, in_=w_hbm[k * P:(k + 1) * P, :]))
            need = dX.n

            @blk.vector
            def _(v, k=k, need=need):
                v.wait_ge(dX.h, need)
                v.tensor_copy(out=wcur_k(k), in_=wst)
                cV.one(v.drain())
        return cV.n

    def proj_fm(dst_r, wneed):
        for s_ in range(SEQ):
            for e in range(DCH):
                @blk.tensor
                def _(t, s_=s_, e=e, wneed=wneed):
                    t.wait_ge(cV.h, wneed)
                    t.wait_ge(cS.h, cS.n)
                    for k in range(DCH):
                        _mm = t.matmul(ps1[:, 256 * (e % 4):256 * (e % 4) + 256],
                                 wcur_k(k)[:, e * 128:(e + 1) * 128],
                                 hn_r(k)[:, s_ * 256:(s_ + 1) * 256],
                                 start=(k == 0), stop=(k == DCH - 1))
                    cT.one(_mm)
                need = cT.n

                @blk.scalar
                def _(s, s_=s_, e=e, dst_r=dst_r, need=need):
                    s.wait_ge(cT.h, need)
                    s.copy(out=dst_r(e)[:, s_ * 256:(s_ + 1) * 256],
                           in_=ps1[:, 256 * (e % 4):256 * (e % 4) + 256])
                    cS.one(s.drain())

    wn = load_dd(wq_h)
    proj_fm(q_r, wn)
    wn = load_dd(wk_h)
    proj_fm(k_r, wn)

    wn = load_dd(wv_h)
    for s_ in range(SEQ):
        for tb in range(2):
            for esl in range(2):
                @blk.tensor
                def _(t, s_=s_, tb=tb, esl=esl, wn=wn):
                    t.wait_ge(cV.h, wn)
                    t.wait_ge(cS.h, cS.n)
                    for k in range(DCH):
                        _mm = t.matmul(ps2[:, 512 * esl:512 * (esl + 1)],
                                 hn_r(k)[:, s_ * 256 + tb * 128:s_ * 256 + (tb + 1) * 128],
                                 wcur_k(k)[:, esl * 512:(esl + 1) * 512],
                                 start=(k == 0), stop=(k == DCH - 1))
                    cT.one(_mm)
                need = cT.n

                @blk.scalar
                def _(s, s_=s_, tb=tb, esl=esl, need=need):
                    s.wait_ge(cT.h, need)
                    s.copy(out=v_r(s_ * 2 + tb)[:, esl * 512:(esl + 1) * 512],
                           in_=ps2[:, 512 * esl:512 * (esl + 1)])
                    cS.one(s.drain())
    vprojd = cS.n

    rp = aC[0:32, 5120:6656]
    for s_ in range(SEQ):
        @blk.sync
        def _(sy, s_=s_):
            sy.wait_ge(cV.h, cV.n)
            dX.dma(sy.dma_start(out=rp[:, 0:256],
                                in_=h_rows[s_, 4, :].unsqueeze(0).broadcast_to([32, KSEL])))
        need = dX.n

        @blk.vector
        def _(v, s_=s_, need=need):
            v.wait_ge(dX.h, need)
            ang = rp[:, 0:256]
            rr = rp[:, 256:512]
            rc = rp[:, 512:768]
            tm = rp[:, 768:1024]
            v.tensor_scalar(out=ang, in0=ang, scalar1=inv32c[0:32, :], scalar2=None,
                            op0=ALU.mult)
            v.tensor_scalar(out=rr, in0=ang, scalar1=1.0 / TWO_PI, scalar2=MAGIC,
                            op0=ALU.mult, op1=ALU.add)
            v.drain()
            v.tensor_scalar(out=rr, in0=rr, scalar1=MAGIC, scalar2=None, op0=ALU.subtract)
            v.drain()
            v.scalar_tensor_tensor(out=rr, in0=rr, scalar=-TWO_PI, in1=ang,
                                   op0=ALU.mult, op1=ALU.add)
            v.tensor_scalar(out=rc, in0=rr, scalar1=TWO_PI / 4, scalar2=None, op0=ALU.add)
            v.tensor_scalar(out=tm, in0=rc, scalar1=TWO_PI / 2, scalar2=None, op0=ALU.is_gt)
            v.drain()
            v.scalar_tensor_tensor(out=rc, in0=tm, scalar=-TWO_PI, in1=rc,
                                   op0=ALU.mult, op1=ALU.add)
            cV.one(v.drain())
        need = cV.n

        @blk.scalar
        def _(s, need=need):
            s.wait_ge(cV.h, need)
            s.activation(out=rp[:, 768:1024], in_=rp[:, 256:512], func=AF.Sin,
                         scale=1.0)
            cS.one(s.activation(out=rp[:, 1024:1280], in_=rp[:, 512:768],
                                func=AF.Sin, scale=1.0))
        need = cS.n

        @blk.vector
        def _(v, need=need):
            v.wait_ge(cS.h, need)
            v.tensor_scalar(out=rp[:, 0:256], in0=rp[:, 768:1024],
                            scalar1=-1.0, scalar2=None, op0=ALU.mult)
            cV.one(v.drain())
        need = cV.n

        @blk.sync
        def _(sy, need=need):
            sy.wait_ge(cV.h, need)
            for rows in range(4):
                dX.dma(sy.dma_start(out=cs_r(0)[rows * 32:(rows + 1) * 32, :],
                                    in_=rp[:, 1024:1280]))
                src = rp[:, 0:256] if rows % 2 == 0 else rp[:, 768:1024]
                dX.dma(sy.dma_start(out=cs_r(1)[rows * 32:(rows + 1) * 32, :], in_=src))
        need = dX.n

        @blk.vector
        def _(v, need=need):
            v.wait_ge(dX.h, need)
            v.tensor_scalar(out=cs_r(2), in0=cs_r(0), scalar1=RSC, scalar2=None,
                            op0=ALU.mult)
            v.tensor_scalar(out=cs_r(3), in0=cs_r(1), scalar1=RSC, scalar2=None,
                            op0=ALU.mult)
            cV.one(v.drain())
        csd = cV.n

        for c in range(DCH):
            @blk.sync
            def _(sy, c=c, s_=s_, csd=csd):
                sy.wait_ge(cV.h, cV.n)
                for (dst, src) in ((scr2(0), q_r(c)), (scr2(1), k_r(c))):
                    for hb2 in range(2):
                        b0 = hb2 * 64
                        dX.dma(sy.dma_start(
                            out=dst[b0:b0 + 32, s_ * 256:(s_ + 1) * 256],
                            in_=src[b0 + 32:b0 + 64, s_ * 256:(s_ + 1) * 256]))
                        dX.dma(sy.dma_start(
                            out=dst[b0 + 32:b0 + 64, s_ * 256:(s_ + 1) * 256],
                            in_=src[b0:b0 + 32, s_ * 256:(s_ + 1) * 256]))
            need = dX.n

            @blk.vector
            def _(v, c=c, s_=s_, need=need):
                v.wait_ge(dX.h, need)
                qs = q_r(c)[:, s_ * 256:(s_ + 1) * 256]
                ks = k_r(c)[:, s_ * 256:(s_ + 1) * 256]
                sh_q = scr2(0)[:, s_ * 256:(s_ + 1) * 256]
                sh_k = scr2(1)[:, s_ * 256:(s_ + 1) * 256]
                tmp = scr2(2)[:, s_ * 256:(s_ + 1) * 256]
                v.tensor_mul(out=tmp, in0=qs, in1=cs_r(2))
                v.tensor_mul(out=sh_q, in0=sh_q, in1=cs_r(3))
                v.tensor_add(out=qs, in0=tmp, in1=sh_q)
                v.tensor_mul(out=tmp, in0=ks, in1=cs_r(0))
                v.tensor_mul(out=sh_k, in0=sh_k, in1=cs_r(1))
                v.tensor_add(out=ks, in0=tmp, in1=sh_k)
                cV.one(v.drain())
    roped = cV.n

    for s_ in range(SEQ):
        for h in range(NH):
            c = h // 2
            hb = (h % 2) * 64
            at0 = scr2(2)
            at1 = scr2(3)
            for qb in range(2):
                @blk.tensor
                def _(t, s_=s_, c=c, hb=hb, qb=qb):
                    t.wait_ge(cV.h, cV.n)
                    _mm = t.matmul(ps1[:, qb * 256:(qb + 1) * 256],
                             q_r(c)[hb:hb + 64,
                                    s_ * 256 + qb * 128:s_ * 256 + (qb + 1) * 128],
                             k_r(c)[hb:hb + 64, s_ * 256:(s_ + 1) * 256],
                             start=True, stop=True)
                    cT.one(_mm)
                need = cT.n

                @blk.vector
                def _(v, s_=s_, qb=qb, need=need):
                    v.wait_ge(cT.h, need)
                    v.tensor_add(out=sc_f(qb), in0=ps1[:, qb * 256:(qb + 1) * 256],
                                 in1=msel_t(s_, qb))
                    v.tensor_reduce(out=Rtmp[:, 1:2], in_=sc_f(qb), axis=AXX, op=ALU.max)
                    v.drain()
                    v.tensor_scalar(out=Rtmp[:, 2:3], in0=Rtmp[:, 1:2], scalar1=-1.0,
                                    scalar2=None, op0=ALU.mult)
                    cV.one(v.drain())
                need = cV.n

                @blk.scalar
                def _(s, qb=qb, need=need):
                    s.wait_ge(cV.h, need)
                    s.activation(out=sqs, in_=sc_f(qb), func=AF.Exp, bias=Rtmp[:, 2:3],
                                 scale=1.0)
                    cS.one(s.drain())
                need = cS.n

                @blk.vector
                def _(v, need=need):
                    v.wait_ge(cS.h, need)
                    v.tensor_reduce(out=Rtmp[:, 3:4], in_=sqs, axis=AXX, op=ALU.add)
                    v.drain()
                    v.reciprocal(out=Rtmp[:, 4:5], in_=Rtmp[:, 3:4])
                    v.drain()
                    v.tensor_scalar(out=sqs, in0=sqs, scalar1=Rtmp[:, 4:5], scalar2=None,
                                    op0=ALU.mult)
                    cV.one(v.drain())
                need = cV.n

                @blk.tensor
                def _(t, qb=qb, need=need):
                    t.wait_ge(cV.h, need)
                    t.transpose(psTb[:, qb * 128:(qb + 1) * 128],
                                sqs[:, 0:128], ident_bf)
                    _mm = t.transpose(psTb[:, 256 + qb * 128:256 + (qb + 1) * 128],
                                sqs[:, 128:256], ident_bf)
                    cT.one(_mm)
                need = cT.n

                @blk.vector
                def _(v, qb=qb, at0=at0, at1=at1, need=need):
                    v.wait_ge(cT.h, need)
                    v.tensor_copy(out=at0[:, qb * 128:(qb + 1) * 128],
                                  in_=psTb[:, qb * 128:(qb + 1) * 128])
                    v.tensor_copy(out=at1[:, qb * 128:(qb + 1) * 128],
                                  in_=psTb[:, 256 + qb * 128:256 + (qb + 1) * 128])
                    cV.one(v.drain())
            need = cV.n

            @blk.tensor
            def _(t, s_=s_, h=h, hb=hb, at0=at0, at1=at1, need=need):
                t.wait_ge(cV.h, need)
                for kb in range(2):
                    at = (at0, at1)[kb]
                    _mm = t.matmul(ps2[hb:hb + 64, 0:256],
                             v_r(s_ * 2 + kb)[:, h * 64:(h + 1) * 64],
                             at[:, 0:256], start=(kb == 0), stop=(kb == 1),
                             tile_position=(0, hb))
                cT.one(_mm)
            need = cT.n

            if h % 2 == 1:
                @blk.scalar
                def _(s, s_=s_, c=c, need=need):
                    s.wait_ge(cT.h, need)
                    s.copy(out=hn_r(c)[:, s_ * 256:(s_ + 1) * 256], in_=ps2[:, 0:256])
                    cS.one(s.drain())
    attnd = cS.n

    wn = load_dd(wo_h)
    for s_ in range(SEQ):
        for e in range(DCH):
            @blk.tensor
            def _(t, s_=s_, e=e, wn=wn):
                t.wait_ge(cV.h, max(wn, cV.n))
                for k in range(DCH):
                    _mm = t.matmul(ps1[:, 256 * (e % 4):256 * (e % 4) + 256],
                             wcur_k(k)[:, e * 128:(e + 1) * 128],
                             hn_r(k)[:, s_ * 256:(s_ + 1) * 256],
                             start=(k == 0), stop=(k == DCH - 1))
                cT.one(_mm)
            need = cT.n

            @blk.vector
            def _(v, s_=s_, e=e, need=need):
                v.wait_ge(cT.h, need)
                v.tensor_add(out=x1_r(e)[:, s_ * 256:(s_ + 1) * 256],
                             in0=ps1[:, 256 * (e % 4):256 * (e % 4) + 256],
                             in1=xsel_t(e)[:, s_ * 256:(s_ + 1) * 256])
                cV.one(v.drain())
    x1d = cV.n

    for s_ in range(SEQ):
        for c in range(DCH):
            @blk.scalar
            def _(s, c=c, s_=s_, x1d=x1d):
                s.wait_ge(cV.h, x1d)
                if not (c == 0 and s_ == 0):
                    s.wait_ge(cT.h, cT.n)
                s.activation(out=sqs, in_=x1_r(c)[:, s_ * 256:(s_ + 1) * 256],
                             func=AF.Square, scale=1.0)
                cS.one(s.drain())
            need = cS.n

            @blk.tensor
            def _(t, c=c, need=need):
                t.wait_ge(cS.h, need)
                _mm = t.matmul(ps1[:, 0:256], ones_bf, sqs, start=(c == 0), stop=(c == DCH - 1))
                cT.one(_mm)
        need = cT.n

        @blk.scalar
        def _(s, need=need):
            s.wait_ge(cT.h, need)
            s.activation(out=rms_b, in_=ps1[:, 0:256], func=AF.Ln, bias=epsrms_c[:, 0:1],
                         scale=1.0 / D)
            s.drain()
            cS.one(s.activation(out=rms_b, in_=rms_b, func=AF.Exp, scale=-0.5))
        need = cS.n

        @blk.vector
        def _(v, s_=s_, need=need):
            v.wait_ge(cS.h, need)
            for c in range(DCH):
                v.scalar_tensor_tensor(out=hn2_r(c)[:, s_ * 256:(s_ + 1) * 256],
                                       in0=x1_r(c)[:, s_ * 256:(s_ + 1) * 256],
                                       scalar=ln2_s[:, c:c + 1], in1=rms_b,
                                       op0=ALU.mult, op1=ALU.mult)
            cV.one(v.drain())
    hn2d = cV.n

    w1bf = aCb[:, 0:32768]

    def w1k(k):
        return w1bf[:, k * 4096:(k + 1) * 4096]

    for k in range(DCH):
        for piece in range(4):
            @blk.sync
            def _(sy, k=k, piece=piece):
                sy.wait_ge(cV.h, cV.n)
                sy.wait_ge(cT.h, cT.n)
                dX.dma(sy.dma_start(out=wst,
                                    in_=w1_h[k * P:(k + 1) * P,
                                             piece * 1024:(piece + 1) * 1024]))
            need = dX.n

            @blk.vector
            def _(v, k=k, piece=piece, need=need):
                v.wait_ge(dX.h, need)
                v.tensor_copy(out=w1k(k)[:, piece * 1024:(piece + 1) * 1024], in_=wst)
                cV.one(v.drain())
    w1ld = cV.n

    for s_ in range(SEQ):
        for f in range(32):
            @blk.tensor
            def _(t, s_=s_, f=f, w1ld=w1ld):
                t.wait_ge(cV.h, w1ld)
                t.wait_ge(cS.h, cS.n)
                for k in range(DCH):
                    _mm = t.matmul(ps1[:, 256 * (f % 4):256 * (f % 4) + 256],
                             w1k(k)[:, f * 128:(f + 1) * 128],
                             hn2_r(k)[:, s_ * 256:(s_ + 1) * 256],
                             start=(k == 0), stop=(k == DCH - 1))
                cT.one(_mm)
            need = cT.n

            @blk.scalar
            def _(s, s_=s_, f=f, need=need):
                s.wait_ge(cT.h, need)
                s.activation(out=h1_r(f, s_),
                             in_=ps1[:, 256 * (f % 4):256 * (f % 4) + 256],
                             func=AF.Gelu_apprx_tanh, scale=1.0)
                cS.one(s.drain())
    h1d = cS.n

    w2bf = aCb[:, 0:32768]

    def w2k(k):
        return w2bf[:, k * 1024:(k + 1) * 1024]

    for k in range(32):
        @blk.sync
        def _(sy, k=k):
            sy.wait_ge(cV.h, cV.n)
            sy.wait_ge(cT.h, cT.n)
            dX.dma(sy.dma_start(out=wst, in_=w2_h[k * P:(k + 1) * P, :]))
        need = dX.n

        @blk.vector
        def _(v, k=k, need=need):
            v.wait_ge(dX.h, need)
            v.tensor_copy(out=w2k(k), in_=wst)
            cV.one(v.drain())
    w2ld = cV.n

    @blk.sync
    def _(sy, need=w2ld):
        sy.wait_ge(cV.h, need)
        for s_ in range(SEQ):
            dX.dma(sy.dma_start(out=wselb(s_),
                                in_=h_rows[s_, 5, :].unsqueeze(0).broadcast_to([P, KSEL])))
    wselbd = dX.n
    corr_b0 = [0]

    for s_ in range(SEQ):
        for e in range(DCH):
            @blk.tensor
            def _(t, s_=s_, e=e, w2ld=w2ld):
                t.wait_ge(cV.h, max(w2ld, cV.n))
                for k in range(32):
                    _mm = t.matmul(ps1[:, 256 * (e % 4):256 * (e % 4) + 256],
                             w2k(k)[:, e * 128:(e + 1) * 128],
                             h1_r(k, s_), start=(k == 0), stop=(k == 31))
                cT.one(_mm)
            need = cT.n

            @blk.vector
            def _(v, s_=s_, e=e, need=need):
                v.wait_ge(cT.h, need)
                v.wait_ge(dX.h, wselbd)
                if s_ == 1 and e == 0:
                    v.wait_ge(dX.h, corr_b0[0])
                x2t = scr2(0)[:, 0:256]
                v.tensor_add(out=x2t, in0=ps1[:, 256 * (e % 4):256 * (e % 4) + 256],
                             in1=x1_r(e)[:, s_ * 256:(s_ + 1) * 256])
                v.drain()
                v.tensor_mul(out=x2t, in0=x2t, in1=wselb(s_))
                v.drain()
                v.tensor_sub(out=corr[:, e * 256:(e + 1) * 256],
                             in0=x2t, in1=xsel_t(e)[:, s_ * 256:(s_ + 1) * 256])
                cV.one(v.drain())
        seq_corr_v = cV.n

        @blk.sync
        def _(sy, s_=s_, need=seq_corr_v):
            sy.wait_ge(cV.h, need)
            dX.dma(sy.dma_start(out=h_corr[s_], in_=corr[:, :]))
        if s_ == 0:
            corr_b0[0] = dX.n
    corrd = cV.n

    if PHASE_LIMIT <= 5:
        _emit_finish()
        return nc

    yac = lambda c: aA[:, c * 1024:(c + 1) * 1024]
    spc6 = lambda c: aA[:, 8192 + c * 1024:8192 + (c + 1) * 1024]
    barA = [aC[:, 0:1024], aC[:, 1024:2048]]
    grep = [aC[:, 2048:3072], aC[:, 3072:4096]]
    crep = [aCb[:, 8192:9216], aCb[:, 9216:10240]]
    hbuf = [aCb[:, 10240:11264], aCb[:, 11264:12288]]
    xh = [aC[:, 6144:7168], aC[:, 7168:8192]]
    outh = [aC[:, 8192:9216], aC[:, 11264:12288]]
    scat = aCb[:, 18432:19456]
    hidx = [aCi16[:, 19456 + i * 256:19456 + (i + 1) * 256] for i in range(4)]
    hstg = aC[:, 12288:13312]
    dtl6 = aCb[:, 20480:22528]

    @blk.sync
    def _(sy, need=corrd):
        sy.wait_ge(cV.h, need)
        sy.wait_ge(cT.h, cT.n)
        dX.dma(sy.dma_start(out=dtl6, in_=h_dtl))
        for s_ in range(SEQ):
            for hf in range(2):
                i = s_ * 2 + hf
                dX.dma(sy.dma_start(out=hstg[:, i * 256:(i + 1) * 256],
                                    in_=h_rows[s_, 1 + hf, :].unsqueeze(0)
                                    .broadcast_to([P, KSEL])))
    need = dX.n

    @blk.vector
    def _(v, need=need):
        v.wait_ge(dX.h, need)
        for i in range(4):
            v.tensor_copy(out=hidx[i], in_=hstg[:, i * 256:(i + 1) * 256])
        cV.one(v.drain())
    hidxd = cV.n

    @blk.gpsimd
    def _(g, need=hidxd):
        g.wait_ge(cV.h, need)
        g.load_library(libcfg.local_scatter)
    lsld = cG.n

    cVs = Ctr(nc, "cVs")
    cSb = Ctr(nc, "cSb")
    corr_rl = [0, 0]

    iglob = 0
    jglob = 0
    gdma = {}
    for s_ in range(SEQ):
        for hf in range(2):
            jh = s_ * 2 + hf
            tlo = s_ * S + hf * HALF

            if hf == 0:
                @blk.sync
                def _(sy, s_=s_):
                    if s_ == 0:
                        sy.wait_ge(cV.h, corrd)
                    else:
                        sy.wait_ge(cG.h, cG.n)
                    dX.dma(sy.dma_start(out=corr[:, :], in_=h_corr[s_]))
                corr_rl[s_] = dX.n

            for rnd in range(2):
                @blk.tensor
                def _(t, s_=s_, hf=hf, rnd=rnd):
                    t.wait_ge(cS.h, cS.n)
                    if rnd == 0:
                        t.wait_ge(cV.h, cV.n)
                        if s_ == 0 and hf == 0:
                            t.wait_ge(dX.h, dX.n)
                        else:
                            t.wait_ge(cVs.h, cVs.n)
                    for ci in range(4):
                        c = rnd * 4 + ci
                        for sl2 in range(2):
                            _mm = t.matmul(pslice(ci * 2 + sl2),
                                     wdt_bf[64 * s_:64 * s_ + 64, c * 128:(c + 1) * 128],
                                     dtl6[64 * s_:64 * s_ + 64,
                                          hf * 1024 + sl2 * 512:hf * 1024 + (sl2 + 1) * 512],
                                     start=True, stop=True)
                    cT.one(_mm)
                need = cT.n

                @blk.scalar
                def _(s, s_=s_, hf=hf, rnd=rnd, need=need):
                    s.wait_ge(cT.h, need)
                    for ci in range(4):
                        c = rnd * 4 + ci
                        for sl2 in range(2):
                            s.activation(out=spc6(c)[:, sl2 * 512:(sl2 + 1) * 512],
                                         in_=pslice(ci * 2 + sl2), func=AF.Exp,
                                         bias=bdt_s[:, c:c + 1], scale=1.0)
                        s.activation(out=spc6(c), in_=spc6(c), func=AF.Ln, bias=1.0,
                                     scale=1.0)
                    cS.one(s.drain())
            spd = cS.n

            @blk.vector
            def _(v, s_=s_, hf=hf, tlo=tlo, spd=spd):
                v.wait_ge(cS.h, spd)
                for c in range(DCH):
                    v.tensor_copy(out=yac(c), in_=u_t(c)[:, tlo:tlo + HALF])
                cV.one(v.drain())
            yinit = cV.n

            for n_ in range(N):
                j = jglob
                jglob += 1

                @blk.sync
                def _(sy, n_=n_, tlo=tlo, j=j, yinit=yinit):
                    if j >= 2:
                        sy.wait_ge(cVs.h, 8 * (j - 1))
                    if j == 0:
                        sy.wait_ge(cV.h, yinit)
                    dGp[j % 2].dma(sy.dma_start(out=grep[n_ % 2],
                                                in_=h_g[n_, tlo:tlo + HALF].unsqueeze(0)
                                                .broadcast_to([P, HALF])))
                    dGp[j % 2].dma(sy.dma_start(out=crep[n_ % 2],
                                                in_=h_cc[n_, tlo:tlo + HALF].unsqueeze(0)
                                                .broadcast_to([P, HALF])))
                gdma[j] = (j % 2, dGp[j % 2].n)

                for c in range(DCH):
                    i = iglob
                    iglob += 1

                    @blk.scalar
                    def _(s, c=c, n_=n_, i=i, spd=spd):
                        if i >= 2:
                            s.wait_ge(cVs.h, i - 1)
                        col = c * 16 + n_
                        cSb.one(s.activation(out=barA[i % 2], in_=spc6(c), func=AF.Exp,
                                              scale=A099[:, col:col + 1],
                                              bias=A001[:, col:col + 1]))

                    @blk.vector
                    def _(v, s_=s_, hf=hf, c=c, n_=n_, i=i, j=j,
                          sbneed=cSb.n, dneed=gdma[j], yinit=yinit):
                        v.wait_ge(cSb.h, sbneed)
                        if c == 0:
                            v.wait_ge(dGp[dneed[0]].h, dneed[1])
                            if j == 0:
                                v.wait_ge(cV.h, yinit)
                        hcol = (s_ * 8 + c) * 16 + n_
                        if hf == 0:
                            v.tensor_tensor_scan(out=hbuf[i % 2], data0=barA[i % 2],
                                                 data1=grep[n_ % 2], initial=0.0,
                                                 op0=ALU.mult, op1=ALU.add)
                            v.drain()
                            v.tensor_copy(out=hstate[:, hcol:hcol + 1],
                                          in_=hbuf[i % 2][:, HALF - 1:HALF])
                        else:
                            v.tensor_tensor_scan(out=hbuf[i % 2], data0=barA[i % 2],
                                                 data1=grep[n_ % 2],
                                                 initial=hstate[:, hcol:hcol + 1],
                                                 op0=ALU.mult, op1=ALU.add)
                        v.tensor_mul(out=hbuf[i % 2], in0=hbuf[i % 2], in1=crep[n_ % 2])
                        v.tensor_add(out=yac(c), in0=yac(c), in1=hbuf[i % 2])
                        cVs.one(v.drain())
            scan_done = cVs.n

            finv = cV.n
            for c in range(DCH):
                @blk.sync
                def _(sy, c=c, tlo=tlo, finv=finv):
                    if c >= 2:
                        sy.wait_ge(cV.h, finv + (c - 1))
                    dHp[c % 2].dma(sy.dma_start(out=xh[c % 2],
                                                in_=xin[c * P:(c + 1) * P, tlo:tlo + HALF]))
                xneed = (c % 2, dHp[c % 2].n)

                @blk.gpsimd
                def _(g, c=c, s_=s_, jh=jh, finv=finv, scan_done=scan_done):
                    if c == 0:
                        g.wait_ge(cVs.h, scan_done)
                    else:
                        g.wait_ge(cV.h, finv + c)
                    if c == 0:
                        g.wait_ge(dX.h, corr_rl[s_])
                    cG.one(g.local_scatter(
                        scat, corr[:, c * 256:(c + 1) * 256],
                        hidx[jh], channels=P, num_elems=HALF, num_idxs=KSEL))
                gneed = cG.n

                @blk.vector
                def _(v, c=c, xneed=xneed, gneed=gneed, scan_done=scan_done):
                    v.wait_ge(dHp[xneed[0]].h, xneed[1])
                    v.wait_ge(cG.h, gneed)
                    v.wait_ge(cVs.h, scan_done)
                    if c >= 2:
                        v.wait_ge(dOp[c % 2].h, dOp[c % 2].n)
                    v.tensor_add(out=outh[c % 2], in0=xh[c % 2], in1=yac(c))
                    v.tensor_add(out=outh[c % 2], in0=outh[c % 2], in1=scat)
                    cV.one(v.drain())
                vneed = cV.n

                @blk.sync
                def _(sy, c=c, tlo=tlo, vneed=vneed):
                    sy.wait_ge(cV.h, vneed)
                    dOp[c % 2].dma(
                        sy.dma_start(out=out_fm[c * P:(c + 1) * P, tlo:tlo + HALF],
                                     in_=outh[c % 2]))

    @blk.sync
    def _(sy):
        sy.wait_ge(dOp[0].h, dOp[0].n)
        sy.wait_ge(dOp[1].h, dOp[1].n)

    blk.__exit__(None, None, None)
    return nc


_NC_CACHE = {}


def _get_nc():
    if "nc" not in _NC_CACHE:
        nc = build_nc()
        lower_extended_insts(nc)
        _NC_CACHE["nc"] = nc
    return _NC_CACHE["nc"]


def kernel(**inputs):
    x = np.asarray(inputs["L0_attn_x"])
    ncores = 8
    eye = np.eye(P, dtype=np.float32)
    ones = np.ones((P, P), dtype=np.float32)
    inv = (1.0 / (10000.0 ** (np.arange(0, HD, 2, dtype=np.float32) / HD))).astype(np.float32)
    cpk = np.zeros((P, 257), dtype=np.float32)
    cpk[:, 0:128] = eye
    cpk[:, 128:256] = ones
    cpk[0:32, 256] = inv
    a_log = np.asarray(inputs["L0_attn_A_log"])
    a_perm = np.ascontiguousarray(
        a_log.reshape(8, 128, 16).transpose(1, 0, 2).reshape(128, 128))

    def c8(vec):
        return np.ascontiguousarray(np.asarray(vec).reshape(8, 128).T)

    mask = np.asarray(inputs["L0_ffn_attention_mask"])[:, 0]
    posi = np.asarray(inputs["L0_ffn_position_ids"])
    base = {
        "w_in": np.asarray(inputs["L0_attn_w_in"]),
        "w_x": np.asarray(inputs["L0_attn_w_x"]),
        "w_dt": np.asarray(inputs["L0_attn_w_dt"]),
        "wq": np.asarray(inputs["L0_ffn_wq"]),
        "wk": np.asarray(inputs["L0_ffn_wk"]),
        "wv": np.asarray(inputs["L0_ffn_wv"]),
        "wo": np.asarray(inputs["L0_ffn_wo"]),
        "w1": np.asarray(inputs["L0_ffn_w1"]),
        "w2": np.asarray(inputs["L0_ffn_w2"]),
        "a_perm": a_perm,
        "bdt8": c8(inputs["L0_attn_b_dt"]),
        "dp8": c8(inputs["L0_attn_Dp"]),
        "ln18": c8(inputs["L0_ffn_ln1"]),
        "ln28": c8(inputs["L0_ffn_ln2"]),
        "rw8": c8(np.asarray(inputs["L0_ffn_router_w"])[:, 0]),
        "rb1": np.asarray(inputs["L0_ffn_router_b"]).reshape(1, 1),
        "cpack": cpk,
    }
    in_maps = []
    for core in range(ncores):
        b0 = 2 * core
        m = dict(base)
        xc = x[b0:b0 + 2]
        m["xin"] = np.ascontiguousarray(xc.transpose(2, 0, 1).reshape(D, T))
        bc0, bc1 = min(b0, 3), min(b0 + 1, 3)
        m["mask0"] = np.ascontiguousarray(mask[bc0, 0:S, 0:S])
        m["mask1"] = np.ascontiguousarray(mask[bc1, 0:S, 0:S])
        m["pos2"] = np.ascontiguousarray(
            np.stack([posi[bc0, 0:S], posi[bc1, 0:S]]).astype(np.int32))
        in_maps.append(m)

    nc = _get_nc()
    res = run_bass_kernel_spmd(nc, in_maps, core_ids=list(range(ncores)))
    outs = []
    for core in range(ncores):
        o = np.asarray(res.results[core]["out_fm"])
        outs.append(o.reshape(D, SEQ, S).transpose(1, 2, 0))
    return np.concatenate(outs, axis=0).astype(np.float32)


# version 03
# speedup vs baseline: 1.0546x; 1.0546x over previous
import numpy as np
import ml_dtypes

import concourse.bass as bass
import concourse.mybir as mybir
import concourse.library_config as libcfg
from concourse.bass import IndirectOffsetOnAxis
from concourse.bass_utils import run_bass_kernel_spmd
from concourse.library_overlay import lower_extended_insts

F32 = mybir.dt.float32
BF16 = mybir.dt.bfloat16
U32 = mybir.dt.uint32
I32 = mybir.dt.int32
I16 = mybir.dt.int16
AF = mybir.ActivationFunctionType
ALU = mybir.AluOpType
AXX = mybir.AxisListType.X

P, D, DCH, S, SEQ = 128, 1024, 8, 2048, 2
T, HALF, N, KSEL, DTRANK, FF, NH, HD = 4096, 1024, 16, 256, 64, 4096, 16, 64
EPS_LN, EPS_RMS, NEG, RSC = 1e-5, 1e-6, -1.0e9, 0.125
TWO_PI = 6.283185307179586
MAGIC = 12582912.0
PHASE_LIMIT = 99
DEBUG_HBM = False


class Ctr:
    def __init__(self, nc, name):
        self.h = nc.alloc_semaphore(name)
        self.n = 0

    def dma(self, instr):
        instr.then_inc(self.h, 16)
        self.n += 16

    def one(self, instr):
        instr.then_inc(self.h, 1)
        self.n += 1

    def seq(self, eng):
        eng.sem_inc(self.h, 1)
        self.n += 1


def build_nc():
    nc = bass.Bass("TRN2", target_bir_lowering=False)

    xin = nc.dram_tensor("xin", [D, T], F32, kind="ExternalInput").ap()
    w_in_h = nc.dram_tensor("w_in", [D, D], F32, kind="ExternalInput").ap()
    w_x_h = nc.dram_tensor("w_x", [D, 96], F32, kind="ExternalInput").ap()
    w_dt_h = nc.dram_tensor("w_dt", [DTRANK, D], F32, kind="ExternalInput").ap()
    wq_h = nc.dram_tensor("wq", [D, D], F32, kind="ExternalInput").ap()
    wk_h = nc.dram_tensor("wk", [D, D], F32, kind="ExternalInput").ap()
    wv_h = nc.dram_tensor("wv", [D, D], F32, kind="ExternalInput").ap()
    wo_h = nc.dram_tensor("wo", [D, D], F32, kind="ExternalInput").ap()
    w1_h = nc.dram_tensor("w1", [D, FF], F32, kind="ExternalInput").ap()
    w2_h = nc.dram_tensor("w2", [FF, D], F32, kind="ExternalInput").ap()
    a_perm = nc.dram_tensor("a_perm", [P, P], F32, kind="ExternalInput").ap()
    bdt8 = nc.dram_tensor("bdt8", [P, DCH], F32, kind="ExternalInput").ap()
    dp8 = nc.dram_tensor("dp8", [P, DCH], F32, kind="ExternalInput").ap()
    ln18 = nc.dram_tensor("ln18", [P, DCH], F32, kind="ExternalInput").ap()
    ln28 = nc.dram_tensor("ln28", [P, DCH], F32, kind="ExternalInput").ap()
    rw8 = nc.dram_tensor("rw8", [P, DCH], F32, kind="ExternalInput").ap()
    rb1 = nc.dram_tensor("rb1", [1, 1], F32, kind="ExternalInput").ap()
    mask_h = [nc.dram_tensor("mask0", [S, S], F32, kind="ExternalInput").ap(),
              nc.dram_tensor("mask1", [S, S], F32, kind="ExternalInput").ap()]
    pos2 = nc.dram_tensor("pos2", [SEQ, S], I32, kind="ExternalInput").ap()
    cpack = nc.dram_tensor("cpack", [P, 257], F32, kind="ExternalInput").ap()
    out_fm = nc.dram_tensor("out_fm", [D, T], F32, kind="ExternalOutput").ap()

    KIND = "ExternalOutput" if DEBUG_HBM else "Internal"
    h_wrow = nc.dram_tensor("h_wrow", [SEQ, 2, S], F32, kind=KIND).ap()
    h_midx = nc.dram_tensor("h_midx", [SEQ, S], F32, kind="Internal").ap()
    h_selw = nc.dram_tensor("h_selw", [SEQ, 256], F32, kind="Internal").ap()
    h_seli = nc.dram_tensor("h_seli", [SEQ, 16, 16], mybir.dt.uint16, kind="Internal").ap()
    h_rows = nc.dram_tensor("h_rows", [SEQ, 6, KSEL], F32, kind=KIND).ap()
    h_g = nc.dram_tensor("h_g", [N, T], F32, kind=KIND).ap()
    h_cc = nc.dram_tensor("h_cc", [N, T], BF16, kind=KIND).ap()
    h_dtl = nc.dram_tensor("h_dtl", [P, S], BF16, kind=KIND).ap()
    h_corr = nc.dram_tensor("h_corr", [SEQ, P, 2048], BF16, kind=KIND).ap()
    h_dbg = nc.dram_tensor("h_dbg", [P, 16384], F32, kind=KIND).ap() if DEBUG_HBM else None

    AR_A = nc.alloc_sbuf_tensor("AR_A", [P, 16384], F32)
    AR_B = nc.alloc_sbuf_tensor("AR_B", [P, 16384], F32)
    AR_C = nc.alloc_sbuf_tensor("AR_C", [P, 16384], F32)
    SMALL = nc.alloc_sbuf_tensor("SMALL", [P, 1792], F32)
    WST = nc.alloc_sbuf_tensor("WST", [P, 1024], F32)
    CORR = nc.alloc_sbuf_tensor("CORR", [P, 2048], BF16)
    SEL = nc.alloc_sbuf_tensor("SEL", [P, 96], mybir.dt.uint16)

    aA, aB, aC = AR_A.ap(), AR_B.ap(), AR_C.ap()
    aAb, aBb, aCb = aA.bitcast(BF16), aB.bitcast(BF16), aC.bitcast(BF16)
    aCi16 = aC.bitcast(I16)
    r1 = aB[0:32, 0:2048]
    r2 = aB[0:32, 2048:4096]
    r2i = r2.bitcast(I32)
    sm = SMALL.ap()
    smb = sm.bitcast(BF16)
    wst = WST.ap()
    corr = CORR.ap()
    sil = SEL.ap()
    slw = aB[0:32, 4096:5632]

    A_f = sm[:, 0:128]
    A099 = sm[:, 128:256]
    A001 = sm[:, 256:384]
    A_bf = smb[:, 768:896]
    bdt_s = sm[:, 448:456]
    dp_s = sm[:, 456:464]
    ln1_s = sm[:, 464:472]
    ln2_s = sm[:, 472:480]
    rw_s = sm[:, 480:488]
    thr2 = sm[:, 488:489]
    rb2 = sm[:, 489:490]
    nRcol = sm[:, 490:491]
    Rtmp = sm[:, 491:499]
    kth_o = sm[:, 500:504]
    w128t = [sm[:, 504:520], sm[:, 520:536]]
    ridf = sm[:, 536:540]
    ridu = sm.bitcast(U32)[:, 540:544]
    psel_raw = sm.bitcast(I32)[:, 544:548]
    psel_f = sm[:, 548:552]
    wsel_raw = sm[:, 552:556]
    inv32c = sm[:, 556:557]
    epsln_c = sm[:, 557:558]
    epsrms_c = sm[:, 558:559]
    identf_st = sm[:, 560:688]
    ones_f = sm[:, 688:816]
    ident_bf = smb[:, 1632:1760]
    ones_bf = smb[:, 1760:1888]
    hstate = sm[:, 944:1200]
    wdt_bf = smb[:, 2400:3424]
    nopv = sm[0:1, 1712:1713]
    nops = sm[0:1, 1713:1714]

    PS1 = nc.alloc_psum_tensor("PS1", [P, 2048], F32)
    PS2 = nc.alloc_psum_tensor("PS2", [P, 2048], F32)
    ps1, ps2 = PS1.ap(), PS2.ap()
    psTb = ps2[:, 1536:2048].bitcast(BF16)

    dX = Ctr(nc, "dX")
    dCst = Ctr(nc, "dCst")
    dXp = [Ctr(nc, "dXp0"), Ctr(nc, "dXp1")]
    dGp = [Ctr(nc, "dGp0"), Ctr(nc, "dGp1")]
    dHp = [Ctr(nc, "dHp0"), Ctr(nc, "dHp1")]
    dOp = [Ctr(nc, "dOp0"), Ctr(nc, "dOp1")]
    dO = Ctr(nc, "dO")
    dGm = Ctr(nc, "dGm")
    cV = Ctr(nc, "cV")
    cS = Ctr(nc, "cS")
    cT = Ctr(nc, "cT")
    cG = Ctr(nc, "cG")

    def xn_t(c):
        return aAb[:, c * 4096:(c + 1) * 4096]

    def u_t(c):
        return aBb[:, c * 4096:(c + 1) * 4096]

    def pslice(sl):
        return (ps1 if sl < 4 else ps2)[:, (sl % 4) * 512:(sl % 4 + 1) * 512]

    blk = nc.Block("main")
    blk.__enter__()

    def _emit_finish():
        @blk.vector
        def _(v):
            v.wait_ge(cS.h, cS.n)
            v.wait_ge(cT.h, cT.n)
            v.wait_ge(cG.h, cG.n)
            v.memset(aC[:, 0:4096], 0.0)
            cV.one(v.drain())

        @blk.sync
        def _(sy):
            sy.wait_ge(dX.h, dX.n)
            sy.wait_ge(cV.h, cV.n)
            sy.wait_ge(dCst.h, dCst.n)
            sy.wait_ge(dGm.h, dGm.n)
            for p in (dXp + dGp + dHp):
                sy.wait_ge(p.h, p.n)
            for c in range(DCH):
                dO.dma(sy.dma_start(out=out_fm[c * P:(c + 1) * P, :], in_=aC[:, 0:4096]))
            sy.wait_ge(dO.h, dO.n)
        blk.__exit__(None, None, None)


    xstage = [aC[:, 0:4096], aC[:, 4096:8192]]
    racc = aC[:, 8192:12288]
    sqb = [aCb[:, 24576:26624], aCb[:, 26624:28672]]

    @blk.sync
    def _(sy):
        for dst, src in [(A_f, a_perm), (bdt_s, bdt8), (dp_s, dp8), (ln1_s, ln18),
                         (ln2_s, ln28), (rw_s, rw8), (rb2[0:1, :], rb1), (rb2[1:2, :], rb1),
                         (identf_st, cpack[:, 0:128]), (ones_f, cpack[:, 128:256]),
                         (inv32c[0:32, :], cpack[0:32, 256:257])]:
            dCst.dma(sy.dma_start(out=dst, in_=src, allow_slow_non_contiguous=True))
    ph1_small = dCst.n

    @blk.gpsimd
    def _(g):
        cG.one(g.iota(r2i[0:2, :], pattern=[[1, 2048]], base=1, channel_multiplier=0))
    iota_done = cG.n

    @blk.scalar
    def _(s):
        s.wait_ge(dCst.h, ph1_small)
        cS.one(s.activation(out=A_f, in_=A_f, func=AF.Exp, scale=1.0))
    aexp = cS.n

    if PHASE_LIMIT <= -1:
        _emit_finish()
        return nc

    vbase_x = cV.n
    for c in range(DCH):
        @blk.sync
        def _(sy, c=c):
            if c >= 2:
                sy.wait_ge(cV.h, vbase_x + (c - 1))
            dXp[c % 2].dma(sy.dma_start(out=xstage[c % 2], in_=xin[c * P:(c + 1) * P, :]))
        need = dXp[c % 2].n

        @blk.vector
        def _(v, c=c, need=need):
            v.wait_ge(dXp[c % 2].h, need)
            if c == 0:
                v.wait_ge(dCst.h, ph1_small)
                v.wait_ge(cG.h, iota_done)
                v.wait_ge(cS.h, aexp)
                v.tensor_copy(out=ident_bf, in_=identf_st)
                v.tensor_copy(out=ones_bf, in_=ones_f)
                v.memset(epsln_c, EPS_LN)
                v.memset(epsrms_c, EPS_RMS)
                v.tensor_copy(out=r2[0:2, :], in_=r2i[0:2, :])
                v.tensor_scalar(out=A_f, in0=A_f, scalar1=-1.0, scalar2=None, op0=ALU.mult)
                v.drain()
                v.tensor_scalar(out=A099, in0=A_f, scalar1=0.099, scalar2=None, op0=ALU.mult)
                v.tensor_scalar(out=A001, in0=A_f, scalar1=0.001, scalar2=None, op0=ALU.mult)
                v.tensor_copy(out=A_bf, in_=A_f)
            v.tensor_copy(out=xn_t(c), in_=xstage[c % 2])
            if c == 0:
                v.scalar_tensor_tensor(out=racc, in0=xstage[0], scalar=rw_s[:, 0:1],
                                       in1=xstage[0], op0=ALU.mult, op1=ALU.bypass)
            else:
                v.scalar_tensor_tensor(out=racc, in0=xstage[c % 2], scalar=rw_s[:, c:c + 1],
                                       in1=racc, op0=ALU.mult, op1=ALU.add)
            cV.one(v.drain())
    xstream = cV.n


    if PHASE_LIMIT <= 0:
        _emit_finish()
        return nc

    m_b = aC[:, 0:2048]
    r_b = aC[:, 2048:4096]
    m_bf = aCb[:, 8192:10240]
    r_bf = aCb[:, 10240:12288]
    msq = aC[:, 6144:8192]
    w_bc = aC[:, 14336:16384]

    for s_ in range(SEQ):
        col0 = s_ * S
        for c in range(DCH):
            @blk.scalar
            def _(s, c=c, s_=s_):
                if s_ == 0 and c == 0:
                    s.wait_ge(cV.h, xstream)
                if not (s_ == 0 and c in (0, 1)):
                    s.wait_ge(cT.h, cT.n)
                s.activation(out=sqb[c % 2], in_=xn_t(c)[:, col0:col0 + S],
                             func=AF.Square, scale=1.0)
                cS.one(s.drain())
            need = cS.n

            @blk.tensor
            def _(t, c=c, s_=s_, need=need):
                t.wait_ge(cS.h, need)
                if s_ == 1 and c == 0:
                    t.wait_ge(cV.h, cV.n)
                for sl in range(4):
                    t.matmul(ps1[:, sl * 512:(sl + 1) * 512], ones_bf,
                             xn_t(c)[:, col0 + sl * 512:col0 + (sl + 1) * 512],
                             start=(c == 0), stop=(c == DCH - 1))
                for sl in range(4):
                    _mm = t.matmul(ps2[:, sl * 512:(sl + 1) * 512], ones_bf,
                             sqb[c % 2][:, sl * 512:(sl + 1) * 512],
                             start=(c == 0), stop=(c == DCH - 1))
                cT.one(_mm)
        smm = cT.n

        if PHASE_LIMIT <= 0.2:
            _emit_finish()
            return nc

        @blk.vector
        def _(v, need=smm):
            v.wait_ge(cT.h, need)
            v.tensor_scalar(out=m_b, in0=ps1[:, 0:2048], scalar1=1.0 / D, scalar2=None,
                            op0=ALU.mult)
            cV.one(v.drain())
        mdone = cV.n

        if DEBUG_HBM:
            @blk.sync
            def _(sy, s_=s_, need=mdone):
                sy.wait_ge(cV.h, need)
                if s_ == 0:
                    dX.dma(sy.dma_start(out=h_dbg[:, 12416:14464], in_=m_b))
                sy.wait_ge(dX.h, dX.n)

        @blk.scalar
        def _(s, need=mdone):
            s.wait_ge(cV.h, need)
            if DEBUG_HBM:
                s.wait_ge(dX.h, dX.n)
            cS.one(s.activation(out=msq, in_=m_b, func=AF.Square, scale=1.0))
        msqd = cS.n

        @blk.vector
        def _(v, need=msqd):
            v.wait_ge(cS.h, need)
            v.scalar_tensor_tensor(out=r_b, in0=ps2[:, 0:2048], scalar=1.0 / D, in1=msq,
                                   op0=ALU.mult, op1=ALU.subtract)
            cV.one(v.drain())
        vard = cV.n

        @blk.scalar
        def _(s, need=vard):
            s.wait_ge(cV.h, need)
            s.activation(out=r_b, in_=r_b, func=AF.Ln, bias=epsln_c[:, 0:1], scale=1.0)
            cS.one(s.activation(out=r_b, in_=r_b, func=AF.Exp, scale=-0.5))
        rstdd = cS.n

        if DEBUG_HBM:
            @blk.sync
            def _(sy, s_=s_, need=rstdd):
                sy.wait_ge(cS.h, need)
                dX.dma(sy.dma_start(out=h_dbg[:, 8192 + s_ * 2048:8192 + (s_ + 1) * 2048],
                                    in_=r_b))

        @blk.vector
        def _(v, s_=s_, need=rstdd):
            v.wait_ge(cS.h, need)
            if DEBUG_HBM:
                v.wait_ge(dX.h, dX.n)
            v.tensor_copy(out=m_bf, in_=m_b)
            v.tensor_copy(out=r_bf, in_=r_b)
            for c in range(DCH):
                xseg = xn_t(c)[:, col0:col0 + S]
                v.tensor_sub(out=sqb[0], in0=xseg, in1=m_bf)
                v.tensor_mul(out=xseg, in0=sqb[0], in1=r_bf)
            cV.one(v.drain())
        xnd = cV.n

        if PHASE_LIMIT <= 0.4:
            _emit_finish()
            return nc

        @blk.tensor
        def _(t, s_=s_, need=xnd):
            t.wait_ge(cV.h, need)
            t.matmul(ps2[:, 0:8], ones_f, rw_s, start=True, stop=True)
            for sl in range(4):
                _mm = t.matmul(ps1[:, sl * 512:(sl + 1) * 512], ones_f,
                         racc[:, col0 + sl * 512:col0 + (sl + 1) * 512],
                         start=True, stop=True)
            cT.one(_mm)
        rxd = cT.n

        @blk.vector
        def _(v, s_=s_, need=rxd):
            v.wait_ge(cT.h, need)
            if DEBUG_HBM:
                v.tensor_copy(out=msq, in_=ps1[:, 0:2048])
            v.tensor_reduce(out=Rtmp[:, 0:1], in_=ps2[:, 0:8], axis=AXX, op=ALU.add)
            v.drain()
            v.tensor_scalar(out=nRcol, in0=Rtmp[:, 0:1], scalar1=-1.0, scalar2=None,
                            op0=ALU.mult)
            v.drain()
            v.scalar_tensor_tensor(out=w_bc, in0=m_b, scalar=nRcol[:, 0:1],
                                   in1=ps1[:, 0:2048], op0=ALU.mult, op1=ALU.add)
            v.drain()
            v.tensor_mul(out=w_bc, in0=w_bc, in1=r_b)
            cV.one(v.drain())
        wd = cV.n

        if DEBUG_HBM:
            @blk.sync
            def _(sy, s_=s_, need=wd):
                sy.wait_ge(cV.h, need)
                dX.dma(sy.dma_start(out=h_dbg[:, 4096 + s_ * 2048:4096 + (s_ + 1) * 2048],
                                    in_=msq))
                if s_ == 0:
                    dX.dma(sy.dma_start(out=h_dbg[:, 0:2048], in_=w_bc))
                    dX.dma(sy.dma_start(out=h_dbg[:, 2048:2056], in_=sm[:, 488:496]))
                    dX.dma(sy.dma_start(out=h_dbg[:, 12288:12416], in_=ones_f))
                sy.wait_ge(dX.h, dX.n)

        if PHASE_LIMIT <= 0.6:
            _emit_finish()
            return nc

        @blk.sync
        def _(sy, s_=s_, need=wd):
            sy.wait_ge(cV.h, need)
            dX.dma(sy.dma_start(out=h_wrow[s_, 0, :], in_=w_bc[0:1, :]))
            sy.wait_ge(dX.h, dX.n)
            dX.dma(sy.dma_start(out=w128t[s_],
                                in_=h_wrow[s_, 0, :].rearrange("(p f) -> p f", p=P)))
    wwrap = dX.n

    if PHASE_LIMIT <= 0.8:
        _emit_finish()
        return nc

    @blk.gpsimd
    def _(g, need=wwrap):
        g.wait_ge(dX.h, need)
        g.load_library(libcfg.attn)
        q = 1.0 - 254.5 / (S - 1)
        g.kth_largest(kth_o[0:1, 0:2], w128t[0], n_per_lane=S // P,
                      k=KSEL + 2, quantile=q)
        cG.one(g.kth_largest(kth_o[0:1, 2:4], w128t[1], n_per_lane=S // P,
                             k=KSEL + 2, quantile=q))
    kthd = cG.n

    @blk.sync
    def _(sy, need=kthd):
        sy.wait_ge(cG.h, need)
        dX.dma(sy.dma_start(out=thr2[0:1, :], in_=kth_o[0:1, 1:2]))
        dX.dma(sy.dma_start(out=thr2[1:2, :], in_=kth_o[0:1, 3:4]))
        dX.dma(sy.dma_start(out=r1[0:2, :], in_=h_wrow[:, 0, :]))
    thrd = dX.n

    @blk.vector
    def _(v, need=thrd):
        v.wait_ge(dX.h, need)
        v.scalar_tensor_tensor(out=r2[0:2, :], in0=r1[0:2, :], scalar=thr2[0:2, :],
                               in1=r2[0:2, :], op0=ALU.is_gt, op1=ALU.mult)
        v.tensor_scalar(out=r2[0:2, :], in0=r2[0:2, :], scalar1=1.0, scalar2=None,
                        op0=ALU.subtract)
        v.scalar_tensor_tensor(out=r1[0:2, :], in0=r1[0:2, :], scalar=rb2[0:2, :],
                               in1=r1[0:2, :], op0=ALU.add, op1=ALU.bypass)
        cV.one(v.drain())
    midxd = cV.n

    @blk.sync
    def _(sy, need=midxd):
        sy.wait_ge(cV.h, need)
        dX.dma(sy.dma_start(out=h_midx, in_=r2[0:2, :]))
        dX.dma(sy.dma_start(out=h_wrow[0, 1, :], in_=r1[0:1, :]))
        dX.dma(sy.dma_start(out=h_wrow[1, 1, :], in_=r1[1:2, :]))
    mbounce = dX.n

    gp_sparse_base = cG.n
    for s_ in range(SEQ):
        @blk.sync
        def _(sy, s_=s_, need=mbounce):
            sy.wait_ge(dX.h, need)
            if s_ == 1:
                sy.wait_ge(cG.h, gp_sparse_base + 1)
            dX.dma(sy.dma_start(out=slw[0:16, 0:128],
                                in_=h_midx[s_].rearrange("(f p) -> p f", p=16),
                                allow_slow_non_contiguous=True))
        wdone = dX.n

        @blk.gpsimd
        def _(g, s_=s_, need=wdone):
            if s_ == 0:
                g.load_library(libcfg.sparse_gather)
            g.wait_ge(dX.h, need)
            nf_u = sil.bitcast(U32)[0:1, 44:45]
            cG.one(g.sparse_gather(out=slw[0:16, 128 + 16 * s_:144 + 16 * s_],
                                   in_=slw[0:16, 0:128], num_found=nf_u))
    sparsed = cG.n

    @blk.vector
    def _(v, need=sparsed):
        v.wait_ge(cG.h, need)
        v.tensor_scalar(out=slw[0:16, 128:160], in0=slw[0:16, 128:160], scalar1=0.0,
                        scalar2=2047.0, op0=ALU.max, op1=ALU.min)
        v.drain()
        v.tensor_copy(out=sil[0:16, 0:32], in_=slw[0:16, 128:160])
        cV.one(v.drain())
    selwd = cV.n

    @blk.sync
    def _(sy, need=selwd):
        sy.wait_ge(cV.h, need)
        for s_ in range(SEQ):
            dX.dma(sy.dma_start(out=h_selw[s_].rearrange("(f p) -> p f", p=16),
                                in_=slw[0:16, 128 + 16 * s_:144 + 16 * s_],
                                allow_slow_non_contiguous=True))
            dX.dma(sy.dma_start(out=h_seli[s_], in_=sil[0:16, 16 * s_:16 * (s_ + 1)]))
        sy.wait_ge(dX.h, dX.n)
        dX.dma(sy.dma_start(out=r2[0:2, 0:256], in_=h_selw))
        for s_ in range(SEQ):
            for grp in range(8):
                dX.dma(sy.dma_start(out=sil[grp * 16:(grp + 1) * 16,
                                            32 + 16 * s_:48 + 16 * s_],
                                    in_=h_seli[s_]))
    lind = dX.n

    @blk.vector
    def _(v, need=lind):
        v.wait_ge(dX.h, need)
        v.memset(r2[0:2, 255:256], -1.0)
        v.drain()
        v.tensor_scalar(out=r2[0:2, 0:256], in0=r2[0:2, 0:256], scalar1=-1.0,
                        scalar2=2047.0, op0=ALU.max, op1=ALU.min)
        v.drain()
        v.tensor_scalar(out=r2[0:2, 512:768], in0=r2[0:2, 0:256], scalar1=1024.0,
                        scalar2=None, op0=ALU.subtract)
        v.tensor_scalar(out=r2[0:2, 768:1024], in0=r2[0:2, 0:256], scalar1=1024.0,
                        scalar2=None, op0=ALU.is_ge)
        v.drain()
        v.scalar_tensor_tensor(out=r2[0:2, 768:1024], in0=r2[0:2, 768:1024],
                               scalar=-2048.0, in1=r2[0:2, 0:256],
                               op0=ALU.mult, op1=ALU.add)
        v.tensor_scalar(out=r2[0:2, 1024:1280], in0=r2[0:2, 0:256], scalar1=0.0,
                        scalar2=None, op0=ALU.max)
        cV.one(v.drain())
    halvd = cV.n

    @blk.sync
    def _(sy, need=halvd):
        sy.wait_ge(cV.h, need)
        for s_ in range(SEQ):
            dX.dma(sy.dma_start(out=h_rows[s_, 0, :], in_=r2[s_:s_ + 1, 0:256]))
            dX.dma(sy.dma_start(out=h_rows[s_, 1, :], in_=r2[s_:s_ + 1, 768:1024]))
            dX.dma(sy.dma_start(out=h_rows[s_, 2, :], in_=r2[s_:s_ + 1, 512:768]))
            dX.dma(sy.dma_start(out=h_rows[s_, 3, :], in_=r2[s_:s_ + 1, 1024:1280]))
        sy.wait_ge(dX.h, dX.n)
        for s_ in range(SEQ):
            for b_ in range(2):
                cc = 2 * s_ + b_
                dX.dma(sy.dma_start(out=ridf[:, cc:cc + 1],
                                    in_=h_rows[s_, 3, b_ * 128:(b_ + 1) * 128].unsqueeze(-1)))
    ridd = dX.n

    @blk.vector
    def _(v, need=ridd):
        v.wait_ge(dX.h, need)
        v.tensor_copy(out=ridu, in_=ridf)
        cV.one(v.drain())
    ridc = cV.n

    @blk.gpsimd
    def _(g, need=ridc):
        g.wait_ge(cV.h, need)
        for s_ in range(SEQ):
            for b_ in range(2):
                cc = 2 * s_ + b_
                dGm.dma(g.indirect_dma_start(
                    out=psel_raw[:, cc:cc + 1], out_offset=None,
                    in_=pos2.rearrange("s t -> (s t)").unsqueeze(-1),
                    in_offset=IndirectOffsetOnAxis(ap=ridu[:, cc:cc + 1], axis=0),
                    element_offset=s_ * S))
                dGm.dma(g.indirect_dma_start(
                    out=wsel_raw[:, cc:cc + 1], out_offset=None,
                    in_=h_wrow.rearrange("s two t -> (s two t)").unsqueeze(-1),
                    in_offset=IndirectOffsetOnAxis(ap=ridu[:, cc:cc + 1], axis=0),
                    element_offset=(s_ * 2 + 1) * S))
        cG.seq(g)
    pseld = cG.n
    pseld_dma = dGm.n

    @blk.vector
    def _(v, need=pseld, needd=pseld_dma):
        v.wait_ge(cG.h, need)
        v.wait_ge(dGm.h, needd)
        v.tensor_copy(out=psel_f, in_=psel_raw)
        cV.one(v.drain())
    pselc = cV.n

    @blk.sync
    def _(sy, need=pselc):
        sy.wait_ge(cV.h, need)
        for s_ in range(SEQ):
            for b_ in range(2):
                cc = 2 * s_ + b_
                dX.dma(sy.dma_start(out=h_rows[s_, 4, b_ * 128:(b_ + 1) * 128].unsqueeze(-1),
                                    in_=psel_f[:, cc:cc + 1]))
                dX.dma(sy.dma_start(out=h_rows[s_, 5, b_ * 128:(b_ + 1) * 128].unsqueeze(-1),
                                    in_=wsel_raw[:, cc:cc + 1]))
    rows_done = dX.n

    if PHASE_LIMIT <= 1:
        _emit_finish()
        return nc

    w_in_bf = aCb[:, 0:8192]

    def winbf(k):
        return w_in_bf[:, k * 1024:(k + 1) * 1024]

    for k in range(DCH):
        @blk.sync
        def _(sy, k=k):
            sy.wait_ge(cV.h, cV.n)
            if k == 0:
                sy.wait_ge(dX.h, rows_done)
            dX.dma(sy.dma_start(out=wst, in_=w_in_h[k * P:(k + 1) * P, :]))
        need = dX.n

        @blk.vector
        def _(v, k=k, need=need):
            v.wait_ge(dX.h, need)
            v.tensor_copy(out=winbf(k), in_=wst)
            cV.one(v.drain())
    winc = cV.n

    for e in range(DCH):
        @blk.tensor
        def _(t, e=e, need=winc):
            t.wait_ge(cV.h, need)
            if e > 0:
                t.wait_ge(cS.h, cS.n)
            for k in range(DCH):
                for sl in range(8):
                    _mm = t.matmul(pslice(sl), winbf(k)[:, e * 128:(e + 1) * 128],
                             xn_t(k)[:, sl * 512:(sl + 1) * 512],
                             start=(k == 0), stop=(k == DCH - 1))
            cT.one(_mm)
        need = cT.n

        @blk.scalar
        def _(s, e=e, need=need):
            s.wait_ge(cT.h, need)
            for sl in range(8):
                s.copy(out=u_t(e)[:, sl * 512:(sl + 1) * 512], in_=pslice(sl))
            cS.one(s.drain())
    xind = cS.n

    if PHASE_LIMIT <= 2:
        _emit_finish()
        return nc

    B0 = aC[0:16, 0:4096]
    C0 = aC[32:48, 0:4096]
    zac = aC[0:16, 8192:12288]
    g0 = B0
    sp_c = aC[:, 4096:8192]
    dt_c = sp_c
    du_c = aCb[:, 24576:28672]
    dtl2 = aCb[:, 28672:30720]
    Cbf = aCb[32:48, 8192:12288]
    wxs = wst.bitcast(BF16)[:, 0:768]

    @blk.sync
    def _(sy, need=xind):
        sy.wait_ge(cS.h, need)
        sy.wait_ge(cV.h, cV.n)
        dX.dma(sy.dma_start(out=wst[0:64, 0:1024], in_=w_dt_h))
        dX.dma(sy.dma_start(out=wst[64:128, 0:1024], in_=w_dt_h))
    need = dX.n

    @blk.vector
    def _(v, need=need):
        v.wait_ge(dX.h, need)
        v.tensor_copy(out=wdt_bf, in_=wst[:, 0:1024])
        cV.one(v.drain())
    wdtc = cV.n

    for k in range(DCH):
        @blk.sync
        def _(sy, k=k):
            sy.wait_ge(cV.h, cV.n)
            dX.dma(sy.dma_start(out=wst[:, 512:608], in_=w_x_h[k * P:(k + 1) * P, :]))
        need = dX.n

        @blk.vector
        def _(v, k=k, need=need):
            v.wait_ge(dX.h, need)
            v.tensor_copy(out=wxs[:, k * 96:(k + 1) * 96], in_=wst[:, 512:608])
            cV.one(v.drain())
    wxc = cV.n

    def proj(mlo, mhi, evac):
        @blk.tensor
        def _(t, mlo=mlo, mhi=mhi):
            t.wait_ge(cV.h, cV.n)
            mm = mhi - mlo
            for k in range(DCH):
                for sl in range(8):
                    _mm = t.matmul(pslice(sl)[0:mm, :], wxs[:, k * 96 + mlo:k * 96 + mhi],
                             u_t(k)[:, sl * 512:(sl + 1) * 512],
                             start=(k == 0), stop=(k == DCH - 1))
            cT.one(_mm)
        need = cT.n

        @blk.vector
        def _(v, evac=evac, need=need):
            v.wait_ge(cT.h, need)
            evac(v)
            cV.one(v.drain())

    def ev_dt(v):
        for sl in range(8):
            s2 = sl // 4
            cb = (sl % 4) * 512
            v.tensor_copy(out=dtl2[64 * s2:64 * (s2 + 1),
                                   (sl % 4) * 512:(sl % 4 + 1) * 512]
                          if False else dtl2[64 * (sl // 4):64 * (sl // 4) + 64,
                                             (sl % 4) * 512:(sl % 4) * 512 + 512],
                          in_=pslice(sl)[0:64, :])

    def ev_B(v):
        for sl in range(8):
            v.tensor_copy(out=B0[:, sl * 512:(sl + 1) * 512], in_=pslice(sl)[0:16, :])

    def ev_C(v):
        for sl in range(8):
            v.tensor_copy(out=C0[:, sl * 512:(sl + 1) * 512], in_=pslice(sl)[0:16, :])

    proj(0, 64, ev_dt)
    proj(64, 80, ev_B)
    proj(80, 96, ev_C)
    projd = cV.n

    @blk.sync
    def _(sy, need=projd):
        sy.wait_ge(cV.h, need)
        dX.dma(sy.dma_start(out=h_dtl, in_=dtl2))
    dtl_saved = dX.n

    for c in range(DCH):
        @blk.tensor
        def _(t, c=c):
            t.wait_ge(cV.h, cV.n)
            t.wait_ge(cS.h, cS.n)
            for sl in range(8):
                rb_ = 64 * (sl // 4)
                _mm = t.matmul(pslice(sl), wdt_bf[rb_:rb_ + 64, c * 128:(c + 1) * 128],
                         dtl2[rb_:rb_ + 64,
                              (sl % 4) * 512:(sl % 4) * 512 + 512],
                         start=True, stop=True)
            cT.one(_mm)
        need = cT.n

        @blk.scalar
        def _(s, c=c, need=need):
            s.wait_ge(cT.h, need)
            for sl in range(8):
                s.activation(out=sp_c[:, sl * 512:(sl + 1) * 512], in_=pslice(sl),
                             func=AF.Exp, bias=bdt_s[:, c:c + 1], scale=1.0)
            s.activation(out=sp_c, in_=sp_c, func=AF.Ln, bias=1.0, scale=1.0)
            cS.one(s.drain())
        need = cS.n

        @blk.vector
        def _(v, c=c, need=need):
            v.wait_ge(cS.h, need)
            v.tensor_scalar(out=dt_c, in0=sp_c, scalar1=0.099, scalar2=0.001,
                            op0=ALU.mult, op1=ALU.add)
            v.tensor_mul(out=du_c, in0=dt_c, in1=u_t(c))
            cV.one(v.drain())
        need = cV.n

        @blk.tensor
        def _(t, c=c, need=need):
            t.wait_ge(cV.h, need)
            for sl in range(8):
                _mm = t.matmul(pslice(sl)[0:16, :], A_bf[:, c * 16:(c + 1) * 16],
                         du_c[:, sl * 512:(sl + 1) * 512], start=True, stop=True)
            cT.one(_mm)
        need = cT.n

        @blk.vector
        def _(v, c=c, need=need):
            v.wait_ge(cT.h, need)
            for half in range(2):
                src = (ps1 if half == 0 else ps2)[0:16, 0:2048]
                if c == 0:
                    v.tensor_copy(out=zac[:, half * 2048:(half + 1) * 2048], in_=src)
                else:
                    v.tensor_add(out=zac[:, half * 2048:(half + 1) * 2048],
                                 in0=zac[:, half * 2048:(half + 1) * 2048], in1=src)
            v.scalar_tensor_tensor(out=u_t(c), in0=u_t(c), scalar=dp_s[:, c:c + 1],
                                   in1=xn_t(c), op0=ALU.mult, op1=ALU.add)
            cV.one(v.drain())
    zdone = cV.n

    @blk.vector
    def _(v, need=zdone):
        v.wait_ge(cV.h, need)
        v.tensor_mul(out=g0, in0=B0, in1=zac)
        v.tensor_copy(out=Cbf, in_=C0)
        cV.one(v.drain())
    gdone = cV.n

    @blk.sync
    def _(sy, need=gdone):
        sy.wait_ge(cV.h, need)
        dX.dma(sy.dma_start(out=h_g, in_=g0))
        dX.dma(sy.dma_start(out=h_cc, in_=Cbf))
    gsaved = dX.n

    if PHASE_LIMIT <= 3:
        _emit_finish()
        return nc

    def xsel_t(c):
        return aAb[:, c * 512:(c + 1) * 512]

    @blk.gpsimd
    def _(g, need=gdone):
        g.wait_ge(cV.h, need)
        g.wait_ge(dX.h, rows_done)
        last = None
        for s_ in range(SEQ):
            for c in range(DCH):
                last = g.indirect_copy(
                    out=xsel_t(c)[:, s_ * 256:(s_ + 1) * 256].unsqueeze(-1),
                    data=xn_t(c)[:, s_ * S:(s_ + 1) * S].unsqueeze(-1),
                    idxs=sil[:, 32 + 16 * s_:48 + 16 * s_],
                    i_know_ap_gather_is_preferred=True)
        cG.one(last)
    xseld = cG.n

    mrow = aC[:, 0:2048]
    mcolf = aC[:, 2048:2304]

    def msel_t(s_, b_):
        i = 2 * s_ + b_
        return aCb[:, 13312 + i * 256:13312 + (i + 1) * 256]

    for s_ in range(SEQ):
        for b_ in range(2):
            cc = 2 * s_ + b_

            @blk.gpsimd
            def _(g, s_=s_, b_=b_, cc=cc, gsaved=gsaved):
                g.wait_ge(cV.h, cV.n)
                g.wait_ge(dX.h, gsaved)
                dGm.dma(g.indirect_dma_start(
                    out=mrow, out_offset=None,
                    in_=mask_h[s_][:, 0:S],
                    in_offset=IndirectOffsetOnAxis(ap=ridu[:, cc:cc + 1], axis=0)))
                g.wait_ge(dGm.h, dGm.n)
                cG.one(g.indirect_copy(
                    out=mcolf.unsqueeze(-1), data=mrow.unsqueeze(-1),
                    idxs=sil[:, 32 + 16 * s_:48 + 16 * s_],
                    i_know_ap_gather_is_preferred=True))
            need = cG.n

            @blk.vector
            def _(v, s_=s_, b_=b_, need=need):
                v.wait_ge(cG.h, need)
                v.tensor_copy(out=msel_t(s_, b_), in_=mcolf)
                v.memset(msel_t(s_, b_)[:, 255:256], NEG)
                cV.one(v.drain())
    mseld = cV.n

    if PHASE_LIMIT <= 4:
        _emit_finish()
        return nc

    hn_r = lambda c: aAb[:, 4096 + c * 512:4096 + (c + 1) * 512]
    q_r = lambda c: aAb[:, 8192 + c * 512:8192 + (c + 1) * 512]
    k_r = lambda c: aAb[:, 12288 + c * 512:12288 + (c + 1) * 512]
    v_r = lambda i: aAb[:, 16384 + i * 1024:16384 + (i + 1) * 1024]
    scr2 = lambda i: aAb[:, 20480 + i * 512:20480 + (i + 1) * 512]
    cs_r = lambda i: aA[:, 11264 + i * 256:11264 + (i + 1) * 256]
    x1_r = lambda c: aAb[:, 24576 + c * 512:24576 + (c + 1) * 512]
    hn2_r = lambda c: aAb[:, 28672 + c * 512:28672 + (c + 1) * 512]
    h1_r = lambda f, s_: aAb[:, 4096 + (f * 2 + s_) * 256:4096 + (f * 2 + s_ + 1) * 256]
    wselb = lambda s_: aA[:, 10752 + s_ * 256:10752 + (s_ + 1) * 256]
    rms_b = aC[:, 4096:4352]
    sqs = aCb[:, 8704:8960]
    sc_f = lambda qb: aC[:, 4608 + qb * 256:4608 + (qb + 1) * 256]

    for s_ in range(SEQ):
        for c in range(DCH):
            @blk.scalar
            def _(s, c=c, s_=s_):
                if c == 0 and s_ == 0:
                    s.wait_ge(cV.h, mseld)
                    s.wait_ge(cG.h, xseld)
                else:
                    s.wait_ge(cT.h, cT.n)
                s.activation(out=sqs, in_=xsel_t(c)[:, s_ * 256:(s_ + 1) * 256],
                             func=AF.Square, scale=1.0)
                cS.one(s.drain())
            need = cS.n

            @blk.tensor
            def _(t, c=c, need=need):
                t.wait_ge(cS.h, need)
                _mm = t.matmul(ps1[:, 0:256], ones_bf, sqs, start=(c == 0), stop=(c == DCH - 1))
                cT.one(_mm)
        need = cT.n

        @blk.scalar
        def _(s, need=need):
            s.wait_ge(cT.h, need)
            s.activation(out=rms_b, in_=ps1[:, 0:256], func=AF.Ln, bias=epsrms_c[:, 0:1],
                         scale=1.0 / D)
            s.drain()
            cS.one(s.activation(out=rms_b, in_=rms_b, func=AF.Exp, scale=-0.5))
        need = cS.n

        @blk.vector
        def _(v, s_=s_, need=need):
            v.wait_ge(cS.h, need)
            for c in range(DCH):
                v.scalar_tensor_tensor(out=hn_r(c)[:, s_ * 256:(s_ + 1) * 256],
                                       in0=xsel_t(c)[:, s_ * 256:(s_ + 1) * 256],
                                       scalar=ln1_s[:, c:c + 1], in1=rms_b,
                                       op0=ALU.mult, op1=ALU.mult)
            cV.one(v.drain())
    hnd = cV.n

    wcur = aCb[:, 0:8192]

    def wcur_k(k):
        return wcur[:, k * 1024:(k + 1) * 1024]

    def load_dd(w_hbm):
        for k in range(DCH):
            @blk.sync
            def _(sy, k=k):
                sy.wait_ge(cV.h, cV.n)
                sy.wait_ge(cT.h, cT.n)
                dX.dma(sy.dma_start(out=wst, in_=w_hbm[k * P:(k + 1) * P, :]))
            need = dX.n

            @blk.vector
            def _(v, k=k, need=need):
                v.wait_ge(dX.h, need)
                v.tensor_copy(out=wcur_k(k), in_=wst)
                cV.one(v.drain())
        return cV.n

    def proj_fm(dst_r, wneed):
        for s_ in range(SEQ):
            for e in range(DCH):
                @blk.tensor
                def _(t, s_=s_, e=e, wneed=wneed):
                    t.wait_ge(cV.h, wneed)
                    t.wait_ge(cS.h, cS.n)
                    for k in range(DCH):
                        _mm = t.matmul(ps1[:, 256 * (e % 4):256 * (e % 4) + 256],
                                 wcur_k(k)[:, e * 128:(e + 1) * 128],
                                 hn_r(k)[:, s_ * 256:(s_ + 1) * 256],
                                 start=(k == 0), stop=(k == DCH - 1))
                    cT.one(_mm)
                need = cT.n

                @blk.scalar
                def _(s, s_=s_, e=e, dst_r=dst_r, need=need):
                    s.wait_ge(cT.h, need)
                    s.copy(out=dst_r(e)[:, s_ * 256:(s_ + 1) * 256],
                           in_=ps1[:, 256 * (e % 4):256 * (e % 4) + 256])
                    cS.one(s.drain())

    wn = load_dd(wq_h)
    proj_fm(q_r, wn)
    wn = load_dd(wk_h)
    proj_fm(k_r, wn)

    wn = load_dd(wv_h)
    for s_ in range(SEQ):
        for tb in range(2):
            for esl in range(2):
                @blk.tensor
                def _(t, s_=s_, tb=tb, esl=esl, wn=wn):
                    t.wait_ge(cV.h, wn)
                    t.wait_ge(cS.h, cS.n)
                    for k in range(DCH):
                        _mm = t.matmul(ps2[:, 512 * esl:512 * (esl + 1)],
                                 hn_r(k)[:, s_ * 256 + tb * 128:s_ * 256 + (tb + 1) * 128],
                                 wcur_k(k)[:, esl * 512:(esl + 1) * 512],
                                 start=(k == 0), stop=(k == DCH - 1))
                    cT.one(_mm)
                need = cT.n

                @blk.scalar
                def _(s, s_=s_, tb=tb, esl=esl, need=need):
                    s.wait_ge(cT.h, need)
                    s.copy(out=v_r(s_ * 2 + tb)[:, esl * 512:(esl + 1) * 512],
                           in_=ps2[:, 512 * esl:512 * (esl + 1)])
                    cS.one(s.drain())
    vprojd = cS.n

    rp = aC[0:32, 5120:6656]
    for s_ in range(SEQ):
        @blk.sync
        def _(sy, s_=s_):
            sy.wait_ge(cV.h, cV.n)
            dX.dma(sy.dma_start(out=rp[:, 0:256],
                                in_=h_rows[s_, 4, :].unsqueeze(0).broadcast_to([32, KSEL])))
        need = dX.n

        @blk.vector
        def _(v, s_=s_, need=need):
            v.wait_ge(dX.h, need)
            ang = rp[:, 0:256]
            rr = rp[:, 256:512]
            rc = rp[:, 512:768]
            tm = rp[:, 768:1024]
            v.tensor_scalar(out=ang, in0=ang, scalar1=inv32c[0:32, :], scalar2=None,
                            op0=ALU.mult)
            v.tensor_scalar(out=rr, in0=ang, scalar1=1.0 / TWO_PI, scalar2=MAGIC,
                            op0=ALU.mult, op1=ALU.add)
            v.drain()
            v.tensor_scalar(out=rr, in0=rr, scalar1=MAGIC, scalar2=None, op0=ALU.subtract)
            v.drain()
            v.scalar_tensor_tensor(out=rr, in0=rr, scalar=-TWO_PI, in1=ang,
                                   op0=ALU.mult, op1=ALU.add)
            v.tensor_scalar(out=rc, in0=rr, scalar1=TWO_PI / 4, scalar2=None, op0=ALU.add)
            v.tensor_scalar(out=tm, in0=rc, scalar1=TWO_PI / 2, scalar2=None, op0=ALU.is_gt)
            v.drain()
            v.scalar_tensor_tensor(out=rc, in0=tm, scalar=-TWO_PI, in1=rc,
                                   op0=ALU.mult, op1=ALU.add)
            cV.one(v.drain())
        need = cV.n

        @blk.scalar
        def _(s, need=need):
            s.wait_ge(cV.h, need)
            s.activation(out=rp[:, 768:1024], in_=rp[:, 256:512], func=AF.Sin,
                         scale=1.0)
            cS.one(s.activation(out=rp[:, 1024:1280], in_=rp[:, 512:768],
                                func=AF.Sin, scale=1.0))
        need = cS.n

        @blk.vector
        def _(v, need=need):
            v.wait_ge(cS.h, need)
            v.tensor_scalar(out=rp[:, 0:256], in0=rp[:, 768:1024],
                            scalar1=-1.0, scalar2=None, op0=ALU.mult)
            cV.one(v.drain())
        need = cV.n

        @blk.sync
        def _(sy, need=need):
            sy.wait_ge(cV.h, need)
            for rows in range(4):
                dX.dma(sy.dma_start(out=cs_r(0)[rows * 32:(rows + 1) * 32, :],
                                    in_=rp[:, 1024:1280]))
                src = rp[:, 0:256] if rows % 2 == 0 else rp[:, 768:1024]
                dX.dma(sy.dma_start(out=cs_r(1)[rows * 32:(rows + 1) * 32, :], in_=src))
        need = dX.n

        @blk.vector
        def _(v, need=need):
            v.wait_ge(dX.h, need)
            v.tensor_scalar(out=cs_r(2), in0=cs_r(0), scalar1=RSC, scalar2=None,
                            op0=ALU.mult)
            v.tensor_scalar(out=cs_r(3), in0=cs_r(1), scalar1=RSC, scalar2=None,
                            op0=ALU.mult)
            cV.one(v.drain())
        csd = cV.n

        for c in range(DCH):
            @blk.sync
            def _(sy, c=c, s_=s_, csd=csd):
                sy.wait_ge(cV.h, cV.n)
                for (dst, src) in ((scr2(0), q_r(c)), (scr2(1), k_r(c))):
                    for hb2 in range(2):
                        b0 = hb2 * 64
                        dX.dma(sy.dma_start(
                            out=dst[b0:b0 + 32, s_ * 256:(s_ + 1) * 256],
                            in_=src[b0 + 32:b0 + 64, s_ * 256:(s_ + 1) * 256]))
                        dX.dma(sy.dma_start(
                            out=dst[b0 + 32:b0 + 64, s_ * 256:(s_ + 1) * 256],
                            in_=src[b0:b0 + 32, s_ * 256:(s_ + 1) * 256]))
            need = dX.n

            @blk.vector
            def _(v, c=c, s_=s_, need=need):
                v.wait_ge(dX.h, need)
                qs = q_r(c)[:, s_ * 256:(s_ + 1) * 256]
                ks = k_r(c)[:, s_ * 256:(s_ + 1) * 256]
                sh_q = scr2(0)[:, s_ * 256:(s_ + 1) * 256]
                sh_k = scr2(1)[:, s_ * 256:(s_ + 1) * 256]
                tmp = scr2(2)[:, s_ * 256:(s_ + 1) * 256]
                v.tensor_mul(out=tmp, in0=qs, in1=cs_r(2))
                v.tensor_mul(out=sh_q, in0=sh_q, in1=cs_r(3))
                v.tensor_add(out=qs, in0=tmp, in1=sh_q)
                v.tensor_mul(out=tmp, in0=ks, in1=cs_r(0))
                v.tensor_mul(out=sh_k, in0=sh_k, in1=cs_r(1))
                v.tensor_add(out=ks, in0=tmp, in1=sh_k)
                cV.one(v.drain())
    roped = cV.n

    for s_ in range(SEQ):
        for h in range(NH):
            c = h // 2
            hb = (h % 2) * 64
            at0 = scr2(2)
            at1 = scr2(3)
            for qb in range(2):
                @blk.tensor
                def _(t, s_=s_, c=c, hb=hb, qb=qb):
                    t.wait_ge(cV.h, cV.n)
                    _mm = t.matmul(ps1[:, qb * 256:(qb + 1) * 256],
                             q_r(c)[hb:hb + 64,
                                    s_ * 256 + qb * 128:s_ * 256 + (qb + 1) * 128],
                             k_r(c)[hb:hb + 64, s_ * 256:(s_ + 1) * 256],
                             start=True, stop=True)
                    cT.one(_mm)
                need = cT.n

                @blk.vector
                def _(v, s_=s_, qb=qb, need=need):
                    v.wait_ge(cT.h, need)
                    v.tensor_add(out=sc_f(qb), in0=ps1[:, qb * 256:(qb + 1) * 256],
                                 in1=msel_t(s_, qb))
                    v.tensor_reduce(out=Rtmp[:, 1:2], in_=sc_f(qb), axis=AXX, op=ALU.max)
                    v.drain()
                    v.tensor_scalar(out=Rtmp[:, 2:3], in0=Rtmp[:, 1:2], scalar1=-1.0,
                                    scalar2=None, op0=ALU.mult)
                    cV.one(v.drain())
                need = cV.n

                @blk.scalar
                def _(s, qb=qb, need=need):
                    s.wait_ge(cV.h, need)
                    s.activation(out=sqs, in_=sc_f(qb), func=AF.Exp, bias=Rtmp[:, 2:3],
                                 scale=1.0)
                    cS.one(s.drain())
                need = cS.n

                @blk.vector
                def _(v, need=need):
                    v.wait_ge(cS.h, need)
                    v.tensor_reduce(out=Rtmp[:, 3:4], in_=sqs, axis=AXX, op=ALU.add)
                    v.drain()
                    v.reciprocal(out=Rtmp[:, 4:5], in_=Rtmp[:, 3:4])
                    v.drain()
                    v.tensor_scalar(out=sqs, in0=sqs, scalar1=Rtmp[:, 4:5], scalar2=None,
                                    op0=ALU.mult)
                    cV.one(v.drain())
                need = cV.n

                @blk.tensor
                def _(t, qb=qb, need=need):
                    t.wait_ge(cV.h, need)
                    t.transpose(psTb[:, qb * 128:(qb + 1) * 128],
                                sqs[:, 0:128], ident_bf)
                    _mm = t.transpose(psTb[:, 256 + qb * 128:256 + (qb + 1) * 128],
                                sqs[:, 128:256], ident_bf)
                    cT.one(_mm)
                need = cT.n

                @blk.vector
                def _(v, qb=qb, at0=at0, at1=at1, need=need):
                    v.wait_ge(cT.h, need)
                    v.tensor_copy(out=at0[:, qb * 128:(qb + 1) * 128],
                                  in_=psTb[:, qb * 128:(qb + 1) * 128])
                    v.tensor_copy(out=at1[:, qb * 128:(qb + 1) * 128],
                                  in_=psTb[:, 256 + qb * 128:256 + (qb + 1) * 128])
                    cV.one(v.drain())
            need = cV.n

            @blk.tensor
            def _(t, s_=s_, h=h, hb=hb, at0=at0, at1=at1, need=need):
                t.wait_ge(cV.h, need)
                for kb in range(2):
                    at = (at0, at1)[kb]
                    _mm = t.matmul(ps2[hb:hb + 64, 0:256],
                             v_r(s_ * 2 + kb)[:, h * 64:(h + 1) * 64],
                             at[:, 0:256], start=(kb == 0), stop=(kb == 1),
                             tile_position=(0, hb))
                cT.one(_mm)
            need = cT.n

            if h % 2 == 1:
                @blk.scalar
                def _(s, s_=s_, c=c, need=need):
                    s.wait_ge(cT.h, need)
                    s.copy(out=hn_r(c)[:, s_ * 256:(s_ + 1) * 256], in_=ps2[:, 0:256])
                    cS.one(s.drain())
    attnd = cS.n

    wn = load_dd(wo_h)
    for s_ in range(SEQ):
        for e in range(DCH):
            @blk.tensor
            def _(t, s_=s_, e=e, wn=wn):
                t.wait_ge(cV.h, max(wn, cV.n))
                for k in range(DCH):
                    _mm = t.matmul(ps1[:, 256 * (e % 4):256 * (e % 4) + 256],
                             wcur_k(k)[:, e * 128:(e + 1) * 128],
                             hn_r(k)[:, s_ * 256:(s_ + 1) * 256],
                             start=(k == 0), stop=(k == DCH - 1))
                cT.one(_mm)
            need = cT.n

            @blk.vector
            def _(v, s_=s_, e=e, need=need):
                v.wait_ge(cT.h, need)
                v.tensor_add(out=x1_r(e)[:, s_ * 256:(s_ + 1) * 256],
                             in0=ps1[:, 256 * (e % 4):256 * (e % 4) + 256],
                             in1=xsel_t(e)[:, s_ * 256:(s_ + 1) * 256])
                cV.one(v.drain())
    x1d = cV.n

    for s_ in range(SEQ):
        for c in range(DCH):
            @blk.scalar
            def _(s, c=c, s_=s_, x1d=x1d):
                s.wait_ge(cV.h, x1d)
                if not (c == 0 and s_ == 0):
                    s.wait_ge(cT.h, cT.n)
                s.activation(out=sqs, in_=x1_r(c)[:, s_ * 256:(s_ + 1) * 256],
                             func=AF.Square, scale=1.0)
                cS.one(s.drain())
            need = cS.n

            @blk.tensor
            def _(t, c=c, need=need):
                t.wait_ge(cS.h, need)
                _mm = t.matmul(ps1[:, 0:256], ones_bf, sqs, start=(c == 0), stop=(c == DCH - 1))
                cT.one(_mm)
        need = cT.n

        @blk.scalar
        def _(s, need=need):
            s.wait_ge(cT.h, need)
            s.activation(out=rms_b, in_=ps1[:, 0:256], func=AF.Ln, bias=epsrms_c[:, 0:1],
                         scale=1.0 / D)
            s.drain()
            cS.one(s.activation(out=rms_b, in_=rms_b, func=AF.Exp, scale=-0.5))
        need = cS.n

        @blk.vector
        def _(v, s_=s_, need=need):
            v.wait_ge(cS.h, need)
            for c in range(DCH):
                v.scalar_tensor_tensor(out=hn2_r(c)[:, s_ * 256:(s_ + 1) * 256],
                                       in0=x1_r(c)[:, s_ * 256:(s_ + 1) * 256],
                                       scalar=ln2_s[:, c:c + 1], in1=rms_b,
                                       op0=ALU.mult, op1=ALU.mult)
            cV.one(v.drain())
    hn2d = cV.n

    w1bf = aCb[:, 0:32768]

    def w1k(k):
        return w1bf[:, k * 4096:(k + 1) * 4096]

    for k in range(DCH):
        for piece in range(4):
            @blk.sync
            def _(sy, k=k, piece=piece):
                sy.wait_ge(cV.h, cV.n)
                sy.wait_ge(cT.h, cT.n)
                dX.dma(sy.dma_start(out=wst,
                                    in_=w1_h[k * P:(k + 1) * P,
                                             piece * 1024:(piece + 1) * 1024]))
            need = dX.n

            @blk.vector
            def _(v, k=k, piece=piece, need=need):
                v.wait_ge(dX.h, need)
                v.tensor_copy(out=w1k(k)[:, piece * 1024:(piece + 1) * 1024], in_=wst)
                cV.one(v.drain())
    w1ld = cV.n

    for s_ in range(SEQ):
        for f in range(32):
            @blk.tensor
            def _(t, s_=s_, f=f, w1ld=w1ld):
                t.wait_ge(cV.h, w1ld)
                t.wait_ge(cS.h, cS.n)
                for k in range(DCH):
                    _mm = t.matmul(ps1[:, 256 * (f % 4):256 * (f % 4) + 256],
                             w1k(k)[:, f * 128:(f + 1) * 128],
                             hn2_r(k)[:, s_ * 256:(s_ + 1) * 256],
                             start=(k == 0), stop=(k == DCH - 1))
                cT.one(_mm)
            need = cT.n

            @blk.scalar
            def _(s, s_=s_, f=f, need=need):
                s.wait_ge(cT.h, need)
                s.activation(out=h1_r(f, s_),
                             in_=ps1[:, 256 * (f % 4):256 * (f % 4) + 256],
                             func=AF.Gelu_apprx_tanh, scale=1.0)
                cS.one(s.drain())
    h1d = cS.n

    w2bf = aCb[:, 0:32768]

    def w2k(k):
        return w2bf[:, k * 1024:(k + 1) * 1024]

    for k in range(32):
        @blk.sync
        def _(sy, k=k):
            sy.wait_ge(cV.h, cV.n)
            sy.wait_ge(cT.h, cT.n)
            dX.dma(sy.dma_start(out=wst, in_=w2_h[k * P:(k + 1) * P, :]))
        need = dX.n

        @blk.vector
        def _(v, k=k, need=need):
            v.wait_ge(dX.h, need)
            v.tensor_copy(out=w2k(k), in_=wst)
            cV.one(v.drain())
    w2ld = cV.n

    @blk.sync
    def _(sy, need=w2ld):
        sy.wait_ge(cV.h, need)
        for s_ in range(SEQ):
            dX.dma(sy.dma_start(out=wselb(s_),
                                in_=h_rows[s_, 5, :].unsqueeze(0).broadcast_to([P, KSEL])))
    wselbd = dX.n
    corr_b0 = [0]

    for s_ in range(SEQ):
        for e in range(DCH):
            @blk.tensor
            def _(t, s_=s_, e=e, w2ld=w2ld):
                t.wait_ge(cV.h, max(w2ld, cV.n))
                for k in range(32):
                    _mm = t.matmul(ps1[:, 256 * (e % 4):256 * (e % 4) + 256],
                             w2k(k)[:, e * 128:(e + 1) * 128],
                             h1_r(k, s_), start=(k == 0), stop=(k == 31))
                cT.one(_mm)
            need = cT.n

            @blk.vector
            def _(v, s_=s_, e=e, need=need):
                v.wait_ge(cT.h, need)
                v.wait_ge(dX.h, wselbd)
                if s_ == 1 and e == 0:
                    v.wait_ge(dX.h, corr_b0[0])
                x2t = scr2(0)[:, 0:256]
                v.tensor_add(out=x2t, in0=ps1[:, 256 * (e % 4):256 * (e % 4) + 256],
                             in1=x1_r(e)[:, s_ * 256:(s_ + 1) * 256])
                v.drain()
                v.tensor_mul(out=x2t, in0=x2t, in1=wselb(s_))
                v.drain()
                v.tensor_sub(out=corr[:, e * 256:(e + 1) * 256],
                             in0=x2t, in1=xsel_t(e)[:, s_ * 256:(s_ + 1) * 256])
                cV.one(v.drain())
        seq_corr_v = cV.n

        @blk.sync
        def _(sy, s_=s_, need=seq_corr_v):
            sy.wait_ge(cV.h, need)
            dX.dma(sy.dma_start(out=h_corr[s_], in_=corr[:, :]))
        if s_ == 0:
            corr_b0[0] = dX.n
    corrd = cV.n

    if PHASE_LIMIT <= 5:
        _emit_finish()
        return nc

    yac = lambda c: aA[:, c * 1024:(c + 1) * 1024]
    spc6 = lambda c: aA[:, 8192 + c * 1024:8192 + (c + 1) * 1024]
    barA = [aC[:, 0:1024], aC[:, 1024:2048]]
    grep = [aC[:, 2048:3072], aC[:, 3072:4096]]
    crep = [aCb[:, 8192:9216], aCb[:, 9216:10240]]
    hbuf = [aCb[:, 10240:11264], aCb[:, 11264:12288]]
    xh = [aC[:, 6144:7168], aC[:, 7168:8192]]
    outh = [aC[:, 8192:9216], aC[:, 11264:12288]]
    scat = aCb[:, 18432:19456]
    hidx = [aCi16[:, 19456 + i * 256:19456 + (i + 1) * 256] for i in range(4)]
    hstg = aC[:, 12288:13312]
    dtl6 = aCb[:, 20480:22528]

    @blk.sync
    def _(sy, need=corrd):
        sy.wait_ge(cV.h, need)
        sy.wait_ge(cT.h, cT.n)
        dX.dma(sy.dma_start(out=dtl6, in_=h_dtl))
        for s_ in range(SEQ):
            for hf in range(2):
                i = s_ * 2 + hf
                dX.dma(sy.dma_start(out=hstg[:, i * 256:(i + 1) * 256],
                                    in_=h_rows[s_, 1 + hf, :].unsqueeze(0)
                                    .broadcast_to([P, KSEL])))
    need = dX.n

    @blk.vector
    def _(v, need=need):
        v.wait_ge(dX.h, need)
        for i in range(4):
            v.tensor_copy(out=hidx[i], in_=hstg[:, i * 256:(i + 1) * 256])
        cV.one(v.drain())
    hidxd = cV.n

    @blk.gpsimd
    def _(g, need=hidxd):
        g.wait_ge(cV.h, need)
        g.load_library(libcfg.local_scatter)
    lsld = cG.n

    cVs = Ctr(nc, "cVs")
    cSb = Ctr(nc, "cSb")
    corr_rl = [0, 0]

    iglob = 0
    jglob = 0
    gdma = {}
    for s_ in range(SEQ):
        for hf in range(2):
            jh = s_ * 2 + hf
            tlo = s_ * S + hf * HALF

            if hf == 0:
                @blk.sync
                def _(sy, s_=s_):
                    if s_ == 0:
                        sy.wait_ge(cV.h, corrd)
                    else:
                        sy.wait_ge(cG.h, cG.n)
                    dX.dma(sy.dma_start(out=corr[:, :], in_=h_corr[s_]))
                corr_rl[s_] = dX.n

            for rnd in range(2):
                @blk.tensor
                def _(t, s_=s_, hf=hf, rnd=rnd):
                    t.wait_ge(cS.h, cS.n)
                    if rnd == 0:
                        t.wait_ge(cV.h, cV.n)
                        if s_ == 0 and hf == 0:
                            t.wait_ge(dX.h, dX.n)
                        else:
                            t.wait_ge(cVs.h, cVs.n)
                    for ci in range(4):
                        c = rnd * 4 + ci
                        for sl2 in range(2):
                            _mm = t.matmul(pslice(ci * 2 + sl2),
                                     wdt_bf[64 * s_:64 * s_ + 64, c * 128:(c + 1) * 128],
                                     dtl6[64 * s_:64 * s_ + 64,
                                          hf * 1024 + sl2 * 512:hf * 1024 + (sl2 + 1) * 512],
                                     start=True, stop=True)
                    cT.one(_mm)
                need = cT.n

                @blk.scalar
                def _(s, s_=s_, hf=hf, rnd=rnd, need=need):
                    s.wait_ge(cT.h, need)
                    for ci in range(4):
                        c = rnd * 4 + ci
                        for sl2 in range(2):
                            s.activation(out=spc6(c)[:, sl2 * 512:(sl2 + 1) * 512],
                                         in_=pslice(ci * 2 + sl2), func=AF.Exp,
                                         bias=bdt_s[:, c:c + 1], scale=1.0)
                        s.activation(out=spc6(c), in_=spc6(c), func=AF.Ln, bias=1.0,
                                     scale=1.0)
                    cS.one(s.drain())
            spd = cS.n

            @blk.vector
            def _(v, s_=s_, hf=hf, tlo=tlo, spd=spd):
                v.wait_ge(cS.h, spd)
                for c in range(DCH - 1):
                    v.tensor_copy(out=yac(c), in_=u_t(c)[:, tlo:tlo + HALF])
                cV.one(v.tensor_copy(out=yac(DCH - 1), in_=u_t(DCH - 1)[:, tlo:tlo + HALF]))
            yinit = cV.n

            for n_ in range(N):
                j = jglob
                jglob += 1

                @blk.sync
                def _(sy, n_=n_, tlo=tlo, j=j, yinit=yinit):
                    if j >= 2:
                        sy.wait_ge(cVs.h, 8 * (j - 1))
                    if j == 0:
                        sy.wait_ge(cV.h, yinit)
                    dGp[j % 2].dma(sy.dma_start(out=grep[n_ % 2],
                                                in_=h_g[n_, tlo:tlo + HALF].unsqueeze(0)
                                                .broadcast_to([P, HALF])))
                    dGp[j % 2].dma(sy.dma_start(out=crep[n_ % 2],
                                                in_=h_cc[n_, tlo:tlo + HALF].unsqueeze(0)
                                                .broadcast_to([P, HALF])))
                gdma[j] = (j % 2, dGp[j % 2].n)

                for c in range(DCH):
                    i = iglob
                    iglob += 1

                    @blk.scalar
                    def _(s, c=c, n_=n_, i=i, spd=spd):
                        if i >= 2:
                            s.wait_ge(cVs.h, i - 1)
                        col = c * 16 + n_
                        cSb.one(s.activation(out=barA[i % 2], in_=spc6(c), func=AF.Exp,
                                              scale=A099[:, col:col + 1],
                                              bias=A001[:, col:col + 1]))

                    @blk.vector
                    def _(v, s_=s_, hf=hf, c=c, n_=n_, i=i, j=j,
                          sbneed=cSb.n, dneed=gdma[j], yinit=yinit):
                        v.wait_ge(cSb.h, sbneed)
                        if c == 0:
                            v.wait_ge(dGp[dneed[0]].h, dneed[1])
                            if j == 0:
                                v.wait_ge(cV.h, yinit)
                        hcol = (s_ * 8 + c) * 16 + n_
                        if hf == 0:
                            v.tensor_tensor_scan(out=hbuf[i % 2], data0=barA[i % 2],
                                                 data1=grep[n_ % 2], initial=0.0,
                                                 op0=ALU.mult, op1=ALU.add)
                            v.drain()
                            v.tensor_copy(out=hstate[:, hcol:hcol + 1],
                                          in_=hbuf[i % 2][:, HALF - 1:HALF])
                        else:
                            v.tensor_tensor_scan(out=hbuf[i % 2], data0=barA[i % 2],
                                                 data1=grep[n_ % 2],
                                                 initial=hstate[:, hcol:hcol + 1],
                                                 op0=ALU.mult, op1=ALU.add)
                        v.tensor_mul(out=hbuf[i % 2], in0=hbuf[i % 2], in1=crep[n_ % 2])
                        cVs.one(v.tensor_add(out=yac(c), in0=yac(c), in1=hbuf[i % 2]))
            scan_done = cVs.n

            finv = cV.n
            for c in range(DCH):
                @blk.sync
                def _(sy, c=c, tlo=tlo, finv=finv):
                    if c >= 2:
                        sy.wait_ge(cV.h, finv + (c - 1))
                    dHp[c % 2].dma(sy.dma_start(out=xh[c % 2],
                                                in_=xin[c * P:(c + 1) * P, tlo:tlo + HALF]))
                xneed = (c % 2, dHp[c % 2].n)

                @blk.gpsimd
                def _(g, c=c, s_=s_, jh=jh, finv=finv, scan_done=scan_done):
                    if c == 0:
                        g.wait_ge(cVs.h, scan_done)
                    else:
                        g.wait_ge(cV.h, finv + c)
                    if c == 0:
                        g.wait_ge(dX.h, corr_rl[s_])
                    cG.one(g.local_scatter(
                        scat, corr[:, c * 256:(c + 1) * 256],
                        hidx[jh], channels=P, num_elems=HALF, num_idxs=KSEL))
                gneed = cG.n

                @blk.vector
                def _(v, c=c, xneed=xneed, gneed=gneed, scan_done=scan_done):
                    v.wait_ge(dHp[xneed[0]].h, xneed[1])
                    v.wait_ge(cG.h, gneed)
                    v.wait_ge(cVs.h, scan_done)
                    if c >= 2:
                        v.wait_ge(dOp[c % 2].h, dOp[c % 2].n)
                    v.tensor_add(out=outh[c % 2], in0=xh[c % 2], in1=yac(c))
                    v.drain()
                    cV.one(v.tensor_add(out=outh[c % 2], in0=outh[c % 2], in1=scat))
                vneed = cV.n

                @blk.sync
                def _(sy, c=c, tlo=tlo, vneed=vneed):
                    sy.wait_ge(cV.h, vneed)
                    dOp[c % 2].dma(
                        sy.dma_start(out=out_fm[c * P:(c + 1) * P, tlo:tlo + HALF],
                                     in_=outh[c % 2]))

    @blk.sync
    def _(sy):
        sy.wait_ge(dOp[0].h, dOp[0].n)
        sy.wait_ge(dOp[1].h, dOp[1].n)

    blk.__exit__(None, None, None)
    return nc


_NC_CACHE = {}


def _get_nc():
    if "nc" not in _NC_CACHE:
        nc = build_nc()
        lower_extended_insts(nc)
        _NC_CACHE["nc"] = nc
    return _NC_CACHE["nc"]


def kernel(**inputs):
    x = np.asarray(inputs["L0_attn_x"])
    ncores = 8
    eye = np.eye(P, dtype=np.float32)
    ones = np.ones((P, P), dtype=np.float32)
    inv = (1.0 / (10000.0 ** (np.arange(0, HD, 2, dtype=np.float32) / HD))).astype(np.float32)
    cpk = np.zeros((P, 257), dtype=np.float32)
    cpk[:, 0:128] = eye
    cpk[:, 128:256] = ones
    cpk[0:32, 256] = inv
    a_log = np.asarray(inputs["L0_attn_A_log"])
    a_perm = np.ascontiguousarray(
        a_log.reshape(8, 128, 16).transpose(1, 0, 2).reshape(128, 128))

    def c8(vec):
        return np.ascontiguousarray(np.asarray(vec).reshape(8, 128).T)

    mask = np.asarray(inputs["L0_ffn_attention_mask"])[:, 0]
    posi = np.asarray(inputs["L0_ffn_position_ids"])
    base = {
        "w_in": np.asarray(inputs["L0_attn_w_in"]),
        "w_x": np.asarray(inputs["L0_attn_w_x"]),
        "w_dt": np.asarray(inputs["L0_attn_w_dt"]),
        "wq": np.asarray(inputs["L0_ffn_wq"]),
        "wk": np.asarray(inputs["L0_ffn_wk"]),
        "wv": np.asarray(inputs["L0_ffn_wv"]),
        "wo": np.asarray(inputs["L0_ffn_wo"]),
        "w1": np.asarray(inputs["L0_ffn_w1"]),
        "w2": np.asarray(inputs["L0_ffn_w2"]),
        "a_perm": a_perm,
        "bdt8": c8(inputs["L0_attn_b_dt"]),
        "dp8": c8(inputs["L0_attn_Dp"]),
        "ln18": c8(inputs["L0_ffn_ln1"]),
        "ln28": c8(inputs["L0_ffn_ln2"]),
        "rw8": c8(np.asarray(inputs["L0_ffn_router_w"])[:, 0]),
        "rb1": np.asarray(inputs["L0_ffn_router_b"]).reshape(1, 1),
        "cpack": cpk,
    }
    in_maps = []
    for core in range(ncores):
        b0 = 2 * core
        m = dict(base)
        xc = x[b0:b0 + 2]
        m["xin"] = np.ascontiguousarray(xc.transpose(2, 0, 1).reshape(D, T))
        bc0, bc1 = min(b0, 3), min(b0 + 1, 3)
        m["mask0"] = np.ascontiguousarray(mask[bc0, 0:S, 0:S])
        m["mask1"] = np.ascontiguousarray(mask[bc1, 0:S, 0:S])
        m["pos2"] = np.ascontiguousarray(
            np.stack([posi[bc0, 0:S], posi[bc1, 0:S]]).astype(np.int32))
        in_maps.append(m)

    nc = _get_nc()
    res = run_bass_kernel_spmd(nc, in_maps, core_ids=list(range(ncores)))
    outs = []
    for core in range(ncores):
        o = np.asarray(res.results[core]["out_fm"])
        outs.append(o.reshape(D, SEQ, S).transpose(1, 2, 0))
    return np.concatenate(outs, axis=0).astype(np.float32)


# version 05
# speedup vs baseline: 1.1529x; 1.0932x over previous
import numpy as np
import ml_dtypes

import concourse.bass as bass
import concourse.mybir as mybir
import concourse.library_config as libcfg
from concourse.bass import IndirectOffsetOnAxis
from concourse.bass_utils import run_bass_kernel_spmd
from concourse.library_overlay import lower_extended_insts

F32 = mybir.dt.float32
BF16 = mybir.dt.bfloat16
U32 = mybir.dt.uint32
I32 = mybir.dt.int32
I16 = mybir.dt.int16
AF = mybir.ActivationFunctionType
ALU = mybir.AluOpType
AXX = mybir.AxisListType.X

P, D, DCH, S, SEQ = 128, 1024, 8, 2048, 2
T, HALF, N, KSEL, DTRANK, FF, NH, HD = 4096, 1024, 16, 256, 64, 4096, 16, 64
EPS_LN, EPS_RMS, NEG, RSC = 1e-5, 1e-6, -1.0e9, 0.125
TWO_PI = 6.283185307179586
MAGIC = 12582912.0
PHASE_LIMIT = 99
DEBUG_HBM = False


class Ctr:
    def __init__(self, nc, name):
        self.h = nc.alloc_semaphore(name)
        self.n = 0

    def dma(self, instr):
        instr.then_inc(self.h, 16)
        self.n += 16

    def one(self, instr):
        instr.then_inc(self.h, 1)
        self.n += 1

    def seq(self, eng):
        eng.sem_inc(self.h, 1)
        self.n += 1


def build_nc():
    nc = bass.Bass("TRN2", target_bir_lowering=False)

    xin = nc.dram_tensor("xin", [D, T], F32, kind="ExternalInput").ap()
    w_in_h = nc.dram_tensor("w_in", [D, D], F32, kind="ExternalInput").ap()
    w_x_h = nc.dram_tensor("w_x", [D, 96], F32, kind="ExternalInput").ap()
    w_dt_h = nc.dram_tensor("w_dt", [DTRANK, D], F32, kind="ExternalInput").ap()
    wq_h = nc.dram_tensor("wq", [D, D], F32, kind="ExternalInput").ap()
    wk_h = nc.dram_tensor("wk", [D, D], F32, kind="ExternalInput").ap()
    wv_h = nc.dram_tensor("wv", [D, D], F32, kind="ExternalInput").ap()
    wo_h = nc.dram_tensor("wo", [D, D], F32, kind="ExternalInput").ap()
    w1_h = nc.dram_tensor("w1", [D, FF], F32, kind="ExternalInput").ap()
    w2_h = nc.dram_tensor("w2", [FF, D], F32, kind="ExternalInput").ap()
    a_perm = nc.dram_tensor("a_perm", [P, P], F32, kind="ExternalInput").ap()
    bdt8 = nc.dram_tensor("bdt8", [P, DCH], F32, kind="ExternalInput").ap()
    dp8 = nc.dram_tensor("dp8", [P, DCH], F32, kind="ExternalInput").ap()
    ln18 = nc.dram_tensor("ln18", [P, DCH], F32, kind="ExternalInput").ap()
    ln28 = nc.dram_tensor("ln28", [P, DCH], F32, kind="ExternalInput").ap()
    rw8 = nc.dram_tensor("rw8", [P, DCH], F32, kind="ExternalInput").ap()
    rb1 = nc.dram_tensor("rb1", [1, 1], F32, kind="ExternalInput").ap()
    mask_h = [nc.dram_tensor("mask0", [S, S], F32, kind="ExternalInput").ap(),
              nc.dram_tensor("mask1", [S, S], F32, kind="ExternalInput").ap()]
    pos2 = nc.dram_tensor("pos2", [SEQ, S], I32, kind="ExternalInput").ap()
    cpack = nc.dram_tensor("cpack", [P, 257], F32, kind="ExternalInput").ap()
    out_fm = nc.dram_tensor("out_fm", [D, T], F32, kind="ExternalOutput").ap()

    KIND = "ExternalOutput" if DEBUG_HBM else "Internal"
    h_wrow = nc.dram_tensor("h_wrow", [SEQ, 2, S], F32, kind=KIND).ap()
    h_midx = nc.dram_tensor("h_midx", [SEQ, S], F32, kind="Internal").ap()
    h_selw = nc.dram_tensor("h_selw", [SEQ, 256], F32, kind="Internal").ap()
    h_seli = nc.dram_tensor("h_seli", [SEQ, 16, 16], mybir.dt.uint16, kind="Internal").ap()
    h_rows = nc.dram_tensor("h_rows", [SEQ, 6, KSEL], F32, kind=KIND).ap()
    h_g = nc.dram_tensor("h_g", [N, T], F32, kind=KIND).ap()
    h_cc = nc.dram_tensor("h_cc", [N, T], BF16, kind=KIND).ap()
    h_dtl = nc.dram_tensor("h_dtl", [P, S], BF16, kind=KIND).ap()
    h_corr = nc.dram_tensor("h_corr", [SEQ, P, 2048], BF16, kind=KIND).ap()
    h_dbg = nc.dram_tensor("h_dbg", [P, 16384], F32, kind=KIND).ap() if DEBUG_HBM else None

    AR_A = nc.alloc_sbuf_tensor("AR_A", [P, 16384], F32)
    AR_B = nc.alloc_sbuf_tensor("AR_B", [P, 16384], F32)
    AR_C = nc.alloc_sbuf_tensor("AR_C", [P, 16384], F32)
    SMALL = nc.alloc_sbuf_tensor("SMALL", [P, 1792], F32)
    WST = nc.alloc_sbuf_tensor("WST", [P, 1024], F32)
    CORR = nc.alloc_sbuf_tensor("CORR", [P, 2048], BF16)
    SEL = nc.alloc_sbuf_tensor("SEL", [P, 96], mybir.dt.uint16)

    aA, aB, aC = AR_A.ap(), AR_B.ap(), AR_C.ap()
    aAb, aBb, aCb = aA.bitcast(BF16), aB.bitcast(BF16), aC.bitcast(BF16)
    aCi16 = aC.bitcast(I16)
    r1 = aB[0:32, 0:2048]
    r2 = aB[0:32, 2048:4096]
    r2i = r2.bitcast(I32)
    sm = SMALL.ap()
    smb = sm.bitcast(BF16)
    wst = WST.ap()
    corr = CORR.ap()
    sil = SEL.ap()
    slw = aB[0:32, 4096:5632]

    A_f = sm[:, 0:128]
    A099 = sm[:, 128:256]
    A001 = sm[:, 256:384]
    A_bf = smb[:, 768:896]
    bdt_s = sm[:, 448:456]
    dp_s = sm[:, 456:464]
    ln1_s = sm[:, 464:472]
    ln2_s = sm[:, 472:480]
    rw_s = sm[:, 480:488]
    thr2 = sm[:, 488:489]
    rb2 = sm[:, 489:490]
    nRcol = sm[:, 490:491]
    Rtmp = sm[:, 491:499]
    kth_o = sm[:, 500:504]
    w128t = [sm[:, 504:520], sm[:, 520:536]]
    ridf = sm[:, 536:540]
    ridu = sm.bitcast(U32)[:, 540:544]
    psel_raw = sm.bitcast(I32)[:, 544:548]
    psel_f = sm[:, 548:552]
    wsel_raw = sm[:, 552:556]
    inv32c = sm[:, 556:557]
    epsln_c = sm[:, 557:558]
    epsrms_c = sm[:, 558:559]
    identf_st = sm[:, 560:688]
    ones_f = sm[:, 688:816]
    ident_bf = smb[:, 1632:1760]
    ones_bf = smb[:, 1760:1888]
    hstate = sm[:, 944:1200]
    wdt_bf = smb[:, 2400:3424]
    nopv = sm[0:1, 1712:1713]
    nops = sm[0:1, 1713:1714]

    PS1 = nc.alloc_psum_tensor("PS1", [P, 2048], F32)
    PS2 = nc.alloc_psum_tensor("PS2", [P, 2048], F32)
    ps1, ps2 = PS1.ap(), PS2.ap()
    psTb = ps2[:, 1536:2048].bitcast(BF16)

    dX = Ctr(nc, "dX")
    dCst = Ctr(nc, "dCst")
    dXp = [Ctr(nc, "dXp0"), Ctr(nc, "dXp1")]
    dGp = [Ctr(nc, "dGp0"), Ctr(nc, "dGp1")]
    dHp = [Ctr(nc, "dHp0"), Ctr(nc, "dHp1")]
    dOp = [Ctr(nc, "dOp0"), Ctr(nc, "dOp1")]
    dO = Ctr(nc, "dO")
    dGm = Ctr(nc, "dGm")
    cV = Ctr(nc, "cV")
    cS = Ctr(nc, "cS")
    cT = Ctr(nc, "cT")
    cG = Ctr(nc, "cG")

    def xn_t(c):
        return aAb[:, c * 4096:(c + 1) * 4096]

    def u_t(c):
        return aBb[:, c * 4096:(c + 1) * 4096]

    def pslice(sl):
        return (ps1 if sl < 4 else ps2)[:, (sl % 4) * 512:(sl % 4 + 1) * 512]

    blk = nc.Block("main")
    blk.__enter__()

    def _emit_finish():
        @blk.vector
        def _(v):
            v.wait_ge(cS.h, cS.n)
            v.wait_ge(cT.h, cT.n)
            v.wait_ge(cG.h, cG.n)
            v.memset(aC[:, 0:4096], 0.0)
            cV.one(v.drain())

        @blk.sync
        def _(sy):
            sy.wait_ge(dX.h, dX.n)
            sy.wait_ge(cV.h, cV.n)
            sy.wait_ge(dCst.h, dCst.n)
            sy.wait_ge(dGm.h, dGm.n)
            for p in (dXp + dGp + dHp):
                sy.wait_ge(p.h, p.n)
            for c in range(DCH):
                dO.dma(sy.dma_start(out=out_fm[c * P:(c + 1) * P, :], in_=aC[:, 0:4096]))
            sy.wait_ge(dO.h, dO.n)
        blk.__exit__(None, None, None)


    xstage = [aC[:, 0:4096], aC[:, 4096:8192]]
    racc = aC[:, 8192:12288]
    sqb = [aCb[:, 24576:26624], aCb[:, 26624:28672]]

    @blk.sync
    def _(sy):
        for dst, src in [(A_f, a_perm), (bdt_s, bdt8), (dp_s, dp8), (ln1_s, ln18),
                         (ln2_s, ln28), (rw_s, rw8), (rb2[0:1, :], rb1), (rb2[1:2, :], rb1),
                         (identf_st, cpack[:, 0:128]), (ones_f, cpack[:, 128:256]),
                         (inv32c[0:32, :], cpack[0:32, 256:257])]:
            dCst.dma(sy.dma_start(out=dst, in_=src, allow_slow_non_contiguous=True))
    ph1_small = dCst.n

    @blk.gpsimd
    def _(g):
        cG.one(g.iota(r2i[0:2, :], pattern=[[1, 2048]], base=1, channel_multiplier=0))
    iota_done = cG.n

    @blk.scalar
    def _(s):
        s.wait_ge(dCst.h, ph1_small)
        cS.one(s.activation(out=A_f, in_=A_f, func=AF.Exp, scale=1.0))
    aexp = cS.n

    if PHASE_LIMIT <= -1:
        _emit_finish()
        return nc

    vbase_x = cV.n
    for c in range(DCH):
        @blk.sync
        def _(sy, c=c):
            if c >= 2:
                sy.wait_ge(cV.h, vbase_x + (c - 1))
            dXp[c % 2].dma(sy.dma_start(out=xstage[c % 2], in_=xin[c * P:(c + 1) * P, :]))
        need = dXp[c % 2].n

        @blk.vector
        def _(v, c=c, need=need):
            v.wait_ge(dXp[c % 2].h, need)
            if c == 0:
                v.wait_ge(dCst.h, ph1_small)
                v.wait_ge(cG.h, iota_done)
                v.wait_ge(cS.h, aexp)
                v.tensor_copy(out=ident_bf, in_=identf_st)
                v.tensor_copy(out=ones_bf, in_=ones_f)
                v.memset(epsln_c, EPS_LN)
                v.memset(epsrms_c, EPS_RMS)
                v.tensor_copy(out=r2[0:2, :], in_=r2i[0:2, :])
                v.tensor_scalar(out=A_f, in0=A_f, scalar1=-1.0, scalar2=None, op0=ALU.mult)
                v.drain()
                v.tensor_scalar(out=A099, in0=A_f, scalar1=0.099, scalar2=None, op0=ALU.mult)
                v.tensor_scalar(out=A001, in0=A_f, scalar1=0.001, scalar2=None, op0=ALU.mult)
                v.tensor_copy(out=A_bf, in_=A_f)
            v.tensor_copy(out=xn_t(c), in_=xstage[c % 2])
            if c == 0:
                v.scalar_tensor_tensor(out=racc, in0=xstage[0], scalar=rw_s[:, 0:1],
                                       in1=xstage[0], op0=ALU.mult, op1=ALU.bypass)
            else:
                v.scalar_tensor_tensor(out=racc, in0=xstage[c % 2], scalar=rw_s[:, c:c + 1],
                                       in1=racc, op0=ALU.mult, op1=ALU.add)
            cV.one(v.drain())
    xstream = cV.n


    if PHASE_LIMIT <= 0:
        _emit_finish()
        return nc

    m_b = aC[:, 0:2048]
    r_b = aC[:, 2048:4096]
    m_bf = aCb[:, 8192:10240]
    r_bf = aCb[:, 10240:12288]
    msq = aC[:, 6144:8192]
    w_bc = aC[:, 14336:16384]

    for s_ in range(SEQ):
        col0 = s_ * S
        for c in range(DCH):
            @blk.scalar
            def _(s, c=c, s_=s_):
                if s_ == 0 and c == 0:
                    s.wait_ge(cV.h, xstream)
                if not (s_ == 0 and c in (0, 1)):
                    s.wait_ge(cT.h, cT.n)
                s.activation(out=sqb[c % 2], in_=xn_t(c)[:, col0:col0 + S],
                             func=AF.Square, scale=1.0)
                cS.one(s.drain())
            need = cS.n

            @blk.tensor
            def _(t, c=c, s_=s_, need=need):
                t.wait_ge(cS.h, need)
                if s_ == 1 and c == 0:
                    t.wait_ge(cV.h, cV.n)
                for sl in range(4):
                    t.matmul(ps1[:, sl * 512:(sl + 1) * 512], ones_bf,
                             xn_t(c)[:, col0 + sl * 512:col0 + (sl + 1) * 512],
                             start=(c == 0), stop=(c == DCH - 1))
                for sl in range(4):
                    _mm = t.matmul(ps2[:, sl * 512:(sl + 1) * 512], ones_bf,
                             sqb[c % 2][:, sl * 512:(sl + 1) * 512],
                             start=(c == 0), stop=(c == DCH - 1))
                cT.one(_mm)
        smm = cT.n

        if PHASE_LIMIT <= 0.2:
            _emit_finish()
            return nc

        @blk.vector
        def _(v, need=smm):
            v.wait_ge(cT.h, need)
            v.tensor_scalar(out=m_b, in0=ps1[:, 0:2048], scalar1=1.0 / D, scalar2=None,
                            op0=ALU.mult)
            cV.one(v.drain())
        mdone = cV.n

        if DEBUG_HBM:
            @blk.sync
            def _(sy, s_=s_, need=mdone):
                sy.wait_ge(cV.h, need)
                if s_ == 0:
                    dX.dma(sy.dma_start(out=h_dbg[:, 12416:14464], in_=m_b))
                sy.wait_ge(dX.h, dX.n)

        @blk.scalar
        def _(s, need=mdone):
            s.wait_ge(cV.h, need)
            if DEBUG_HBM:
                s.wait_ge(dX.h, dX.n)
            cS.one(s.activation(out=msq, in_=m_b, func=AF.Square, scale=1.0))
        msqd = cS.n

        @blk.vector
        def _(v, need=msqd):
            v.wait_ge(cS.h, need)
            v.scalar_tensor_tensor(out=r_b, in0=ps2[:, 0:2048], scalar=1.0 / D, in1=msq,
                                   op0=ALU.mult, op1=ALU.subtract)
            cV.one(v.drain())
        vard = cV.n

        @blk.scalar
        def _(s, need=vard):
            s.wait_ge(cV.h, need)
            s.activation(out=r_b, in_=r_b, func=AF.Ln, bias=epsln_c[:, 0:1], scale=1.0)
            cS.one(s.activation(out=r_b, in_=r_b, func=AF.Exp, scale=-0.5))
        rstdd = cS.n

        if DEBUG_HBM:
            @blk.sync
            def _(sy, s_=s_, need=rstdd):
                sy.wait_ge(cS.h, need)
                dX.dma(sy.dma_start(out=h_dbg[:, 8192 + s_ * 2048:8192 + (s_ + 1) * 2048],
                                    in_=r_b))

        @blk.vector
        def _(v, s_=s_, need=rstdd):
            v.wait_ge(cS.h, need)
            if DEBUG_HBM:
                v.wait_ge(dX.h, dX.n)
            v.tensor_copy(out=m_bf, in_=m_b)
            v.tensor_copy(out=r_bf, in_=r_b)
            for c in range(DCH):
                xseg = xn_t(c)[:, col0:col0 + S]
                v.tensor_sub(out=sqb[0], in0=xseg, in1=m_bf)
                v.tensor_mul(out=xseg, in0=sqb[0], in1=r_bf)
            cV.one(v.drain())
        xnd = cV.n

        if PHASE_LIMIT <= 0.4:
            _emit_finish()
            return nc

        @blk.tensor
        def _(t, s_=s_, need=xnd):
            t.wait_ge(cV.h, need)
            t.matmul(ps2[:, 0:8], ones_f, rw_s, start=True, stop=True)
            for sl in range(4):
                _mm = t.matmul(ps1[:, sl * 512:(sl + 1) * 512], ones_f,
                         racc[:, col0 + sl * 512:col0 + (sl + 1) * 512],
                         start=True, stop=True)
            cT.one(_mm)
        rxd = cT.n

        @blk.vector
        def _(v, s_=s_, need=rxd):
            v.wait_ge(cT.h, need)
            if DEBUG_HBM:
                v.tensor_copy(out=msq, in_=ps1[:, 0:2048])
            v.tensor_reduce(out=Rtmp[:, 0:1], in_=ps2[:, 0:8], axis=AXX, op=ALU.add)
            v.drain()
            v.tensor_scalar(out=nRcol, in0=Rtmp[:, 0:1], scalar1=-1.0, scalar2=None,
                            op0=ALU.mult)
            v.drain()
            v.scalar_tensor_tensor(out=w_bc, in0=m_b, scalar=nRcol[:, 0:1],
                                   in1=ps1[:, 0:2048], op0=ALU.mult, op1=ALU.add)
            v.drain()
            v.tensor_mul(out=w_bc, in0=w_bc, in1=r_b)
            cV.one(v.drain())
        wd = cV.n

        if DEBUG_HBM:
            @blk.sync
            def _(sy, s_=s_, need=wd):
                sy.wait_ge(cV.h, need)
                dX.dma(sy.dma_start(out=h_dbg[:, 4096 + s_ * 2048:4096 + (s_ + 1) * 2048],
                                    in_=msq))
                if s_ == 0:
                    dX.dma(sy.dma_start(out=h_dbg[:, 0:2048], in_=w_bc))
                    dX.dma(sy.dma_start(out=h_dbg[:, 2048:2056], in_=sm[:, 488:496]))
                    dX.dma(sy.dma_start(out=h_dbg[:, 12288:12416], in_=ones_f))
                sy.wait_ge(dX.h, dX.n)

        if PHASE_LIMIT <= 0.6:
            _emit_finish()
            return nc

        @blk.sync
        def _(sy, s_=s_, need=wd):
            sy.wait_ge(cV.h, need)
            dX.dma(sy.dma_start(out=h_wrow[s_, 0, :], in_=w_bc[0:1, :]))
            sy.wait_ge(dX.h, dX.n)
            dX.dma(sy.dma_start(out=w128t[s_],
                                in_=h_wrow[s_, 0, :].rearrange("(p f) -> p f", p=P)))
    wwrap = dX.n

    if PHASE_LIMIT <= 0.8:
        _emit_finish()
        return nc

    @blk.gpsimd
    def _(g, need=wwrap):
        g.wait_ge(dX.h, need)
        g.load_library(libcfg.attn)
        q = 1.0 - 254.5 / (S - 1)
        g.kth_largest(kth_o[0:1, 0:2], w128t[0], n_per_lane=S // P,
                      k=KSEL + 2, quantile=q)
        cG.one(g.kth_largest(kth_o[0:1, 2:4], w128t[1], n_per_lane=S // P,
                             k=KSEL + 2, quantile=q))
    kthd = cG.n

    @blk.sync
    def _(sy, need=kthd):
        sy.wait_ge(cG.h, need)
        dX.dma(sy.dma_start(out=thr2[0:1, :], in_=kth_o[0:1, 1:2]))
        dX.dma(sy.dma_start(out=thr2[1:2, :], in_=kth_o[0:1, 3:4]))
        dX.dma(sy.dma_start(out=r1[0:2, :], in_=h_wrow[:, 0, :]))
    thrd = dX.n

    @blk.vector
    def _(v, need=thrd):
        v.wait_ge(dX.h, need)
        v.scalar_tensor_tensor(out=r2[0:2, :], in0=r1[0:2, :], scalar=thr2[0:2, :],
                               in1=r2[0:2, :], op0=ALU.is_gt, op1=ALU.mult)
        v.tensor_scalar(out=r2[0:2, :], in0=r2[0:2, :], scalar1=1.0, scalar2=None,
                        op0=ALU.subtract)
        v.scalar_tensor_tensor(out=r1[0:2, :], in0=r1[0:2, :], scalar=rb2[0:2, :],
                               in1=r1[0:2, :], op0=ALU.add, op1=ALU.bypass)
        cV.one(v.drain())
    midxd = cV.n

    @blk.sync
    def _(sy, need=midxd):
        sy.wait_ge(cV.h, need)
        dX.dma(sy.dma_start(out=h_midx, in_=r2[0:2, :]))
        dX.dma(sy.dma_start(out=h_wrow[0, 1, :], in_=r1[0:1, :]))
        dX.dma(sy.dma_start(out=h_wrow[1, 1, :], in_=r1[1:2, :]))
    mbounce = dX.n

    gp_sparse_base = cG.n
    for s_ in range(SEQ):
        @blk.sync
        def _(sy, s_=s_, need=mbounce):
            sy.wait_ge(dX.h, need)
            if s_ == 1:
                sy.wait_ge(cG.h, gp_sparse_base + 1)
            dX.dma(sy.dma_start(out=slw[0:16, 0:128],
                                in_=h_midx[s_].rearrange("(f p) -> p f", p=16),
                                allow_slow_non_contiguous=True))
        wdone = dX.n

        @blk.gpsimd
        def _(g, s_=s_, need=wdone):
            if s_ == 0:
                g.load_library(libcfg.sparse_gather)
            g.wait_ge(dX.h, need)
            nf_u = sil.bitcast(U32)[0:1, 44:45]
            cG.one(g.sparse_gather(out=slw[0:16, 128 + 16 * s_:144 + 16 * s_],
                                   in_=slw[0:16, 0:128], num_found=nf_u))
    sparsed = cG.n

    @blk.vector
    def _(v, need=sparsed):
        v.wait_ge(cG.h, need)
        v.tensor_scalar(out=slw[0:16, 128:160], in0=slw[0:16, 128:160], scalar1=0.0,
                        scalar2=2047.0, op0=ALU.max, op1=ALU.min)
        v.drain()
        v.tensor_copy(out=sil[0:16, 0:32], in_=slw[0:16, 128:160])
        cV.one(v.drain())
    selwd = cV.n

    @blk.sync
    def _(sy, need=selwd):
        sy.wait_ge(cV.h, need)
        for s_ in range(SEQ):
            dX.dma(sy.dma_start(out=h_selw[s_].rearrange("(f p) -> p f", p=16),
                                in_=slw[0:16, 128 + 16 * s_:144 + 16 * s_],
                                allow_slow_non_contiguous=True))
            dX.dma(sy.dma_start(out=h_seli[s_], in_=sil[0:16, 16 * s_:16 * (s_ + 1)]))
        sy.wait_ge(dX.h, dX.n)
        dX.dma(sy.dma_start(out=r2[0:2, 0:256], in_=h_selw))
        for s_ in range(SEQ):
            for grp in range(8):
                dX.dma(sy.dma_start(out=sil[grp * 16:(grp + 1) * 16,
                                            32 + 16 * s_:48 + 16 * s_],
                                    in_=h_seli[s_]))
    lind = dX.n

    @blk.vector
    def _(v, need=lind):
        v.wait_ge(dX.h, need)
        v.memset(r2[0:2, 255:256], -1.0)
        v.drain()
        v.tensor_scalar(out=r2[0:2, 0:256], in0=r2[0:2, 0:256], scalar1=-1.0,
                        scalar2=2047.0, op0=ALU.max, op1=ALU.min)
        v.drain()
        v.tensor_scalar(out=r2[0:2, 512:768], in0=r2[0:2, 0:256], scalar1=1024.0,
                        scalar2=None, op0=ALU.subtract)
        v.tensor_scalar(out=r2[0:2, 768:1024], in0=r2[0:2, 0:256], scalar1=1024.0,
                        scalar2=None, op0=ALU.is_ge)
        v.drain()
        v.scalar_tensor_tensor(out=r2[0:2, 768:1024], in0=r2[0:2, 768:1024],
                               scalar=-2048.0, in1=r2[0:2, 0:256],
                               op0=ALU.mult, op1=ALU.add)
        v.tensor_scalar(out=r2[0:2, 1024:1280], in0=r2[0:2, 0:256], scalar1=0.0,
                        scalar2=None, op0=ALU.max)
        cV.one(v.drain())
    halvd = cV.n

    @blk.sync
    def _(sy, need=halvd):
        sy.wait_ge(cV.h, need)
        for s_ in range(SEQ):
            dX.dma(sy.dma_start(out=h_rows[s_, 0, :], in_=r2[s_:s_ + 1, 0:256]))
            dX.dma(sy.dma_start(out=h_rows[s_, 1, :], in_=r2[s_:s_ + 1, 768:1024]))
            dX.dma(sy.dma_start(out=h_rows[s_, 2, :], in_=r2[s_:s_ + 1, 512:768]))
            dX.dma(sy.dma_start(out=h_rows[s_, 3, :], in_=r2[s_:s_ + 1, 1024:1280]))
        sy.wait_ge(dX.h, dX.n)
        for s_ in range(SEQ):
            for b_ in range(2):
                cc = 2 * s_ + b_
                dX.dma(sy.dma_start(out=ridf[:, cc:cc + 1],
                                    in_=h_rows[s_, 3, b_ * 128:(b_ + 1) * 128].unsqueeze(-1)))
    ridd = dX.n

    @blk.vector
    def _(v, need=ridd):
        v.wait_ge(dX.h, need)
        v.tensor_copy(out=ridu, in_=ridf)
        cV.one(v.drain())
    ridc = cV.n

    @blk.gpsimd
    def _(g, need=ridc):
        g.wait_ge(cV.h, need)
        for s_ in range(SEQ):
            for b_ in range(2):
                cc = 2 * s_ + b_
                dGm.dma(g.indirect_dma_start(
                    out=psel_raw[:, cc:cc + 1], out_offset=None,
                    in_=pos2.rearrange("s t -> (s t)").unsqueeze(-1),
                    in_offset=IndirectOffsetOnAxis(ap=ridu[:, cc:cc + 1], axis=0),
                    element_offset=s_ * S))
                dGm.dma(g.indirect_dma_start(
                    out=wsel_raw[:, cc:cc + 1], out_offset=None,
                    in_=h_wrow.rearrange("s two t -> (s two t)").unsqueeze(-1),
                    in_offset=IndirectOffsetOnAxis(ap=ridu[:, cc:cc + 1], axis=0),
                    element_offset=(s_ * 2 + 1) * S))
        cG.seq(g)
    pseld = cG.n
    pseld_dma = dGm.n

    @blk.vector
    def _(v, need=pseld, needd=pseld_dma):
        v.wait_ge(cG.h, need)
        v.wait_ge(dGm.h, needd)
        v.tensor_copy(out=psel_f, in_=psel_raw)
        cV.one(v.drain())
    pselc = cV.n

    @blk.sync
    def _(sy, need=pselc):
        sy.wait_ge(cV.h, need)
        for s_ in range(SEQ):
            for b_ in range(2):
                cc = 2 * s_ + b_
                dX.dma(sy.dma_start(out=h_rows[s_, 4, b_ * 128:(b_ + 1) * 128].unsqueeze(-1),
                                    in_=psel_f[:, cc:cc + 1]))
                dX.dma(sy.dma_start(out=h_rows[s_, 5, b_ * 128:(b_ + 1) * 128].unsqueeze(-1),
                                    in_=wsel_raw[:, cc:cc + 1]))
    rows_done = dX.n

    if PHASE_LIMIT <= 1:
        _emit_finish()
        return nc

    w_in_bf = aCb[:, 0:8192]

    def winbf(k):
        return w_in_bf[:, k * 1024:(k + 1) * 1024]

    for k in range(DCH):
        @blk.sync
        def _(sy, k=k):
            sy.wait_ge(cV.h, cV.n)
            if k == 0:
                sy.wait_ge(dX.h, rows_done)
            dX.dma(sy.dma_start(out=wst, in_=w_in_h[k * P:(k + 1) * P, :]))
        need = dX.n

        @blk.vector
        def _(v, k=k, need=need):
            v.wait_ge(dX.h, need)
            v.tensor_copy(out=winbf(k), in_=wst)
            cV.one(v.drain())
    winc = cV.n

    for e in range(DCH):
        @blk.tensor
        def _(t, e=e, need=winc):
            t.wait_ge(cV.h, need)
            if e > 0:
                t.wait_ge(cS.h, cS.n)
            for k in range(DCH):
                for sl in range(8):
                    _mm = t.matmul(pslice(sl), winbf(k)[:, e * 128:(e + 1) * 128],
                             xn_t(k)[:, sl * 512:(sl + 1) * 512],
                             start=(k == 0), stop=(k == DCH - 1))
            cT.one(_mm)
        need = cT.n

        @blk.scalar
        def _(s, e=e, need=need):
            s.wait_ge(cT.h, need)
            for sl in range(8):
                s.copy(out=u_t(e)[:, sl * 512:(sl + 1) * 512], in_=pslice(sl))
            cS.one(s.drain())
    xind = cS.n

    if PHASE_LIMIT <= 2:
        _emit_finish()
        return nc

    B0 = aC[0:16, 0:4096]
    C0 = aC[32:48, 0:4096]
    zac = aC[0:16, 8192:12288]
    g0 = B0
    sp_c = aC[:, 4096:8192]
    dt_c = sp_c
    du_c = aCb[:, 24576:28672]
    dtl2 = aCb[:, 28672:30720]
    Cbf = aCb[32:48, 8192:12288]
    wxs = wst.bitcast(BF16)[:, 0:768]

    @blk.sync
    def _(sy, need=xind):
        sy.wait_ge(cS.h, need)
        sy.wait_ge(cV.h, cV.n)
        dX.dma(sy.dma_start(out=wst[0:64, 0:1024], in_=w_dt_h))
        dX.dma(sy.dma_start(out=wst[64:128, 0:1024], in_=w_dt_h))
    need = dX.n

    @blk.vector
    def _(v, need=need):
        v.wait_ge(dX.h, need)
        v.tensor_copy(out=wdt_bf, in_=wst[:, 0:1024])
        cV.one(v.drain())
    wdtc = cV.n

    for k in range(DCH):
        @blk.sync
        def _(sy, k=k):
            sy.wait_ge(cV.h, cV.n)
            dX.dma(sy.dma_start(out=wst[:, 512:608], in_=w_x_h[k * P:(k + 1) * P, :]))
        need = dX.n

        @blk.vector
        def _(v, k=k, need=need):
            v.wait_ge(dX.h, need)
            v.tensor_copy(out=wxs[:, k * 96:(k + 1) * 96], in_=wst[:, 512:608])
            cV.one(v.drain())
    wxc = cV.n

    def proj(mlo, mhi, evac):
        @blk.tensor
        def _(t, mlo=mlo, mhi=mhi):
            t.wait_ge(cV.h, cV.n)
            mm = mhi - mlo
            for k in range(DCH):
                for sl in range(8):
                    _mm = t.matmul(pslice(sl)[0:mm, :], wxs[:, k * 96 + mlo:k * 96 + mhi],
                             u_t(k)[:, sl * 512:(sl + 1) * 512],
                             start=(k == 0), stop=(k == DCH - 1))
            cT.one(_mm)
        need = cT.n

        @blk.vector
        def _(v, evac=evac, need=need):
            v.wait_ge(cT.h, need)
            evac(v)
            cV.one(v.drain())

    def ev_dt(v):
        for sl in range(8):
            s2 = sl // 4
            cb = (sl % 4) * 512
            v.tensor_copy(out=dtl2[64 * s2:64 * (s2 + 1),
                                   (sl % 4) * 512:(sl % 4 + 1) * 512]
                          if False else dtl2[64 * (sl // 4):64 * (sl // 4) + 64,
                                             (sl % 4) * 512:(sl % 4) * 512 + 512],
                          in_=pslice(sl)[0:64, :])

    def ev_B(v):
        for sl in range(8):
            v.tensor_copy(out=B0[:, sl * 512:(sl + 1) * 512], in_=pslice(sl)[0:16, :])

    def ev_C(v):
        for sl in range(8):
            v.tensor_copy(out=C0[:, sl * 512:(sl + 1) * 512], in_=pslice(sl)[0:16, :])

    proj(0, 64, ev_dt)
    proj(64, 80, ev_B)
    proj(80, 96, ev_C)
    projd = cV.n

    @blk.sync
    def _(sy, need=projd):
        sy.wait_ge(cV.h, need)
        dX.dma(sy.dma_start(out=h_dtl, in_=dtl2))
    dtl_saved = dX.n

    for c in range(DCH):
        @blk.tensor
        def _(t, c=c):
            t.wait_ge(cV.h, cV.n)
            t.wait_ge(cS.h, cS.n)
            for sl in range(8):
                rb_ = 64 * (sl // 4)
                _mm = t.matmul(pslice(sl), wdt_bf[rb_:rb_ + 64, c * 128:(c + 1) * 128],
                         dtl2[rb_:rb_ + 64,
                              (sl % 4) * 512:(sl % 4) * 512 + 512],
                         start=True, stop=True)
            cT.one(_mm)
        need = cT.n

        @blk.scalar
        def _(s, c=c, need=need):
            s.wait_ge(cT.h, need)
            for sl in range(8):
                s.activation(out=sp_c[:, sl * 512:(sl + 1) * 512], in_=pslice(sl),
                             func=AF.Exp, bias=bdt_s[:, c:c + 1], scale=1.0)
            s.activation(out=sp_c, in_=sp_c, func=AF.Ln, bias=1.0, scale=1.0)
            cS.one(s.drain())
        need = cS.n

        @blk.vector
        def _(v, c=c, need=need):
            v.wait_ge(cS.h, need)
            v.tensor_scalar(out=dt_c, in0=sp_c, scalar1=0.099, scalar2=0.001,
                            op0=ALU.mult, op1=ALU.add)
            v.tensor_mul(out=du_c, in0=dt_c, in1=u_t(c))
            cV.one(v.drain())
        need = cV.n

        @blk.tensor
        def _(t, c=c, need=need):
            t.wait_ge(cV.h, need)
            for sl in range(8):
                _mm = t.matmul(pslice(sl)[0:16, :], A_bf[:, c * 16:(c + 1) * 16],
                         du_c[:, sl * 512:(sl + 1) * 512], start=True, stop=True)
            cT.one(_mm)
        need = cT.n

        @blk.vector
        def _(v, c=c, need=need):
            v.wait_ge(cT.h, need)
            for half in range(2):
                src = (ps1 if half == 0 else ps2)[0:16, 0:2048]
                if c == 0:
                    v.tensor_copy(out=zac[:, half * 2048:(half + 1) * 2048], in_=src)
                else:
                    v.tensor_add(out=zac[:, half * 2048:(half + 1) * 2048],
                                 in0=zac[:, half * 2048:(half + 1) * 2048], in1=src)
            v.scalar_tensor_tensor(out=u_t(c), in0=u_t(c), scalar=dp_s[:, c:c + 1],
                                   in1=xn_t(c), op0=ALU.mult, op1=ALU.add)
            cV.one(v.drain())
    zdone = cV.n

    @blk.vector
    def _(v, need=zdone):
        v.wait_ge(cV.h, need)
        v.tensor_mul(out=g0, in0=B0, in1=zac)
        v.tensor_copy(out=Cbf, in_=C0)
        cV.one(v.drain())
    gdone = cV.n

    @blk.sync
    def _(sy, need=gdone):
        sy.wait_ge(cV.h, need)
        dX.dma(sy.dma_start(out=h_g, in_=g0))
        dX.dma(sy.dma_start(out=h_cc, in_=Cbf))
    gsaved = dX.n

    if PHASE_LIMIT <= 3:
        _emit_finish()
        return nc

    def xsel_t(c):
        return aAb[:, c * 512:(c + 1) * 512]

    @blk.gpsimd
    def _(g, need=gdone):
        g.wait_ge(cV.h, need)
        g.wait_ge(dX.h, rows_done)
        last = None
        for s_ in range(SEQ):
            for c in range(DCH):
                last = g.indirect_copy(
                    out=xsel_t(c)[:, s_ * 256:(s_ + 1) * 256].unsqueeze(-1),
                    data=xn_t(c)[:, s_ * S:(s_ + 1) * S].unsqueeze(-1),
                    idxs=sil[:, 32 + 16 * s_:48 + 16 * s_],
                    i_know_ap_gather_is_preferred=True)
        cG.one(last)
    xseld = cG.n

    mrow = aC[:, 0:2048]
    mcolf = aC[:, 2048:2304]

    def msel_t(s_, b_):
        i = 2 * s_ + b_
        return aCb[:, 13312 + i * 256:13312 + (i + 1) * 256]

    for s_ in range(SEQ):
        for b_ in range(2):
            cc = 2 * s_ + b_

            @blk.gpsimd
            def _(g, s_=s_, b_=b_, cc=cc, gsaved=gsaved):
                g.wait_ge(cV.h, cV.n)
                g.wait_ge(dX.h, gsaved)
                dGm.dma(g.indirect_dma_start(
                    out=mrow, out_offset=None,
                    in_=mask_h[s_][:, 0:S],
                    in_offset=IndirectOffsetOnAxis(ap=ridu[:, cc:cc + 1], axis=0)))
                g.wait_ge(dGm.h, dGm.n)
                cG.one(g.indirect_copy(
                    out=mcolf.unsqueeze(-1), data=mrow.unsqueeze(-1),
                    idxs=sil[:, 32 + 16 * s_:48 + 16 * s_],
                    i_know_ap_gather_is_preferred=True))
            need = cG.n

            @blk.vector
            def _(v, s_=s_, b_=b_, need=need):
                v.wait_ge(cG.h, need)
                v.tensor_copy(out=msel_t(s_, b_), in_=mcolf)
                v.memset(msel_t(s_, b_)[:, 255:256], NEG)
                cV.one(v.drain())
    mseld = cV.n

    if PHASE_LIMIT <= 4:
        _emit_finish()
        return nc

    hn_r = lambda c: aAb[:, 4096 + c * 512:4096 + (c + 1) * 512]
    q_r = lambda c: aAb[:, 8192 + c * 512:8192 + (c + 1) * 512]
    k_r = lambda c: aAb[:, 12288 + c * 512:12288 + (c + 1) * 512]
    v_r = lambda i: aAb[:, 16384 + i * 1024:16384 + (i + 1) * 1024]
    scr2 = lambda i: aAb[:, 20480 + i * 512:20480 + (i + 1) * 512]
    cs_r = lambda i: aA[:, 11264 + i * 256:11264 + (i + 1) * 256]
    x1_r = lambda c: aAb[:, 24576 + c * 512:24576 + (c + 1) * 512]
    hn2_r = lambda c: aAb[:, 28672 + c * 512:28672 + (c + 1) * 512]
    h1_r = lambda f, s_: aAb[:, 4096 + (f * 2 + s_) * 256:4096 + (f * 2 + s_ + 1) * 256]
    wselb = lambda s_: aA[:, 10752 + s_ * 256:10752 + (s_ + 1) * 256]
    rms_b = aC[:, 4096:4352]
    sqs = aCb[:, 8704:8960]
    sc_f = lambda qb: aC[:, 4608 + qb * 256:4608 + (qb + 1) * 256]

    for s_ in range(SEQ):
        for c in range(DCH):
            @blk.scalar
            def _(s, c=c, s_=s_):
                if c == 0 and s_ == 0:
                    s.wait_ge(cV.h, mseld)
                    s.wait_ge(cG.h, xseld)
                else:
                    s.wait_ge(cT.h, cT.n)
                s.activation(out=sqs, in_=xsel_t(c)[:, s_ * 256:(s_ + 1) * 256],
                             func=AF.Square, scale=1.0)
                cS.one(s.drain())
            need = cS.n

            @blk.tensor
            def _(t, c=c, need=need):
                t.wait_ge(cS.h, need)
                _mm = t.matmul(ps1[:, 0:256], ones_bf, sqs, start=(c == 0), stop=(c == DCH - 1))
                cT.one(_mm)
        need = cT.n

        @blk.scalar
        def _(s, need=need):
            s.wait_ge(cT.h, need)
            s.activation(out=rms_b, in_=ps1[:, 0:256], func=AF.Ln, bias=epsrms_c[:, 0:1],
                         scale=1.0 / D)
            s.drain()
            cS.one(s.activation(out=rms_b, in_=rms_b, func=AF.Exp, scale=-0.5))
        need = cS.n

        @blk.vector
        def _(v, s_=s_, need=need):
            v.wait_ge(cS.h, need)
            for c in range(DCH):
                v.scalar_tensor_tensor(out=hn_r(c)[:, s_ * 256:(s_ + 1) * 256],
                                       in0=xsel_t(c)[:, s_ * 256:(s_ + 1) * 256],
                                       scalar=ln1_s[:, c:c + 1], in1=rms_b,
                                       op0=ALU.mult, op1=ALU.mult)
            cV.one(v.drain())
    hnd = cV.n

    wcur = aCb[:, 0:8192]

    def wcur_k(k):
        return wcur[:, k * 1024:(k + 1) * 1024]

    wst2 = corr.bitcast(F32)

    def load_dd(w_hbm):
        tn0 = cT.n
        vn0 = cV.n
        casts = []
        for k in range(DCH):
            stg = wst if k % 2 == 0 else wst2
            @blk.sync
            def _(sy, k=k, w_hbm=w_hbm, stg=stg, vn0=vn0, casts=tuple(casts)):
                if k < 2:
                    sy.wait_ge(cV.h, vn0)
                else:
                    sy.wait_ge(cV.h, casts[k - 2])
                dXp[k % 2].dma(sy.dma_start(out=stg, in_=w_hbm[k * P:(k + 1) * P, :]))
            need = dXp[k % 2].n

            @blk.vector
            def _(v, k=k, need=need, stg=stg, tn0=tn0):
                v.wait_ge(dXp[k % 2].h, need)
                if k < 2:
                    v.wait_ge(cT.h, tn0)
                v.tensor_copy(out=wcur_k(k), in_=stg)
                cV.one(v.drain())
            casts.append(cV.n)
        return cV.n

    def proj_fm(dst_r, wneed):
        sbase = cS.n
        for s_ in range(SEQ):
            for e in range(DCH):
                q = s_ * DCH + e
                @blk.tensor
                def _(t, s_=s_, e=e, q=q, wneed=wneed, sbase=sbase):
                    t.wait_ge(cV.h, wneed)
                    if q >= 4:
                        t.wait_ge(cS.h, sbase + q - 3)
                    else:
                        t.wait_ge(cS.h, sbase)
                    for k in range(DCH):
                        _mm = t.matmul(ps1[:, 512 * (e % 4):512 * (e % 4) + 256],
                                 wcur_k(k)[:, e * 128:(e + 1) * 128],
                                 hn_r(k)[:, s_ * 256:(s_ + 1) * 256],
                                 start=(k == 0), stop=(k == DCH - 1))
                    cT.one(_mm)
                need = cT.n

                @blk.scalar
                def _(s, s_=s_, e=e, dst_r=dst_r, need=need):
                    s.wait_ge(cT.h, need)
                    cS.one(s.copy(out=dst_r(e)[:, s_ * 256:(s_ + 1) * 256],
                                  in_=ps1[:, 512 * (e % 4):512 * (e % 4) + 256]))

    wn = load_dd(wq_h)
    proj_fm(q_r, wn)
    wn = load_dd(wk_h)
    proj_fm(k_r, wn)

    wn = load_dd(wv_h)
    for s_ in range(SEQ):
        for tb in range(2):
            for esl in range(2):
                @blk.tensor
                def _(t, s_=s_, tb=tb, esl=esl, wn=wn):
                    t.wait_ge(cV.h, wn)
                    t.wait_ge(cS.h, cS.n)
                    for k in range(DCH):
                        _mm = t.matmul(ps2[:, 512 * esl:512 * (esl + 1)],
                                 hn_r(k)[:, s_ * 256 + tb * 128:s_ * 256 + (tb + 1) * 128],
                                 wcur_k(k)[:, esl * 512:(esl + 1) * 512],
                                 start=(k == 0), stop=(k == DCH - 1))
                    cT.one(_mm)
                need = cT.n

                @blk.scalar
                def _(s, s_=s_, tb=tb, esl=esl, need=need):
                    s.wait_ge(cT.h, need)
                    s.copy(out=v_r(s_ * 2 + tb)[:, esl * 512:(esl + 1) * 512],
                           in_=ps2[:, 512 * esl:512 * (esl + 1)])
                    cS.one(s.drain())
    vprojd = cS.n

    rp = aC[0:32, 5120:6656]
    for s_ in range(SEQ):
        @blk.sync
        def _(sy, s_=s_):
            sy.wait_ge(cV.h, cV.n)
            dX.dma(sy.dma_start(out=rp[:, 0:256],
                                in_=h_rows[s_, 4, :].unsqueeze(0).broadcast_to([32, KSEL])))
        need = dX.n

        @blk.vector
        def _(v, s_=s_, need=need):
            v.wait_ge(dX.h, need)
            ang = rp[:, 0:256]
            rr = rp[:, 256:512]
            rc = rp[:, 512:768]
            tm = rp[:, 768:1024]
            v.tensor_scalar(out=ang, in0=ang, scalar1=inv32c[0:32, :], scalar2=None,
                            op0=ALU.mult)
            v.tensor_scalar(out=rr, in0=ang, scalar1=1.0 / TWO_PI, scalar2=MAGIC,
                            op0=ALU.mult, op1=ALU.add)
            v.drain()
            v.tensor_scalar(out=rr, in0=rr, scalar1=MAGIC, scalar2=None, op0=ALU.subtract)
            v.drain()
            v.scalar_tensor_tensor(out=rr, in0=rr, scalar=-TWO_PI, in1=ang,
                                   op0=ALU.mult, op1=ALU.add)
            v.tensor_scalar(out=rc, in0=rr, scalar1=TWO_PI / 4, scalar2=None, op0=ALU.add)
            v.tensor_scalar(out=tm, in0=rc, scalar1=TWO_PI / 2, scalar2=None, op0=ALU.is_gt)
            v.drain()
            v.scalar_tensor_tensor(out=rc, in0=tm, scalar=-TWO_PI, in1=rc,
                                   op0=ALU.mult, op1=ALU.add)
            cV.one(v.drain())
        need = cV.n

        @blk.scalar
        def _(s, need=need):
            s.wait_ge(cV.h, need)
            s.activation(out=rp[:, 768:1024], in_=rp[:, 256:512], func=AF.Sin,
                         scale=1.0)
            cS.one(s.activation(out=rp[:, 1024:1280], in_=rp[:, 512:768],
                                func=AF.Sin, scale=1.0))
        need = cS.n

        @blk.vector
        def _(v, need=need):
            v.wait_ge(cS.h, need)
            v.tensor_scalar(out=rp[:, 0:256], in0=rp[:, 768:1024],
                            scalar1=-1.0, scalar2=None, op0=ALU.mult)
            cV.one(v.drain())
        need = cV.n

        @blk.sync
        def _(sy, need=need):
            sy.wait_ge(cV.h, need)
            for rows in range(4):
                dX.dma(sy.dma_start(out=cs_r(0)[rows * 32:(rows + 1) * 32, :],
                                    in_=rp[:, 1024:1280]))
                src = rp[:, 0:256] if rows % 2 == 0 else rp[:, 768:1024]
                dX.dma(sy.dma_start(out=cs_r(1)[rows * 32:(rows + 1) * 32, :], in_=src))
        need = dX.n

        @blk.vector
        def _(v, need=need):
            v.wait_ge(dX.h, need)
            v.tensor_scalar(out=cs_r(2), in0=cs_r(0), scalar1=RSC, scalar2=None,
                            op0=ALU.mult)
            v.tensor_scalar(out=cs_r(3), in0=cs_r(1), scalar1=RSC, scalar2=None,
                            op0=ALU.mult)
            cV.one(v.drain())
        csd = cV.n

        for c in range(DCH):
            @blk.sync
            def _(sy, c=c, s_=s_, csd=csd):
                sy.wait_ge(cV.h, cV.n)
                for (dst, src) in ((scr2(0), q_r(c)), (scr2(1), k_r(c))):
                    for hb2 in range(2):
                        b0 = hb2 * 64
                        dX.dma(sy.dma_start(
                            out=dst[b0:b0 + 32, s_ * 256:(s_ + 1) * 256],
                            in_=src[b0 + 32:b0 + 64, s_ * 256:(s_ + 1) * 256]))
                        dX.dma(sy.dma_start(
                            out=dst[b0 + 32:b0 + 64, s_ * 256:(s_ + 1) * 256],
                            in_=src[b0:b0 + 32, s_ * 256:(s_ + 1) * 256]))
            need = dX.n

            @blk.vector
            def _(v, c=c, s_=s_, need=need):
                v.wait_ge(dX.h, need)
                qs = q_r(c)[:, s_ * 256:(s_ + 1) * 256]
                ks = k_r(c)[:, s_ * 256:(s_ + 1) * 256]
                sh_q = scr2(0)[:, s_ * 256:(s_ + 1) * 256]
                sh_k = scr2(1)[:, s_ * 256:(s_ + 1) * 256]
                tmp = scr2(2)[:, s_ * 256:(s_ + 1) * 256]
                v.tensor_mul(out=tmp, in0=qs, in1=cs_r(2))
                v.tensor_mul(out=sh_q, in0=sh_q, in1=cs_r(3))
                v.tensor_add(out=qs, in0=tmp, in1=sh_q)
                v.tensor_mul(out=tmp, in0=ks, in1=cs_r(0))
                v.tensor_mul(out=sh_k, in0=sh_k, in1=cs_r(1))
                v.tensor_add(out=ks, in0=tmp, in1=sh_k)
                cV.one(v.drain())
    roped = cV.n

    for s_ in range(SEQ):
        for h in range(NH):
            c = h // 2
            hb = (h % 2) * 64
            at0 = scr2(2)
            at1 = scr2(3)
            for qb in range(2):
                @blk.tensor
                def _(t, s_=s_, c=c, hb=hb, qb=qb):
                    t.wait_ge(cV.h, cV.n)
                    _mm = t.matmul(ps1[:, qb * 256:(qb + 1) * 256],
                             q_r(c)[hb:hb + 64,
                                    s_ * 256 + qb * 128:s_ * 256 + (qb + 1) * 128],
                             k_r(c)[hb:hb + 64, s_ * 256:(s_ + 1) * 256],
                             start=True, stop=True)
                    cT.one(_mm)
                need = cT.n

                @blk.vector
                def _(v, s_=s_, qb=qb, need=need):
                    v.wait_ge(cT.h, need)
                    v.tensor_add(out=sc_f(qb), in0=ps1[:, qb * 256:(qb + 1) * 256],
                                 in1=msel_t(s_, qb))
                    v.tensor_reduce(out=Rtmp[:, 1:2], in_=sc_f(qb), axis=AXX, op=ALU.max)
                    v.drain()
                    v.tensor_scalar(out=Rtmp[:, 2:3], in0=Rtmp[:, 1:2], scalar1=-1.0,
                                    scalar2=None, op0=ALU.mult)
                    cV.one(v.drain())
                need = cV.n

                @blk.scalar
                def _(s, qb=qb, need=need):
                    s.wait_ge(cV.h, need)
                    s.activation(out=sqs, in_=sc_f(qb), func=AF.Exp, bias=Rtmp[:, 2:3],
                                 scale=1.0)
                    cS.one(s.drain())
                need = cS.n

                @blk.vector
                def _(v, need=need):
                    v.wait_ge(cS.h, need)
                    v.tensor_reduce(out=Rtmp[:, 3:4], in_=sqs, axis=AXX, op=ALU.add)
                    v.drain()
                    v.reciprocal(out=Rtmp[:, 4:5], in_=Rtmp[:, 3:4])
                    v.drain()
                    v.tensor_scalar(out=sqs, in0=sqs, scalar1=Rtmp[:, 4:5], scalar2=None,
                                    op0=ALU.mult)
                    cV.one(v.drain())
                need = cV.n

                @blk.tensor
                def _(t, qb=qb, need=need):
                    t.wait_ge(cV.h, need)
                    t.transpose(psTb[:, qb * 128:(qb + 1) * 128],
                                sqs[:, 0:128], ident_bf)
                    _mm = t.transpose(psTb[:, 256 + qb * 128:256 + (qb + 1) * 128],
                                sqs[:, 128:256], ident_bf)
                    cT.one(_mm)
                need = cT.n

                @blk.vector
                def _(v, qb=qb, at0=at0, at1=at1, need=need):
                    v.wait_ge(cT.h, need)
                    v.tensor_copy(out=at0[:, qb * 128:(qb + 1) * 128],
                                  in_=psTb[:, qb * 128:(qb + 1) * 128])
                    v.tensor_copy(out=at1[:, qb * 128:(qb + 1) * 128],
                                  in_=psTb[:, 256 + qb * 128:256 + (qb + 1) * 128])
                    cV.one(v.drain())
            need = cV.n

            @blk.tensor
            def _(t, s_=s_, h=h, hb=hb, at0=at0, at1=at1, need=need):
                t.wait_ge(cV.h, need)
                for kb in range(2):
                    at = (at0, at1)[kb]
                    _mm = t.matmul(ps2[hb:hb + 64, 0:256],
                             v_r(s_ * 2 + kb)[:, h * 64:(h + 1) * 64],
                             at[:, 0:256], start=(kb == 0), stop=(kb == 1),
                             tile_position=(0, hb))
                cT.one(_mm)
            need = cT.n

            if h % 2 == 1:
                @blk.scalar
                def _(s, s_=s_, c=c, need=need):
                    s.wait_ge(cT.h, need)
                    s.copy(out=hn_r(c)[:, s_ * 256:(s_ + 1) * 256], in_=ps2[:, 0:256])
                    cS.one(s.drain())
    attnd = cS.n

    wn = load_dd(wo_h)
    wovb = cV.n
    for s_ in range(SEQ):
        for e in range(DCH):
            q = s_ * DCH + e
            @blk.tensor
            def _(t, s_=s_, e=e, q=q, wn=wn):
                t.wait_ge(cV.h, wn)
                if q >= 4:
                    t.wait_ge(cV.h, wovb + q - 3)
                else:
                    t.wait_ge(cS.h, attnd)
                for k in range(DCH):
                    _mm = t.matmul(ps1[:, 512 * (e % 4):512 * (e % 4) + 256],
                             wcur_k(k)[:, e * 128:(e + 1) * 128],
                             hn_r(k)[:, s_ * 256:(s_ + 1) * 256],
                             start=(k == 0), stop=(k == DCH - 1))
                cT.one(_mm)
            need = cT.n

            @blk.vector
            def _(v, s_=s_, e=e, need=need):
                v.wait_ge(cT.h, need)
                cV.one(v.tensor_add(out=x1_r(e)[:, s_ * 256:(s_ + 1) * 256],
                                    in0=ps1[:, 512 * (e % 4):512 * (e % 4) + 256],
                                    in1=xsel_t(e)[:, s_ * 256:(s_ + 1) * 256]))
    x1d = cV.n

    for s_ in range(SEQ):
        for c in range(DCH):
            @blk.scalar
            def _(s, c=c, s_=s_, x1d=x1d):
                s.wait_ge(cV.h, x1d)
                if not (c == 0 and s_ == 0):
                    s.wait_ge(cT.h, cT.n)
                s.activation(out=sqs, in_=x1_r(c)[:, s_ * 256:(s_ + 1) * 256],
                             func=AF.Square, scale=1.0)
                cS.one(s.drain())
            need = cS.n

            @blk.tensor
            def _(t, c=c, need=need):
                t.wait_ge(cS.h, need)
                _mm = t.matmul(ps1[:, 0:256], ones_bf, sqs, start=(c == 0), stop=(c == DCH - 1))
                cT.one(_mm)
        need = cT.n

        @blk.scalar
        def _(s, need=need):
            s.wait_ge(cT.h, need)
            s.activation(out=rms_b, in_=ps1[:, 0:256], func=AF.Ln, bias=epsrms_c[:, 0:1],
                         scale=1.0 / D)
            s.drain()
            cS.one(s.activation(out=rms_b, in_=rms_b, func=AF.Exp, scale=-0.5))
        need = cS.n

        @blk.vector
        def _(v, s_=s_, need=need):
            v.wait_ge(cS.h, need)
            for c in range(DCH):
                v.scalar_tensor_tensor(out=hn2_r(c)[:, s_ * 256:(s_ + 1) * 256],
                                       in0=x1_r(c)[:, s_ * 256:(s_ + 1) * 256],
                                       scalar=ln2_s[:, c:c + 1], in1=rms_b,
                                       op0=ALU.mult, op1=ALU.mult)
            cV.one(v.drain())
    hn2d = cV.n

    w1bf = aCb[:, 0:32768]

    def w1k(k):
        return w1bf[:, k * 4096:(k + 1) * 4096]

    tn0 = cT.n
    vn0 = cV.n
    casts = []
    for k in range(DCH):
        for piece in range(4):
            r = k * 4 + piece
            stg = wst if r % 2 == 0 else wst2
            @blk.sync
            def _(sy, k=k, piece=piece, r=r, stg=stg, vn0=vn0, casts=tuple(casts)):
                if r < 2:
                    sy.wait_ge(cV.h, vn0)
                else:
                    sy.wait_ge(cV.h, casts[r - 2])
                dXp[r % 2].dma(sy.dma_start(out=stg,
                                    in_=w1_h[k * P:(k + 1) * P,
                                             piece * 1024:(piece + 1) * 1024]))
            need = dXp[r % 2].n

            @blk.vector
            def _(v, k=k, piece=piece, r=r, need=need, stg=stg, tn0=tn0):
                v.wait_ge(dXp[r % 2].h, need)
                if r < 2:
                    v.wait_ge(cT.h, tn0)
                v.tensor_copy(out=w1k(k)[:, piece * 1024:(piece + 1) * 1024], in_=stg)
                cV.one(v.drain())
            casts.append(cV.n)
    w1ld = cV.n

    w1sb = cS.n
    for s_ in range(SEQ):
        for f in range(32):
            q = s_ * 32 + f
            @blk.tensor
            def _(t, s_=s_, f=f, q=q, w1ld=w1ld):
                t.wait_ge(cV.h, w1ld)
                if q >= 4:
                    t.wait_ge(cS.h, w1sb + q - 3)
                else:
                    t.wait_ge(cS.h, w1sb)
                for k in range(DCH):
                    _mm = t.matmul(ps1[:, 512 * (f % 4):512 * (f % 4) + 256],
                             w1k(k)[:, f * 128:(f + 1) * 128],
                             hn2_r(k)[:, s_ * 256:(s_ + 1) * 256],
                             start=(k == 0), stop=(k == DCH - 1))
                cT.one(_mm)
            need = cT.n

            @blk.scalar
            def _(s, s_=s_, f=f, need=need):
                s.wait_ge(cT.h, need)
                cS.one(s.activation(out=h1_r(f, s_),
                                    in_=ps1[:, 512 * (f % 4):512 * (f % 4) + 256],
                                    func=AF.Gelu_apprx_tanh, scale=1.0))
    h1d = cS.n

    w2bf = aCb[:, 0:32768]

    def w2k(k):
        return w2bf[:, k * 1024:(k + 1) * 1024]

    tn0 = cT.n
    vn0 = cV.n
    casts = []
    for k in range(32):
        stg = wst if k % 2 == 0 else wst2
        @blk.sync
        def _(sy, k=k, stg=stg, vn0=vn0, casts=tuple(casts)):
            if k < 2:
                sy.wait_ge(cV.h, vn0)
            else:
                sy.wait_ge(cV.h, casts[k - 2])
            dXp[k % 2].dma(sy.dma_start(out=stg, in_=w2_h[k * P:(k + 1) * P, :]))
        need = dXp[k % 2].n

        @blk.vector
        def _(v, k=k, need=need, stg=stg, tn0=tn0):
            v.wait_ge(dXp[k % 2].h, need)
            if k < 2:
                v.wait_ge(cT.h, tn0)
            v.tensor_copy(out=w2k(k), in_=stg)
            cV.one(v.drain())
        casts.append(cV.n)
    w2ld = cV.n

    @blk.sync
    def _(sy, need=w2ld):
        sy.wait_ge(cV.h, need)
        for s_ in range(SEQ):
            dX.dma(sy.dma_start(out=wselb(s_),
                                in_=h_rows[s_, 5, :].unsqueeze(0).broadcast_to([P, KSEL])))
    wselbd = dX.n
    corr_b0 = [0]

    w2vb = cV.n
    for s_ in range(SEQ):
        for e in range(DCH):
            q = s_ * DCH + e
            @blk.tensor
            def _(t, s_=s_, e=e, q=q, w2ld=w2ld):
                t.wait_ge(cV.h, w2ld)
                if q >= 4:
                    t.wait_ge(cV.h, w2vb + q - 3)
                else:
                    t.wait_ge(cS.h, h1d)
                for k in range(32):
                    _mm = t.matmul(ps1[:, 512 * (e % 4):512 * (e % 4) + 256],
                             w2k(k)[:, e * 128:(e + 1) * 128],
                             h1_r(k, s_), start=(k == 0), stop=(k == 31))
                cT.one(_mm)
            need = cT.n

            @blk.vector
            def _(v, s_=s_, e=e, need=need):
                v.wait_ge(cT.h, need)
                v.wait_ge(dX.h, wselbd)
                if s_ == 1 and e == 0:
                    v.wait_ge(dX.h, corr_b0[0])
                x2t = scr2(0)[:, 0:256]
                v.tensor_add(out=x2t, in0=ps1[:, 512 * (e % 4):512 * (e % 4) + 256],
                             in1=x1_r(e)[:, s_ * 256:(s_ + 1) * 256])
                v.drain()
                v.tensor_mul(out=x2t, in0=x2t, in1=wselb(s_))
                v.drain()
                v.tensor_sub(out=corr[:, e * 256:(e + 1) * 256],
                             in0=x2t, in1=xsel_t(e)[:, s_ * 256:(s_ + 1) * 256])
                cV.one(v.drain())
        seq_corr_v = cV.n

        @blk.sync
        def _(sy, s_=s_, need=seq_corr_v):
            sy.wait_ge(cV.h, need)
            dX.dma(sy.dma_start(out=h_corr[s_], in_=corr[:, :]))
        if s_ == 0:
            corr_b0[0] = dX.n
    corrd = cV.n

    if PHASE_LIMIT <= 5:
        _emit_finish()
        return nc

    yac = lambda c: aA[:, c * 1024:(c + 1) * 1024]
    spc6 = lambda c: aA[:, 8192 + c * 1024:8192 + (c + 1) * 1024]
    barA = [aC[:, 0:1024], aC[:, 1024:2048]]
    grep = [aC[:, 2048:3072], aC[:, 3072:4096]]
    crep = [aCb[:, 8192:9216], aCb[:, 9216:10240]]
    hbuf = [aCb[:, 10240:11264], aCb[:, 11264:12288]]
    xh = [aC[:, 6144:7168], aC[:, 7168:8192]]
    outh = [aC[:, 8192:9216], aC[:, 11264:12288]]
    scat = aCb[:, 18432:19456]
    hidx = [aCi16[:, 19456 + i * 256:19456 + (i + 1) * 256] for i in range(4)]
    hstg = aC[:, 12288:13312]
    dtl6 = aCb[:, 20480:22528]

    @blk.sync
    def _(sy, need=corrd):
        sy.wait_ge(cV.h, need)
        sy.wait_ge(cT.h, cT.n)
        dX.dma(sy.dma_start(out=dtl6, in_=h_dtl))
        for s_ in range(SEQ):
            for hf in range(2):
                i = s_ * 2 + hf
                dX.dma(sy.dma_start(out=hstg[:, i * 256:(i + 1) * 256],
                                    in_=h_rows[s_, 1 + hf, :].unsqueeze(0)
                                    .broadcast_to([P, KSEL])))
    need = dX.n

    @blk.vector
    def _(v, need=need):
        v.wait_ge(dX.h, need)
        for i in range(4):
            v.tensor_copy(out=hidx[i], in_=hstg[:, i * 256:(i + 1) * 256])
        cV.one(v.drain())
    hidxd = cV.n

    @blk.gpsimd
    def _(g, need=hidxd):
        g.wait_ge(cV.h, need)
        g.load_library(libcfg.local_scatter)
    lsld = cG.n

    cVs = Ctr(nc, "cVs")
    cSb = Ctr(nc, "cSb")
    corr_rl = [0, 0]

    iglob = 0
    jglob = 0
    gdma = {}
    for s_ in range(SEQ):
        for hf in range(2):
            jh = s_ * 2 + hf
            tlo = s_ * S + hf * HALF

            if hf == 0:
                @blk.sync
                def _(sy, s_=s_):
                    if s_ == 0:
                        sy.wait_ge(cV.h, corrd)
                    else:
                        sy.wait_ge(cG.h, cG.n)
                    dX.dma(sy.dma_start(out=corr[:, :], in_=h_corr[s_]))
                corr_rl[s_] = dX.n

            for rnd in range(2):
                @blk.tensor
                def _(t, s_=s_, hf=hf, rnd=rnd):
                    t.wait_ge(cS.h, cS.n)
                    if rnd == 0:
                        t.wait_ge(cV.h, cV.n)
                        if s_ == 0 and hf == 0:
                            t.wait_ge(dX.h, dX.n)
                        else:
                            t.wait_ge(cVs.h, cVs.n)
                    for ci in range(4):
                        c = rnd * 4 + ci
                        for sl2 in range(2):
                            _mm = t.matmul(pslice(ci * 2 + sl2),
                                     wdt_bf[64 * s_:64 * s_ + 64, c * 128:(c + 1) * 128],
                                     dtl6[64 * s_:64 * s_ + 64,
                                          hf * 1024 + sl2 * 512:hf * 1024 + (sl2 + 1) * 512],
                                     start=True, stop=True)
                    cT.one(_mm)
                need = cT.n

                @blk.scalar
                def _(s, s_=s_, hf=hf, rnd=rnd, need=need):
                    s.wait_ge(cT.h, need)
                    for ci in range(4):
                        c = rnd * 4 + ci
                        for sl2 in range(2):
                            s.activation(out=spc6(c)[:, sl2 * 512:(sl2 + 1) * 512],
                                         in_=pslice(ci * 2 + sl2), func=AF.Exp,
                                         bias=bdt_s[:, c:c + 1], scale=1.0)
                        s.activation(out=spc6(c), in_=spc6(c), func=AF.Ln, bias=1.0,
                                     scale=1.0)
                    cS.one(s.drain())
            spd = cS.n

            @blk.vector
            def _(v, s_=s_, hf=hf, tlo=tlo, spd=spd):
                v.wait_ge(cS.h, spd)
                for c in range(DCH - 1):
                    v.tensor_copy(out=yac(c), in_=u_t(c)[:, tlo:tlo + HALF])
                cV.one(v.tensor_copy(out=yac(DCH - 1), in_=u_t(DCH - 1)[:, tlo:tlo + HALF]))
            yinit = cV.n

            for n_ in range(N):
                j = jglob
                jglob += 1

                @blk.sync
                def _(sy, n_=n_, tlo=tlo, j=j, yinit=yinit):
                    if j >= 2:
                        sy.wait_ge(cVs.h, 8 * (j - 1))
                    if j == 0:
                        sy.wait_ge(cV.h, yinit)
                    dGp[j % 2].dma(sy.dma_start(out=grep[n_ % 2],
                                                in_=h_g[n_, tlo:tlo + HALF].unsqueeze(0)
                                                .broadcast_to([P, HALF])))
                    dGp[j % 2].dma(sy.dma_start(out=crep[n_ % 2],
                                                in_=h_cc[n_, tlo:tlo + HALF].unsqueeze(0)
                                                .broadcast_to([P, HALF])))
                gdma[j] = (j % 2, dGp[j % 2].n)

                for c in range(DCH):
                    i = iglob
                    iglob += 1

                    @blk.scalar
                    def _(s, c=c, n_=n_, i=i, spd=spd):
                        if i >= 2:
                            s.wait_ge(cVs.h, i - 1)
                        col = c * 16 + n_
                        cSb.one(s.activation(out=barA[i % 2], in_=spc6(c), func=AF.Exp,
                                              scale=A099[:, col:col + 1],
                                              bias=A001[:, col:col + 1]))

                    @blk.vector
                    def _(v, s_=s_, hf=hf, c=c, n_=n_, i=i, j=j,
                          sbneed=cSb.n, dneed=gdma[j], yinit=yinit):
                        v.wait_ge(cSb.h, sbneed)
                        if c == 0:
                            v.wait_ge(dGp[dneed[0]].h, dneed[1])
                            if j == 0:
                                v.wait_ge(cV.h, yinit)
                        hcol = (s_ * 8 + c) * 16 + n_
                        if hf == 0:
                            v.tensor_tensor_scan(out=hbuf[i % 2], data0=barA[i % 2],
                                                 data1=grep[n_ % 2], initial=0.0,
                                                 op0=ALU.mult, op1=ALU.add)
                            v.drain()
                            v.tensor_copy(out=hstate[:, hcol:hcol + 1],
                                          in_=hbuf[i % 2][:, HALF - 1:HALF])
                        else:
                            v.tensor_tensor_scan(out=hbuf[i % 2], data0=barA[i % 2],
                                                 data1=grep[n_ % 2],
                                                 initial=hstate[:, hcol:hcol + 1],
                                                 op0=ALU.mult, op1=ALU.add)
                        v.tensor_mul(out=hbuf[i % 2], in0=hbuf[i % 2], in1=crep[n_ % 2])
                        cVs.one(v.tensor_add(out=yac(c), in0=yac(c), in1=hbuf[i % 2]))
            scan_done = cVs.n

            finv = cV.n
            for c in range(DCH):
                @blk.sync
                def _(sy, c=c, tlo=tlo, finv=finv):
                    if c >= 2:
                        sy.wait_ge(cV.h, finv + (c - 1))
                    dHp[c % 2].dma(sy.dma_start(out=xh[c % 2],
                                                in_=xin[c * P:(c + 1) * P, tlo:tlo + HALF]))
                xneed = (c % 2, dHp[c % 2].n)

                @blk.gpsimd
                def _(g, c=c, s_=s_, jh=jh, finv=finv, scan_done=scan_done):
                    if c == 0:
                        g.wait_ge(cVs.h, scan_done)
                    else:
                        g.wait_ge(cV.h, finv + c)
                    if c == 0:
                        g.wait_ge(dX.h, corr_rl[s_])
                    cG.one(g.local_scatter(
                        scat, corr[:, c * 256:(c + 1) * 256],
                        hidx[jh], channels=P, num_elems=HALF, num_idxs=KSEL))
                gneed = cG.n

                @blk.vector
                def _(v, c=c, xneed=xneed, gneed=gneed, scan_done=scan_done):
                    v.wait_ge(dHp[xneed[0]].h, xneed[1])
                    v.wait_ge(cG.h, gneed)
                    v.wait_ge(cVs.h, scan_done)
                    if c >= 2:
                        v.wait_ge(dOp[c % 2].h, dOp[c % 2].n)
                    v.tensor_add(out=outh[c % 2], in0=xh[c % 2], in1=yac(c))
                    v.drain()
                    cV.one(v.tensor_add(out=outh[c % 2], in0=outh[c % 2], in1=scat))
                vneed = cV.n

                @blk.sync
                def _(sy, c=c, tlo=tlo, vneed=vneed):
                    sy.wait_ge(cV.h, vneed)
                    dOp[c % 2].dma(
                        sy.dma_start(out=out_fm[c * P:(c + 1) * P, tlo:tlo + HALF],
                                     in_=outh[c % 2]))

    @blk.sync
    def _(sy):
        sy.wait_ge(dOp[0].h, dOp[0].n)
        sy.wait_ge(dOp[1].h, dOp[1].n)

    blk.__exit__(None, None, None)
    return nc


_NC_CACHE = {}


def _get_nc():
    if "nc" not in _NC_CACHE:
        nc = build_nc()
        lower_extended_insts(nc)
        _NC_CACHE["nc"] = nc
    return _NC_CACHE["nc"]


def kernel(**inputs):
    x = np.asarray(inputs["L0_attn_x"])
    ncores = 8
    eye = np.eye(P, dtype=np.float32)
    ones = np.ones((P, P), dtype=np.float32)
    inv = (1.0 / (10000.0 ** (np.arange(0, HD, 2, dtype=np.float32) / HD))).astype(np.float32)
    cpk = np.zeros((P, 257), dtype=np.float32)
    cpk[:, 0:128] = eye
    cpk[:, 128:256] = ones
    cpk[0:32, 256] = inv
    a_log = np.asarray(inputs["L0_attn_A_log"])
    a_perm = np.ascontiguousarray(
        a_log.reshape(8, 128, 16).transpose(1, 0, 2).reshape(128, 128))

    def c8(vec):
        return np.ascontiguousarray(np.asarray(vec).reshape(8, 128).T)

    mask = np.asarray(inputs["L0_ffn_attention_mask"])[:, 0]
    posi = np.asarray(inputs["L0_ffn_position_ids"])
    base = {
        "w_in": np.asarray(inputs["L0_attn_w_in"]),
        "w_x": np.asarray(inputs["L0_attn_w_x"]),
        "w_dt": np.asarray(inputs["L0_attn_w_dt"]),
        "wq": np.asarray(inputs["L0_ffn_wq"]),
        "wk": np.asarray(inputs["L0_ffn_wk"]),
        "wv": np.asarray(inputs["L0_ffn_wv"]),
        "wo": np.asarray(inputs["L0_ffn_wo"]),
        "w1": np.asarray(inputs["L0_ffn_w1"]),
        "w2": np.asarray(inputs["L0_ffn_w2"]),
        "a_perm": a_perm,
        "bdt8": c8(inputs["L0_attn_b_dt"]),
        "dp8": c8(inputs["L0_attn_Dp"]),
        "ln18": c8(inputs["L0_ffn_ln1"]),
        "ln28": c8(inputs["L0_ffn_ln2"]),
        "rw8": c8(np.asarray(inputs["L0_ffn_router_w"])[:, 0]),
        "rb1": np.asarray(inputs["L0_ffn_router_b"]).reshape(1, 1),
        "cpack": cpk,
    }
    in_maps = []
    for core in range(ncores):
        b0 = 2 * core
        m = dict(base)
        xc = x[b0:b0 + 2]
        m["xin"] = np.ascontiguousarray(xc.transpose(2, 0, 1).reshape(D, T))
        bc0, bc1 = min(b0, 3), min(b0 + 1, 3)
        m["mask0"] = np.ascontiguousarray(mask[bc0, 0:S, 0:S])
        m["mask1"] = np.ascontiguousarray(mask[bc1, 0:S, 0:S])
        m["pos2"] = np.ascontiguousarray(
            np.stack([posi[bc0, 0:S], posi[bc1, 0:S]]).astype(np.int32))
        in_maps.append(m)

    nc = _get_nc()
    res = run_bass_kernel_spmd(nc, in_maps, core_ids=list(range(ncores)))
    outs = []
    for core in range(ncores):
        o = np.asarray(res.results[core]["out_fm"])
        outs.append(o.reshape(D, SEQ, S).transpose(1, 2, 0))
    return np.concatenate(outs, axis=0).astype(np.float32)


# version 06
# speedup vs baseline: 1.1642x; 1.0097x over previous
import numpy as np
import ml_dtypes

import concourse.bass as bass
import concourse.mybir as mybir
import concourse.library_config as libcfg
from concourse.bass import IndirectOffsetOnAxis
from concourse.bass_utils import run_bass_kernel_spmd
from concourse.library_overlay import lower_extended_insts

F32 = mybir.dt.float32
BF16 = mybir.dt.bfloat16
U32 = mybir.dt.uint32
I32 = mybir.dt.int32
I16 = mybir.dt.int16
AF = mybir.ActivationFunctionType
ALU = mybir.AluOpType
AXX = mybir.AxisListType.X

P, D, DCH, S, SEQ = 128, 1024, 8, 2048, 2
T, HALF, N, KSEL, DTRANK, FF, NH, HD = 4096, 1024, 16, 256, 64, 4096, 16, 64
EPS_LN, EPS_RMS, NEG, RSC = 1e-5, 1e-6, -1.0e9, 0.125
TWO_PI = 6.283185307179586
MAGIC = 12582912.0
PHASE_LIMIT = 99
DEBUG_HBM = False


class Ctr:
    def __init__(self, nc, name):
        self.h = nc.alloc_semaphore(name)
        self.n = 0

    def dma(self, instr):
        instr.then_inc(self.h, 16)
        self.n += 16

    def one(self, instr):
        instr.then_inc(self.h, 1)
        self.n += 1

    def seq(self, eng):
        eng.sem_inc(self.h, 1)
        self.n += 1


def build_nc():
    nc = bass.Bass("TRN2", target_bir_lowering=False)

    xin = nc.dram_tensor("xin", [D, T], F32, kind="ExternalInput").ap()
    w_in_h = nc.dram_tensor("w_in", [D, D], F32, kind="ExternalInput").ap()
    w_x_h = nc.dram_tensor("w_x", [D, 96], F32, kind="ExternalInput").ap()
    w_dt_h = nc.dram_tensor("w_dt", [DTRANK, D], F32, kind="ExternalInput").ap()
    wq_h = nc.dram_tensor("wq", [D, D], F32, kind="ExternalInput").ap()
    wk_h = nc.dram_tensor("wk", [D, D], F32, kind="ExternalInput").ap()
    wv_h = nc.dram_tensor("wv", [D, D], F32, kind="ExternalInput").ap()
    wo_h = nc.dram_tensor("wo", [D, D], F32, kind="ExternalInput").ap()
    w1_h = nc.dram_tensor("w1", [D, FF], F32, kind="ExternalInput").ap()
    w2_h = nc.dram_tensor("w2", [FF, D], F32, kind="ExternalInput").ap()
    a_perm = nc.dram_tensor("a_perm", [P, P], F32, kind="ExternalInput").ap()
    bdt8 = nc.dram_tensor("bdt8", [P, DCH], F32, kind="ExternalInput").ap()
    dp8 = nc.dram_tensor("dp8", [P, DCH], F32, kind="ExternalInput").ap()
    ln18 = nc.dram_tensor("ln18", [P, DCH], F32, kind="ExternalInput").ap()
    ln28 = nc.dram_tensor("ln28", [P, DCH], F32, kind="ExternalInput").ap()
    rw8 = nc.dram_tensor("rw8", [P, DCH], F32, kind="ExternalInput").ap()
    rb1 = nc.dram_tensor("rb1", [1, 1], F32, kind="ExternalInput").ap()
    mask_h = [nc.dram_tensor("mask0", [S, S], F32, kind="ExternalInput").ap(),
              nc.dram_tensor("mask1", [S, S], F32, kind="ExternalInput").ap()]
    pos2 = nc.dram_tensor("pos2", [SEQ, S], I32, kind="ExternalInput").ap()
    cpack = nc.dram_tensor("cpack", [P, 257], F32, kind="ExternalInput").ap()
    out_fm = nc.dram_tensor("out_fm", [D, T], F32, kind="ExternalOutput").ap()

    KIND = "ExternalOutput" if DEBUG_HBM else "Internal"
    h_wrow = nc.dram_tensor("h_wrow", [SEQ, 2, S], F32, kind=KIND).ap()
    h_midx = nc.dram_tensor("h_midx", [SEQ, S], F32, kind="Internal").ap()
    h_selw = nc.dram_tensor("h_selw", [SEQ, 256], F32, kind="Internal").ap()
    h_seli = nc.dram_tensor("h_seli", [SEQ, 16, 16], mybir.dt.uint16, kind="Internal").ap()
    h_rows = nc.dram_tensor("h_rows", [SEQ, 6, KSEL], F32, kind=KIND).ap()
    h_g = nc.dram_tensor("h_g", [N, T], F32, kind=KIND).ap()
    h_cc = nc.dram_tensor("h_cc", [N, T], BF16, kind=KIND).ap()
    h_dtl = nc.dram_tensor("h_dtl", [P, S], BF16, kind=KIND).ap()
    h_corr = nc.dram_tensor("h_corr", [SEQ, P, 2048], BF16, kind=KIND).ap()
    h_dbg = nc.dram_tensor("h_dbg", [P, 16384], F32, kind=KIND).ap() if DEBUG_HBM else None

    AR_A = nc.alloc_sbuf_tensor("AR_A", [P, 16384], F32)
    AR_B = nc.alloc_sbuf_tensor("AR_B", [P, 16384], F32)
    AR_C = nc.alloc_sbuf_tensor("AR_C", [P, 16384], F32)
    SMALL = nc.alloc_sbuf_tensor("SMALL", [P, 1792], F32)
    WST = nc.alloc_sbuf_tensor("WST", [P, 1024], F32)
    CORR = nc.alloc_sbuf_tensor("CORR", [P, 2048], BF16)
    SEL = nc.alloc_sbuf_tensor("SEL", [P, 96], mybir.dt.uint16)

    aA, aB, aC = AR_A.ap(), AR_B.ap(), AR_C.ap()
    aAb, aBb, aCb = aA.bitcast(BF16), aB.bitcast(BF16), aC.bitcast(BF16)
    aCi16 = aC.bitcast(I16)
    r1 = aB[0:32, 0:2048]
    r2 = aB[0:32, 2048:4096]
    r2i = r2.bitcast(I32)
    sm = SMALL.ap()
    smb = sm.bitcast(BF16)
    wst = WST.ap()
    corr = CORR.ap()
    sil = SEL.ap()
    slw = aB[0:32, 4096:5632]

    A_f = sm[:, 0:128]
    A099 = sm[:, 128:256]
    A001 = sm[:, 256:384]
    A_bf = smb[:, 768:896]
    bdt_s = sm[:, 448:456]
    dp_s = sm[:, 456:464]
    ln1_s = sm[:, 464:472]
    ln2_s = sm[:, 472:480]
    rw_s = sm[:, 480:488]
    thr2 = sm[:, 488:489]
    rb2 = sm[:, 489:490]
    nRcol = sm[:, 490:491]
    Rtmp = sm[:, 491:499]
    kth_o = sm[:, 500:504]
    w128t = [sm[:, 504:520], sm[:, 520:536]]
    ridf = sm[:, 536:540]
    ridu = sm.bitcast(U32)[:, 540:544]
    psel_raw = sm.bitcast(I32)[:, 544:548]
    psel_f = sm[:, 548:552]
    wsel_raw = sm[:, 552:556]
    inv32c = sm[:, 556:557]
    epsln_c = sm[:, 557:558]
    epsrms_c = sm[:, 558:559]
    identf_st = sm[:, 560:688]
    ones_f = sm[:, 688:816]
    ident_bf = smb[:, 1632:1760]
    ones_bf = smb[:, 1760:1888]
    hstate = sm[:, 944:1200]
    wdt_bf = smb[:, 2400:3424]
    nopv = sm[0:1, 1712:1713]
    nops = sm[0:1, 1713:1714]

    PS1 = nc.alloc_psum_tensor("PS1", [P, 2048], F32)
    PS2 = nc.alloc_psum_tensor("PS2", [P, 2048], F32)
    ps1, ps2 = PS1.ap(), PS2.ap()
    psTb = ps2[:, 1536:2048].bitcast(BF16)

    dX = Ctr(nc, "dX")
    dCst = Ctr(nc, "dCst")
    dXp = [Ctr(nc, "dXp0"), Ctr(nc, "dXp1")]
    dGp = [Ctr(nc, "dGp0"), Ctr(nc, "dGp1")]
    dHp = [Ctr(nc, "dHp0"), Ctr(nc, "dHp1")]
    dOp = [Ctr(nc, "dOp0"), Ctr(nc, "dOp1")]
    dO = Ctr(nc, "dO")
    dGm = Ctr(nc, "dGm")
    cV = Ctr(nc, "cV")
    cS = Ctr(nc, "cS")
    cT = Ctr(nc, "cT")
    cG = Ctr(nc, "cG")

    def xn_t(c):
        return aAb[:, c * 4096:(c + 1) * 4096]

    def u_t(c):
        return aBb[:, c * 4096:(c + 1) * 4096]

    def pslice(sl):
        return (ps1 if sl < 4 else ps2)[:, (sl % 4) * 512:(sl % 4 + 1) * 512]

    blk = nc.Block("main")
    blk.__enter__()

    def _emit_finish():
        @blk.vector
        def _(v):
            v.wait_ge(cS.h, cS.n)
            v.wait_ge(cT.h, cT.n)
            v.wait_ge(cG.h, cG.n)
            v.memset(aC[:, 0:4096], 0.0)
            cV.one(v.drain())

        @blk.sync
        def _(sy):
            sy.wait_ge(dX.h, dX.n)
            sy.wait_ge(cV.h, cV.n)
            sy.wait_ge(dCst.h, dCst.n)
            sy.wait_ge(dGm.h, dGm.n)
            for p in (dXp + dGp + dHp):
                sy.wait_ge(p.h, p.n)
            for c in range(DCH):
                dO.dma(sy.dma_start(out=out_fm[c * P:(c + 1) * P, :], in_=aC[:, 0:4096]))
            sy.wait_ge(dO.h, dO.n)
        blk.__exit__(None, None, None)


    xstage = [aC[:, 0:4096], aC[:, 4096:8192]]
    racc = aC[:, 8192:12288]
    sqb = [aCb[:, 24576:26624], aCb[:, 26624:28672]]

    @blk.sync
    def _(sy):
        for dst, src in [(A_f, a_perm), (bdt_s, bdt8), (dp_s, dp8), (ln1_s, ln18),
                         (ln2_s, ln28), (rw_s, rw8), (rb2[0:1, :], rb1), (rb2[1:2, :], rb1),
                         (identf_st, cpack[:, 0:128]), (ones_f, cpack[:, 128:256]),
                         (inv32c[0:32, :], cpack[0:32, 256:257])]:
            dCst.dma(sy.dma_start(out=dst, in_=src, allow_slow_non_contiguous=True))
    ph1_small = dCst.n

    @blk.gpsimd
    def _(g):
        cG.one(g.iota(r2i[0:2, :], pattern=[[1, 2048]], base=1, channel_multiplier=0))
    iota_done = cG.n

    @blk.scalar
    def _(s):
        s.wait_ge(dCst.h, ph1_small)
        cS.one(s.activation(out=A_f, in_=A_f, func=AF.Exp, scale=1.0))
    aexp = cS.n

    if PHASE_LIMIT <= -1:
        _emit_finish()
        return nc

    vbase_x = cV.n
    for c in range(DCH):
        @blk.sync
        def _(sy, c=c):
            if c >= 2:
                sy.wait_ge(cV.h, vbase_x + (c - 1))
            dXp[c % 2].dma(sy.dma_start(out=xstage[c % 2], in_=xin[c * P:(c + 1) * P, :]))
        need = dXp[c % 2].n

        @blk.vector
        def _(v, c=c, need=need):
            v.wait_ge(dXp[c % 2].h, need)
            if c == 0:
                v.wait_ge(dCst.h, ph1_small)
                v.wait_ge(cG.h, iota_done)
                v.wait_ge(cS.h, aexp)
                v.tensor_copy(out=ident_bf, in_=identf_st)
                v.tensor_copy(out=ones_bf, in_=ones_f)
                v.memset(epsln_c, EPS_LN)
                v.memset(epsrms_c, EPS_RMS)
                v.tensor_copy(out=r2[0:2, :], in_=r2i[0:2, :])
                v.tensor_scalar(out=A_f, in0=A_f, scalar1=-1.0, scalar2=None, op0=ALU.mult)
                v.drain()
                v.tensor_scalar(out=A099, in0=A_f, scalar1=0.099, scalar2=None, op0=ALU.mult)
                v.tensor_scalar(out=A001, in0=A_f, scalar1=0.001, scalar2=None, op0=ALU.mult)
                v.tensor_copy(out=A_bf, in_=A_f)
            v.tensor_copy(out=xn_t(c), in_=xstage[c % 2])
            if c == 0:
                v.scalar_tensor_tensor(out=racc, in0=xstage[0], scalar=rw_s[:, 0:1],
                                       in1=xstage[0], op0=ALU.mult, op1=ALU.bypass)
            else:
                v.scalar_tensor_tensor(out=racc, in0=xstage[c % 2], scalar=rw_s[:, c:c + 1],
                                       in1=racc, op0=ALU.mult, op1=ALU.add)
            cV.one(v.drain())
    xstream = cV.n


    if PHASE_LIMIT <= 0:
        _emit_finish()
        return nc

    m_b = aC[:, 0:2048]
    r_b = aC[:, 2048:4096]
    m_bf = aCb[:, 8192:10240]
    r_bf = aCb[:, 10240:12288]
    msq = aC[:, 6144:8192]
    w_bc = aC[:, 14336:16384]

    for s_ in range(SEQ):
        col0 = s_ * S
        for c in range(DCH):
            @blk.scalar
            def _(s, c=c, s_=s_):
                if s_ == 0 and c == 0:
                    s.wait_ge(cV.h, xstream)
                if not (s_ == 0 and c in (0, 1)):
                    s.wait_ge(cT.h, cT.n)
                s.activation(out=sqb[c % 2], in_=xn_t(c)[:, col0:col0 + S],
                             func=AF.Square, scale=1.0)
                cS.one(s.drain())
            need = cS.n

            @blk.tensor
            def _(t, c=c, s_=s_, need=need):
                t.wait_ge(cS.h, need)
                if s_ == 1 and c == 0:
                    t.wait_ge(cV.h, cV.n)
                for sl in range(4):
                    t.matmul(ps1[:, sl * 512:(sl + 1) * 512], ones_bf,
                             xn_t(c)[:, col0 + sl * 512:col0 + (sl + 1) * 512],
                             start=(c == 0), stop=(c == DCH - 1))
                for sl in range(4):
                    _mm = t.matmul(ps2[:, sl * 512:(sl + 1) * 512], ones_bf,
                             sqb[c % 2][:, sl * 512:(sl + 1) * 512],
                             start=(c == 0), stop=(c == DCH - 1))
                cT.one(_mm)
        smm = cT.n

        if PHASE_LIMIT <= 0.2:
            _emit_finish()
            return nc

        @blk.vector
        def _(v, need=smm):
            v.wait_ge(cT.h, need)
            v.tensor_scalar(out=m_b, in0=ps1[:, 0:2048], scalar1=1.0 / D, scalar2=None,
                            op0=ALU.mult)
            cV.one(v.drain())
        mdone = cV.n

        if DEBUG_HBM:
            @blk.sync
            def _(sy, s_=s_, need=mdone):
                sy.wait_ge(cV.h, need)
                if s_ == 0:
                    dX.dma(sy.dma_start(out=h_dbg[:, 12416:14464], in_=m_b))
                sy.wait_ge(dX.h, dX.n)

        @blk.scalar
        def _(s, need=mdone):
            s.wait_ge(cV.h, need)
            if DEBUG_HBM:
                s.wait_ge(dX.h, dX.n)
            cS.one(s.activation(out=msq, in_=m_b, func=AF.Square, scale=1.0))
        msqd = cS.n

        @blk.vector
        def _(v, need=msqd):
            v.wait_ge(cS.h, need)
            v.scalar_tensor_tensor(out=r_b, in0=ps2[:, 0:2048], scalar=1.0 / D, in1=msq,
                                   op0=ALU.mult, op1=ALU.subtract)
            cV.one(v.drain())
        vard = cV.n

        @blk.scalar
        def _(s, need=vard):
            s.wait_ge(cV.h, need)
            s.activation(out=r_b, in_=r_b, func=AF.Ln, bias=epsln_c[:, 0:1], scale=1.0)
            cS.one(s.activation(out=r_b, in_=r_b, func=AF.Exp, scale=-0.5))
        rstdd = cS.n

        if DEBUG_HBM:
            @blk.sync
            def _(sy, s_=s_, need=rstdd):
                sy.wait_ge(cS.h, need)
                dX.dma(sy.dma_start(out=h_dbg[:, 8192 + s_ * 2048:8192 + (s_ + 1) * 2048],
                                    in_=r_b))

        @blk.vector
        def _(v, s_=s_, need=rstdd):
            v.wait_ge(cS.h, need)
            if DEBUG_HBM:
                v.wait_ge(dX.h, dX.n)
            v.tensor_copy(out=m_bf, in_=m_b)
            v.tensor_copy(out=r_bf, in_=r_b)
            for c in range(DCH):
                xseg = xn_t(c)[:, col0:col0 + S]
                v.tensor_sub(out=sqb[0], in0=xseg, in1=m_bf)
                v.tensor_mul(out=xseg, in0=sqb[0], in1=r_bf)
            cV.one(v.drain())
        xnd = cV.n

        if PHASE_LIMIT <= 0.4:
            _emit_finish()
            return nc

        @blk.tensor
        def _(t, s_=s_, need=xnd):
            t.wait_ge(cV.h, need)
            t.matmul(ps2[:, 0:8], ones_f, rw_s, start=True, stop=True)
            for sl in range(4):
                _mm = t.matmul(ps1[:, sl * 512:(sl + 1) * 512], ones_f,
                         racc[:, col0 + sl * 512:col0 + (sl + 1) * 512],
                         start=True, stop=True)
            cT.one(_mm)
        rxd = cT.n

        @blk.vector
        def _(v, s_=s_, need=rxd):
            v.wait_ge(cT.h, need)
            if DEBUG_HBM:
                v.tensor_copy(out=msq, in_=ps1[:, 0:2048])
            v.tensor_reduce(out=Rtmp[:, 0:1], in_=ps2[:, 0:8], axis=AXX, op=ALU.add)
            v.drain()
            v.tensor_scalar(out=nRcol, in0=Rtmp[:, 0:1], scalar1=-1.0, scalar2=None,
                            op0=ALU.mult)
            v.drain()
            v.scalar_tensor_tensor(out=w_bc, in0=m_b, scalar=nRcol[:, 0:1],
                                   in1=ps1[:, 0:2048], op0=ALU.mult, op1=ALU.add)
            v.drain()
            v.tensor_mul(out=w_bc, in0=w_bc, in1=r_b)
            cV.one(v.drain())
        wd = cV.n

        if DEBUG_HBM:
            @blk.sync
            def _(sy, s_=s_, need=wd):
                sy.wait_ge(cV.h, need)
                dX.dma(sy.dma_start(out=h_dbg[:, 4096 + s_ * 2048:4096 + (s_ + 1) * 2048],
                                    in_=msq))
                if s_ == 0:
                    dX.dma(sy.dma_start(out=h_dbg[:, 0:2048], in_=w_bc))
                    dX.dma(sy.dma_start(out=h_dbg[:, 2048:2056], in_=sm[:, 488:496]))
                    dX.dma(sy.dma_start(out=h_dbg[:, 12288:12416], in_=ones_f))
                sy.wait_ge(dX.h, dX.n)

        if PHASE_LIMIT <= 0.6:
            _emit_finish()
            return nc

        @blk.sync
        def _(sy, s_=s_, need=wd):
            sy.wait_ge(cV.h, need)
            dX.dma(sy.dma_start(out=h_wrow[s_, 0, :], in_=w_bc[0:1, :]))
            sy.wait_ge(dX.h, dX.n)
            dX.dma(sy.dma_start(out=w128t[s_],
                                in_=h_wrow[s_, 0, :].rearrange("(p f) -> p f", p=P)))
    wwrap = dX.n

    if PHASE_LIMIT <= 0.8:
        _emit_finish()
        return nc

    @blk.gpsimd
    def _(g, need=wwrap):
        g.wait_ge(dX.h, need)
        g.load_library(libcfg.attn)
        q = 1.0 - 254.5 / (S - 1)
        g.kth_largest(kth_o[0:1, 0:2], w128t[0], n_per_lane=S // P,
                      k=KSEL + 2, quantile=q)
        cG.one(g.kth_largest(kth_o[0:1, 2:4], w128t[1], n_per_lane=S // P,
                             k=KSEL + 2, quantile=q))
    kthd = cG.n

    @blk.sync
    def _(sy, need=kthd):
        sy.wait_ge(cG.h, need)
        dX.dma(sy.dma_start(out=thr2[0:1, :], in_=kth_o[0:1, 1:2]))
        dX.dma(sy.dma_start(out=thr2[1:2, :], in_=kth_o[0:1, 3:4]))
        dX.dma(sy.dma_start(out=r1[0:2, :], in_=h_wrow[:, 0, :]))
    thrd = dX.n

    @blk.vector
    def _(v, need=thrd):
        v.wait_ge(dX.h, need)
        v.scalar_tensor_tensor(out=r2[0:2, :], in0=r1[0:2, :], scalar=thr2[0:2, :],
                               in1=r2[0:2, :], op0=ALU.is_gt, op1=ALU.mult)
        v.tensor_scalar(out=r2[0:2, :], in0=r2[0:2, :], scalar1=1.0, scalar2=None,
                        op0=ALU.subtract)
        v.scalar_tensor_tensor(out=r1[0:2, :], in0=r1[0:2, :], scalar=rb2[0:2, :],
                               in1=r1[0:2, :], op0=ALU.add, op1=ALU.bypass)
        cV.one(v.drain())
    midxd = cV.n

    @blk.sync
    def _(sy, need=midxd):
        sy.wait_ge(cV.h, need)
        dX.dma(sy.dma_start(out=h_midx, in_=r2[0:2, :]))
        dX.dma(sy.dma_start(out=h_wrow[0, 1, :], in_=r1[0:1, :]))
        dX.dma(sy.dma_start(out=h_wrow[1, 1, :], in_=r1[1:2, :]))
    mbounce = dX.n

    gp_sparse_base = cG.n
    for s_ in range(SEQ):
        @blk.sync
        def _(sy, s_=s_, need=mbounce):
            sy.wait_ge(dX.h, need)
            if s_ == 1:
                sy.wait_ge(cG.h, gp_sparse_base + 1)
            dX.dma(sy.dma_start(out=slw[0:16, 0:128],
                                in_=h_midx[s_].rearrange("(f p) -> p f", p=16),
                                allow_slow_non_contiguous=True))
        wdone = dX.n

        @blk.gpsimd
        def _(g, s_=s_, need=wdone):
            if s_ == 0:
                g.load_library(libcfg.sparse_gather)
            g.wait_ge(dX.h, need)
            nf_u = sil.bitcast(U32)[0:1, 44:45]
            cG.one(g.sparse_gather(out=slw[0:16, 128 + 16 * s_:144 + 16 * s_],
                                   in_=slw[0:16, 0:128], num_found=nf_u))
    sparsed = cG.n

    @blk.vector
    def _(v, need=sparsed):
        v.wait_ge(cG.h, need)
        v.tensor_scalar(out=slw[0:16, 128:160], in0=slw[0:16, 128:160], scalar1=0.0,
                        scalar2=2047.0, op0=ALU.max, op1=ALU.min)
        v.drain()
        v.tensor_copy(out=sil[0:16, 0:32], in_=slw[0:16, 128:160])
        cV.one(v.drain())
    selwd = cV.n

    @blk.sync
    def _(sy, need=selwd):
        sy.wait_ge(cV.h, need)
        for s_ in range(SEQ):
            dX.dma(sy.dma_start(out=h_selw[s_].rearrange("(f p) -> p f", p=16),
                                in_=slw[0:16, 128 + 16 * s_:144 + 16 * s_],
                                allow_slow_non_contiguous=True))
            dX.dma(sy.dma_start(out=h_seli[s_], in_=sil[0:16, 16 * s_:16 * (s_ + 1)]))
        sy.wait_ge(dX.h, dX.n)
        dX.dma(sy.dma_start(out=r2[0:2, 0:256], in_=h_selw))
        for s_ in range(SEQ):
            for grp in range(8):
                dX.dma(sy.dma_start(out=sil[grp * 16:(grp + 1) * 16,
                                            32 + 16 * s_:48 + 16 * s_],
                                    in_=h_seli[s_]))
    lind = dX.n

    @blk.vector
    def _(v, need=lind):
        v.wait_ge(dX.h, need)
        v.memset(r2[0:2, 255:256], -1.0)
        v.drain()
        v.tensor_scalar(out=r2[0:2, 0:256], in0=r2[0:2, 0:256], scalar1=-1.0,
                        scalar2=2047.0, op0=ALU.max, op1=ALU.min)
        v.drain()
        v.tensor_scalar(out=r2[0:2, 512:768], in0=r2[0:2, 0:256], scalar1=1024.0,
                        scalar2=None, op0=ALU.subtract)
        v.tensor_scalar(out=r2[0:2, 768:1024], in0=r2[0:2, 0:256], scalar1=1024.0,
                        scalar2=None, op0=ALU.is_ge)
        v.drain()
        v.scalar_tensor_tensor(out=r2[0:2, 768:1024], in0=r2[0:2, 768:1024],
                               scalar=-2048.0, in1=r2[0:2, 0:256],
                               op0=ALU.mult, op1=ALU.add)
        v.tensor_scalar(out=r2[0:2, 1024:1280], in0=r2[0:2, 0:256], scalar1=0.0,
                        scalar2=None, op0=ALU.max)
        cV.one(v.drain())
    halvd = cV.n

    @blk.sync
    def _(sy, need=halvd):
        sy.wait_ge(cV.h, need)
        for s_ in range(SEQ):
            dX.dma(sy.dma_start(out=h_rows[s_, 0, :], in_=r2[s_:s_ + 1, 0:256]))
            dX.dma(sy.dma_start(out=h_rows[s_, 1, :], in_=r2[s_:s_ + 1, 768:1024]))
            dX.dma(sy.dma_start(out=h_rows[s_, 2, :], in_=r2[s_:s_ + 1, 512:768]))
            dX.dma(sy.dma_start(out=h_rows[s_, 3, :], in_=r2[s_:s_ + 1, 1024:1280]))
        sy.wait_ge(dX.h, dX.n)
        for s_ in range(SEQ):
            for b_ in range(2):
                cc = 2 * s_ + b_
                dX.dma(sy.dma_start(out=ridf[:, cc:cc + 1],
                                    in_=h_rows[s_, 3, b_ * 128:(b_ + 1) * 128].unsqueeze(-1)))
    ridd = dX.n

    @blk.vector
    def _(v, need=ridd):
        v.wait_ge(dX.h, need)
        v.tensor_copy(out=ridu, in_=ridf)
        cV.one(v.drain())
    ridc = cV.n

    @blk.gpsimd
    def _(g, need=ridc):
        g.wait_ge(cV.h, need)
        for s_ in range(SEQ):
            for b_ in range(2):
                cc = 2 * s_ + b_
                dGm.dma(g.indirect_dma_start(
                    out=psel_raw[:, cc:cc + 1], out_offset=None,
                    in_=pos2.rearrange("s t -> (s t)").unsqueeze(-1),
                    in_offset=IndirectOffsetOnAxis(ap=ridu[:, cc:cc + 1], axis=0),
                    element_offset=s_ * S))
                dGm.dma(g.indirect_dma_start(
                    out=wsel_raw[:, cc:cc + 1], out_offset=None,
                    in_=h_wrow.rearrange("s two t -> (s two t)").unsqueeze(-1),
                    in_offset=IndirectOffsetOnAxis(ap=ridu[:, cc:cc + 1], axis=0),
                    element_offset=(s_ * 2 + 1) * S))
        cG.seq(g)
    pseld = cG.n
    pseld_dma = dGm.n

    @blk.vector
    def _(v, need=pseld, needd=pseld_dma):
        v.wait_ge(cG.h, need)
        v.wait_ge(dGm.h, needd)
        v.tensor_copy(out=psel_f, in_=psel_raw)
        cV.one(v.drain())
    pselc = cV.n

    @blk.sync
    def _(sy, need=pselc):
        sy.wait_ge(cV.h, need)
        for s_ in range(SEQ):
            for b_ in range(2):
                cc = 2 * s_ + b_
                dX.dma(sy.dma_start(out=h_rows[s_, 4, b_ * 128:(b_ + 1) * 128].unsqueeze(-1),
                                    in_=psel_f[:, cc:cc + 1]))
                dX.dma(sy.dma_start(out=h_rows[s_, 5, b_ * 128:(b_ + 1) * 128].unsqueeze(-1),
                                    in_=wsel_raw[:, cc:cc + 1]))
    rows_done = dX.n

    if PHASE_LIMIT <= 1:
        _emit_finish()
        return nc

    w_in_bf = aCb[:, 0:8192]

    def winbf(k):
        return w_in_bf[:, k * 1024:(k + 1) * 1024]

    for k in range(DCH):
        @blk.sync
        def _(sy, k=k):
            sy.wait_ge(cV.h, cV.n)
            if k == 0:
                sy.wait_ge(dX.h, rows_done)
            dX.dma(sy.dma_start(out=wst, in_=w_in_h[k * P:(k + 1) * P, :]))
        need = dX.n

        @blk.vector
        def _(v, k=k, need=need):
            v.wait_ge(dX.h, need)
            v.tensor_copy(out=winbf(k), in_=wst)
            cV.one(v.drain())
    winc = cV.n

    for e in range(DCH):
        @blk.tensor
        def _(t, e=e, need=winc):
            t.wait_ge(cV.h, need)
            if e > 0:
                t.wait_ge(cS.h, cS.n)
            for k in range(DCH):
                for sl in range(8):
                    _mm = t.matmul(pslice(sl), winbf(k)[:, e * 128:(e + 1) * 128],
                             xn_t(k)[:, sl * 512:(sl + 1) * 512],
                             start=(k == 0), stop=(k == DCH - 1))
            cT.one(_mm)
        need = cT.n

        @blk.scalar
        def _(s, e=e, need=need):
            s.wait_ge(cT.h, need)
            for sl in range(8):
                s.copy(out=u_t(e)[:, sl * 512:(sl + 1) * 512], in_=pslice(sl))
            cS.one(s.drain())
    xind = cS.n

    if PHASE_LIMIT <= 2:
        _emit_finish()
        return nc

    B0 = aC[0:16, 0:4096]
    C0 = aC[32:48, 0:4096]
    zac = aC[0:16, 8192:12288]
    g0 = B0
    sp_c = aC[:, 4096:8192]
    dt_c = sp_c
    du_c = aCb[:, 24576:28672]
    dtl2 = aCb[:, 28672:30720]
    Cbf = aCb[32:48, 8192:12288]
    wxs = wst.bitcast(BF16)[:, 0:768]

    @blk.sync
    def _(sy, need=xind):
        sy.wait_ge(cS.h, need)
        sy.wait_ge(cV.h, cV.n)
        dX.dma(sy.dma_start(out=wst[0:64, 0:1024], in_=w_dt_h))
        dX.dma(sy.dma_start(out=wst[64:128, 0:1024], in_=w_dt_h))
    need = dX.n

    @blk.vector
    def _(v, need=need):
        v.wait_ge(dX.h, need)
        v.tensor_copy(out=wdt_bf, in_=wst[:, 0:1024])
        cV.one(v.drain())
    wdtc = cV.n

    for k in range(DCH):
        @blk.sync
        def _(sy, k=k):
            sy.wait_ge(cV.h, cV.n)
            dX.dma(sy.dma_start(out=wst[:, 512:608], in_=w_x_h[k * P:(k + 1) * P, :]))
        need = dX.n

        @blk.vector
        def _(v, k=k, need=need):
            v.wait_ge(dX.h, need)
            v.tensor_copy(out=wxs[:, k * 96:(k + 1) * 96], in_=wst[:, 512:608])
            cV.one(v.drain())
    wxc = cV.n

    def proj(mlo, mhi, evac):
        @blk.tensor
        def _(t, mlo=mlo, mhi=mhi):
            t.wait_ge(cV.h, cV.n)
            mm = mhi - mlo
            for k in range(DCH):
                for sl in range(8):
                    _mm = t.matmul(pslice(sl)[0:mm, :], wxs[:, k * 96 + mlo:k * 96 + mhi],
                             u_t(k)[:, sl * 512:(sl + 1) * 512],
                             start=(k == 0), stop=(k == DCH - 1))
            cT.one(_mm)
        need = cT.n

        @blk.vector
        def _(v, evac=evac, need=need):
            v.wait_ge(cT.h, need)
            evac(v)
            cV.one(v.drain())

    def ev_dt(v):
        for sl in range(8):
            s2 = sl // 4
            cb = (sl % 4) * 512
            v.tensor_copy(out=dtl2[64 * s2:64 * (s2 + 1),
                                   (sl % 4) * 512:(sl % 4 + 1) * 512]
                          if False else dtl2[64 * (sl // 4):64 * (sl // 4) + 64,
                                             (sl % 4) * 512:(sl % 4) * 512 + 512],
                          in_=pslice(sl)[0:64, :])

    def ev_B(v):
        for sl in range(8):
            v.tensor_copy(out=B0[:, sl * 512:(sl + 1) * 512], in_=pslice(sl)[0:16, :])

    def ev_C(v):
        for sl in range(8):
            v.tensor_copy(out=C0[:, sl * 512:(sl + 1) * 512], in_=pslice(sl)[0:16, :])

    proj(0, 64, ev_dt)
    proj(64, 80, ev_B)
    proj(80, 96, ev_C)
    projd = cV.n

    @blk.sync
    def _(sy, need=projd):
        sy.wait_ge(cV.h, need)
        dX.dma(sy.dma_start(out=h_dtl, in_=dtl2))
    dtl_saved = dX.n

    for c in range(DCH):
        @blk.tensor
        def _(t, c=c):
            t.wait_ge(cV.h, cV.n)
            t.wait_ge(cS.h, cS.n)
            for sl in range(8):
                rb_ = 64 * (sl // 4)
                _mm = t.matmul(pslice(sl), wdt_bf[rb_:rb_ + 64, c * 128:(c + 1) * 128],
                         dtl2[rb_:rb_ + 64,
                              (sl % 4) * 512:(sl % 4) * 512 + 512],
                         start=True, stop=True)
            cT.one(_mm)
        need = cT.n

        @blk.scalar
        def _(s, c=c, need=need):
            s.wait_ge(cT.h, need)
            for sl in range(8):
                s.activation(out=sp_c[:, sl * 512:(sl + 1) * 512], in_=pslice(sl),
                             func=AF.Exp, bias=bdt_s[:, c:c + 1], scale=1.0)
            s.activation(out=sp_c, in_=sp_c, func=AF.Ln, bias=1.0, scale=1.0)
            cS.one(s.drain())
        need = cS.n

        @blk.vector
        def _(v, c=c, need=need):
            v.wait_ge(cS.h, need)
            v.tensor_scalar(out=dt_c, in0=sp_c, scalar1=0.099, scalar2=0.001,
                            op0=ALU.mult, op1=ALU.add)
            v.tensor_mul(out=du_c, in0=dt_c, in1=u_t(c))
            cV.one(v.drain())
        need = cV.n

        @blk.tensor
        def _(t, c=c, need=need):
            t.wait_ge(cV.h, need)
            for sl in range(8):
                _mm = t.matmul(pslice(sl)[0:16, :], A_bf[:, c * 16:(c + 1) * 16],
                         du_c[:, sl * 512:(sl + 1) * 512], start=True, stop=True)
            cT.one(_mm)
        need = cT.n

        @blk.vector
        def _(v, c=c, need=need):
            v.wait_ge(cT.h, need)
            for half in range(2):
                src = (ps1 if half == 0 else ps2)[0:16, 0:2048]
                if c == 0:
                    v.tensor_copy(out=zac[:, half * 2048:(half + 1) * 2048], in_=src)
                else:
                    v.tensor_add(out=zac[:, half * 2048:(half + 1) * 2048],
                                 in0=zac[:, half * 2048:(half + 1) * 2048], in1=src)
            v.scalar_tensor_tensor(out=u_t(c), in0=u_t(c), scalar=dp_s[:, c:c + 1],
                                   in1=xn_t(c), op0=ALU.mult, op1=ALU.add)
            cV.one(v.drain())
    zdone = cV.n

    @blk.vector
    def _(v, need=zdone):
        v.wait_ge(cV.h, need)
        v.tensor_mul(out=g0, in0=B0, in1=zac)
        v.tensor_copy(out=Cbf, in_=C0)
        cV.one(v.drain())
    gdone = cV.n

    @blk.sync
    def _(sy, need=gdone):
        sy.wait_ge(cV.h, need)
        dX.dma(sy.dma_start(out=h_g, in_=g0))
        dX.dma(sy.dma_start(out=h_cc, in_=Cbf))
    gsaved = dX.n

    if PHASE_LIMIT <= 3:
        _emit_finish()
        return nc

    def xsel_t(c):
        return aAb[:, c * 512:(c + 1) * 512]

    @blk.gpsimd
    def _(g, need=gdone):
        g.wait_ge(cV.h, need)
        g.wait_ge(dX.h, rows_done)
        last = None
        for s_ in range(SEQ):
            for c in range(DCH):
                last = g.indirect_copy(
                    out=xsel_t(c)[:, s_ * 256:(s_ + 1) * 256].unsqueeze(-1),
                    data=xn_t(c)[:, s_ * S:(s_ + 1) * S].unsqueeze(-1),
                    idxs=sil[:, 32 + 16 * s_:48 + 16 * s_],
                    i_know_ap_gather_is_preferred=True)
        cG.one(last)
    xseld = cG.n

    mrow = aC[:, 0:2048]
    mcolf = aC[:, 2048:2304]

    def msel_t(s_, b_):
        i = 2 * s_ + b_
        return aCb[:, 13312 + i * 256:13312 + (i + 1) * 256]

    for s_ in range(SEQ):
        for b_ in range(2):
            cc = 2 * s_ + b_

            @blk.gpsimd
            def _(g, s_=s_, b_=b_, cc=cc, gsaved=gsaved):
                g.wait_ge(cV.h, cV.n)
                g.wait_ge(dX.h, gsaved)
                dGm.dma(g.indirect_dma_start(
                    out=mrow, out_offset=None,
                    in_=mask_h[s_][:, 0:S],
                    in_offset=IndirectOffsetOnAxis(ap=ridu[:, cc:cc + 1], axis=0)))
                g.wait_ge(dGm.h, dGm.n)
                cG.one(g.indirect_copy(
                    out=mcolf.unsqueeze(-1), data=mrow.unsqueeze(-1),
                    idxs=sil[:, 32 + 16 * s_:48 + 16 * s_],
                    i_know_ap_gather_is_preferred=True))
            need = cG.n

            @blk.vector
            def _(v, s_=s_, b_=b_, need=need):
                v.wait_ge(cG.h, need)
                v.tensor_copy(out=msel_t(s_, b_), in_=mcolf)
                v.memset(msel_t(s_, b_)[:, 255:256], NEG)
                cV.one(v.drain())
    mseld = cV.n

    if PHASE_LIMIT <= 4:
        _emit_finish()
        return nc

    hn_r = lambda c: aAb[:, 4096 + c * 512:4096 + (c + 1) * 512]
    q_r = lambda c: aAb[:, 8192 + c * 512:8192 + (c + 1) * 512]
    k_r = lambda c: aAb[:, 12288 + c * 512:12288 + (c + 1) * 512]
    v_r = lambda i: aAb[:, 16384 + i * 1024:16384 + (i + 1) * 1024]
    scr2 = lambda i: aAb[:, 20480 + i * 512:20480 + (i + 1) * 512]
    cs_r = lambda i: aA[:, 11264 + i * 256:11264 + (i + 1) * 256]
    x1_r = lambda c: aAb[:, 24576 + c * 512:24576 + (c + 1) * 512]
    hn2_r = lambda c: aAb[:, 28672 + c * 512:28672 + (c + 1) * 512]
    h1_r = lambda f, s_: aAb[:, 4096 + (f * 2 + s_) * 256:4096 + (f * 2 + s_ + 1) * 256]
    wselb = lambda s_: aA[:, 10752 + s_ * 256:10752 + (s_ + 1) * 256]
    rms_b = aC[:, 4096:4352]
    sqs = aCb[:, 8704:8960]
    sc_f = lambda qb: aC[:, 4608 + qb * 256:4608 + (qb + 1) * 256]

    for s_ in range(SEQ):
        for c in range(DCH):
            @blk.scalar
            def _(s, c=c, s_=s_):
                if c == 0 and s_ == 0:
                    s.wait_ge(cV.h, mseld)
                    s.wait_ge(cG.h, xseld)
                else:
                    s.wait_ge(cT.h, cT.n)
                s.activation(out=sqs, in_=xsel_t(c)[:, s_ * 256:(s_ + 1) * 256],
                             func=AF.Square, scale=1.0)
                cS.one(s.drain())
            need = cS.n

            @blk.tensor
            def _(t, c=c, need=need):
                t.wait_ge(cS.h, need)
                _mm = t.matmul(ps1[:, 0:256], ones_bf, sqs, start=(c == 0), stop=(c == DCH - 1))
                cT.one(_mm)
        need = cT.n

        @blk.scalar
        def _(s, need=need):
            s.wait_ge(cT.h, need)
            s.activation(out=rms_b, in_=ps1[:, 0:256], func=AF.Ln, bias=epsrms_c[:, 0:1],
                         scale=1.0 / D)
            s.drain()
            cS.one(s.activation(out=rms_b, in_=rms_b, func=AF.Exp, scale=-0.5))
        need = cS.n

        @blk.vector
        def _(v, s_=s_, need=need):
            v.wait_ge(cS.h, need)
            for c in range(DCH):
                v.scalar_tensor_tensor(out=hn_r(c)[:, s_ * 256:(s_ + 1) * 256],
                                       in0=xsel_t(c)[:, s_ * 256:(s_ + 1) * 256],
                                       scalar=ln1_s[:, c:c + 1], in1=rms_b,
                                       op0=ALU.mult, op1=ALU.mult)
            cV.one(v.drain())
    hnd = cV.n

    wcur = aCb[:, 0:8192]

    def wcur_k(k):
        return wcur[:, k * 1024:(k + 1) * 1024]

    wst2 = corr.bitcast(F32)

    def load_dd(w_hbm):
        tn0 = cT.n
        vn0 = cV.n
        casts = []
        for k in range(DCH):
            stg = wst if k % 2 == 0 else wst2
            @blk.sync
            def _(sy, k=k, w_hbm=w_hbm, stg=stg, vn0=vn0, casts=tuple(casts)):
                if k < 2:
                    sy.wait_ge(cV.h, vn0)
                else:
                    sy.wait_ge(cV.h, casts[k - 2])
                dXp[k % 2].dma(sy.dma_start(out=stg, in_=w_hbm[k * P:(k + 1) * P, :]))
            need = dXp[k % 2].n

            @blk.vector
            def _(v, k=k, need=need, stg=stg, tn0=tn0):
                v.wait_ge(dXp[k % 2].h, need)
                if k < 2:
                    v.wait_ge(cT.h, tn0)
                v.tensor_copy(out=wcur_k(k), in_=stg)
                cV.one(v.drain())
            casts.append(cV.n)
        return cV.n

    def proj_fm(dst_r, wneed):
        sbase = cS.n
        for s_ in range(SEQ):
            for e in range(DCH):
                q = s_ * DCH + e
                @blk.tensor
                def _(t, s_=s_, e=e, q=q, wneed=wneed, sbase=sbase):
                    t.wait_ge(cV.h, wneed)
                    if q >= 4:
                        t.wait_ge(cS.h, sbase + q - 3)
                    else:
                        t.wait_ge(cS.h, sbase)
                    for k in range(DCH):
                        _mm = t.matmul(ps1[:, 512 * (e % 4):512 * (e % 4) + 256],
                                 wcur_k(k)[:, e * 128:(e + 1) * 128],
                                 hn_r(k)[:, s_ * 256:(s_ + 1) * 256],
                                 start=(k == 0), stop=(k == DCH - 1))
                    cT.one(_mm)
                need = cT.n

                @blk.scalar
                def _(s, s_=s_, e=e, dst_r=dst_r, need=need):
                    s.wait_ge(cT.h, need)
                    cS.one(s.copy(out=dst_r(e)[:, s_ * 256:(s_ + 1) * 256],
                                  in_=ps1[:, 512 * (e % 4):512 * (e % 4) + 256]))

    wn = load_dd(wq_h)
    proj_fm(q_r, wn)
    wn = load_dd(wk_h)
    proj_fm(k_r, wn)

    wn = load_dd(wv_h)
    for s_ in range(SEQ):
        for tb in range(2):
            for esl in range(2):
                @blk.tensor
                def _(t, s_=s_, tb=tb, esl=esl, wn=wn):
                    t.wait_ge(cV.h, wn)
                    t.wait_ge(cS.h, cS.n)
                    for k in range(DCH):
                        _mm = t.matmul(ps2[:, 512 * esl:512 * (esl + 1)],
                                 hn_r(k)[:, s_ * 256 + tb * 128:s_ * 256 + (tb + 1) * 128],
                                 wcur_k(k)[:, esl * 512:(esl + 1) * 512],
                                 start=(k == 0), stop=(k == DCH - 1))
                    cT.one(_mm)
                need = cT.n

                @blk.scalar
                def _(s, s_=s_, tb=tb, esl=esl, need=need):
                    s.wait_ge(cT.h, need)
                    s.copy(out=v_r(s_ * 2 + tb)[:, esl * 512:(esl + 1) * 512],
                           in_=ps2[:, 512 * esl:512 * (esl + 1)])
                    cS.one(s.drain())
    vprojd = cS.n

    rp = aC[0:32, 5120:6656]
    for s_ in range(SEQ):
        @blk.sync
        def _(sy, s_=s_):
            sy.wait_ge(cV.h, cV.n)
            dX.dma(sy.dma_start(out=rp[:, 0:256],
                                in_=h_rows[s_, 4, :].unsqueeze(0).broadcast_to([32, KSEL])))
        need = dX.n

        @blk.vector
        def _(v, s_=s_, need=need):
            v.wait_ge(dX.h, need)
            ang = rp[:, 0:256]
            rr = rp[:, 256:512]
            rc = rp[:, 512:768]
            tm = rp[:, 768:1024]
            v.tensor_scalar(out=ang, in0=ang, scalar1=inv32c[0:32, :], scalar2=None,
                            op0=ALU.mult)
            v.tensor_scalar(out=rr, in0=ang, scalar1=1.0 / TWO_PI, scalar2=MAGIC,
                            op0=ALU.mult, op1=ALU.add)
            v.drain()
            v.tensor_scalar(out=rr, in0=rr, scalar1=MAGIC, scalar2=None, op0=ALU.subtract)
            v.drain()
            v.scalar_tensor_tensor(out=rr, in0=rr, scalar=-TWO_PI, in1=ang,
                                   op0=ALU.mult, op1=ALU.add)
            v.tensor_scalar(out=rc, in0=rr, scalar1=TWO_PI / 4, scalar2=None, op0=ALU.add)
            v.tensor_scalar(out=tm, in0=rc, scalar1=TWO_PI / 2, scalar2=None, op0=ALU.is_gt)
            v.drain()
            v.scalar_tensor_tensor(out=rc, in0=tm, scalar=-TWO_PI, in1=rc,
                                   op0=ALU.mult, op1=ALU.add)
            cV.one(v.drain())
        need = cV.n

        @blk.scalar
        def _(s, need=need):
            s.wait_ge(cV.h, need)
            s.activation(out=rp[:, 768:1024], in_=rp[:, 256:512], func=AF.Sin,
                         scale=1.0)
            cS.one(s.activation(out=rp[:, 1024:1280], in_=rp[:, 512:768],
                                func=AF.Sin, scale=1.0))
        need = cS.n

        @blk.vector
        def _(v, need=need):
            v.wait_ge(cS.h, need)
            v.tensor_scalar(out=rp[:, 0:256], in0=rp[:, 768:1024],
                            scalar1=-1.0, scalar2=None, op0=ALU.mult)
            cV.one(v.drain())
        need = cV.n

        @blk.sync
        def _(sy, need=need):
            sy.wait_ge(cV.h, need)
            for rows in range(4):
                dX.dma(sy.dma_start(out=cs_r(0)[rows * 32:(rows + 1) * 32, :],
                                    in_=rp[:, 1024:1280]))
                src = rp[:, 0:256] if rows % 2 == 0 else rp[:, 768:1024]
                dX.dma(sy.dma_start(out=cs_r(1)[rows * 32:(rows + 1) * 32, :], in_=src))
        need = dX.n

        @blk.vector
        def _(v, need=need):
            v.wait_ge(dX.h, need)
            v.tensor_scalar(out=cs_r(2), in0=cs_r(0), scalar1=RSC, scalar2=None,
                            op0=ALU.mult)
            v.tensor_scalar(out=cs_r(3), in0=cs_r(1), scalar1=RSC, scalar2=None,
                            op0=ALU.mult)
            cV.one(v.drain())
        csd = cV.n

        for c in range(DCH):
            @blk.sync
            def _(sy, c=c, s_=s_, csd=csd):
                sy.wait_ge(cV.h, cV.n)
                for (dst, src) in ((scr2(0), q_r(c)), (scr2(1), k_r(c))):
                    for hb2 in range(2):
                        b0 = hb2 * 64
                        dX.dma(sy.dma_start(
                            out=dst[b0:b0 + 32, s_ * 256:(s_ + 1) * 256],
                            in_=src[b0 + 32:b0 + 64, s_ * 256:(s_ + 1) * 256]))
                        dX.dma(sy.dma_start(
                            out=dst[b0 + 32:b0 + 64, s_ * 256:(s_ + 1) * 256],
                            in_=src[b0:b0 + 32, s_ * 256:(s_ + 1) * 256]))
            need = dX.n

            @blk.vector
            def _(v, c=c, s_=s_, need=need):
                v.wait_ge(dX.h, need)
                qs = q_r(c)[:, s_ * 256:(s_ + 1) * 256]
                ks = k_r(c)[:, s_ * 256:(s_ + 1) * 256]
                sh_q = scr2(0)[:, s_ * 256:(s_ + 1) * 256]
                sh_k = scr2(1)[:, s_ * 256:(s_ + 1) * 256]
                tmp = scr2(2)[:, s_ * 256:(s_ + 1) * 256]
                v.tensor_mul(out=tmp, in0=qs, in1=cs_r(2))
                v.tensor_mul(out=sh_q, in0=sh_q, in1=cs_r(3))
                v.tensor_add(out=qs, in0=tmp, in1=sh_q)
                v.tensor_mul(out=tmp, in0=ks, in1=cs_r(0))
                v.tensor_mul(out=sh_k, in0=sh_k, in1=cs_r(1))
                v.tensor_add(out=ks, in0=tmp, in1=sh_k)
                cV.one(v.drain())
    roped = cV.n

    for s_ in range(SEQ):
        for h in range(NH):
            c = h // 2
            hb = (h % 2) * 64
            at0 = scr2(2)
            at1 = scr2(3)
            for qb in range(2):
                @blk.tensor
                def _(t, s_=s_, c=c, hb=hb, qb=qb):
                    t.wait_ge(cV.h, cV.n)
                    _mm = t.matmul(ps1[:, qb * 256:(qb + 1) * 256],
                             q_r(c)[hb:hb + 64,
                                    s_ * 256 + qb * 128:s_ * 256 + (qb + 1) * 128],
                             k_r(c)[hb:hb + 64, s_ * 256:(s_ + 1) * 256],
                             start=True, stop=True)
                    cT.one(_mm)
                need = cT.n

                @blk.vector
                def _(v, s_=s_, qb=qb, need=need):
                    v.wait_ge(cT.h, need)
                    v.tensor_add(out=sc_f(qb), in0=ps1[:, qb * 256:(qb + 1) * 256],
                                 in1=msel_t(s_, qb))
                    v.tensor_reduce(out=Rtmp[:, 1:2], in_=sc_f(qb), axis=AXX, op=ALU.max)
                    v.drain()
                    v.tensor_scalar(out=Rtmp[:, 2:3], in0=Rtmp[:, 1:2], scalar1=-1.0,
                                    scalar2=None, op0=ALU.mult)
                    cV.one(v.drain())
                need = cV.n

                @blk.scalar
                def _(s, qb=qb, need=need):
                    s.wait_ge(cV.h, need)
                    s.activation(out=sqs, in_=sc_f(qb), func=AF.Exp, bias=Rtmp[:, 2:3],
                                 scale=1.0)
                    cS.one(s.drain())
                need = cS.n

                @blk.vector
                def _(v, need=need):
                    v.wait_ge(cS.h, need)
                    v.tensor_reduce(out=Rtmp[:, 3:4], in_=sqs, axis=AXX, op=ALU.add)
                    v.drain()
                    v.reciprocal(out=Rtmp[:, 4:5], in_=Rtmp[:, 3:4])
                    v.drain()
                    v.tensor_scalar(out=sqs, in0=sqs, scalar1=Rtmp[:, 4:5], scalar2=None,
                                    op0=ALU.mult)
                    cV.one(v.drain())
                need = cV.n

                @blk.tensor
                def _(t, qb=qb, need=need):
                    t.wait_ge(cV.h, need)
                    t.transpose(psTb[:, qb * 128:(qb + 1) * 128],
                                sqs[:, 0:128], ident_bf)
                    _mm = t.transpose(psTb[:, 256 + qb * 128:256 + (qb + 1) * 128],
                                sqs[:, 128:256], ident_bf)
                    cT.one(_mm)
                need = cT.n

                @blk.vector
                def _(v, qb=qb, at0=at0, at1=at1, need=need):
                    v.wait_ge(cT.h, need)
                    v.tensor_copy(out=at0[:, qb * 128:(qb + 1) * 128],
                                  in_=psTb[:, qb * 128:(qb + 1) * 128])
                    v.tensor_copy(out=at1[:, qb * 128:(qb + 1) * 128],
                                  in_=psTb[:, 256 + qb * 128:256 + (qb + 1) * 128])
                    cV.one(v.drain())
            need = cV.n

            @blk.tensor
            def _(t, s_=s_, h=h, hb=hb, at0=at0, at1=at1, need=need):
                t.wait_ge(cV.h, need)
                for kb in range(2):
                    at = (at0, at1)[kb]
                    _mm = t.matmul(ps2[hb:hb + 64, 0:256],
                             v_r(s_ * 2 + kb)[:, h * 64:(h + 1) * 64],
                             at[:, 0:256], start=(kb == 0), stop=(kb == 1),
                             tile_position=(0, hb))
                cT.one(_mm)
            need = cT.n

            if h % 2 == 1:
                @blk.scalar
                def _(s, s_=s_, c=c, need=need):
                    s.wait_ge(cT.h, need)
                    s.copy(out=hn_r(c)[:, s_ * 256:(s_ + 1) * 256], in_=ps2[:, 0:256])
                    cS.one(s.drain())
    attnd = cS.n

    wn = load_dd(wo_h)
    wovb = cV.n
    for s_ in range(SEQ):
        for e in range(DCH):
            q = s_ * DCH + e
            @blk.tensor
            def _(t, s_=s_, e=e, q=q, wn=wn):
                t.wait_ge(cV.h, wn)
                if q >= 4:
                    t.wait_ge(cV.h, wovb + q - 3)
                else:
                    t.wait_ge(cS.h, attnd)
                for k in range(DCH):
                    _mm = t.matmul(ps1[:, 512 * (e % 4):512 * (e % 4) + 256],
                             wcur_k(k)[:, e * 128:(e + 1) * 128],
                             hn_r(k)[:, s_ * 256:(s_ + 1) * 256],
                             start=(k == 0), stop=(k == DCH - 1))
                cT.one(_mm)
            need = cT.n

            @blk.vector
            def _(v, s_=s_, e=e, need=need):
                v.wait_ge(cT.h, need)
                cV.one(v.tensor_add(out=x1_r(e)[:, s_ * 256:(s_ + 1) * 256],
                                    in0=ps1[:, 512 * (e % 4):512 * (e % 4) + 256],
                                    in1=xsel_t(e)[:, s_ * 256:(s_ + 1) * 256]))
    x1d = cV.n

    for s_ in range(SEQ):
        for c in range(DCH):
            @blk.scalar
            def _(s, c=c, s_=s_, x1d=x1d):
                s.wait_ge(cV.h, x1d)
                if not (c == 0 and s_ == 0):
                    s.wait_ge(cT.h, cT.n)
                s.activation(out=sqs, in_=x1_r(c)[:, s_ * 256:(s_ + 1) * 256],
                             func=AF.Square, scale=1.0)
                cS.one(s.drain())
            need = cS.n

            @blk.tensor
            def _(t, c=c, need=need):
                t.wait_ge(cS.h, need)
                _mm = t.matmul(ps1[:, 0:256], ones_bf, sqs, start=(c == 0), stop=(c == DCH - 1))
                cT.one(_mm)
        need = cT.n

        @blk.scalar
        def _(s, need=need):
            s.wait_ge(cT.h, need)
            s.activation(out=rms_b, in_=ps1[:, 0:256], func=AF.Ln, bias=epsrms_c[:, 0:1],
                         scale=1.0 / D)
            s.drain()
            cS.one(s.activation(out=rms_b, in_=rms_b, func=AF.Exp, scale=-0.5))
        need = cS.n

        @blk.vector
        def _(v, s_=s_, need=need):
            v.wait_ge(cS.h, need)
            for c in range(DCH):
                v.scalar_tensor_tensor(out=hn2_r(c)[:, s_ * 256:(s_ + 1) * 256],
                                       in0=x1_r(c)[:, s_ * 256:(s_ + 1) * 256],
                                       scalar=ln2_s[:, c:c + 1], in1=rms_b,
                                       op0=ALU.mult, op1=ALU.mult)
            cV.one(v.drain())
    hn2d = cV.n

    w1bf = aCb[:, 0:32768]

    def w1k(k):
        return w1bf[:, k * 4096:(k + 1) * 4096]

    tn0 = cT.n
    vn0 = cV.n
    casts = []
    for k in range(DCH):
        for piece in range(4):
            r = k * 4 + piece
            stg = wst if r % 2 == 0 else wst2
            @blk.sync
            def _(sy, k=k, piece=piece, r=r, stg=stg, vn0=vn0, casts=tuple(casts)):
                if r < 2:
                    sy.wait_ge(cV.h, vn0)
                else:
                    sy.wait_ge(cV.h, casts[r - 2])
                dXp[r % 2].dma(sy.dma_start(out=stg,
                                    in_=w1_h[k * P:(k + 1) * P,
                                             piece * 1024:(piece + 1) * 1024]))
            need = dXp[r % 2].n

            @blk.vector
            def _(v, k=k, piece=piece, r=r, need=need, stg=stg, tn0=tn0):
                v.wait_ge(dXp[r % 2].h, need)
                if r < 2:
                    v.wait_ge(cT.h, tn0)
                v.tensor_copy(out=w1k(k)[:, piece * 1024:(piece + 1) * 1024], in_=stg)
                cV.one(v.drain())
            casts.append(cV.n)
    w1ld = cV.n

    w1sb = cS.n
    for s_ in range(SEQ):
        for f in range(32):
            q = s_ * 32 + f
            @blk.tensor
            def _(t, s_=s_, f=f, q=q, w1ld=w1ld):
                t.wait_ge(cV.h, w1ld)
                if q >= 4:
                    t.wait_ge(cS.h, w1sb + q - 3)
                else:
                    t.wait_ge(cS.h, w1sb)
                for k in range(DCH):
                    _mm = t.matmul(ps1[:, 512 * (f % 4):512 * (f % 4) + 256],
                             w1k(k)[:, f * 128:(f + 1) * 128],
                             hn2_r(k)[:, s_ * 256:(s_ + 1) * 256],
                             start=(k == 0), stop=(k == DCH - 1))
                cT.one(_mm)
            need = cT.n

            @blk.scalar
            def _(s, s_=s_, f=f, need=need):
                s.wait_ge(cT.h, need)
                cS.one(s.activation(out=h1_r(f, s_),
                                    in_=ps1[:, 512 * (f % 4):512 * (f % 4) + 256],
                                    func=AF.Gelu_apprx_tanh, scale=1.0))
    h1d = cS.n

    w2bf = aCb[:, 0:32768]

    def w2k(k):
        return w2bf[:, k * 1024:(k + 1) * 1024]

    tn0 = cT.n
    vn0 = cV.n
    casts = []
    for k in range(32):
        stg = wst if k % 2 == 0 else wst2
        @blk.sync
        def _(sy, k=k, stg=stg, vn0=vn0, casts=tuple(casts)):
            if k < 2:
                sy.wait_ge(cV.h, vn0)
            else:
                sy.wait_ge(cV.h, casts[k - 2])
            dXp[k % 2].dma(sy.dma_start(out=stg, in_=w2_h[k * P:(k + 1) * P, :]))
        need = dXp[k % 2].n

        @blk.vector
        def _(v, k=k, need=need, stg=stg, tn0=tn0):
            v.wait_ge(dXp[k % 2].h, need)
            if k < 2:
                v.wait_ge(cT.h, tn0)
            v.tensor_copy(out=w2k(k), in_=stg)
            cV.one(v.drain())
        casts.append(cV.n)
    w2ld = cV.n

    @blk.sync
    def _(sy, need=w2ld):
        sy.wait_ge(cV.h, need)
        for s_ in range(SEQ):
            dX.dma(sy.dma_start(out=wselb(s_),
                                in_=h_rows[s_, 5, :].unsqueeze(0).broadcast_to([P, KSEL])))
    wselbd = dX.n
    corr_b0 = [0]

    w2vb = cV.n
    for s_ in range(SEQ):
        for e in range(DCH):
            q = s_ * DCH + e
            @blk.tensor
            def _(t, s_=s_, e=e, q=q, w2ld=w2ld):
                t.wait_ge(cV.h, w2ld)
                if q >= 4:
                    t.wait_ge(cV.h, w2vb + q - 3)
                else:
                    t.wait_ge(cS.h, h1d)
                for k in range(32):
                    _mm = t.matmul(ps1[:, 512 * (e % 4):512 * (e % 4) + 256],
                             w2k(k)[:, e * 128:(e + 1) * 128],
                             h1_r(k, s_), start=(k == 0), stop=(k == 31))
                cT.one(_mm)
            need = cT.n

            @blk.vector
            def _(v, s_=s_, e=e, need=need):
                v.wait_ge(cT.h, need)
                v.wait_ge(dX.h, wselbd)
                if s_ == 1 and e == 0:
                    v.wait_ge(dX.h, corr_b0[0])
                x2t = scr2(0)[:, 0:256]
                v.tensor_add(out=x2t, in0=ps1[:, 512 * (e % 4):512 * (e % 4) + 256],
                             in1=x1_r(e)[:, s_ * 256:(s_ + 1) * 256])
                v.drain()
                v.tensor_mul(out=x2t, in0=x2t, in1=wselb(s_))
                v.drain()
                v.tensor_sub(out=corr[:, e * 256:(e + 1) * 256],
                             in0=x2t, in1=xsel_t(e)[:, s_ * 256:(s_ + 1) * 256])
                cV.one(v.drain())
        seq_corr_v = cV.n

        @blk.sync
        def _(sy, s_=s_, need=seq_corr_v):
            sy.wait_ge(cV.h, need)
            dX.dma(sy.dma_start(out=h_corr[s_], in_=corr[:, :]))
        if s_ == 0:
            corr_b0[0] = dX.n
    corrd = cV.n

    if PHASE_LIMIT <= 5:
        _emit_finish()
        return nc

    yac = lambda c: aA[:, c * 1024:(c + 1) * 1024]
    spc6 = lambda c: aA[:, 8192 + c * 1024:8192 + (c + 1) * 1024]
    barA = [aC[:, 0:1024], aC[:, 1024:2048]]
    grep = [aC[:, 2048:3072], aC[:, 3072:4096]]
    crep = [aCb[:, 8192:9216], aCb[:, 9216:10240]]
    hbuf = [aCb[:, 10240:11264], aCb[:, 11264:12288]]
    xh = [aC[:, 6144:7168], aC[:, 7168:8192]]
    outh = [aC[:, 8192:9216], aC[:, 11264:12288]]
    scat = aCb[:, 18432:19456]
    hidx = [aCi16[:, 19456 + i * 256:19456 + (i + 1) * 256] for i in range(4)]
    hstg = aC[:, 12288:13312]
    dtl6 = aCb[:, 20480:22528]

    @blk.sync
    def _(sy, need=corrd):
        sy.wait_ge(cV.h, need)
        sy.wait_ge(cT.h, cT.n)
        dX.dma(sy.dma_start(out=dtl6, in_=h_dtl))
        for s_ in range(SEQ):
            for hf in range(2):
                i = s_ * 2 + hf
                dX.dma(sy.dma_start(out=hstg[:, i * 256:(i + 1) * 256],
                                    in_=h_rows[s_, 1 + hf, :].unsqueeze(0)
                                    .broadcast_to([P, KSEL])))
    need = dX.n

    @blk.vector
    def _(v, need=need):
        v.wait_ge(dX.h, need)
        for i in range(4):
            v.tensor_copy(out=hidx[i], in_=hstg[:, i * 256:(i + 1) * 256])
        cV.one(v.drain())
    hidxd = cV.n

    @blk.gpsimd
    def _(g, need=hidxd):
        g.wait_ge(cV.h, need)
        g.load_library(libcfg.local_scatter)
    lsld = cG.n

    cVs = Ctr(nc, "cVs")
    cSb = Ctr(nc, "cSb")
    corr_rl = [0, 0]

    iglob = 0
    jglob = 0
    gdma = {}
    spd_last = [None]
    for s_ in range(SEQ):
        for hf in range(2):
            jh = s_ * 2 + hf
            tlo = s_ * S + hf * HALF

            if hf == 0:
                @blk.sync
                def _(sy, s_=s_):
                    if s_ == 0:
                        sy.wait_ge(cV.h, corrd)
                    else:
                        sy.wait_ge(cG.h, cG.n)
                    dX.dma(sy.dma_start(out=corr[:, :], in_=h_corr[s_]))
                corr_rl[s_] = dX.n

            for rnd in range(2):
                @blk.tensor
                def _(t, s_=s_, hf=hf, rnd=rnd, spd_last=spd_last[0]):
                    t.wait_ge(cS.h, cS.n)
                    if rnd == 0:
                        if s_ == 0 and hf == 0:
                            t.wait_ge(cV.h, cV.n)
                            t.wait_ge(dX.h, dX.n)
                        else:
                            t.wait_ge(cS.h, spd_last)
                    for ci in range(4):
                        c = rnd * 4 + ci
                        for sl2 in range(2):
                            _mm = t.matmul(pslice(ci * 2 + sl2),
                                     wdt_bf[64 * s_:64 * s_ + 64, c * 128:(c + 1) * 128],
                                     dtl6[64 * s_:64 * s_ + 64,
                                          hf * 1024 + sl2 * 512:hf * 1024 + (sl2 + 1) * 512],
                                     start=True, stop=True)
                    cT.one(_mm)
                need = cT.n

                @blk.scalar
                def _(s, s_=s_, hf=hf, rnd=rnd, need=need):
                    s.wait_ge(cT.h, need)
                    for ci in range(4):
                        c = rnd * 4 + ci
                        for sl2 in range(2):
                            s.activation(out=spc6(c)[:, sl2 * 512:(sl2 + 1) * 512],
                                         in_=pslice(ci * 2 + sl2), func=AF.Exp,
                                         bias=bdt_s[:, c:c + 1], scale=1.0)
                        s.activation(out=spc6(c), in_=spc6(c), func=AF.Ln, bias=1.0,
                                     scale=1.0)
                    cS.one(s.drain())
            spd = cS.n
            spd_last[0] = spd

            @blk.vector
            def _(v, s_=s_, hf=hf, tlo=tlo, spd=spd):
                v.wait_ge(cS.h, spd)
                for c in range(DCH - 1):
                    v.tensor_copy(out=yac(c), in_=u_t(c)[:, tlo:tlo + HALF])
                cV.one(v.tensor_copy(out=yac(DCH - 1), in_=u_t(DCH - 1)[:, tlo:tlo + HALF]))
            yinit = cV.n

            for n_ in range(N):
                j = jglob
                jglob += 1

                @blk.sync
                def _(sy, n_=n_, tlo=tlo, j=j, yinit=yinit):
                    if j >= 2:
                        sy.wait_ge(cVs.h, 8 * (j - 1))
                    if j == 0:
                        sy.wait_ge(cV.h, yinit)
                    dGp[j % 2].dma(sy.dma_start(out=grep[n_ % 2],
                                                in_=h_g[n_, tlo:tlo + HALF].unsqueeze(0)
                                                .broadcast_to([P, HALF])))
                    dGp[j % 2].dma(sy.dma_start(out=crep[n_ % 2],
                                                in_=h_cc[n_, tlo:tlo + HALF].unsqueeze(0)
                                                .broadcast_to([P, HALF])))
                gdma[j] = (j % 2, dGp[j % 2].n)

                for c in range(DCH):
                    i = iglob
                    iglob += 1

                    @blk.scalar
                    def _(s, c=c, n_=n_, i=i, spd=spd):
                        if i >= 2:
                            s.wait_ge(cVs.h, i - 1)
                        col = c * 16 + n_
                        cSb.one(s.activation(out=barA[i % 2], in_=spc6(c), func=AF.Exp,
                                              scale=A099[:, col:col + 1],
                                              bias=A001[:, col:col + 1]))

                    @blk.vector
                    def _(v, s_=s_, hf=hf, c=c, n_=n_, i=i, j=j,
                          sbneed=cSb.n, dneed=gdma[j], yinit=yinit):
                        v.wait_ge(cSb.h, sbneed)
                        if c == 0:
                            v.wait_ge(dGp[dneed[0]].h, dneed[1])
                            if j == 0:
                                v.wait_ge(cV.h, yinit)
                        hcol = (s_ * 8 + c) * 16 + n_
                        if hf == 0:
                            v.tensor_tensor_scan(out=hbuf[i % 2], data0=barA[i % 2],
                                                 data1=grep[n_ % 2], initial=0.0,
                                                 op0=ALU.mult, op1=ALU.add)
                            v.drain()
                            v.tensor_copy(out=hstate[:, hcol:hcol + 1],
                                          in_=hbuf[i % 2][:, HALF - 1:HALF])
                        else:
                            v.tensor_tensor_scan(out=hbuf[i % 2], data0=barA[i % 2],
                                                 data1=grep[n_ % 2],
                                                 initial=hstate[:, hcol:hcol + 1],
                                                 op0=ALU.mult, op1=ALU.add)
                        v.tensor_mul(out=hbuf[i % 2], in0=hbuf[i % 2], in1=crep[n_ % 2])
                        cVs.one(v.tensor_add(out=yac(c), in0=yac(c), in1=hbuf[i % 2]))
            scan_done = cVs.n

            finv = cV.n
            for c in range(DCH):
                @blk.sync
                def _(sy, c=c, tlo=tlo, finv=finv):
                    if c >= 2:
                        sy.wait_ge(cV.h, finv + (c - 1))
                    dHp[c % 2].dma(sy.dma_start(out=xh[c % 2],
                                                in_=xin[c * P:(c + 1) * P, tlo:tlo + HALF]))
                xneed = (c % 2, dHp[c % 2].n)

                @blk.gpsimd
                def _(g, c=c, s_=s_, jh=jh, finv=finv, scan_done=scan_done):
                    if c == 0:
                        g.wait_ge(cVs.h, scan_done)
                    else:
                        g.wait_ge(cV.h, finv + c)
                    if c == 0:
                        g.wait_ge(dX.h, corr_rl[s_])
                    cG.one(g.local_scatter(
                        scat, corr[:, c * 256:(c + 1) * 256],
                        hidx[jh], channels=P, num_elems=HALF, num_idxs=KSEL))
                gneed = cG.n

                @blk.vector
                def _(v, c=c, xneed=xneed, gneed=gneed, scan_done=scan_done):
                    v.wait_ge(dHp[xneed[0]].h, xneed[1])
                    v.wait_ge(cG.h, gneed)
                    v.wait_ge(cVs.h, scan_done)
                    if c >= 2:
                        v.wait_ge(dOp[c % 2].h, dOp[c % 2].n)
                    v.tensor_add(out=outh[c % 2], in0=xh[c % 2], in1=yac(c))
                    v.drain()
                    cV.one(v.tensor_add(out=outh[c % 2], in0=outh[c % 2], in1=scat))
                vneed = cV.n

                @blk.sync
                def _(sy, c=c, tlo=tlo, vneed=vneed):
                    sy.wait_ge(cV.h, vneed)
                    dOp[c % 2].dma(
                        sy.dma_start(out=out_fm[c * P:(c + 1) * P, tlo:tlo + HALF],
                                     in_=outh[c % 2]))

    @blk.sync
    def _(sy):
        sy.wait_ge(dOp[0].h, dOp[0].n)
        sy.wait_ge(dOp[1].h, dOp[1].n)

    blk.__exit__(None, None, None)
    return nc


_NC_CACHE = {}


def _get_nc():
    if "nc" not in _NC_CACHE:
        nc = build_nc()
        lower_extended_insts(nc)
        _NC_CACHE["nc"] = nc
    return _NC_CACHE["nc"]


def kernel(**inputs):
    x = np.asarray(inputs["L0_attn_x"])
    ncores = 8
    eye = np.eye(P, dtype=np.float32)
    ones = np.ones((P, P), dtype=np.float32)
    inv = (1.0 / (10000.0 ** (np.arange(0, HD, 2, dtype=np.float32) / HD))).astype(np.float32)
    cpk = np.zeros((P, 257), dtype=np.float32)
    cpk[:, 0:128] = eye
    cpk[:, 128:256] = ones
    cpk[0:32, 256] = inv
    a_log = np.asarray(inputs["L0_attn_A_log"])
    a_perm = np.ascontiguousarray(
        a_log.reshape(8, 128, 16).transpose(1, 0, 2).reshape(128, 128))

    def c8(vec):
        return np.ascontiguousarray(np.asarray(vec).reshape(8, 128).T)

    mask = np.asarray(inputs["L0_ffn_attention_mask"])[:, 0]
    posi = np.asarray(inputs["L0_ffn_position_ids"])
    base = {
        "w_in": np.asarray(inputs["L0_attn_w_in"]),
        "w_x": np.asarray(inputs["L0_attn_w_x"]),
        "w_dt": np.asarray(inputs["L0_attn_w_dt"]),
        "wq": np.asarray(inputs["L0_ffn_wq"]),
        "wk": np.asarray(inputs["L0_ffn_wk"]),
        "wv": np.asarray(inputs["L0_ffn_wv"]),
        "wo": np.asarray(inputs["L0_ffn_wo"]),
        "w1": np.asarray(inputs["L0_ffn_w1"]),
        "w2": np.asarray(inputs["L0_ffn_w2"]),
        "a_perm": a_perm,
        "bdt8": c8(inputs["L0_attn_b_dt"]),
        "dp8": c8(inputs["L0_attn_Dp"]),
        "ln18": c8(inputs["L0_ffn_ln1"]),
        "ln28": c8(inputs["L0_ffn_ln2"]),
        "rw8": c8(np.asarray(inputs["L0_ffn_router_w"])[:, 0]),
        "rb1": np.asarray(inputs["L0_ffn_router_b"]).reshape(1, 1),
        "cpack": cpk,
    }
    in_maps = []
    for core in range(ncores):
        b0 = 2 * core
        m = dict(base)
        xc = x[b0:b0 + 2]
        m["xin"] = np.ascontiguousarray(xc.transpose(2, 0, 1).reshape(D, T))
        bc0, bc1 = min(b0, 3), min(b0 + 1, 3)
        m["mask0"] = np.ascontiguousarray(mask[bc0, 0:S, 0:S])
        m["mask1"] = np.ascontiguousarray(mask[bc1, 0:S, 0:S])
        m["pos2"] = np.ascontiguousarray(
            np.stack([posi[bc0, 0:S], posi[bc1, 0:S]]).astype(np.int32))
        in_maps.append(m)

    nc = _get_nc()
    res = run_bass_kernel_spmd(nc, in_maps, core_ids=list(range(ncores)))
    outs = []
    for core in range(ncores):
        o = np.asarray(res.results[core]["out_fm"])
        outs.append(o.reshape(D, SEQ, S).transpose(1, 2, 0))
    return np.concatenate(outs, axis=0).astype(np.float32)


# version 07
# speedup vs baseline: 1.1654x; 1.0010x over previous
import numpy as np
import ml_dtypes

import concourse.bass as bass
import concourse.mybir as mybir
import concourse.library_config as libcfg
from concourse.bass import IndirectOffsetOnAxis
from concourse.bass_utils import run_bass_kernel_spmd
from concourse.library_overlay import lower_extended_insts

F32 = mybir.dt.float32
BF16 = mybir.dt.bfloat16
U32 = mybir.dt.uint32
I32 = mybir.dt.int32
I16 = mybir.dt.int16
AF = mybir.ActivationFunctionType
ALU = mybir.AluOpType
AXX = mybir.AxisListType.X

P, D, DCH, S, SEQ = 128, 1024, 8, 2048, 2
T, HALF, N, KSEL, DTRANK, FF, NH, HD = 4096, 1024, 16, 256, 64, 4096, 16, 64
EPS_LN, EPS_RMS, NEG, RSC = 1e-5, 1e-6, -1.0e9, 0.125
TWO_PI = 6.283185307179586
MAGIC = 12582912.0
PHASE_LIMIT = 99
DEBUG_HBM = False


class Ctr:
    def __init__(self, nc, name):
        self.h = nc.alloc_semaphore(name)
        self.n = 0

    def dma(self, instr):
        instr.then_inc(self.h, 16)
        self.n += 16

    def one(self, instr):
        instr.then_inc(self.h, 1)
        self.n += 1

    def seq(self, eng):
        eng.sem_inc(self.h, 1)
        self.n += 1


def build_nc():
    nc = bass.Bass("TRN2", target_bir_lowering=False)

    xin = nc.dram_tensor("xin", [D, T], F32, kind="ExternalInput").ap()
    w_in_h = nc.dram_tensor("w_in", [D, D], F32, kind="ExternalInput").ap()
    w_x_h = nc.dram_tensor("w_x", [D, 96], F32, kind="ExternalInput").ap()
    w_dt_h = nc.dram_tensor("w_dt", [DTRANK, D], F32, kind="ExternalInput").ap()
    wq_h = nc.dram_tensor("wq", [D, D], F32, kind="ExternalInput").ap()
    wk_h = nc.dram_tensor("wk", [D, D], F32, kind="ExternalInput").ap()
    wv_h = nc.dram_tensor("wv", [D, D], F32, kind="ExternalInput").ap()
    wo_h = nc.dram_tensor("wo", [D, D], F32, kind="ExternalInput").ap()
    w1_h = nc.dram_tensor("w1", [D, FF], F32, kind="ExternalInput").ap()
    w2_h = nc.dram_tensor("w2", [FF, D], F32, kind="ExternalInput").ap()
    a_perm = nc.dram_tensor("a_perm", [P, P], F32, kind="ExternalInput").ap()
    bdt8 = nc.dram_tensor("bdt8", [P, DCH], F32, kind="ExternalInput").ap()
    dp8 = nc.dram_tensor("dp8", [P, DCH], F32, kind="ExternalInput").ap()
    ln18 = nc.dram_tensor("ln18", [P, DCH], F32, kind="ExternalInput").ap()
    ln28 = nc.dram_tensor("ln28", [P, DCH], F32, kind="ExternalInput").ap()
    rw8 = nc.dram_tensor("rw8", [P, DCH], F32, kind="ExternalInput").ap()
    rb1 = nc.dram_tensor("rb1", [1, 1], F32, kind="ExternalInput").ap()
    mask_h = [nc.dram_tensor("mask0", [S, S], F32, kind="ExternalInput").ap(),
              nc.dram_tensor("mask1", [S, S], F32, kind="ExternalInput").ap()]
    pos2 = nc.dram_tensor("pos2", [SEQ, S], I32, kind="ExternalInput").ap()
    cpack = nc.dram_tensor("cpack", [P, 257], F32, kind="ExternalInput").ap()
    out_fm = nc.dram_tensor("out_fm", [D, T], F32, kind="ExternalOutput").ap()

    KIND = "ExternalOutput" if DEBUG_HBM else "Internal"
    h_wrow = nc.dram_tensor("h_wrow", [SEQ, 2, S], F32, kind=KIND).ap()
    h_midx = nc.dram_tensor("h_midx", [SEQ, S], F32, kind="Internal").ap()
    h_selw = nc.dram_tensor("h_selw", [SEQ, 256], F32, kind="Internal").ap()
    h_seli = nc.dram_tensor("h_seli", [SEQ, 16, 16], mybir.dt.uint16, kind="Internal").ap()
    h_rows = nc.dram_tensor("h_rows", [SEQ, 6, KSEL], F32, kind=KIND).ap()
    h_g = nc.dram_tensor("h_g", [N, T], F32, kind=KIND).ap()
    h_cc = nc.dram_tensor("h_cc", [N, T], BF16, kind=KIND).ap()
    h_dtl = nc.dram_tensor("h_dtl", [P, S], BF16, kind=KIND).ap()
    h_corr = nc.dram_tensor("h_corr", [SEQ, P, 2048], BF16, kind=KIND).ap()
    h_dbg = nc.dram_tensor("h_dbg", [P, 16384], F32, kind=KIND).ap() if DEBUG_HBM else None

    AR_A = nc.alloc_sbuf_tensor("AR_A", [P, 16384], F32)
    AR_B = nc.alloc_sbuf_tensor("AR_B", [P, 16384], F32)
    AR_C = nc.alloc_sbuf_tensor("AR_C", [P, 16384], F32)
    SMALL = nc.alloc_sbuf_tensor("SMALL", [P, 1792], F32)
    WST = nc.alloc_sbuf_tensor("WST", [P, 1024], F32)
    CORR = nc.alloc_sbuf_tensor("CORR", [P, 2048], BF16)
    SEL = nc.alloc_sbuf_tensor("SEL", [P, 96], mybir.dt.uint16)

    aA, aB, aC = AR_A.ap(), AR_B.ap(), AR_C.ap()
    aAb, aBb, aCb = aA.bitcast(BF16), aB.bitcast(BF16), aC.bitcast(BF16)
    aCi16 = aC.bitcast(I16)
    r1 = aB[0:32, 0:2048]
    r2 = aB[0:32, 2048:4096]
    r2i = r2.bitcast(I32)
    sm = SMALL.ap()
    smb = sm.bitcast(BF16)
    wst = WST.ap()
    corr = CORR.ap()
    sil = SEL.ap()
    slw = aB[0:32, 4096:5632]

    A_f = sm[:, 0:128]
    A099 = sm[:, 128:256]
    A001 = sm[:, 256:384]
    A_bf = smb[:, 768:896]
    bdt_s = sm[:, 448:456]
    dp_s = sm[:, 456:464]
    ln1_s = sm[:, 464:472]
    ln2_s = sm[:, 472:480]
    rw_s = sm[:, 480:488]
    thr2 = sm[:, 488:489]
    rb2 = sm[:, 489:490]
    nRcol = sm[:, 490:491]
    Rtmp = sm[:, 491:499]
    kth_o = sm[:, 500:504]
    w128t = [sm[:, 504:520], sm[:, 520:536]]
    ridf = sm[:, 536:540]
    ridu = sm.bitcast(U32)[:, 540:544]
    psel_raw = sm.bitcast(I32)[:, 544:548]
    psel_f = sm[:, 548:552]
    wsel_raw = sm[:, 552:556]
    inv32c = sm[:, 556:557]
    epsln_c = sm[:, 557:558]
    epsrms_c = sm[:, 558:559]
    identf_st = sm[:, 560:688]
    ones_f = sm[:, 688:816]
    ident_bf = smb[:, 1632:1760]
    ones_bf = smb[:, 1760:1888]
    hstate = sm[:, 944:1200]
    wdt_bf = smb[:, 2400:3424]
    nopv = sm[0:1, 1712:1713]
    nops = sm[0:1, 1713:1714]

    PS1 = nc.alloc_psum_tensor("PS1", [P, 2048], F32)
    PS2 = nc.alloc_psum_tensor("PS2", [P, 2048], F32)
    ps1, ps2 = PS1.ap(), PS2.ap()
    psTb = ps2[:, 1536:2048].bitcast(BF16)

    dX = Ctr(nc, "dX")
    dCst = Ctr(nc, "dCst")
    dXp = [Ctr(nc, "dXp0"), Ctr(nc, "dXp1")]
    dGp = [Ctr(nc, "dGp0"), Ctr(nc, "dGp1")]
    dHp = [Ctr(nc, "dHp0"), Ctr(nc, "dHp1")]
    dOp = [Ctr(nc, "dOp0"), Ctr(nc, "dOp1")]
    dO = Ctr(nc, "dO")
    dGm = Ctr(nc, "dGm")
    cV = Ctr(nc, "cV")
    cS = Ctr(nc, "cS")
    cT = Ctr(nc, "cT")
    cG = Ctr(nc, "cG")

    def xn_t(c):
        return aAb[:, c * 4096:(c + 1) * 4096]

    def u_t(c):
        return aBb[:, c * 4096:(c + 1) * 4096]

    def pslice(sl):
        return (ps1 if sl < 4 else ps2)[:, (sl % 4) * 512:(sl % 4 + 1) * 512]

    blk = nc.Block("main")
    blk.__enter__()

    def _emit_finish():
        @blk.vector
        def _(v):
            v.wait_ge(cS.h, cS.n)
            v.wait_ge(cT.h, cT.n)
            v.wait_ge(cG.h, cG.n)
            v.memset(aC[:, 0:4096], 0.0)
            cV.one(v.drain())

        @blk.sync
        def _(sy):
            sy.wait_ge(dX.h, dX.n)
            sy.wait_ge(cV.h, cV.n)
            sy.wait_ge(dCst.h, dCst.n)
            sy.wait_ge(dGm.h, dGm.n)
            for p in (dXp + dGp + dHp):
                sy.wait_ge(p.h, p.n)
            for c in range(DCH):
                dO.dma(sy.dma_start(out=out_fm[c * P:(c + 1) * P, :], in_=aC[:, 0:4096]))
            sy.wait_ge(dO.h, dO.n)
        blk.__exit__(None, None, None)


    xstage = [aC[:, 0:4096], aC[:, 4096:8192]]
    racc = aC[:, 8192:12288]
    sqb = [aCb[:, 24576:26624], aCb[:, 26624:28672]]

    @blk.sync
    def _(sy):
        for dst, src in [(A_f, a_perm), (bdt_s, bdt8), (dp_s, dp8), (ln1_s, ln18),
                         (ln2_s, ln28), (rw_s, rw8), (rb2[0:1, :], rb1), (rb2[1:2, :], rb1),
                         (identf_st, cpack[:, 0:128]), (ones_f, cpack[:, 128:256]),
                         (inv32c[0:32, :], cpack[0:32, 256:257])]:
            dCst.dma(sy.dma_start(out=dst, in_=src, allow_slow_non_contiguous=True))
    ph1_small = dCst.n

    @blk.gpsimd
    def _(g):
        cG.one(g.iota(r2i[0:2, :], pattern=[[1, 2048]], base=1, channel_multiplier=0))
    iota_done = cG.n

    @blk.scalar
    def _(s):
        s.wait_ge(dCst.h, ph1_small)
        cS.one(s.activation(out=A_f, in_=A_f, func=AF.Exp, scale=1.0))
    aexp = cS.n

    if PHASE_LIMIT <= -1:
        _emit_finish()
        return nc

    vbase_x = cV.n
    for c in range(DCH):
        @blk.sync
        def _(sy, c=c):
            if c >= 2:
                sy.wait_ge(cV.h, vbase_x + (c - 1))
            dXp[c % 2].dma(sy.dma_start(out=xstage[c % 2], in_=xin[c * P:(c + 1) * P, :]))
        need = dXp[c % 2].n

        @blk.vector
        def _(v, c=c, need=need):
            v.wait_ge(dXp[c % 2].h, need)
            if c == 0:
                v.wait_ge(dCst.h, ph1_small)
                v.wait_ge(cG.h, iota_done)
                v.wait_ge(cS.h, aexp)
                v.tensor_copy(out=ident_bf, in_=identf_st)
                v.tensor_copy(out=ones_bf, in_=ones_f)
                v.memset(epsln_c, EPS_LN)
                v.memset(epsrms_c, EPS_RMS)
                v.tensor_copy(out=r2[0:2, :], in_=r2i[0:2, :])
                v.tensor_scalar(out=A_f, in0=A_f, scalar1=-1.0, scalar2=None, op0=ALU.mult)
                v.drain()
                v.tensor_scalar(out=A099, in0=A_f, scalar1=0.099, scalar2=None, op0=ALU.mult)
                v.tensor_scalar(out=A001, in0=A_f, scalar1=0.001, scalar2=None, op0=ALU.mult)
                v.tensor_copy(out=A_bf, in_=A_f)
            v.tensor_copy(out=xn_t(c), in_=xstage[c % 2])
            if c == 0:
                v.scalar_tensor_tensor(out=racc, in0=xstage[0], scalar=rw_s[:, 0:1],
                                       in1=xstage[0], op0=ALU.mult, op1=ALU.bypass)
            else:
                v.scalar_tensor_tensor(out=racc, in0=xstage[c % 2], scalar=rw_s[:, c:c + 1],
                                       in1=racc, op0=ALU.mult, op1=ALU.add)
            cV.one(v.drain())
    xstream = cV.n


    if PHASE_LIMIT <= 0:
        _emit_finish()
        return nc

    m_b = aC[:, 0:2048]
    r_b = aC[:, 2048:4096]
    m_bf = aCb[:, 8192:10240]
    r_bf = aCb[:, 10240:12288]
    msq = aC[:, 6144:8192]
    w_bc = aC[:, 14336:16384]

    for s_ in range(SEQ):
        col0 = s_ * S
        for c in range(DCH):
            @blk.scalar
            def _(s, c=c, s_=s_):
                if s_ == 0 and c == 0:
                    s.wait_ge(cV.h, xstream)
                if not (s_ == 0 and c in (0, 1)):
                    s.wait_ge(cT.h, cT.n)
                s.activation(out=sqb[c % 2], in_=xn_t(c)[:, col0:col0 + S],
                             func=AF.Square, scale=1.0)
                cS.one(s.drain())
            need = cS.n

            @blk.tensor
            def _(t, c=c, s_=s_, need=need):
                t.wait_ge(cS.h, need)
                if s_ == 1 and c == 0:
                    t.wait_ge(cV.h, cV.n)
                for sl in range(4):
                    t.matmul(ps1[:, sl * 512:(sl + 1) * 512], ones_bf,
                             xn_t(c)[:, col0 + sl * 512:col0 + (sl + 1) * 512],
                             start=(c == 0), stop=(c == DCH - 1))
                for sl in range(4):
                    _mm = t.matmul(ps2[:, sl * 512:(sl + 1) * 512], ones_bf,
                             sqb[c % 2][:, sl * 512:(sl + 1) * 512],
                             start=(c == 0), stop=(c == DCH - 1))
                cT.one(_mm)
        smm = cT.n

        if PHASE_LIMIT <= 0.2:
            _emit_finish()
            return nc

        @blk.vector
        def _(v, need=smm):
            v.wait_ge(cT.h, need)
            v.tensor_scalar(out=m_b, in0=ps1[:, 0:2048], scalar1=1.0 / D, scalar2=None,
                            op0=ALU.mult)
            cV.one(v.drain())
        mdone = cV.n

        if DEBUG_HBM:
            @blk.sync
            def _(sy, s_=s_, need=mdone):
                sy.wait_ge(cV.h, need)
                if s_ == 0:
                    dX.dma(sy.dma_start(out=h_dbg[:, 12416:14464], in_=m_b))
                sy.wait_ge(dX.h, dX.n)

        @blk.scalar
        def _(s, need=mdone):
            s.wait_ge(cV.h, need)
            if DEBUG_HBM:
                s.wait_ge(dX.h, dX.n)
            cS.one(s.activation(out=msq, in_=m_b, func=AF.Square, scale=1.0))
        msqd = cS.n

        @blk.vector
        def _(v, need=msqd):
            v.wait_ge(cS.h, need)
            v.scalar_tensor_tensor(out=r_b, in0=ps2[:, 0:2048], scalar=1.0 / D, in1=msq,
                                   op0=ALU.mult, op1=ALU.subtract)
            cV.one(v.drain())
        vard = cV.n

        @blk.scalar
        def _(s, need=vard):
            s.wait_ge(cV.h, need)
            s.activation(out=r_b, in_=r_b, func=AF.Ln, bias=epsln_c[:, 0:1], scale=1.0)
            cS.one(s.activation(out=r_b, in_=r_b, func=AF.Exp, scale=-0.5))
        rstdd = cS.n

        if DEBUG_HBM:
            @blk.sync
            def _(sy, s_=s_, need=rstdd):
                sy.wait_ge(cS.h, need)
                dX.dma(sy.dma_start(out=h_dbg[:, 8192 + s_ * 2048:8192 + (s_ + 1) * 2048],
                                    in_=r_b))

        @blk.vector
        def _(v, s_=s_, need=rstdd):
            v.wait_ge(cS.h, need)
            if DEBUG_HBM:
                v.wait_ge(dX.h, dX.n)
            v.tensor_copy(out=m_bf, in_=m_b)
            v.tensor_copy(out=r_bf, in_=r_b)
            for c in range(DCH):
                xseg = xn_t(c)[:, col0:col0 + S]
                v.tensor_sub(out=sqb[0], in0=xseg, in1=m_bf)
                v.tensor_mul(out=xseg, in0=sqb[0], in1=r_bf)
            cV.one(v.drain())
        xnd = cV.n

        if PHASE_LIMIT <= 0.4:
            _emit_finish()
            return nc

        @blk.tensor
        def _(t, s_=s_, need=xnd):
            t.wait_ge(cV.h, need)
            t.matmul(ps2[:, 0:8], ones_f, rw_s, start=True, stop=True)
            for sl in range(4):
                _mm = t.matmul(ps1[:, sl * 512:(sl + 1) * 512], ones_f,
                         racc[:, col0 + sl * 512:col0 + (sl + 1) * 512],
                         start=True, stop=True)
            cT.one(_mm)
        rxd = cT.n

        @blk.vector
        def _(v, s_=s_, need=rxd):
            v.wait_ge(cT.h, need)
            if DEBUG_HBM:
                v.tensor_copy(out=msq, in_=ps1[:, 0:2048])
            v.tensor_reduce(out=Rtmp[:, 0:1], in_=ps2[:, 0:8], axis=AXX, op=ALU.add)
            v.drain()
            v.tensor_scalar(out=nRcol, in0=Rtmp[:, 0:1], scalar1=-1.0, scalar2=None,
                            op0=ALU.mult)
            v.drain()
            v.scalar_tensor_tensor(out=w_bc, in0=m_b, scalar=nRcol[:, 0:1],
                                   in1=ps1[:, 0:2048], op0=ALU.mult, op1=ALU.add)
            v.drain()
            v.tensor_mul(out=w_bc, in0=w_bc, in1=r_b)
            cV.one(v.drain())
        wd = cV.n

        if DEBUG_HBM:
            @blk.sync
            def _(sy, s_=s_, need=wd):
                sy.wait_ge(cV.h, need)
                dX.dma(sy.dma_start(out=h_dbg[:, 4096 + s_ * 2048:4096 + (s_ + 1) * 2048],
                                    in_=msq))
                if s_ == 0:
                    dX.dma(sy.dma_start(out=h_dbg[:, 0:2048], in_=w_bc))
                    dX.dma(sy.dma_start(out=h_dbg[:, 2048:2056], in_=sm[:, 488:496]))
                    dX.dma(sy.dma_start(out=h_dbg[:, 12288:12416], in_=ones_f))
                sy.wait_ge(dX.h, dX.n)

        if PHASE_LIMIT <= 0.6:
            _emit_finish()
            return nc

        @blk.sync
        def _(sy, s_=s_, need=wd):
            sy.wait_ge(cV.h, need)
            dX.dma(sy.dma_start(out=h_wrow[s_, 0, :], in_=w_bc[0:1, :]))
            sy.wait_ge(dX.h, dX.n)
            dX.dma(sy.dma_start(out=w128t[s_],
                                in_=h_wrow[s_, 0, :].rearrange("(p f) -> p f", p=P)))
    wwrap = dX.n

    if PHASE_LIMIT <= 0.8:
        _emit_finish()
        return nc

    @blk.gpsimd
    def _(g, need=wwrap):
        g.wait_ge(dX.h, need)
        g.load_library(libcfg.attn)
        q = 1.0 - 254.5 / (S - 1)
        g.kth_largest(kth_o[0:1, 0:2], w128t[0], n_per_lane=S // P,
                      k=KSEL + 2, quantile=q)
        cG.one(g.kth_largest(kth_o[0:1, 2:4], w128t[1], n_per_lane=S // P,
                             k=KSEL + 2, quantile=q))
    kthd = cG.n

    @blk.sync
    def _(sy, need=kthd):
        sy.wait_ge(cG.h, need)
        dX.dma(sy.dma_start(out=thr2[0:1, :], in_=kth_o[0:1, 1:2]))
        dX.dma(sy.dma_start(out=thr2[1:2, :], in_=kth_o[0:1, 3:4]))
        dX.dma(sy.dma_start(out=r1[0:2, :], in_=h_wrow[:, 0, :]))
    thrd = dX.n

    @blk.vector
    def _(v, need=thrd):
        v.wait_ge(dX.h, need)
        v.scalar_tensor_tensor(out=r2[0:2, :], in0=r1[0:2, :], scalar=thr2[0:2, :],
                               in1=r2[0:2, :], op0=ALU.is_gt, op1=ALU.mult)
        v.tensor_scalar(out=r2[0:2, :], in0=r2[0:2, :], scalar1=1.0, scalar2=None,
                        op0=ALU.subtract)
        v.scalar_tensor_tensor(out=r1[0:2, :], in0=r1[0:2, :], scalar=rb2[0:2, :],
                               in1=r1[0:2, :], op0=ALU.add, op1=ALU.bypass)
        cV.one(v.drain())
    midxd = cV.n

    @blk.sync
    def _(sy, need=midxd):
        sy.wait_ge(cV.h, need)
        dX.dma(sy.dma_start(out=h_midx, in_=r2[0:2, :]))
        dX.dma(sy.dma_start(out=h_wrow[0, 1, :], in_=r1[0:1, :]))
        dX.dma(sy.dma_start(out=h_wrow[1, 1, :], in_=r1[1:2, :]))
    mbounce = dX.n

    gp_sparse_base = cG.n
    for s_ in range(SEQ):
        @blk.sync
        def _(sy, s_=s_, need=mbounce):
            sy.wait_ge(dX.h, need)
            if s_ == 1:
                sy.wait_ge(cG.h, gp_sparse_base + 1)
            dX.dma(sy.dma_start(out=slw[0:16, 0:128],
                                in_=h_midx[s_].rearrange("(f p) -> p f", p=16),
                                allow_slow_non_contiguous=True))
        wdone = dX.n

        @blk.gpsimd
        def _(g, s_=s_, need=wdone):
            if s_ == 0:
                g.load_library(libcfg.sparse_gather)
            g.wait_ge(dX.h, need)
            nf_u = sil.bitcast(U32)[0:1, 44:45]
            cG.one(g.sparse_gather(out=slw[0:16, 128 + 16 * s_:144 + 16 * s_],
                                   in_=slw[0:16, 0:128], num_found=nf_u))
    sparsed = cG.n

    @blk.vector
    def _(v, need=sparsed):
        v.wait_ge(cG.h, need)
        v.tensor_scalar(out=slw[0:16, 128:160], in0=slw[0:16, 128:160], scalar1=0.0,
                        scalar2=2047.0, op0=ALU.max, op1=ALU.min)
        v.drain()
        v.tensor_copy(out=sil[0:16, 0:32], in_=slw[0:16, 128:160])
        cV.one(v.drain())
    selwd = cV.n

    @blk.sync
    def _(sy, need=selwd):
        sy.wait_ge(cV.h, need)
        for s_ in range(SEQ):
            dX.dma(sy.dma_start(out=h_selw[s_].rearrange("(f p) -> p f", p=16),
                                in_=slw[0:16, 128 + 16 * s_:144 + 16 * s_],
                                allow_slow_non_contiguous=True))
            dX.dma(sy.dma_start(out=h_seli[s_], in_=sil[0:16, 16 * s_:16 * (s_ + 1)]))
        sy.wait_ge(dX.h, dX.n)
        dX.dma(sy.dma_start(out=r2[0:2, 0:256], in_=h_selw))
        for s_ in range(SEQ):
            for grp in range(8):
                dX.dma(sy.dma_start(out=sil[grp * 16:(grp + 1) * 16,
                                            32 + 16 * s_:48 + 16 * s_],
                                    in_=h_seli[s_]))
    lind = dX.n

    @blk.vector
    def _(v, need=lind):
        v.wait_ge(dX.h, need)
        v.memset(r2[0:2, 255:256], -1.0)
        v.drain()
        v.tensor_scalar(out=r2[0:2, 0:256], in0=r2[0:2, 0:256], scalar1=-1.0,
                        scalar2=2047.0, op0=ALU.max, op1=ALU.min)
        v.drain()
        v.tensor_scalar(out=r2[0:2, 512:768], in0=r2[0:2, 0:256], scalar1=1024.0,
                        scalar2=None, op0=ALU.subtract)
        v.tensor_scalar(out=r2[0:2, 768:1024], in0=r2[0:2, 0:256], scalar1=1024.0,
                        scalar2=None, op0=ALU.is_ge)
        v.drain()
        v.scalar_tensor_tensor(out=r2[0:2, 768:1024], in0=r2[0:2, 768:1024],
                               scalar=-2048.0, in1=r2[0:2, 0:256],
                               op0=ALU.mult, op1=ALU.add)
        v.tensor_scalar(out=r2[0:2, 1024:1280], in0=r2[0:2, 0:256], scalar1=0.0,
                        scalar2=None, op0=ALU.max)
        cV.one(v.drain())
    halvd = cV.n

    @blk.sync
    def _(sy, need=halvd):
        sy.wait_ge(cV.h, need)
        for s_ in range(SEQ):
            dX.dma(sy.dma_start(out=h_rows[s_, 0, :], in_=r2[s_:s_ + 1, 0:256]))
            dX.dma(sy.dma_start(out=h_rows[s_, 1, :], in_=r2[s_:s_ + 1, 768:1024]))
            dX.dma(sy.dma_start(out=h_rows[s_, 2, :], in_=r2[s_:s_ + 1, 512:768]))
            dX.dma(sy.dma_start(out=h_rows[s_, 3, :], in_=r2[s_:s_ + 1, 1024:1280]))
        sy.wait_ge(dX.h, dX.n)
        for s_ in range(SEQ):
            for b_ in range(2):
                cc = 2 * s_ + b_
                dX.dma(sy.dma_start(out=ridf[:, cc:cc + 1],
                                    in_=h_rows[s_, 3, b_ * 128:(b_ + 1) * 128].unsqueeze(-1)))
    ridd = dX.n

    @blk.vector
    def _(v, need=ridd):
        v.wait_ge(dX.h, need)
        v.tensor_copy(out=ridu, in_=ridf)
        cV.one(v.drain())
    ridc = cV.n

    @blk.gpsimd
    def _(g, need=ridc):
        g.wait_ge(cV.h, need)
        for s_ in range(SEQ):
            for b_ in range(2):
                cc = 2 * s_ + b_
                dGm.dma(g.indirect_dma_start(
                    out=psel_raw[:, cc:cc + 1], out_offset=None,
                    in_=pos2.rearrange("s t -> (s t)").unsqueeze(-1),
                    in_offset=IndirectOffsetOnAxis(ap=ridu[:, cc:cc + 1], axis=0),
                    element_offset=s_ * S))
                dGm.dma(g.indirect_dma_start(
                    out=wsel_raw[:, cc:cc + 1], out_offset=None,
                    in_=h_wrow.rearrange("s two t -> (s two t)").unsqueeze(-1),
                    in_offset=IndirectOffsetOnAxis(ap=ridu[:, cc:cc + 1], axis=0),
                    element_offset=(s_ * 2 + 1) * S))
        cG.seq(g)
    pseld = cG.n
    pseld_dma = dGm.n

    @blk.vector
    def _(v, need=pseld, needd=pseld_dma):
        v.wait_ge(cG.h, need)
        v.wait_ge(dGm.h, needd)
        v.tensor_copy(out=psel_f, in_=psel_raw)
        cV.one(v.drain())
    pselc = cV.n

    @blk.sync
    def _(sy, need=pselc):
        sy.wait_ge(cV.h, need)
        for s_ in range(SEQ):
            for b_ in range(2):
                cc = 2 * s_ + b_
                dX.dma(sy.dma_start(out=h_rows[s_, 4, b_ * 128:(b_ + 1) * 128].unsqueeze(-1),
                                    in_=psel_f[:, cc:cc + 1]))
                dX.dma(sy.dma_start(out=h_rows[s_, 5, b_ * 128:(b_ + 1) * 128].unsqueeze(-1),
                                    in_=wsel_raw[:, cc:cc + 1]))
    rows_done = dX.n

    if PHASE_LIMIT <= 1:
        _emit_finish()
        return nc

    w_in_bf = aCb[:, 0:8192]

    def winbf(k):
        return w_in_bf[:, k * 1024:(k + 1) * 1024]

    for k in range(DCH):
        @blk.sync
        def _(sy, k=k):
            sy.wait_ge(cV.h, cV.n)
            if k == 0:
                sy.wait_ge(dX.h, rows_done)
            dX.dma(sy.dma_start(out=wst, in_=w_in_h[k * P:(k + 1) * P, :]))
        need = dX.n

        @blk.vector
        def _(v, k=k, need=need):
            v.wait_ge(dX.h, need)
            v.tensor_copy(out=winbf(k), in_=wst)
            cV.one(v.drain())
    winc = cV.n

    for e in range(DCH):
        @blk.tensor
        def _(t, e=e, need=winc):
            t.wait_ge(cV.h, need)
            if e > 0:
                t.wait_ge(cS.h, cS.n)
            for k in range(DCH):
                for sl in range(8):
                    _mm = t.matmul(pslice(sl), winbf(k)[:, e * 128:(e + 1) * 128],
                             xn_t(k)[:, sl * 512:(sl + 1) * 512],
                             start=(k == 0), stop=(k == DCH - 1))
            cT.one(_mm)
        need = cT.n

        @blk.scalar
        def _(s, e=e, need=need):
            s.wait_ge(cT.h, need)
            for sl in range(8):
                s.copy(out=u_t(e)[:, sl * 512:(sl + 1) * 512], in_=pslice(sl))
            cS.one(s.drain())
    xind = cS.n

    if PHASE_LIMIT <= 2:
        _emit_finish()
        return nc

    B0 = aC[0:16, 0:4096]
    C0 = aC[32:48, 0:4096]
    zac = aC[0:16, 8192:12288]
    g0 = B0
    sp_c = aC[:, 4096:8192]
    dt_c = sp_c
    du_c = aCb[:, 24576:28672]
    dtl2 = aCb[:, 28672:30720]
    Cbf = aCb[32:48, 8192:12288]
    wxs = wst.bitcast(BF16)[:, 0:768]

    @blk.sync
    def _(sy, need=xind):
        sy.wait_ge(cS.h, need)
        sy.wait_ge(cV.h, cV.n)
        dX.dma(sy.dma_start(out=wst[0:64, 0:1024], in_=w_dt_h))
        dX.dma(sy.dma_start(out=wst[64:128, 0:1024], in_=w_dt_h))
    need = dX.n

    @blk.vector
    def _(v, need=need):
        v.wait_ge(dX.h, need)
        v.tensor_copy(out=wdt_bf, in_=wst[:, 0:1024])
        cV.one(v.drain())
    wdtc = cV.n

    for k in range(DCH):
        @blk.sync
        def _(sy, k=k):
            sy.wait_ge(cV.h, cV.n)
            dX.dma(sy.dma_start(out=wst[:, 512:608], in_=w_x_h[k * P:(k + 1) * P, :]))
        need = dX.n

        @blk.vector
        def _(v, k=k, need=need):
            v.wait_ge(dX.h, need)
            v.tensor_copy(out=wxs[:, k * 96:(k + 1) * 96], in_=wst[:, 512:608])
            cV.one(v.drain())
    wxc = cV.n

    def proj(mlo, mhi, evac):
        @blk.tensor
        def _(t, mlo=mlo, mhi=mhi):
            t.wait_ge(cV.h, cV.n)
            mm = mhi - mlo
            for k in range(DCH):
                for sl in range(8):
                    _mm = t.matmul(pslice(sl)[0:mm, :], wxs[:, k * 96 + mlo:k * 96 + mhi],
                             u_t(k)[:, sl * 512:(sl + 1) * 512],
                             start=(k == 0), stop=(k == DCH - 1))
            cT.one(_mm)
        need = cT.n

        @blk.vector
        def _(v, evac=evac, need=need):
            v.wait_ge(cT.h, need)
            evac(v)
            cV.one(v.drain())

    def ev_dt(v):
        for sl in range(8):
            s2 = sl // 4
            cb = (sl % 4) * 512
            v.tensor_copy(out=dtl2[64 * s2:64 * (s2 + 1),
                                   (sl % 4) * 512:(sl % 4 + 1) * 512]
                          if False else dtl2[64 * (sl // 4):64 * (sl // 4) + 64,
                                             (sl % 4) * 512:(sl % 4) * 512 + 512],
                          in_=pslice(sl)[0:64, :])

    def ev_B(v):
        for sl in range(8):
            v.tensor_copy(out=B0[:, sl * 512:(sl + 1) * 512], in_=pslice(sl)[0:16, :])

    def ev_C(v):
        for sl in range(8):
            v.tensor_copy(out=C0[:, sl * 512:(sl + 1) * 512], in_=pslice(sl)[0:16, :])

    proj(0, 64, ev_dt)
    proj(64, 80, ev_B)
    proj(80, 96, ev_C)
    projd = cV.n

    @blk.sync
    def _(sy, need=projd):
        sy.wait_ge(cV.h, need)
        dX.dma(sy.dma_start(out=h_dtl, in_=dtl2))
    dtl_saved = dX.n

    for c in range(DCH):
        @blk.tensor
        def _(t, c=c):
            t.wait_ge(cV.h, cV.n)
            t.wait_ge(cS.h, cS.n)
            for sl in range(8):
                rb_ = 64 * (sl // 4)
                _mm = t.matmul(pslice(sl), wdt_bf[rb_:rb_ + 64, c * 128:(c + 1) * 128],
                         dtl2[rb_:rb_ + 64,
                              (sl % 4) * 512:(sl % 4) * 512 + 512],
                         start=True, stop=True)
            cT.one(_mm)
        need = cT.n

        @blk.scalar
        def _(s, c=c, need=need):
            s.wait_ge(cT.h, need)
            for sl in range(8):
                s.activation(out=sp_c[:, sl * 512:(sl + 1) * 512], in_=pslice(sl),
                             func=AF.Exp, bias=bdt_s[:, c:c + 1], scale=1.0)
            s.activation(out=sp_c, in_=sp_c, func=AF.Ln, bias=1.0, scale=1.0)
            cS.one(s.drain())
        need = cS.n

        @blk.vector
        def _(v, c=c, need=need):
            v.wait_ge(cS.h, need)
            v.tensor_scalar(out=dt_c, in0=sp_c, scalar1=0.099, scalar2=0.001,
                            op0=ALU.mult, op1=ALU.add)
            v.tensor_mul(out=du_c, in0=dt_c, in1=u_t(c))
            cV.one(v.drain())
        need = cV.n

        @blk.tensor
        def _(t, c=c, need=need):
            t.wait_ge(cV.h, need)
            for sl in range(8):
                _mm = t.matmul(pslice(sl)[0:16, :], A_bf[:, c * 16:(c + 1) * 16],
                         du_c[:, sl * 512:(sl + 1) * 512], start=True, stop=True)
            cT.one(_mm)
        need = cT.n

        @blk.vector
        def _(v, c=c, need=need):
            v.wait_ge(cT.h, need)
            for half in range(2):
                src = (ps1 if half == 0 else ps2)[0:16, 0:2048]
                if c == 0:
                    v.tensor_copy(out=zac[:, half * 2048:(half + 1) * 2048], in_=src)
                else:
                    v.tensor_add(out=zac[:, half * 2048:(half + 1) * 2048],
                                 in0=zac[:, half * 2048:(half + 1) * 2048], in1=src)
            v.scalar_tensor_tensor(out=u_t(c), in0=u_t(c), scalar=dp_s[:, c:c + 1],
                                   in1=xn_t(c), op0=ALU.mult, op1=ALU.add)
            cV.one(v.drain())
    zdone = cV.n

    @blk.vector
    def _(v, need=zdone):
        v.wait_ge(cV.h, need)
        v.tensor_mul(out=g0, in0=B0, in1=zac)
        v.tensor_copy(out=Cbf, in_=C0)
        cV.one(v.drain())
    gdone = cV.n

    @blk.sync
    def _(sy, need=gdone):
        sy.wait_ge(cV.h, need)
        dX.dma(sy.dma_start(out=h_g, in_=g0))
        dX.dma(sy.dma_start(out=h_cc, in_=Cbf))
    gsaved = dX.n

    if PHASE_LIMIT <= 3:
        _emit_finish()
        return nc

    def xsel_t(c):
        return aAb[:, c * 512:(c + 1) * 512]

    @blk.gpsimd
    def _(g, need=gdone):
        g.wait_ge(cV.h, need)
        g.wait_ge(dX.h, rows_done)
        last = None
        for s_ in range(SEQ):
            for c in range(DCH):
                last = g.indirect_copy(
                    out=xsel_t(c)[:, s_ * 256:(s_ + 1) * 256].unsqueeze(-1),
                    data=xn_t(c)[:, s_ * S:(s_ + 1) * S].unsqueeze(-1),
                    idxs=sil[:, 32 + 16 * s_:48 + 16 * s_],
                    i_know_ap_gather_is_preferred=True)
        cG.one(last)
    xseld = cG.n

    mrow = aC[:, 0:2048]
    mcolf = aC[:, 2048:2304]

    def msel_t(s_, b_):
        i = 2 * s_ + b_
        return aCb[:, 13312 + i * 256:13312 + (i + 1) * 256]

    for s_ in range(SEQ):
        for b_ in range(2):
            cc = 2 * s_ + b_

            @blk.gpsimd
            def _(g, s_=s_, b_=b_, cc=cc, gsaved=gsaved):
                g.wait_ge(cV.h, cV.n)
                g.wait_ge(dX.h, gsaved)
                dGm.dma(g.indirect_dma_start(
                    out=mrow, out_offset=None,
                    in_=mask_h[s_][:, 0:S],
                    in_offset=IndirectOffsetOnAxis(ap=ridu[:, cc:cc + 1], axis=0)))
                g.wait_ge(dGm.h, dGm.n)
                cG.one(g.indirect_copy(
                    out=mcolf.unsqueeze(-1), data=mrow.unsqueeze(-1),
                    idxs=sil[:, 32 + 16 * s_:48 + 16 * s_],
                    i_know_ap_gather_is_preferred=True))
            need = cG.n

            @blk.vector
            def _(v, s_=s_, b_=b_, need=need):
                v.wait_ge(cG.h, need)
                v.tensor_copy(out=msel_t(s_, b_), in_=mcolf)
                v.memset(msel_t(s_, b_)[:, 255:256], NEG)
                cV.one(v.drain())
    mseld = cV.n

    if PHASE_LIMIT <= 4:
        _emit_finish()
        return nc

    hn_r = lambda c: aAb[:, 4096 + c * 512:4096 + (c + 1) * 512]
    q_r = lambda c: aAb[:, 8192 + c * 512:8192 + (c + 1) * 512]
    k_r = lambda c: aAb[:, 12288 + c * 512:12288 + (c + 1) * 512]
    v_r = lambda i: aAb[:, 16384 + i * 1024:16384 + (i + 1) * 1024]
    scr2 = lambda i: aAb[:, 20480 + i * 512:20480 + (i + 1) * 512]
    cs_r = lambda i: aA[:, 11264 + i * 256:11264 + (i + 1) * 256]
    x1_r = lambda c: aAb[:, 24576 + c * 512:24576 + (c + 1) * 512]
    hn2_r = lambda c: aAb[:, 28672 + c * 512:28672 + (c + 1) * 512]
    h1_r = lambda f, s_: aAb[:, 4096 + (f * 2 + s_) * 256:4096 + (f * 2 + s_ + 1) * 256]
    wselb = lambda s_: aA[:, 10752 + s_ * 256:10752 + (s_ + 1) * 256]
    rms_b = aC[:, 4096:4352]
    sqs = aCb[:, 8704:8960]
    sc_f = lambda qb: aC[:, 4608 + qb * 256:4608 + (qb + 1) * 256]

    for s_ in range(SEQ):
        for c in range(DCH):
            @blk.scalar
            def _(s, c=c, s_=s_):
                if c == 0 and s_ == 0:
                    s.wait_ge(cV.h, mseld)
                    s.wait_ge(cG.h, xseld)
                else:
                    s.wait_ge(cT.h, cT.n)
                s.activation(out=sqs, in_=xsel_t(c)[:, s_ * 256:(s_ + 1) * 256],
                             func=AF.Square, scale=1.0)
                cS.one(s.drain())
            need = cS.n

            @blk.tensor
            def _(t, c=c, need=need):
                t.wait_ge(cS.h, need)
                _mm = t.matmul(ps1[:, 0:256], ones_bf, sqs, start=(c == 0), stop=(c == DCH - 1))
                cT.one(_mm)
        need = cT.n

        @blk.scalar
        def _(s, need=need):
            s.wait_ge(cT.h, need)
            s.activation(out=rms_b, in_=ps1[:, 0:256], func=AF.Ln, bias=epsrms_c[:, 0:1],
                         scale=1.0 / D)
            s.drain()
            cS.one(s.activation(out=rms_b, in_=rms_b, func=AF.Exp, scale=-0.5))
        need = cS.n

        @blk.vector
        def _(v, s_=s_, need=need):
            v.wait_ge(cS.h, need)
            for c in range(DCH):
                v.scalar_tensor_tensor(out=hn_r(c)[:, s_ * 256:(s_ + 1) * 256],
                                       in0=xsel_t(c)[:, s_ * 256:(s_ + 1) * 256],
                                       scalar=ln1_s[:, c:c + 1], in1=rms_b,
                                       op0=ALU.mult, op1=ALU.mult)
            cV.one(v.drain())
    hnd = cV.n

    wcur = aCb[:, 0:8192]

    def wcur_k(k):
        return wcur[:, k * 1024:(k + 1) * 1024]

    wst2 = corr.bitcast(F32)

    def load_dd(w_hbm):
        tn0 = cT.n
        vn0 = cV.n
        casts = []
        for k in range(DCH):
            stg = wst if k % 2 == 0 else wst2
            @blk.sync
            def _(sy, k=k, w_hbm=w_hbm, stg=stg, vn0=vn0, casts=tuple(casts)):
                if k < 2:
                    sy.wait_ge(cV.h, vn0)
                else:
                    sy.wait_ge(cV.h, casts[k - 2])
                dXp[k % 2].dma(sy.dma_start(out=stg, in_=w_hbm[k * P:(k + 1) * P, :]))
            need = dXp[k % 2].n

            @blk.vector
            def _(v, k=k, need=need, stg=stg, tn0=tn0):
                v.wait_ge(dXp[k % 2].h, need)
                if k < 2:
                    v.wait_ge(cT.h, tn0)
                v.tensor_copy(out=wcur_k(k), in_=stg)
                cV.one(v.drain())
            casts.append(cV.n)
        return cV.n

    def proj_fm(dst_r, wneed):
        sbase = cS.n
        for s_ in range(SEQ):
            for e in range(DCH):
                q = s_ * DCH + e
                @blk.tensor
                def _(t, s_=s_, e=e, q=q, wneed=wneed, sbase=sbase):
                    t.wait_ge(cV.h, wneed)
                    if q >= 4:
                        t.wait_ge(cS.h, sbase + q - 3)
                    else:
                        t.wait_ge(cS.h, sbase)
                    for k in range(DCH):
                        _mm = t.matmul(ps1[:, 512 * (e % 4):512 * (e % 4) + 256],
                                 wcur_k(k)[:, e * 128:(e + 1) * 128],
                                 hn_r(k)[:, s_ * 256:(s_ + 1) * 256],
                                 start=(k == 0), stop=(k == DCH - 1))
                    cT.one(_mm)
                need = cT.n

                @blk.scalar
                def _(s, s_=s_, e=e, dst_r=dst_r, need=need):
                    s.wait_ge(cT.h, need)
                    cS.one(s.copy(out=dst_r(e)[:, s_ * 256:(s_ + 1) * 256],
                                  in_=ps1[:, 512 * (e % 4):512 * (e % 4) + 256]))

    wn = load_dd(wq_h)
    proj_fm(q_r, wn)
    wn = load_dd(wk_h)
    proj_fm(k_r, wn)

    wn = load_dd(wv_h)
    for s_ in range(SEQ):
        for tb in range(2):
            for esl in range(2):
                @blk.tensor
                def _(t, s_=s_, tb=tb, esl=esl, wn=wn):
                    t.wait_ge(cV.h, wn)
                    t.wait_ge(cS.h, cS.n)
                    for k in range(DCH):
                        _mm = t.matmul(ps2[:, 512 * esl:512 * (esl + 1)],
                                 hn_r(k)[:, s_ * 256 + tb * 128:s_ * 256 + (tb + 1) * 128],
                                 wcur_k(k)[:, esl * 512:(esl + 1) * 512],
                                 start=(k == 0), stop=(k == DCH - 1))
                    cT.one(_mm)
                need = cT.n

                @blk.scalar
                def _(s, s_=s_, tb=tb, esl=esl, need=need):
                    s.wait_ge(cT.h, need)
                    s.copy(out=v_r(s_ * 2 + tb)[:, esl * 512:(esl + 1) * 512],
                           in_=ps2[:, 512 * esl:512 * (esl + 1)])
                    cS.one(s.drain())
    vprojd = cS.n

    rp = aC[0:32, 5120:6656]
    for s_ in range(SEQ):
        @blk.sync
        def _(sy, s_=s_):
            sy.wait_ge(cV.h, cV.n)
            dX.dma(sy.dma_start(out=rp[:, 0:256],
                                in_=h_rows[s_, 4, :].unsqueeze(0).broadcast_to([32, KSEL])))
        need = dX.n

        @blk.vector
        def _(v, s_=s_, need=need):
            v.wait_ge(dX.h, need)
            ang = rp[:, 0:256]
            rr = rp[:, 256:512]
            rc = rp[:, 512:768]
            tm = rp[:, 768:1024]
            v.tensor_scalar(out=ang, in0=ang, scalar1=inv32c[0:32, :], scalar2=None,
                            op0=ALU.mult)
            v.tensor_scalar(out=rr, in0=ang, scalar1=1.0 / TWO_PI, scalar2=MAGIC,
                            op0=ALU.mult, op1=ALU.add)
            v.drain()
            v.tensor_scalar(out=rr, in0=rr, scalar1=MAGIC, scalar2=None, op0=ALU.subtract)
            v.drain()
            v.scalar_tensor_tensor(out=rr, in0=rr, scalar=-TWO_PI, in1=ang,
                                   op0=ALU.mult, op1=ALU.add)
            v.tensor_scalar(out=rc, in0=rr, scalar1=TWO_PI / 4, scalar2=None, op0=ALU.add)
            v.tensor_scalar(out=tm, in0=rc, scalar1=TWO_PI / 2, scalar2=None, op0=ALU.is_gt)
            v.drain()
            v.scalar_tensor_tensor(out=rc, in0=tm, scalar=-TWO_PI, in1=rc,
                                   op0=ALU.mult, op1=ALU.add)
            cV.one(v.drain())
        need = cV.n

        @blk.scalar
        def _(s, need=need):
            s.wait_ge(cV.h, need)
            s.activation(out=rp[:, 768:1024], in_=rp[:, 256:512], func=AF.Sin,
                         scale=1.0)
            cS.one(s.activation(out=rp[:, 1024:1280], in_=rp[:, 512:768],
                                func=AF.Sin, scale=1.0))
        need = cS.n

        @blk.vector
        def _(v, need=need):
            v.wait_ge(cS.h, need)
            v.tensor_scalar(out=rp[:, 0:256], in0=rp[:, 768:1024],
                            scalar1=-1.0, scalar2=None, op0=ALU.mult)
            cV.one(v.drain())
        need = cV.n

        @blk.sync
        def _(sy, need=need):
            sy.wait_ge(cV.h, need)
            for rows in range(4):
                dX.dma(sy.dma_start(out=cs_r(0)[rows * 32:(rows + 1) * 32, :],
                                    in_=rp[:, 1024:1280]))
                src = rp[:, 0:256] if rows % 2 == 0 else rp[:, 768:1024]
                dX.dma(sy.dma_start(out=cs_r(1)[rows * 32:(rows + 1) * 32, :], in_=src))
        need = dX.n

        @blk.vector
        def _(v, need=need):
            v.wait_ge(dX.h, need)
            v.tensor_scalar(out=cs_r(2), in0=cs_r(0), scalar1=RSC, scalar2=None,
                            op0=ALU.mult)
            v.tensor_scalar(out=cs_r(3), in0=cs_r(1), scalar1=RSC, scalar2=None,
                            op0=ALU.mult)
            cV.one(v.drain())
        csd = cV.n

        for c in range(DCH):
            @blk.sync
            def _(sy, c=c, s_=s_, csd=csd):
                sy.wait_ge(cV.h, cV.n)
                for (dst, src) in ((scr2(0), q_r(c)), (scr2(1), k_r(c))):
                    for hb2 in range(2):
                        b0 = hb2 * 64
                        dX.dma(sy.dma_start(
                            out=dst[b0:b0 + 32, s_ * 256:(s_ + 1) * 256],
                            in_=src[b0 + 32:b0 + 64, s_ * 256:(s_ + 1) * 256]))
                        dX.dma(sy.dma_start(
                            out=dst[b0 + 32:b0 + 64, s_ * 256:(s_ + 1) * 256],
                            in_=src[b0:b0 + 32, s_ * 256:(s_ + 1) * 256]))
            need = dX.n

            @blk.vector
            def _(v, c=c, s_=s_, need=need):
                v.wait_ge(dX.h, need)
                qs = q_r(c)[:, s_ * 256:(s_ + 1) * 256]
                ks = k_r(c)[:, s_ * 256:(s_ + 1) * 256]
                sh_q = scr2(0)[:, s_ * 256:(s_ + 1) * 256]
                sh_k = scr2(1)[:, s_ * 256:(s_ + 1) * 256]
                tmp = scr2(2)[:, s_ * 256:(s_ + 1) * 256]
                v.tensor_mul(out=tmp, in0=qs, in1=cs_r(2))
                v.tensor_mul(out=sh_q, in0=sh_q, in1=cs_r(3))
                v.tensor_add(out=qs, in0=tmp, in1=sh_q)
                v.tensor_mul(out=tmp, in0=ks, in1=cs_r(0))
                v.tensor_mul(out=sh_k, in0=sh_k, in1=cs_r(1))
                v.tensor_add(out=ks, in0=tmp, in1=sh_k)
                cV.one(v.drain())
    roped = cV.n

    for s_ in range(SEQ):
        for h in range(NH):
            c = h // 2
            hb = (h % 2) * 64
            at0 = scr2(2)
            at1 = scr2(3)
            for qb in range(2):
                @blk.tensor
                def _(t, s_=s_, c=c, hb=hb, qb=qb):
                    t.wait_ge(cV.h, cV.n)
                    _mm = t.matmul(ps1[:, qb * 256:(qb + 1) * 256],
                             q_r(c)[hb:hb + 64,
                                    s_ * 256 + qb * 128:s_ * 256 + (qb + 1) * 128],
                             k_r(c)[hb:hb + 64, s_ * 256:(s_ + 1) * 256],
                             start=True, stop=True)
                    cT.one(_mm)
                need = cT.n

                @blk.vector
                def _(v, s_=s_, qb=qb, need=need):
                    v.wait_ge(cT.h, need)
                    v.tensor_add(out=sc_f(qb), in0=ps1[:, qb * 256:(qb + 1) * 256],
                                 in1=msel_t(s_, qb))
                    v.tensor_reduce(out=Rtmp[:, 1:2], in_=sc_f(qb), axis=AXX, op=ALU.max)
                    v.drain()
                    v.tensor_scalar(out=Rtmp[:, 2:3], in0=Rtmp[:, 1:2], scalar1=-1.0,
                                    scalar2=None, op0=ALU.mult)
                    cV.one(v.drain())
                need = cV.n

                @blk.scalar
                def _(s, qb=qb, need=need):
                    s.wait_ge(cV.h, need)
                    s.activation(out=sqs, in_=sc_f(qb), func=AF.Exp, bias=Rtmp[:, 2:3],
                                 scale=1.0)
                    cS.one(s.drain())
                need = cS.n

                @blk.vector
                def _(v, need=need):
                    v.wait_ge(cS.h, need)
                    v.tensor_reduce(out=Rtmp[:, 3:4], in_=sqs, axis=AXX, op=ALU.add)
                    v.drain()
                    v.reciprocal(out=Rtmp[:, 4:5], in_=Rtmp[:, 3:4])
                    v.drain()
                    v.tensor_scalar(out=sqs, in0=sqs, scalar1=Rtmp[:, 4:5], scalar2=None,
                                    op0=ALU.mult)
                    cV.one(v.drain())
                need = cV.n

                @blk.tensor
                def _(t, qb=qb, need=need):
                    t.wait_ge(cV.h, need)
                    t.transpose(psTb[:, qb * 128:(qb + 1) * 128],
                                sqs[:, 0:128], ident_bf)
                    _mm = t.transpose(psTb[:, 256 + qb * 128:256 + (qb + 1) * 128],
                                sqs[:, 128:256], ident_bf)
                    cT.one(_mm)
                need = cT.n

                @blk.vector
                def _(v, qb=qb, at0=at0, at1=at1, need=need):
                    v.wait_ge(cT.h, need)
                    v.tensor_copy(out=at0[:, qb * 128:(qb + 1) * 128],
                                  in_=psTb[:, qb * 128:(qb + 1) * 128])
                    v.tensor_copy(out=at1[:, qb * 128:(qb + 1) * 128],
                                  in_=psTb[:, 256 + qb * 128:256 + (qb + 1) * 128])
                    cV.one(v.drain())
            need = cV.n

            @blk.tensor
            def _(t, s_=s_, h=h, hb=hb, at0=at0, at1=at1, need=need):
                t.wait_ge(cV.h, need)
                for kb in range(2):
                    at = (at0, at1)[kb]
                    _mm = t.matmul(ps2[hb:hb + 64, 0:256],
                             v_r(s_ * 2 + kb)[:, h * 64:(h + 1) * 64],
                             at[:, 0:256], start=(kb == 0), stop=(kb == 1),
                             tile_position=(0, hb))
                cT.one(_mm)
            need = cT.n

            if h % 2 == 1:
                @blk.scalar
                def _(s, s_=s_, c=c, need=need):
                    s.wait_ge(cT.h, need)
                    s.copy(out=hn_r(c)[:, s_ * 256:(s_ + 1) * 256], in_=ps2[:, 0:256])
                    cS.one(s.drain())
    attnd = cS.n

    wn = load_dd(wo_h)
    wovb = cV.n
    for s_ in range(SEQ):
        for e in range(DCH):
            q = s_ * DCH + e
            @blk.tensor
            def _(t, s_=s_, e=e, q=q, wn=wn):
                t.wait_ge(cV.h, wn)
                if q >= 4:
                    t.wait_ge(cV.h, wovb + q - 3)
                else:
                    t.wait_ge(cS.h, attnd)
                for k in range(DCH):
                    _mm = t.matmul(ps1[:, 512 * (e % 4):512 * (e % 4) + 256],
                             wcur_k(k)[:, e * 128:(e + 1) * 128],
                             hn_r(k)[:, s_ * 256:(s_ + 1) * 256],
                             start=(k == 0), stop=(k == DCH - 1))
                cT.one(_mm)
            need = cT.n

            @blk.vector
            def _(v, s_=s_, e=e, need=need):
                v.wait_ge(cT.h, need)
                cV.one(v.tensor_add(out=x1_r(e)[:, s_ * 256:(s_ + 1) * 256],
                                    in0=ps1[:, 512 * (e % 4):512 * (e % 4) + 256],
                                    in1=xsel_t(e)[:, s_ * 256:(s_ + 1) * 256]))
    x1d = cV.n

    for s_ in range(SEQ):
        for c in range(DCH):
            @blk.scalar
            def _(s, c=c, s_=s_, x1d=x1d):
                s.wait_ge(cV.h, x1d)
                if not (c == 0 and s_ == 0):
                    s.wait_ge(cT.h, cT.n)
                s.activation(out=sqs, in_=x1_r(c)[:, s_ * 256:(s_ + 1) * 256],
                             func=AF.Square, scale=1.0)
                cS.one(s.drain())
            need = cS.n

            @blk.tensor
            def _(t, c=c, need=need):
                t.wait_ge(cS.h, need)
                _mm = t.matmul(ps1[:, 0:256], ones_bf, sqs, start=(c == 0), stop=(c == DCH - 1))
                cT.one(_mm)
        need = cT.n

        @blk.scalar
        def _(s, need=need):
            s.wait_ge(cT.h, need)
            s.activation(out=rms_b, in_=ps1[:, 0:256], func=AF.Ln, bias=epsrms_c[:, 0:1],
                         scale=1.0 / D)
            s.drain()
            cS.one(s.activation(out=rms_b, in_=rms_b, func=AF.Exp, scale=-0.5))
        need = cS.n

        @blk.vector
        def _(v, s_=s_, need=need):
            v.wait_ge(cS.h, need)
            for c in range(DCH):
                v.scalar_tensor_tensor(out=hn2_r(c)[:, s_ * 256:(s_ + 1) * 256],
                                       in0=x1_r(c)[:, s_ * 256:(s_ + 1) * 256],
                                       scalar=ln2_s[:, c:c + 1], in1=rms_b,
                                       op0=ALU.mult, op1=ALU.mult)
            cV.one(v.drain())
    hn2d = cV.n

    w1bf = aCb[:, 0:32768]

    def w1k(k):
        return w1bf[:, k * 4096:(k + 1) * 4096]

    tn0 = cT.n
    vn0 = cV.n
    casts = []
    for k in range(DCH):
        for piece in range(4):
            r = k * 4 + piece
            stg = wst if r % 2 == 0 else wst2
            @blk.sync
            def _(sy, k=k, piece=piece, r=r, stg=stg, vn0=vn0, casts=tuple(casts)):
                if r < 2:
                    sy.wait_ge(cV.h, vn0)
                else:
                    sy.wait_ge(cV.h, casts[r - 2])
                dXp[r % 2].dma(sy.dma_start(out=stg,
                                    in_=w1_h[k * P:(k + 1) * P,
                                             piece * 1024:(piece + 1) * 1024]))
            need = dXp[r % 2].n

            @blk.vector
            def _(v, k=k, piece=piece, r=r, need=need, stg=stg, tn0=tn0):
                v.wait_ge(dXp[r % 2].h, need)
                if r < 2:
                    v.wait_ge(cT.h, tn0)
                v.tensor_copy(out=w1k(k)[:, piece * 1024:(piece + 1) * 1024], in_=stg)
                cV.one(v.drain())
            casts.append(cV.n)
    w1ld = cV.n

    w1sb = cS.n
    for s_ in range(SEQ):
        for f in range(32):
            q = s_ * 32 + f
            @blk.tensor
            def _(t, s_=s_, f=f, q=q, w1ld=w1ld):
                t.wait_ge(cV.h, w1ld)
                if q >= 4:
                    t.wait_ge(cS.h, w1sb + q - 3)
                else:
                    t.wait_ge(cS.h, w1sb)
                for k in range(DCH):
                    _mm = t.matmul(ps1[:, 512 * (f % 4):512 * (f % 4) + 256],
                             w1k(k)[:, f * 128:(f + 1) * 128],
                             hn2_r(k)[:, s_ * 256:(s_ + 1) * 256],
                             start=(k == 0), stop=(k == DCH - 1))
                cT.one(_mm)
            need = cT.n

            @blk.scalar
            def _(s, s_=s_, f=f, need=need):
                s.wait_ge(cT.h, need)
                cS.one(s.activation(out=h1_r(f, s_),
                                    in_=ps1[:, 512 * (f % 4):512 * (f % 4) + 256],
                                    func=AF.Gelu_apprx_tanh, scale=1.0))
    h1d = cS.n

    w2bf = aCb[:, 0:32768]

    def w2k(k):
        return w2bf[:, k * 1024:(k + 1) * 1024]

    tn0 = cT.n
    vn0 = cV.n
    casts = []
    for k in range(32):
        stg = wst if k % 2 == 0 else wst2
        @blk.sync
        def _(sy, k=k, stg=stg, vn0=vn0, casts=tuple(casts)):
            if k < 2:
                sy.wait_ge(cV.h, vn0)
            else:
                sy.wait_ge(cV.h, casts[k - 2])
            dXp[k % 2].dma(sy.dma_start(out=stg, in_=w2_h[k * P:(k + 1) * P, :]))
        need = dXp[k % 2].n

        @blk.vector
        def _(v, k=k, need=need, stg=stg, tn0=tn0):
            v.wait_ge(dXp[k % 2].h, need)
            if k < 2:
                v.wait_ge(cT.h, tn0)
            v.tensor_copy(out=w2k(k), in_=stg)
            cV.one(v.drain())
        casts.append(cV.n)
    w2ld = cV.n

    @blk.sync
    def _(sy, need=w2ld):
        sy.wait_ge(cV.h, need)
        for s_ in range(SEQ):
            dX.dma(sy.dma_start(out=wselb(s_),
                                in_=h_rows[s_, 5, :].unsqueeze(0).broadcast_to([P, KSEL])))
    wselbd = dX.n
    corr_b0 = [0]

    w2vb = cV.n
    for s_ in range(SEQ):
        for e in range(DCH):
            q = s_ * DCH + e
            @blk.tensor
            def _(t, s_=s_, e=e, q=q, w2ld=w2ld):
                t.wait_ge(cV.h, w2ld)
                if q >= 4:
                    t.wait_ge(cV.h, w2vb + q - 3)
                else:
                    t.wait_ge(cS.h, h1d)
                for k in range(32):
                    _mm = t.matmul(ps1[:, 512 * (e % 4):512 * (e % 4) + 256],
                             w2k(k)[:, e * 128:(e + 1) * 128],
                             h1_r(k, s_), start=(k == 0), stop=(k == 31))
                cT.one(_mm)
            need = cT.n

            @blk.vector
            def _(v, s_=s_, e=e, need=need):
                v.wait_ge(cT.h, need)
                v.wait_ge(dX.h, wselbd)
                if s_ == 1 and e == 0:
                    v.wait_ge(dX.h, corr_b0[0])
                x2t = scr2(0)[:, 0:256]
                v.tensor_add(out=x2t, in0=ps1[:, 512 * (e % 4):512 * (e % 4) + 256],
                             in1=x1_r(e)[:, s_ * 256:(s_ + 1) * 256])
                v.drain()
                v.tensor_mul(out=x2t, in0=x2t, in1=wselb(s_))
                v.drain()
                v.tensor_sub(out=corr[:, e * 256:(e + 1) * 256],
                             in0=x2t, in1=xsel_t(e)[:, s_ * 256:(s_ + 1) * 256])
                cV.one(v.drain())
        seq_corr_v = cV.n

        @blk.sync
        def _(sy, s_=s_, need=seq_corr_v):
            sy.wait_ge(cV.h, need)
            dX.dma(sy.dma_start(out=h_corr[s_], in_=corr[:, :]))
        if s_ == 0:
            corr_b0[0] = dX.n
    corrd = cV.n

    if PHASE_LIMIT <= 5:
        _emit_finish()
        return nc

    yac = lambda c: aA[:, c * 1024:(c + 1) * 1024]
    spc6 = lambda c: aA[:, 8192 + c * 1024:8192 + (c + 1) * 1024]
    barA = [aC[:, 0:1024], aC[:, 1024:2048]]
    grep = [aC[:, 2048:3072], aC[:, 3072:4096]]
    crep = [aCb[:, 8192:9216], aCb[:, 9216:10240]]
    hbuf = [aCb[:, 10240:11264], aCb[:, 11264:12288]]
    xh = [aC[:, 6144:7168], aC[:, 7168:8192]]
    outh = [aC[:, 8192:9216], aC[:, 11264:12288]]
    scats = [aCb[:, 18432:19456], aCb[:, 26624:27648]]
    hidx = [aCi16[:, 19456 + i * 256:19456 + (i + 1) * 256] for i in range(4)]
    hstg = aC[:, 12288:13312]
    dtl6 = aCb[:, 20480:22528]

    @blk.sync
    def _(sy, need=corrd):
        sy.wait_ge(cV.h, need)
        sy.wait_ge(cT.h, cT.n)
        dX.dma(sy.dma_start(out=dtl6, in_=h_dtl))
        for s_ in range(SEQ):
            for hf in range(2):
                i = s_ * 2 + hf
                dX.dma(sy.dma_start(out=hstg[:, i * 256:(i + 1) * 256],
                                    in_=h_rows[s_, 1 + hf, :].unsqueeze(0)
                                    .broadcast_to([P, KSEL])))
    need = dX.n

    @blk.vector
    def _(v, need=need):
        v.wait_ge(dX.h, need)
        for i in range(4):
            v.tensor_copy(out=hidx[i], in_=hstg[:, i * 256:(i + 1) * 256])
        cV.one(v.drain())
    hidxd = cV.n

    @blk.gpsimd
    def _(g, need=hidxd):
        g.wait_ge(cV.h, need)
        g.load_library(libcfg.local_scatter)
    lsld = cG.n

    cVs = Ctr(nc, "cVs")
    cSb = Ctr(nc, "cSb")
    corr_rl = [0, 0]

    iglob = 0
    jglob = 0
    gdma = {}
    spd_last = [None]
    for s_ in range(SEQ):
        for hf in range(2):
            jh = s_ * 2 + hf
            tlo = s_ * S + hf * HALF

            if hf == 0:
                @blk.sync
                def _(sy, s_=s_):
                    if s_ == 0:
                        sy.wait_ge(cV.h, corrd)
                    else:
                        sy.wait_ge(cG.h, cG.n)
                    dX.dma(sy.dma_start(out=corr[:, :], in_=h_corr[s_]))
                corr_rl[s_] = dX.n

            for rnd in range(2):
                @blk.tensor
                def _(t, s_=s_, hf=hf, rnd=rnd, spd_last=spd_last[0]):
                    t.wait_ge(cS.h, cS.n)
                    if rnd == 0:
                        if s_ == 0 and hf == 0:
                            t.wait_ge(cV.h, cV.n)
                            t.wait_ge(dX.h, dX.n)
                        else:
                            t.wait_ge(cS.h, spd_last)
                    for ci in range(4):
                        c = rnd * 4 + ci
                        for sl2 in range(2):
                            _mm = t.matmul(pslice(ci * 2 + sl2),
                                     wdt_bf[64 * s_:64 * s_ + 64, c * 128:(c + 1) * 128],
                                     dtl6[64 * s_:64 * s_ + 64,
                                          hf * 1024 + sl2 * 512:hf * 1024 + (sl2 + 1) * 512],
                                     start=True, stop=True)
                    cT.one(_mm)
                need = cT.n

                @blk.scalar
                def _(s, s_=s_, hf=hf, rnd=rnd, need=need):
                    s.wait_ge(cT.h, need)
                    for ci in range(4):
                        c = rnd * 4 + ci
                        for sl2 in range(2):
                            s.activation(out=spc6(c)[:, sl2 * 512:(sl2 + 1) * 512],
                                         in_=pslice(ci * 2 + sl2), func=AF.Exp,
                                         bias=bdt_s[:, c:c + 1], scale=1.0)
                        s.activation(out=spc6(c), in_=spc6(c), func=AF.Ln, bias=1.0,
                                     scale=1.0)
                    cS.one(s.drain())
            spd = cS.n
            spd_last[0] = spd

            @blk.vector
            def _(v, s_=s_, hf=hf, tlo=tlo, spd=spd):
                v.wait_ge(cS.h, spd)
                for c in range(DCH - 1):
                    v.tensor_copy(out=yac(c), in_=u_t(c)[:, tlo:tlo + HALF])
                cV.one(v.tensor_copy(out=yac(DCH - 1), in_=u_t(DCH - 1)[:, tlo:tlo + HALF]))
            yinit = cV.n

            for n_ in range(N):
                j = jglob
                jglob += 1

                @blk.sync
                def _(sy, n_=n_, tlo=tlo, j=j, yinit=yinit):
                    if j >= 2:
                        sy.wait_ge(cVs.h, 8 * (j - 1))
                    if j == 0:
                        sy.wait_ge(cV.h, yinit)
                    dGp[j % 2].dma(sy.dma_start(out=grep[n_ % 2],
                                                in_=h_g[n_, tlo:tlo + HALF].unsqueeze(0)
                                                .broadcast_to([P, HALF])))
                    dGp[j % 2].dma(sy.dma_start(out=crep[n_ % 2],
                                                in_=h_cc[n_, tlo:tlo + HALF].unsqueeze(0)
                                                .broadcast_to([P, HALF])))
                gdma[j] = (j % 2, dGp[j % 2].n)

                for c in range(DCH):
                    i = iglob
                    iglob += 1

                    @blk.scalar
                    def _(s, c=c, n_=n_, i=i, spd=spd):
                        if i >= 2:
                            s.wait_ge(cVs.h, i - 1)
                        col = c * 16 + n_
                        cSb.one(s.activation(out=barA[i % 2], in_=spc6(c), func=AF.Exp,
                                              scale=A099[:, col:col + 1],
                                              bias=A001[:, col:col + 1]))

                    @blk.vector
                    def _(v, s_=s_, hf=hf, c=c, n_=n_, i=i, j=j,
                          sbneed=cSb.n, dneed=gdma[j], yinit=yinit):
                        v.wait_ge(cSb.h, sbneed)
                        if c == 0:
                            v.wait_ge(dGp[dneed[0]].h, dneed[1])
                            if j == 0:
                                v.wait_ge(cV.h, yinit)
                        hcol = (s_ * 8 + c) * 16 + n_
                        if hf == 0:
                            v.tensor_tensor_scan(out=hbuf[i % 2], data0=barA[i % 2],
                                                 data1=grep[n_ % 2], initial=0.0,
                                                 op0=ALU.mult, op1=ALU.add)
                            v.drain()
                            v.tensor_copy(out=hstate[:, hcol:hcol + 1],
                                          in_=hbuf[i % 2][:, HALF - 1:HALF])
                        else:
                            v.tensor_tensor_scan(out=hbuf[i % 2], data0=barA[i % 2],
                                                 data1=grep[n_ % 2],
                                                 initial=hstate[:, hcol:hcol + 1],
                                                 op0=ALU.mult, op1=ALU.add)
                        v.tensor_mul(out=hbuf[i % 2], in0=hbuf[i % 2], in1=crep[n_ % 2])
                        cVs.one(v.tensor_add(out=yac(c), in0=yac(c), in1=hbuf[i % 2]))
            scan_done = cVs.n

            finv = cV.n
            for c in range(DCH):
                @blk.sync
                def _(sy, c=c, tlo=tlo, finv=finv):
                    if c >= 2:
                        sy.wait_ge(cV.h, finv + (c - 1))
                    dHp[c % 2].dma(sy.dma_start(out=xh[c % 2],
                                                in_=xin[c * P:(c + 1) * P, tlo:tlo + HALF]))
                xneed = (c % 2, dHp[c % 2].n)

                @blk.gpsimd
                def _(g, c=c, s_=s_, jh=jh, finv=finv, scan_done=scan_done):
                    if c == 0:
                        g.wait_ge(cVs.h, scan_done)
                    else:
                        g.wait_ge(cV.h, finv + c - 1)
                    if c == 0:
                        g.wait_ge(dX.h, corr_rl[s_])
                    cG.one(g.local_scatter(
                        scats[c % 2], corr[:, c * 256:(c + 1) * 256],
                        hidx[jh], channels=P, num_elems=HALF, num_idxs=KSEL))
                gneed = cG.n

                @blk.vector
                def _(v, c=c, xneed=xneed, gneed=gneed, scan_done=scan_done):
                    v.wait_ge(dHp[xneed[0]].h, xneed[1])
                    v.wait_ge(cG.h, gneed)
                    v.wait_ge(cVs.h, scan_done)
                    if c >= 2:
                        v.wait_ge(dOp[c % 2].h, dOp[c % 2].n)
                    v.tensor_add(out=outh[c % 2], in0=xh[c % 2], in1=yac(c))
                    v.drain()
                    cV.one(v.tensor_add(out=outh[c % 2], in0=outh[c % 2],
                                        in1=scats[c % 2]))
                vneed = cV.n

                @blk.sync
                def _(sy, c=c, tlo=tlo, vneed=vneed):
                    sy.wait_ge(cV.h, vneed)
                    dOp[c % 2].dma(
                        sy.dma_start(out=out_fm[c * P:(c + 1) * P, tlo:tlo + HALF],
                                     in_=outh[c % 2]))

    @blk.sync
    def _(sy):
        sy.wait_ge(dOp[0].h, dOp[0].n)
        sy.wait_ge(dOp[1].h, dOp[1].n)

    blk.__exit__(None, None, None)
    return nc


_NC_CACHE = {}


def _get_nc():
    if "nc" not in _NC_CACHE:
        nc = build_nc()
        lower_extended_insts(nc)
        _NC_CACHE["nc"] = nc
    return _NC_CACHE["nc"]


def kernel(**inputs):
    x = np.asarray(inputs["L0_attn_x"])
    ncores = 8
    eye = np.eye(P, dtype=np.float32)
    ones = np.ones((P, P), dtype=np.float32)
    inv = (1.0 / (10000.0 ** (np.arange(0, HD, 2, dtype=np.float32) / HD))).astype(np.float32)
    cpk = np.zeros((P, 257), dtype=np.float32)
    cpk[:, 0:128] = eye
    cpk[:, 128:256] = ones
    cpk[0:32, 256] = inv
    a_log = np.asarray(inputs["L0_attn_A_log"])
    a_perm = np.ascontiguousarray(
        a_log.reshape(8, 128, 16).transpose(1, 0, 2).reshape(128, 128))

    def c8(vec):
        return np.ascontiguousarray(np.asarray(vec).reshape(8, 128).T)

    mask = np.asarray(inputs["L0_ffn_attention_mask"])[:, 0]
    posi = np.asarray(inputs["L0_ffn_position_ids"])
    base = {
        "w_in": np.asarray(inputs["L0_attn_w_in"]),
        "w_x": np.asarray(inputs["L0_attn_w_x"]),
        "w_dt": np.asarray(inputs["L0_attn_w_dt"]),
        "wq": np.asarray(inputs["L0_ffn_wq"]),
        "wk": np.asarray(inputs["L0_ffn_wk"]),
        "wv": np.asarray(inputs["L0_ffn_wv"]),
        "wo": np.asarray(inputs["L0_ffn_wo"]),
        "w1": np.asarray(inputs["L0_ffn_w1"]),
        "w2": np.asarray(inputs["L0_ffn_w2"]),
        "a_perm": a_perm,
        "bdt8": c8(inputs["L0_attn_b_dt"]),
        "dp8": c8(inputs["L0_attn_Dp"]),
        "ln18": c8(inputs["L0_ffn_ln1"]),
        "ln28": c8(inputs["L0_ffn_ln2"]),
        "rw8": c8(np.asarray(inputs["L0_ffn_router_w"])[:, 0]),
        "rb1": np.asarray(inputs["L0_ffn_router_b"]).reshape(1, 1),
        "cpack": cpk,
    }
    in_maps = []
    for core in range(ncores):
        b0 = 2 * core
        m = dict(base)
        xc = x[b0:b0 + 2]
        m["xin"] = np.ascontiguousarray(xc.transpose(2, 0, 1).reshape(D, T))
        bc0, bc1 = min(b0, 3), min(b0 + 1, 3)
        m["mask0"] = np.ascontiguousarray(mask[bc0, 0:S, 0:S])
        m["mask1"] = np.ascontiguousarray(mask[bc1, 0:S, 0:S])
        m["pos2"] = np.ascontiguousarray(
            np.stack([posi[bc0, 0:S], posi[bc1, 0:S]]).astype(np.int32))
        in_maps.append(m)

    nc = _get_nc()
    res = run_bass_kernel_spmd(nc, in_maps, core_ids=list(range(ncores)))
    outs = []
    for core in range(ncores):
        o = np.asarray(res.results[core]["out_fm"])
        outs.append(o.reshape(D, SEQ, S).transpose(1, 2, 0))
    return np.concatenate(outs, axis=0).astype(np.float32)


# version 08 (final)
# speedup vs baseline: 1.1707x; 1.0046x over previous
import numpy as np
import ml_dtypes

import concourse.bass as bass
import concourse.mybir as mybir
import concourse.library_config as libcfg
from concourse.bass import IndirectOffsetOnAxis
from concourse.bass_utils import run_bass_kernel_spmd
from concourse.library_overlay import lower_extended_insts

F32 = mybir.dt.float32
BF16 = mybir.dt.bfloat16
U32 = mybir.dt.uint32
I32 = mybir.dt.int32
I16 = mybir.dt.int16
AF = mybir.ActivationFunctionType
ALU = mybir.AluOpType
AXX = mybir.AxisListType.X

P, D, DCH, S, SEQ = 128, 1024, 8, 2048, 2
T, HALF, N, KSEL, DTRANK, FF, NH, HD = 4096, 1024, 16, 256, 64, 4096, 16, 64
EPS_LN, EPS_RMS, NEG, RSC = 1e-5, 1e-6, -1.0e9, 0.125
TWO_PI = 6.283185307179586
MAGIC = 12582912.0
PHASE_LIMIT = 99
DEBUG_HBM = False


class Ctr:
    def __init__(self, nc, name):
        self.h = nc.alloc_semaphore(name)
        self.n = 0

    def dma(self, instr):
        instr.then_inc(self.h, 16)
        self.n += 16

    def one(self, instr):
        instr.then_inc(self.h, 1)
        self.n += 1

    def seq(self, eng):
        eng.sem_inc(self.h, 1)
        self.n += 1


def build_nc():
    nc = bass.Bass("TRN2", target_bir_lowering=False)

    xin = nc.dram_tensor("xin", [D, T], F32, kind="ExternalInput").ap()
    w_in_h = nc.dram_tensor("w_in", [D, D], F32, kind="ExternalInput").ap()
    w_x_h = nc.dram_tensor("w_x", [D, 96], F32, kind="ExternalInput").ap()
    w_dt_h = nc.dram_tensor("w_dt", [DTRANK, D], F32, kind="ExternalInput").ap()
    wq_h = nc.dram_tensor("wq", [D, D], F32, kind="ExternalInput").ap()
    wk_h = nc.dram_tensor("wk", [D, D], F32, kind="ExternalInput").ap()
    wv_h = nc.dram_tensor("wv", [D, D], F32, kind="ExternalInput").ap()
    wo_h = nc.dram_tensor("wo", [D, D], F32, kind="ExternalInput").ap()
    w1_h = nc.dram_tensor("w1", [D, FF], F32, kind="ExternalInput").ap()
    w2_h = nc.dram_tensor("w2", [FF, D], F32, kind="ExternalInput").ap()
    a_perm = nc.dram_tensor("a_perm", [P, P], F32, kind="ExternalInput").ap()
    bdt8 = nc.dram_tensor("bdt8", [P, DCH], F32, kind="ExternalInput").ap()
    dp8 = nc.dram_tensor("dp8", [P, DCH], F32, kind="ExternalInput").ap()
    ln18 = nc.dram_tensor("ln18", [P, DCH], F32, kind="ExternalInput").ap()
    ln28 = nc.dram_tensor("ln28", [P, DCH], F32, kind="ExternalInput").ap()
    rw8 = nc.dram_tensor("rw8", [P, DCH], F32, kind="ExternalInput").ap()
    rb1 = nc.dram_tensor("rb1", [1, 1], F32, kind="ExternalInput").ap()
    mask_h = [nc.dram_tensor("mask0", [S, S], F32, kind="ExternalInput").ap(),
              nc.dram_tensor("mask1", [S, S], F32, kind="ExternalInput").ap()]
    pos2 = nc.dram_tensor("pos2", [SEQ, S], I32, kind="ExternalInput").ap()
    cpack = nc.dram_tensor("cpack", [P, 257], F32, kind="ExternalInput").ap()
    out_fm = nc.dram_tensor("out_fm", [D, T], F32, kind="ExternalOutput").ap()

    KIND = "ExternalOutput" if DEBUG_HBM else "Internal"
    h_wrow = nc.dram_tensor("h_wrow", [SEQ, 2, S], F32, kind=KIND).ap()
    h_midx = nc.dram_tensor("h_midx", [SEQ, S], F32, kind="Internal").ap()
    h_selw = nc.dram_tensor("h_selw", [SEQ, 256], F32, kind="Internal").ap()
    h_seli = nc.dram_tensor("h_seli", [SEQ, 16, 16], mybir.dt.uint16, kind="Internal").ap()
    h_rows = nc.dram_tensor("h_rows", [SEQ, 6, KSEL], F32, kind=KIND).ap()
    h_g = nc.dram_tensor("h_g", [N, T], F32, kind=KIND).ap()
    h_cc = nc.dram_tensor("h_cc", [N, T], BF16, kind=KIND).ap()
    h_dtl = nc.dram_tensor("h_dtl", [P, S], BF16, kind=KIND).ap()
    h_corr = nc.dram_tensor("h_corr", [SEQ, P, 2048], BF16, kind=KIND).ap()
    h_dbg = nc.dram_tensor("h_dbg", [P, 16384], F32, kind=KIND).ap() if DEBUG_HBM else None

    AR_A = nc.alloc_sbuf_tensor("AR_A", [P, 16384], F32)
    AR_B = nc.alloc_sbuf_tensor("AR_B", [P, 16384], F32)
    AR_C = nc.alloc_sbuf_tensor("AR_C", [P, 16384], F32)
    SMALL = nc.alloc_sbuf_tensor("SMALL", [P, 1792], F32)
    WST = nc.alloc_sbuf_tensor("WST", [P, 1024], F32)
    CORR = nc.alloc_sbuf_tensor("CORR", [P, 2048], BF16)
    SEL = nc.alloc_sbuf_tensor("SEL", [P, 96], mybir.dt.uint16)

    aA, aB, aC = AR_A.ap(), AR_B.ap(), AR_C.ap()
    aAb, aBb, aCb = aA.bitcast(BF16), aB.bitcast(BF16), aC.bitcast(BF16)
    aCi16 = aC.bitcast(I16)
    r1 = aB[0:32, 0:2048]
    r2 = aB[0:32, 2048:4096]
    r2i = r2.bitcast(I32)
    sm = SMALL.ap()
    smb = sm.bitcast(BF16)
    wst = WST.ap()
    corr = CORR.ap()
    sil = SEL.ap()
    slw = aB[0:32, 4096:5632]

    A_f = sm[:, 0:128]
    A099 = sm[:, 128:256]
    A001 = sm[:, 256:384]
    A_bf = smb[:, 768:896]
    bdt_s = sm[:, 448:456]
    dp_s = sm[:, 456:464]
    ln1_s = sm[:, 464:472]
    ln2_s = sm[:, 472:480]
    rw_s = sm[:, 480:488]
    thr2 = sm[:, 488:489]
    rb2 = sm[:, 489:490]
    nRcol = sm[:, 490:491]
    Rtmp = sm[:, 491:499]
    kth_o = sm[:, 500:504]
    w128t = [sm[:, 504:520], sm[:, 520:536]]
    ridf = sm[:, 536:540]
    ridu = sm.bitcast(U32)[:, 540:544]
    psel_raw = sm.bitcast(I32)[:, 544:548]
    psel_f = sm[:, 548:552]
    wsel_raw = sm[:, 552:556]
    inv32c = sm[:, 556:557]
    epsln_c = sm[:, 557:558]
    epsrms_c = sm[:, 558:559]
    identf_st = sm[:, 560:688]
    ones_f = sm[:, 688:816]
    ident_bf = smb[:, 1632:1760]
    ones_bf = smb[:, 1760:1888]
    hstate = sm[:, 944:1200]
    wdt_bf = smb[:, 2400:3424]
    nopv = sm[0:1, 1712:1713]
    nops = sm[0:1, 1713:1714]

    PS1 = nc.alloc_psum_tensor("PS1", [P, 2048], F32)
    PS2 = nc.alloc_psum_tensor("PS2", [P, 2048], F32)
    ps1, ps2 = PS1.ap(), PS2.ap()
    psTb = ps2[:, 1536:2048].bitcast(BF16)

    dX = Ctr(nc, "dX")
    dCst = Ctr(nc, "dCst")
    dXp = [Ctr(nc, "dXp0"), Ctr(nc, "dXp1")]
    dGp = [Ctr(nc, "dGp0"), Ctr(nc, "dGp1")]
    dHp = [Ctr(nc, "dHp0"), Ctr(nc, "dHp1")]
    dOp = [Ctr(nc, "dOp0"), Ctr(nc, "dOp1")]
    dO = Ctr(nc, "dO")
    dGm = Ctr(nc, "dGm")
    cV = Ctr(nc, "cV")
    cS = Ctr(nc, "cS")
    cT = Ctr(nc, "cT")
    cG = Ctr(nc, "cG")

    def xn_t(c):
        return aAb[:, c * 4096:(c + 1) * 4096]

    def u_t(c):
        return aBb[:, c * 4096:(c + 1) * 4096]

    def pslice(sl):
        return (ps1 if sl < 4 else ps2)[:, (sl % 4) * 512:(sl % 4 + 1) * 512]

    blk = nc.Block("main")
    blk.__enter__()

    def _emit_finish():
        @blk.vector
        def _(v):
            v.wait_ge(cS.h, cS.n)
            v.wait_ge(cT.h, cT.n)
            v.wait_ge(cG.h, cG.n)
            v.memset(aC[:, 0:4096], 0.0)
            cV.one(v.drain())

        @blk.sync
        def _(sy):
            sy.wait_ge(dX.h, dX.n)
            sy.wait_ge(cV.h, cV.n)
            sy.wait_ge(dCst.h, dCst.n)
            sy.wait_ge(dGm.h, dGm.n)
            for p in (dXp + dGp + dHp):
                sy.wait_ge(p.h, p.n)
            for c in range(DCH):
                dO.dma(sy.dma_start(out=out_fm[c * P:(c + 1) * P, :], in_=aC[:, 0:4096]))
            sy.wait_ge(dO.h, dO.n)
        blk.__exit__(None, None, None)


    xstage = [aC[:, 0:4096], aC[:, 4096:8192]]
    racc = aC[:, 8192:12288]
    sqb = [aCb[:, 24576:26624], aCb[:, 26624:28672]]

    @blk.sync
    def _(sy):
        for dst, src in [(A_f, a_perm), (bdt_s, bdt8), (dp_s, dp8), (ln1_s, ln18),
                         (ln2_s, ln28), (rw_s, rw8), (rb2[0:1, :], rb1), (rb2[1:2, :], rb1),
                         (identf_st, cpack[:, 0:128]), (ones_f, cpack[:, 128:256]),
                         (inv32c[0:32, :], cpack[0:32, 256:257])]:
            dCst.dma(sy.dma_start(out=dst, in_=src, allow_slow_non_contiguous=True))
    ph1_small = dCst.n

    @blk.gpsimd
    def _(g):
        cG.one(g.iota(r2i[0:2, :], pattern=[[1, 2048]], base=1, channel_multiplier=0))
    iota_done = cG.n

    @blk.scalar
    def _(s):
        s.wait_ge(dCst.h, ph1_small)
        cS.one(s.activation(out=A_f, in_=A_f, func=AF.Exp, scale=1.0))
    aexp = cS.n

    if PHASE_LIMIT <= -1:
        _emit_finish()
        return nc

    vbase_x = cV.n
    for c in range(DCH):
        @blk.sync
        def _(sy, c=c):
            if c >= 2:
                sy.wait_ge(cV.h, vbase_x + (c - 1))
            dXp[c % 2].dma(sy.dma_start(out=xstage[c % 2], in_=xin[c * P:(c + 1) * P, :]))
        need = dXp[c % 2].n

        @blk.vector
        def _(v, c=c, need=need):
            v.wait_ge(dXp[c % 2].h, need)
            if c == 0:
                v.wait_ge(dCst.h, ph1_small)
                v.wait_ge(cG.h, iota_done)
                v.wait_ge(cS.h, aexp)
                v.tensor_copy(out=ident_bf, in_=identf_st)
                v.tensor_copy(out=ones_bf, in_=ones_f)
                v.memset(epsln_c, EPS_LN)
                v.memset(epsrms_c, EPS_RMS)
                v.tensor_copy(out=r2[0:2, :], in_=r2i[0:2, :])
                v.tensor_scalar(out=A_f, in0=A_f, scalar1=-1.0, scalar2=None, op0=ALU.mult)
                v.drain()
                v.tensor_scalar(out=A099, in0=A_f, scalar1=0.099, scalar2=None, op0=ALU.mult)
                v.tensor_scalar(out=A001, in0=A_f, scalar1=0.001, scalar2=None, op0=ALU.mult)
                v.tensor_copy(out=A_bf, in_=A_f)
            v.tensor_copy(out=xn_t(c), in_=xstage[c % 2])
            if c == 0:
                v.scalar_tensor_tensor(out=racc, in0=xstage[0], scalar=rw_s[:, 0:1],
                                       in1=xstage[0], op0=ALU.mult, op1=ALU.bypass)
            else:
                v.scalar_tensor_tensor(out=racc, in0=xstage[c % 2], scalar=rw_s[:, c:c + 1],
                                       in1=racc, op0=ALU.mult, op1=ALU.add)
            cV.one(v.drain())
    xstream = cV.n


    if PHASE_LIMIT <= 0:
        _emit_finish()
        return nc

    m_b = aC[:, 0:2048]
    r_b = aC[:, 2048:4096]
    m_bf = aCb[:, 8192:10240]
    r_bf = aCb[:, 10240:12288]
    msq = aC[:, 6144:8192]
    w_bc = aC[:, 14336:16384]

    for s_ in range(SEQ):
        col0 = s_ * S
        for c in range(DCH):
            @blk.scalar
            def _(s, c=c, s_=s_):
                if s_ == 0 and c == 0:
                    s.wait_ge(cV.h, xstream)
                if not (s_ == 0 and c in (0, 1)):
                    s.wait_ge(cT.h, cT.n)
                s.activation(out=sqb[c % 2], in_=xn_t(c)[:, col0:col0 + S],
                             func=AF.Square, scale=1.0)
                cS.one(s.drain())
            need = cS.n

            @blk.tensor
            def _(t, c=c, s_=s_, need=need):
                t.wait_ge(cS.h, need)
                if s_ == 1 and c == 0:
                    t.wait_ge(cV.h, cV.n)
                for sl in range(4):
                    t.matmul(ps1[:, sl * 512:(sl + 1) * 512], ones_bf,
                             xn_t(c)[:, col0 + sl * 512:col0 + (sl + 1) * 512],
                             start=(c == 0), stop=(c == DCH - 1))
                for sl in range(4):
                    _mm = t.matmul(ps2[:, sl * 512:(sl + 1) * 512], ones_bf,
                             sqb[c % 2][:, sl * 512:(sl + 1) * 512],
                             start=(c == 0), stop=(c == DCH - 1))
                cT.one(_mm)
        smm = cT.n

        if PHASE_LIMIT <= 0.2:
            _emit_finish()
            return nc

        @blk.vector
        def _(v, need=smm):
            v.wait_ge(cT.h, need)
            v.tensor_scalar(out=m_b, in0=ps1[:, 0:2048], scalar1=1.0 / D, scalar2=None,
                            op0=ALU.mult)
            cV.one(v.drain())
        mdone = cV.n

        if DEBUG_HBM:
            @blk.sync
            def _(sy, s_=s_, need=mdone):
                sy.wait_ge(cV.h, need)
                if s_ == 0:
                    dX.dma(sy.dma_start(out=h_dbg[:, 12416:14464], in_=m_b))
                sy.wait_ge(dX.h, dX.n)

        @blk.scalar
        def _(s, need=mdone):
            s.wait_ge(cV.h, need)
            if DEBUG_HBM:
                s.wait_ge(dX.h, dX.n)
            cS.one(s.activation(out=msq, in_=m_b, func=AF.Square, scale=1.0))
        msqd = cS.n

        @blk.vector
        def _(v, need=msqd):
            v.wait_ge(cS.h, need)
            v.scalar_tensor_tensor(out=r_b, in0=ps2[:, 0:2048], scalar=1.0 / D, in1=msq,
                                   op0=ALU.mult, op1=ALU.subtract)
            cV.one(v.drain())
        vard = cV.n

        @blk.scalar
        def _(s, need=vard):
            s.wait_ge(cV.h, need)
            s.activation(out=r_b, in_=r_b, func=AF.Ln, bias=epsln_c[:, 0:1], scale=1.0)
            cS.one(s.activation(out=r_b, in_=r_b, func=AF.Exp, scale=-0.5))
        rstdd = cS.n

        if DEBUG_HBM:
            @blk.sync
            def _(sy, s_=s_, need=rstdd):
                sy.wait_ge(cS.h, need)
                dX.dma(sy.dma_start(out=h_dbg[:, 8192 + s_ * 2048:8192 + (s_ + 1) * 2048],
                                    in_=r_b))

        @blk.vector
        def _(v, s_=s_, need=rstdd):
            v.wait_ge(cS.h, need)
            if DEBUG_HBM:
                v.wait_ge(dX.h, dX.n)
            v.tensor_copy(out=m_bf, in_=m_b)
            v.tensor_copy(out=r_bf, in_=r_b)
            for c in range(DCH):
                xseg = xn_t(c)[:, col0:col0 + S]
                v.tensor_sub(out=sqb[0], in0=xseg, in1=m_bf)
                v.tensor_mul(out=xseg, in0=sqb[0], in1=r_bf)
            cV.one(v.drain())
        xnd = cV.n

        if PHASE_LIMIT <= 0.4:
            _emit_finish()
            return nc

        @blk.tensor
        def _(t, s_=s_, need=xnd):
            t.wait_ge(cV.h, need)
            t.matmul(ps2[:, 0:8], ones_f, rw_s, start=True, stop=True)
            for sl in range(4):
                _mm = t.matmul(ps1[:, sl * 512:(sl + 1) * 512], ones_f,
                         racc[:, col0 + sl * 512:col0 + (sl + 1) * 512],
                         start=True, stop=True)
            cT.one(_mm)
        rxd = cT.n

        @blk.vector
        def _(v, s_=s_, need=rxd):
            v.wait_ge(cT.h, need)
            if DEBUG_HBM:
                v.tensor_copy(out=msq, in_=ps1[:, 0:2048])
            v.tensor_reduce(out=Rtmp[:, 0:1], in_=ps2[:, 0:8], axis=AXX, op=ALU.add)
            v.drain()
            v.tensor_scalar(out=nRcol, in0=Rtmp[:, 0:1], scalar1=-1.0, scalar2=None,
                            op0=ALU.mult)
            v.drain()
            v.scalar_tensor_tensor(out=w_bc, in0=m_b, scalar=nRcol[:, 0:1],
                                   in1=ps1[:, 0:2048], op0=ALU.mult, op1=ALU.add)
            v.drain()
            v.tensor_mul(out=w_bc, in0=w_bc, in1=r_b)
            cV.one(v.drain())
        wd = cV.n

        if DEBUG_HBM:
            @blk.sync
            def _(sy, s_=s_, need=wd):
                sy.wait_ge(cV.h, need)
                dX.dma(sy.dma_start(out=h_dbg[:, 4096 + s_ * 2048:4096 + (s_ + 1) * 2048],
                                    in_=msq))
                if s_ == 0:
                    dX.dma(sy.dma_start(out=h_dbg[:, 0:2048], in_=w_bc))
                    dX.dma(sy.dma_start(out=h_dbg[:, 2048:2056], in_=sm[:, 488:496]))
                    dX.dma(sy.dma_start(out=h_dbg[:, 12288:12416], in_=ones_f))
                sy.wait_ge(dX.h, dX.n)

        if PHASE_LIMIT <= 0.6:
            _emit_finish()
            return nc

        @blk.sync
        def _(sy, s_=s_, need=wd):
            sy.wait_ge(cV.h, need)
            dX.dma(sy.dma_start(out=h_wrow[s_, 0, :], in_=w_bc[0:1, :]))
            sy.wait_ge(dX.h, dX.n)
            dX.dma(sy.dma_start(out=w128t[s_],
                                in_=h_wrow[s_, 0, :].rearrange("(p f) -> p f", p=P)))
    wwrap = dX.n

    if PHASE_LIMIT <= 0.8:
        _emit_finish()
        return nc

    @blk.gpsimd
    def _(g, need=wwrap):
        g.wait_ge(dX.h, need)
        g.load_library(libcfg.attn)
        q = 1.0 - 254.5 / (S - 1)
        g.kth_largest(kth_o[0:1, 0:2], w128t[0], n_per_lane=S // P,
                      k=KSEL + 2, quantile=q)
        cG.one(g.kth_largest(kth_o[0:1, 2:4], w128t[1], n_per_lane=S // P,
                             k=KSEL + 2, quantile=q))
    kthd = cG.n

    @blk.sync
    def _(sy, need=kthd):
        sy.wait_ge(cG.h, need)
        dX.dma(sy.dma_start(out=thr2[0:1, :], in_=kth_o[0:1, 1:2]))
        dX.dma(sy.dma_start(out=thr2[1:2, :], in_=kth_o[0:1, 3:4]))
        dX.dma(sy.dma_start(out=r1[0:2, :], in_=h_wrow[:, 0, :]))
    thrd = dX.n

    @blk.vector
    def _(v, need=thrd):
        v.wait_ge(dX.h, need)
        v.scalar_tensor_tensor(out=r2[0:2, :], in0=r1[0:2, :], scalar=thr2[0:2, :],
                               in1=r2[0:2, :], op0=ALU.is_gt, op1=ALU.mult)
        v.tensor_scalar(out=r2[0:2, :], in0=r2[0:2, :], scalar1=1.0, scalar2=None,
                        op0=ALU.subtract)
        v.scalar_tensor_tensor(out=r1[0:2, :], in0=r1[0:2, :], scalar=rb2[0:2, :],
                               in1=r1[0:2, :], op0=ALU.add, op1=ALU.bypass)
        cV.one(v.drain())
    midxd = cV.n

    @blk.sync
    def _(sy, need=midxd):
        sy.wait_ge(cV.h, need)
        dX.dma(sy.dma_start(out=h_midx, in_=r2[0:2, :]))
        dX.dma(sy.dma_start(out=h_wrow[0, 1, :], in_=r1[0:1, :]))
        dX.dma(sy.dma_start(out=h_wrow[1, 1, :], in_=r1[1:2, :]))
    mbounce = dX.n

    gp_sparse_base = cG.n
    for s_ in range(SEQ):
        @blk.sync
        def _(sy, s_=s_, need=mbounce):
            sy.wait_ge(dX.h, need)
            if s_ == 1:
                sy.wait_ge(cG.h, gp_sparse_base + 1)
            dX.dma(sy.dma_start(out=slw[0:16, 0:128],
                                in_=h_midx[s_].rearrange("(f p) -> p f", p=16),
                                allow_slow_non_contiguous=True))
        wdone = dX.n

        @blk.gpsimd
        def _(g, s_=s_, need=wdone):
            if s_ == 0:
                g.load_library(libcfg.sparse_gather)
            g.wait_ge(dX.h, need)
            nf_u = sil.bitcast(U32)[0:1, 44:45]
            cG.one(g.sparse_gather(out=slw[0:16, 128 + 16 * s_:144 + 16 * s_],
                                   in_=slw[0:16, 0:128], num_found=nf_u))
    sparsed = cG.n

    @blk.vector
    def _(v, need=sparsed):
        v.wait_ge(cG.h, need)
        v.tensor_scalar(out=slw[0:16, 128:160], in0=slw[0:16, 128:160], scalar1=0.0,
                        scalar2=2047.0, op0=ALU.max, op1=ALU.min)
        v.drain()
        v.tensor_copy(out=sil[0:16, 0:32], in_=slw[0:16, 128:160])
        cV.one(v.drain())
    selwd = cV.n

    @blk.sync
    def _(sy, need=selwd):
        sy.wait_ge(cV.h, need)
        for s_ in range(SEQ):
            dX.dma(sy.dma_start(out=h_selw[s_].rearrange("(f p) -> p f", p=16),
                                in_=slw[0:16, 128 + 16 * s_:144 + 16 * s_],
                                allow_slow_non_contiguous=True))
            dX.dma(sy.dma_start(out=h_seli[s_], in_=sil[0:16, 16 * s_:16 * (s_ + 1)]))
        sy.wait_ge(dX.h, dX.n)
        dX.dma(sy.dma_start(out=r2[0:2, 0:256], in_=h_selw))
        for s_ in range(SEQ):
            for grp in range(8):
                dX.dma(sy.dma_start(out=sil[grp * 16:(grp + 1) * 16,
                                            32 + 16 * s_:48 + 16 * s_],
                                    in_=h_seli[s_]))
    lind = dX.n

    @blk.vector
    def _(v, need=lind):
        v.wait_ge(dX.h, need)
        v.memset(r2[0:2, 255:256], -1.0)
        v.drain()
        v.tensor_scalar(out=r2[0:2, 0:256], in0=r2[0:2, 0:256], scalar1=-1.0,
                        scalar2=2047.0, op0=ALU.max, op1=ALU.min)
        v.drain()
        v.tensor_scalar(out=r2[0:2, 512:768], in0=r2[0:2, 0:256], scalar1=1024.0,
                        scalar2=None, op0=ALU.subtract)
        v.tensor_scalar(out=r2[0:2, 768:1024], in0=r2[0:2, 0:256], scalar1=1024.0,
                        scalar2=None, op0=ALU.is_ge)
        v.drain()
        v.scalar_tensor_tensor(out=r2[0:2, 768:1024], in0=r2[0:2, 768:1024],
                               scalar=-2048.0, in1=r2[0:2, 0:256],
                               op0=ALU.mult, op1=ALU.add)
        v.tensor_scalar(out=r2[0:2, 1024:1280], in0=r2[0:2, 0:256], scalar1=0.0,
                        scalar2=None, op0=ALU.max)
        cV.one(v.drain())
    halvd = cV.n

    @blk.sync
    def _(sy, need=halvd):
        sy.wait_ge(cV.h, need)
        for s_ in range(SEQ):
            dX.dma(sy.dma_start(out=h_rows[s_, 0, :], in_=r2[s_:s_ + 1, 0:256]))
            dX.dma(sy.dma_start(out=h_rows[s_, 1, :], in_=r2[s_:s_ + 1, 768:1024]))
            dX.dma(sy.dma_start(out=h_rows[s_, 2, :], in_=r2[s_:s_ + 1, 512:768]))
            dX.dma(sy.dma_start(out=h_rows[s_, 3, :], in_=r2[s_:s_ + 1, 1024:1280]))
        sy.wait_ge(dX.h, dX.n)
        for s_ in range(SEQ):
            for b_ in range(2):
                cc = 2 * s_ + b_
                dX.dma(sy.dma_start(out=ridf[:, cc:cc + 1],
                                    in_=h_rows[s_, 3, b_ * 128:(b_ + 1) * 128].unsqueeze(-1)))
    ridd = dX.n

    @blk.vector
    def _(v, need=ridd):
        v.wait_ge(dX.h, need)
        v.tensor_copy(out=ridu, in_=ridf)
        cV.one(v.drain())
    ridc = cV.n

    @blk.gpsimd
    def _(g, need=ridc):
        g.wait_ge(cV.h, need)
        for s_ in range(SEQ):
            for b_ in range(2):
                cc = 2 * s_ + b_
                dGm.dma(g.indirect_dma_start(
                    out=psel_raw[:, cc:cc + 1], out_offset=None,
                    in_=pos2.rearrange("s t -> (s t)").unsqueeze(-1),
                    in_offset=IndirectOffsetOnAxis(ap=ridu[:, cc:cc + 1], axis=0),
                    element_offset=s_ * S))
                dGm.dma(g.indirect_dma_start(
                    out=wsel_raw[:, cc:cc + 1], out_offset=None,
                    in_=h_wrow.rearrange("s two t -> (s two t)").unsqueeze(-1),
                    in_offset=IndirectOffsetOnAxis(ap=ridu[:, cc:cc + 1], axis=0),
                    element_offset=(s_ * 2 + 1) * S))
        cG.seq(g)
    pseld = cG.n
    pseld_dma = dGm.n

    @blk.vector
    def _(v, need=pseld, needd=pseld_dma):
        v.wait_ge(cG.h, need)
        v.wait_ge(dGm.h, needd)
        v.tensor_copy(out=psel_f, in_=psel_raw)
        cV.one(v.drain())
    pselc = cV.n

    @blk.sync
    def _(sy, need=pselc):
        sy.wait_ge(cV.h, need)
        for s_ in range(SEQ):
            for b_ in range(2):
                cc = 2 * s_ + b_
                dX.dma(sy.dma_start(out=h_rows[s_, 4, b_ * 128:(b_ + 1) * 128].unsqueeze(-1),
                                    in_=psel_f[:, cc:cc + 1]))
                dX.dma(sy.dma_start(out=h_rows[s_, 5, b_ * 128:(b_ + 1) * 128].unsqueeze(-1),
                                    in_=wsel_raw[:, cc:cc + 1]))
    rows_done = dX.n

    if PHASE_LIMIT <= 1:
        _emit_finish()
        return nc

    w_in_bf = aCb[:, 0:8192]

    def winbf(k):
        return w_in_bf[:, k * 1024:(k + 1) * 1024]

    vn0 = cV.n
    casts = []
    for k in range(DCH):
        stg = wst if k % 2 == 0 else corr.bitcast(F32)
        @blk.sync
        def _(sy, k=k, stg=stg, vn0=vn0, casts=tuple(casts)):
            if k < 2:
                sy.wait_ge(cV.h, vn0)
            else:
                sy.wait_ge(cV.h, casts[k - 2])
            if k == 0:
                sy.wait_ge(dX.h, rows_done)
            dXp[k % 2].dma(sy.dma_start(out=stg, in_=w_in_h[k * P:(k + 1) * P, :]))
        need = dXp[k % 2].n

        @blk.vector
        def _(v, k=k, need=need, stg=stg):
            v.wait_ge(dXp[k % 2].h, need)
            v.tensor_copy(out=winbf(k), in_=stg)
            cV.one(v.drain())
        casts.append(cV.n)
    winc = cV.n

    for e in range(DCH):
        @blk.tensor
        def _(t, e=e, need=winc):
            t.wait_ge(cV.h, need)
            if e > 0:
                t.wait_ge(cS.h, cS.n)
            for k in range(DCH):
                for sl in range(8):
                    _mm = t.matmul(pslice(sl), winbf(k)[:, e * 128:(e + 1) * 128],
                             xn_t(k)[:, sl * 512:(sl + 1) * 512],
                             start=(k == 0), stop=(k == DCH - 1))
            cT.one(_mm)
        need = cT.n

        @blk.scalar
        def _(s, e=e, need=need):
            s.wait_ge(cT.h, need)
            for sl in range(8):
                s.copy(out=u_t(e)[:, sl * 512:(sl + 1) * 512], in_=pslice(sl))
            cS.one(s.drain())
    xind = cS.n

    if PHASE_LIMIT <= 2:
        _emit_finish()
        return nc

    B0 = aC[0:16, 0:4096]
    C0 = aC[32:48, 0:4096]
    zac = aC[0:16, 8192:12288]
    g0 = B0
    sp_c = aC[:, 4096:8192]
    dt_c = sp_c
    du_c = aCb[:, 24576:28672]
    dtl2 = aCb[:, 28672:30720]
    Cbf = aCb[32:48, 8192:12288]
    wxs = wst.bitcast(BF16)[:, 0:768]

    @blk.sync
    def _(sy, need=xind):
        sy.wait_ge(cS.h, need)
        sy.wait_ge(cV.h, cV.n)
        dX.dma(sy.dma_start(out=wst[0:64, 0:1024], in_=w_dt_h))
        dX.dma(sy.dma_start(out=wst[64:128, 0:1024], in_=w_dt_h))
    need = dX.n

    @blk.vector
    def _(v, need=need):
        v.wait_ge(dX.h, need)
        v.tensor_copy(out=wdt_bf, in_=wst[:, 0:1024])
        cV.one(v.drain())
    wdtc = cV.n

    for k in range(DCH):
        @blk.sync
        def _(sy, k=k):
            sy.wait_ge(cV.h, cV.n)
            dX.dma(sy.dma_start(out=wst[:, 512:608], in_=w_x_h[k * P:(k + 1) * P, :]))
        need = dX.n

        @blk.vector
        def _(v, k=k, need=need):
            v.wait_ge(dX.h, need)
            v.tensor_copy(out=wxs[:, k * 96:(k + 1) * 96], in_=wst[:, 512:608])
            cV.one(v.drain())
    wxc = cV.n

    def proj(mlo, mhi, evac):
        @blk.tensor
        def _(t, mlo=mlo, mhi=mhi):
            t.wait_ge(cV.h, cV.n)
            mm = mhi - mlo
            for k in range(DCH):
                for sl in range(8):
                    _mm = t.matmul(pslice(sl)[0:mm, :], wxs[:, k * 96 + mlo:k * 96 + mhi],
                             u_t(k)[:, sl * 512:(sl + 1) * 512],
                             start=(k == 0), stop=(k == DCH - 1))
            cT.one(_mm)
        need = cT.n

        @blk.vector
        def _(v, evac=evac, need=need):
            v.wait_ge(cT.h, need)
            evac(v)
            cV.one(v.drain())

    def ev_dt(v):
        for sl in range(8):
            s2 = sl // 4
            cb = (sl % 4) * 512
            v.tensor_copy(out=dtl2[64 * s2:64 * (s2 + 1),
                                   (sl % 4) * 512:(sl % 4 + 1) * 512]
                          if False else dtl2[64 * (sl // 4):64 * (sl // 4) + 64,
                                             (sl % 4) * 512:(sl % 4) * 512 + 512],
                          in_=pslice(sl)[0:64, :])

    def ev_B(v):
        for sl in range(8):
            v.tensor_copy(out=B0[:, sl * 512:(sl + 1) * 512], in_=pslice(sl)[0:16, :])

    def ev_C(v):
        for sl in range(8):
            v.tensor_copy(out=C0[:, sl * 512:(sl + 1) * 512], in_=pslice(sl)[0:16, :])

    proj(0, 64, ev_dt)
    proj(64, 80, ev_B)
    proj(80, 96, ev_C)
    projd = cV.n

    @blk.sync
    def _(sy, need=projd):
        sy.wait_ge(cV.h, need)
        dX.dma(sy.dma_start(out=h_dtl, in_=dtl2))
    dtl_saved = dX.n

    for c in range(DCH):
        @blk.tensor
        def _(t, c=c):
            t.wait_ge(cV.h, cV.n)
            t.wait_ge(cS.h, cS.n)
            for sl in range(8):
                rb_ = 64 * (sl // 4)
                _mm = t.matmul(pslice(sl), wdt_bf[rb_:rb_ + 64, c * 128:(c + 1) * 128],
                         dtl2[rb_:rb_ + 64,
                              (sl % 4) * 512:(sl % 4) * 512 + 512],
                         start=True, stop=True)
            cT.one(_mm)
        need = cT.n

        @blk.scalar
        def _(s, c=c, need=need):
            s.wait_ge(cT.h, need)
            for sl in range(8):
                s.activation(out=sp_c[:, sl * 512:(sl + 1) * 512], in_=pslice(sl),
                             func=AF.Exp, bias=bdt_s[:, c:c + 1], scale=1.0)
            s.activation(out=sp_c, in_=sp_c, func=AF.Ln, bias=1.0, scale=1.0)
            cS.one(s.drain())
        need = cS.n

        @blk.vector
        def _(v, c=c, need=need):
            v.wait_ge(cS.h, need)
            v.tensor_scalar(out=dt_c, in0=sp_c, scalar1=0.099, scalar2=0.001,
                            op0=ALU.mult, op1=ALU.add)
            v.tensor_mul(out=du_c, in0=dt_c, in1=u_t(c))
            cV.one(v.drain())
        need = cV.n

        @blk.tensor
        def _(t, c=c, need=need):
            t.wait_ge(cV.h, need)
            for sl in range(8):
                _mm = t.matmul(pslice(sl)[0:16, :], A_bf[:, c * 16:(c + 1) * 16],
                         du_c[:, sl * 512:(sl + 1) * 512], start=True, stop=True)
            cT.one(_mm)
        need = cT.n

        @blk.vector
        def _(v, c=c, need=need):
            v.wait_ge(cT.h, need)
            for half in range(2):
                src = (ps1 if half == 0 else ps2)[0:16, 0:2048]
                if c == 0:
                    v.tensor_copy(out=zac[:, half * 2048:(half + 1) * 2048], in_=src)
                else:
                    v.tensor_add(out=zac[:, half * 2048:(half + 1) * 2048],
                                 in0=zac[:, half * 2048:(half + 1) * 2048], in1=src)
            v.scalar_tensor_tensor(out=u_t(c), in0=u_t(c), scalar=dp_s[:, c:c + 1],
                                   in1=xn_t(c), op0=ALU.mult, op1=ALU.add)
            cV.one(v.drain())
    zdone = cV.n

    @blk.vector
    def _(v, need=zdone):
        v.wait_ge(cV.h, need)
        v.tensor_mul(out=g0, in0=B0, in1=zac)
        v.tensor_copy(out=Cbf, in_=C0)
        cV.one(v.drain())
    gdone = cV.n

    @blk.sync
    def _(sy, need=gdone):
        sy.wait_ge(cV.h, need)
        dX.dma(sy.dma_start(out=h_g, in_=g0))
        dX.dma(sy.dma_start(out=h_cc, in_=Cbf))
    gsaved = dX.n

    if PHASE_LIMIT <= 3:
        _emit_finish()
        return nc

    def xsel_t(c):
        return aAb[:, c * 512:(c + 1) * 512]

    @blk.gpsimd
    def _(g, need=gdone):
        g.wait_ge(cV.h, need)
        g.wait_ge(dX.h, rows_done)
        last = None
        for s_ in range(SEQ):
            for c in range(DCH):
                last = g.indirect_copy(
                    out=xsel_t(c)[:, s_ * 256:(s_ + 1) * 256].unsqueeze(-1),
                    data=xn_t(c)[:, s_ * S:(s_ + 1) * S].unsqueeze(-1),
                    idxs=sil[:, 32 + 16 * s_:48 + 16 * s_],
                    i_know_ap_gather_is_preferred=True)
        cG.one(last)
    xseld = cG.n

    mrow = aC[:, 0:2048]
    mcolf = aC[:, 2048:2304]

    def msel_t(s_, b_):
        i = 2 * s_ + b_
        return aCb[:, 13312 + i * 256:13312 + (i + 1) * 256]

    for s_ in range(SEQ):
        for b_ in range(2):
            cc = 2 * s_ + b_

            @blk.gpsimd
            def _(g, s_=s_, b_=b_, cc=cc, gsaved=gsaved):
                g.wait_ge(cV.h, cV.n)
                g.wait_ge(dX.h, gsaved)
                dGm.dma(g.indirect_dma_start(
                    out=mrow, out_offset=None,
                    in_=mask_h[s_][:, 0:S],
                    in_offset=IndirectOffsetOnAxis(ap=ridu[:, cc:cc + 1], axis=0)))
                g.wait_ge(dGm.h, dGm.n)
                cG.one(g.indirect_copy(
                    out=mcolf.unsqueeze(-1), data=mrow.unsqueeze(-1),
                    idxs=sil[:, 32 + 16 * s_:48 + 16 * s_],
                    i_know_ap_gather_is_preferred=True))
            need = cG.n

            @blk.vector
            def _(v, s_=s_, b_=b_, need=need):
                v.wait_ge(cG.h, need)
                v.tensor_copy(out=msel_t(s_, b_), in_=mcolf)
                v.memset(msel_t(s_, b_)[:, 255:256], NEG)
                cV.one(v.drain())
    mseld = cV.n

    if PHASE_LIMIT <= 4:
        _emit_finish()
        return nc

    hn_r = lambda c: aAb[:, 4096 + c * 512:4096 + (c + 1) * 512]
    q_r = lambda c: aAb[:, 8192 + c * 512:8192 + (c + 1) * 512]
    k_r = lambda c: aAb[:, 12288 + c * 512:12288 + (c + 1) * 512]
    v_r = lambda i: aAb[:, 16384 + i * 1024:16384 + (i + 1) * 1024]
    scr2 = lambda i: aAb[:, 20480 + i * 512:20480 + (i + 1) * 512]
    cs_r = lambda i: aA[:, 11264 + i * 256:11264 + (i + 1) * 256]
    x1_r = lambda c: aAb[:, 24576 + c * 512:24576 + (c + 1) * 512]
    hn2_r = lambda c: aAb[:, 28672 + c * 512:28672 + (c + 1) * 512]
    h1_r = lambda f, s_: aAb[:, 4096 + (f * 2 + s_) * 256:4096 + (f * 2 + s_ + 1) * 256]
    wselb = lambda s_: aA[:, 10752 + s_ * 256:10752 + (s_ + 1) * 256]
    rms_b = aC[:, 4096:4352]
    sqs = aCb[:, 8704:8960]
    sc_f = lambda qb: aC[:, 4608 + qb * 256:4608 + (qb + 1) * 256]

    for s_ in range(SEQ):
        for c in range(DCH):
            @blk.scalar
            def _(s, c=c, s_=s_):
                if c == 0 and s_ == 0:
                    s.wait_ge(cV.h, mseld)
                    s.wait_ge(cG.h, xseld)
                else:
                    s.wait_ge(cT.h, cT.n)
                s.activation(out=sqs, in_=xsel_t(c)[:, s_ * 256:(s_ + 1) * 256],
                             func=AF.Square, scale=1.0)
                cS.one(s.drain())
            need = cS.n

            @blk.tensor
            def _(t, c=c, need=need):
                t.wait_ge(cS.h, need)
                _mm = t.matmul(ps1[:, 0:256], ones_bf, sqs, start=(c == 0), stop=(c == DCH - 1))
                cT.one(_mm)
        need = cT.n

        @blk.scalar
        def _(s, need=need):
            s.wait_ge(cT.h, need)
            s.activation(out=rms_b, in_=ps1[:, 0:256], func=AF.Ln, bias=epsrms_c[:, 0:1],
                         scale=1.0 / D)
            s.drain()
            cS.one(s.activation(out=rms_b, in_=rms_b, func=AF.Exp, scale=-0.5))
        need = cS.n

        @blk.vector
        def _(v, s_=s_, need=need):
            v.wait_ge(cS.h, need)
            for c in range(DCH):
                v.scalar_tensor_tensor(out=hn_r(c)[:, s_ * 256:(s_ + 1) * 256],
                                       in0=xsel_t(c)[:, s_ * 256:(s_ + 1) * 256],
                                       scalar=ln1_s[:, c:c + 1], in1=rms_b,
                                       op0=ALU.mult, op1=ALU.mult)
            cV.one(v.drain())
    hnd = cV.n

    wcur = aCb[:, 0:8192]

    def wcur_k(k):
        return wcur[:, k * 1024:(k + 1) * 1024]

    wst2 = corr.bitcast(F32)

    def load_dd(w_hbm):
        tn0 = cT.n
        vn0 = cV.n
        casts = []
        for k in range(DCH):
            stg = wst if k % 2 == 0 else wst2
            @blk.sync
            def _(sy, k=k, w_hbm=w_hbm, stg=stg, vn0=vn0, casts=tuple(casts)):
                if k < 2:
                    sy.wait_ge(cV.h, vn0)
                else:
                    sy.wait_ge(cV.h, casts[k - 2])
                dXp[k % 2].dma(sy.dma_start(out=stg, in_=w_hbm[k * P:(k + 1) * P, :]))
            need = dXp[k % 2].n

            @blk.vector
            def _(v, k=k, need=need, stg=stg, tn0=tn0):
                v.wait_ge(dXp[k % 2].h, need)
                if k < 2:
                    v.wait_ge(cT.h, tn0)
                v.tensor_copy(out=wcur_k(k), in_=stg)
                cV.one(v.drain())
            casts.append(cV.n)
        return cV.n

    def proj_fm(dst_r, wneed):
        sbase = cS.n
        for s_ in range(SEQ):
            for e in range(DCH):
                q = s_ * DCH + e
                @blk.tensor
                def _(t, s_=s_, e=e, q=q, wneed=wneed, sbase=sbase):
                    t.wait_ge(cV.h, wneed)
                    if q >= 4:
                        t.wait_ge(cS.h, sbase + q - 3)
                    else:
                        t.wait_ge(cS.h, sbase)
                    for k in range(DCH):
                        _mm = t.matmul(ps1[:, 512 * (e % 4):512 * (e % 4) + 256],
                                 wcur_k(k)[:, e * 128:(e + 1) * 128],
                                 hn_r(k)[:, s_ * 256:(s_ + 1) * 256],
                                 start=(k == 0), stop=(k == DCH - 1))
                    cT.one(_mm)
                need = cT.n

                @blk.scalar
                def _(s, s_=s_, e=e, dst_r=dst_r, need=need):
                    s.wait_ge(cT.h, need)
                    cS.one(s.copy(out=dst_r(e)[:, s_ * 256:(s_ + 1) * 256],
                                  in_=ps1[:, 512 * (e % 4):512 * (e % 4) + 256]))

    wn = load_dd(wq_h)
    proj_fm(q_r, wn)
    wn = load_dd(wk_h)
    proj_fm(k_r, wn)

    wn = load_dd(wv_h)
    for s_ in range(SEQ):
        for tb in range(2):
            for esl in range(2):
                @blk.tensor
                def _(t, s_=s_, tb=tb, esl=esl, wn=wn):
                    t.wait_ge(cV.h, wn)
                    t.wait_ge(cS.h, cS.n)
                    for k in range(DCH):
                        _mm = t.matmul(ps2[:, 512 * esl:512 * (esl + 1)],
                                 hn_r(k)[:, s_ * 256 + tb * 128:s_ * 256 + (tb + 1) * 128],
                                 wcur_k(k)[:, esl * 512:(esl + 1) * 512],
                                 start=(k == 0), stop=(k == DCH - 1))
                    cT.one(_mm)
                need = cT.n

                @blk.scalar
                def _(s, s_=s_, tb=tb, esl=esl, need=need):
                    s.wait_ge(cT.h, need)
                    s.copy(out=v_r(s_ * 2 + tb)[:, esl * 512:(esl + 1) * 512],
                           in_=ps2[:, 512 * esl:512 * (esl + 1)])
                    cS.one(s.drain())
    vprojd = cS.n

    rp = aC[0:32, 5120:6656]
    for s_ in range(SEQ):
        @blk.sync
        def _(sy, s_=s_):
            sy.wait_ge(cV.h, cV.n)
            dX.dma(sy.dma_start(out=rp[:, 0:256],
                                in_=h_rows[s_, 4, :].unsqueeze(0).broadcast_to([32, KSEL])))
        need = dX.n

        @blk.vector
        def _(v, s_=s_, need=need):
            v.wait_ge(dX.h, need)
            ang = rp[:, 0:256]
            rr = rp[:, 256:512]
            rc = rp[:, 512:768]
            tm = rp[:, 768:1024]
            v.tensor_scalar(out=ang, in0=ang, scalar1=inv32c[0:32, :], scalar2=None,
                            op0=ALU.mult)
            v.tensor_scalar(out=rr, in0=ang, scalar1=1.0 / TWO_PI, scalar2=MAGIC,
                            op0=ALU.mult, op1=ALU.add)
            v.drain()
            v.tensor_scalar(out=rr, in0=rr, scalar1=MAGIC, scalar2=None, op0=ALU.subtract)
            v.drain()
            v.scalar_tensor_tensor(out=rr, in0=rr, scalar=-TWO_PI, in1=ang,
                                   op0=ALU.mult, op1=ALU.add)
            v.tensor_scalar(out=rc, in0=rr, scalar1=TWO_PI / 4, scalar2=None, op0=ALU.add)
            v.tensor_scalar(out=tm, in0=rc, scalar1=TWO_PI / 2, scalar2=None, op0=ALU.is_gt)
            v.drain()
            v.scalar_tensor_tensor(out=rc, in0=tm, scalar=-TWO_PI, in1=rc,
                                   op0=ALU.mult, op1=ALU.add)
            cV.one(v.drain())
        need = cV.n

        @blk.scalar
        def _(s, need=need):
            s.wait_ge(cV.h, need)
            s.activation(out=rp[:, 768:1024], in_=rp[:, 256:512], func=AF.Sin,
                         scale=1.0)
            cS.one(s.activation(out=rp[:, 1024:1280], in_=rp[:, 512:768],
                                func=AF.Sin, scale=1.0))
        need = cS.n

        @blk.vector
        def _(v, need=need):
            v.wait_ge(cS.h, need)
            v.tensor_scalar(out=rp[:, 0:256], in0=rp[:, 768:1024],
                            scalar1=-1.0, scalar2=None, op0=ALU.mult)
            cV.one(v.drain())
        need = cV.n

        @blk.sync
        def _(sy, need=need):
            sy.wait_ge(cV.h, need)
            for rows in range(4):
                dX.dma(sy.dma_start(out=cs_r(0)[rows * 32:(rows + 1) * 32, :],
                                    in_=rp[:, 1024:1280]))
                src = rp[:, 0:256] if rows % 2 == 0 else rp[:, 768:1024]
                dX.dma(sy.dma_start(out=cs_r(1)[rows * 32:(rows + 1) * 32, :], in_=src))
        need = dX.n

        @blk.vector
        def _(v, need=need):
            v.wait_ge(dX.h, need)
            v.tensor_scalar(out=cs_r(2), in0=cs_r(0), scalar1=RSC, scalar2=None,
                            op0=ALU.mult)
            v.tensor_scalar(out=cs_r(3), in0=cs_r(1), scalar1=RSC, scalar2=None,
                            op0=ALU.mult)
            cV.one(v.drain())
        csd = cV.n

        for c in range(DCH):
            @blk.sync
            def _(sy, c=c, s_=s_, csd=csd):
                sy.wait_ge(cV.h, cV.n)
                for (dst, src) in ((scr2(0), q_r(c)), (scr2(1), k_r(c))):
                    for hb2 in range(2):
                        b0 = hb2 * 64
                        dX.dma(sy.dma_start(
                            out=dst[b0:b0 + 32, s_ * 256:(s_ + 1) * 256],
                            in_=src[b0 + 32:b0 + 64, s_ * 256:(s_ + 1) * 256]))
                        dX.dma(sy.dma_start(
                            out=dst[b0 + 32:b0 + 64, s_ * 256:(s_ + 1) * 256],
                            in_=src[b0:b0 + 32, s_ * 256:(s_ + 1) * 256]))
            need = dX.n

            @blk.vector
            def _(v, c=c, s_=s_, need=need):
                v.wait_ge(dX.h, need)
                qs = q_r(c)[:, s_ * 256:(s_ + 1) * 256]
                ks = k_r(c)[:, s_ * 256:(s_ + 1) * 256]
                sh_q = scr2(0)[:, s_ * 256:(s_ + 1) * 256]
                sh_k = scr2(1)[:, s_ * 256:(s_ + 1) * 256]
                tmp = scr2(2)[:, s_ * 256:(s_ + 1) * 256]
                v.tensor_mul(out=tmp, in0=qs, in1=cs_r(2))
                v.tensor_mul(out=sh_q, in0=sh_q, in1=cs_r(3))
                v.tensor_add(out=qs, in0=tmp, in1=sh_q)
                v.tensor_mul(out=tmp, in0=ks, in1=cs_r(0))
                v.tensor_mul(out=sh_k, in0=sh_k, in1=cs_r(1))
                v.tensor_add(out=ks, in0=tmp, in1=sh_k)
                cV.one(v.drain())
    roped = cV.n

    for s_ in range(SEQ):
        for h in range(NH):
            c = h // 2
            hb = (h % 2) * 64
            at0 = scr2(2)
            at1 = scr2(3)
            for qb in range(2):
                @blk.tensor
                def _(t, s_=s_, c=c, hb=hb, qb=qb):
                    t.wait_ge(cV.h, cV.n)
                    _mm = t.matmul(ps1[:, qb * 256:(qb + 1) * 256],
                             q_r(c)[hb:hb + 64,
                                    s_ * 256 + qb * 128:s_ * 256 + (qb + 1) * 128],
                             k_r(c)[hb:hb + 64, s_ * 256:(s_ + 1) * 256],
                             start=True, stop=True)
                    cT.one(_mm)
                need = cT.n

                @blk.vector
                def _(v, s_=s_, qb=qb, need=need):
                    v.wait_ge(cT.h, need)
                    v.tensor_add(out=sc_f(qb), in0=ps1[:, qb * 256:(qb + 1) * 256],
                                 in1=msel_t(s_, qb))
                    v.tensor_reduce(out=Rtmp[:, 1:2], in_=sc_f(qb), axis=AXX, op=ALU.max)
                    v.drain()
                    v.tensor_scalar(out=Rtmp[:, 2:3], in0=Rtmp[:, 1:2], scalar1=-1.0,
                                    scalar2=None, op0=ALU.mult)
                    cV.one(v.drain())
                need = cV.n

                @blk.scalar
                def _(s, qb=qb, need=need):
                    s.wait_ge(cV.h, need)
                    s.activation(out=sqs, in_=sc_f(qb), func=AF.Exp, bias=Rtmp[:, 2:3],
                                 scale=1.0)
                    cS.one(s.drain())
                need = cS.n

                @blk.vector
                def _(v, need=need):
                    v.wait_ge(cS.h, need)
                    v.tensor_reduce(out=Rtmp[:, 3:4], in_=sqs, axis=AXX, op=ALU.add)
                    v.drain()
                    v.reciprocal(out=Rtmp[:, 4:5], in_=Rtmp[:, 3:4])
                    v.drain()
                    v.tensor_scalar(out=sqs, in0=sqs, scalar1=Rtmp[:, 4:5], scalar2=None,
                                    op0=ALU.mult)
                    cV.one(v.drain())
                need = cV.n

                @blk.tensor
                def _(t, qb=qb, need=need):
                    t.wait_ge(cV.h, need)
                    t.transpose(psTb[:, qb * 128:(qb + 1) * 128],
                                sqs[:, 0:128], ident_bf)
                    _mm = t.transpose(psTb[:, 256 + qb * 128:256 + (qb + 1) * 128],
                                sqs[:, 128:256], ident_bf)
                    cT.one(_mm)
                need = cT.n

                @blk.vector
                def _(v, qb=qb, at0=at0, at1=at1, need=need):
                    v.wait_ge(cT.h, need)
                    v.tensor_copy(out=at0[:, qb * 128:(qb + 1) * 128],
                                  in_=psTb[:, qb * 128:(qb + 1) * 128])
                    v.tensor_copy(out=at1[:, qb * 128:(qb + 1) * 128],
                                  in_=psTb[:, 256 + qb * 128:256 + (qb + 1) * 128])
                    cV.one(v.drain())
            need = cV.n

            @blk.tensor
            def _(t, s_=s_, h=h, hb=hb, at0=at0, at1=at1, need=need):
                t.wait_ge(cV.h, need)
                for kb in range(2):
                    at = (at0, at1)[kb]
                    _mm = t.matmul(ps2[hb:hb + 64, 0:256],
                             v_r(s_ * 2 + kb)[:, h * 64:(h + 1) * 64],
                             at[:, 0:256], start=(kb == 0), stop=(kb == 1),
                             tile_position=(0, hb))
                cT.one(_mm)
            need = cT.n

            if h % 2 == 1:
                @blk.scalar
                def _(s, s_=s_, c=c, need=need):
                    s.wait_ge(cT.h, need)
                    s.copy(out=hn_r(c)[:, s_ * 256:(s_ + 1) * 256], in_=ps2[:, 0:256])
                    cS.one(s.drain())
    attnd = cS.n

    wn = load_dd(wo_h)
    wovb = cV.n
    for s_ in range(SEQ):
        for e in range(DCH):
            q = s_ * DCH + e
            @blk.tensor
            def _(t, s_=s_, e=e, q=q, wn=wn):
                t.wait_ge(cV.h, wn)
                if q >= 4:
                    t.wait_ge(cV.h, wovb + q - 3)
                else:
                    t.wait_ge(cS.h, attnd)
                for k in range(DCH):
                    _mm = t.matmul(ps1[:, 512 * (e % 4):512 * (e % 4) + 256],
                             wcur_k(k)[:, e * 128:(e + 1) * 128],
                             hn_r(k)[:, s_ * 256:(s_ + 1) * 256],
                             start=(k == 0), stop=(k == DCH - 1))
                cT.one(_mm)
            need = cT.n

            @blk.vector
            def _(v, s_=s_, e=e, need=need):
                v.wait_ge(cT.h, need)
                cV.one(v.tensor_add(out=x1_r(e)[:, s_ * 256:(s_ + 1) * 256],
                                    in0=ps1[:, 512 * (e % 4):512 * (e % 4) + 256],
                                    in1=xsel_t(e)[:, s_ * 256:(s_ + 1) * 256]))
    x1d = cV.n

    for s_ in range(SEQ):
        for c in range(DCH):
            @blk.scalar
            def _(s, c=c, s_=s_, x1d=x1d):
                s.wait_ge(cV.h, x1d)
                if not (c == 0 and s_ == 0):
                    s.wait_ge(cT.h, cT.n)
                s.activation(out=sqs, in_=x1_r(c)[:, s_ * 256:(s_ + 1) * 256],
                             func=AF.Square, scale=1.0)
                cS.one(s.drain())
            need = cS.n

            @blk.tensor
            def _(t, c=c, need=need):
                t.wait_ge(cS.h, need)
                _mm = t.matmul(ps1[:, 0:256], ones_bf, sqs, start=(c == 0), stop=(c == DCH - 1))
                cT.one(_mm)
        need = cT.n

        @blk.scalar
        def _(s, need=need):
            s.wait_ge(cT.h, need)
            s.activation(out=rms_b, in_=ps1[:, 0:256], func=AF.Ln, bias=epsrms_c[:, 0:1],
                         scale=1.0 / D)
            s.drain()
            cS.one(s.activation(out=rms_b, in_=rms_b, func=AF.Exp, scale=-0.5))
        need = cS.n

        @blk.vector
        def _(v, s_=s_, need=need):
            v.wait_ge(cS.h, need)
            for c in range(DCH):
                v.scalar_tensor_tensor(out=hn2_r(c)[:, s_ * 256:(s_ + 1) * 256],
                                       in0=x1_r(c)[:, s_ * 256:(s_ + 1) * 256],
                                       scalar=ln2_s[:, c:c + 1], in1=rms_b,
                                       op0=ALU.mult, op1=ALU.mult)
            cV.one(v.drain())
    hn2d = cV.n

    w1bf = aCb[:, 0:32768]

    def w1k(k):
        return w1bf[:, k * 4096:(k + 1) * 4096]

    tn0 = cT.n
    vn0 = cV.n
    casts = []
    for k in range(DCH):
        for piece in range(4):
            r = k * 4 + piece
            stg = wst if r % 2 == 0 else wst2
            @blk.sync
            def _(sy, k=k, piece=piece, r=r, stg=stg, vn0=vn0, casts=tuple(casts)):
                if r < 2:
                    sy.wait_ge(cV.h, vn0)
                else:
                    sy.wait_ge(cV.h, casts[r - 2])
                dXp[r % 2].dma(sy.dma_start(out=stg,
                                    in_=w1_h[k * P:(k + 1) * P,
                                             piece * 1024:(piece + 1) * 1024]))
            need = dXp[r % 2].n

            @blk.vector
            def _(v, k=k, piece=piece, r=r, need=need, stg=stg, tn0=tn0):
                v.wait_ge(dXp[r % 2].h, need)
                if r < 2:
                    v.wait_ge(cT.h, tn0)
                v.tensor_copy(out=w1k(k)[:, piece * 1024:(piece + 1) * 1024], in_=stg)
                cV.one(v.drain())
            casts.append(cV.n)
    w1ld = cV.n

    w1sb = cS.n
    for s_ in range(SEQ):
        for f in range(32):
            q = s_ * 32 + f
            @blk.tensor
            def _(t, s_=s_, f=f, q=q, w1ld=w1ld):
                t.wait_ge(cV.h, w1ld)
                if q >= 4:
                    t.wait_ge(cS.h, w1sb + q - 3)
                else:
                    t.wait_ge(cS.h, w1sb)
                for k in range(DCH):
                    _mm = t.matmul(ps1[:, 512 * (f % 4):512 * (f % 4) + 256],
                             w1k(k)[:, f * 128:(f + 1) * 128],
                             hn2_r(k)[:, s_ * 256:(s_ + 1) * 256],
                             start=(k == 0), stop=(k == DCH - 1))
                cT.one(_mm)
            need = cT.n

            @blk.scalar
            def _(s, s_=s_, f=f, need=need):
                s.wait_ge(cT.h, need)
                cS.one(s.activation(out=h1_r(f, s_),
                                    in_=ps1[:, 512 * (f % 4):512 * (f % 4) + 256],
                                    func=AF.Gelu_apprx_tanh, scale=1.0))
    h1d = cS.n

    w2bf = aCb[:, 0:32768]

    def w2k(k):
        return w2bf[:, k * 1024:(k + 1) * 1024]

    tn0 = cT.n
    vn0 = cV.n
    casts = []
    for k in range(32):
        stg = wst if k % 2 == 0 else wst2
        @blk.sync
        def _(sy, k=k, stg=stg, vn0=vn0, casts=tuple(casts)):
            if k < 2:
                sy.wait_ge(cV.h, vn0)
            else:
                sy.wait_ge(cV.h, casts[k - 2])
            dXp[k % 2].dma(sy.dma_start(out=stg, in_=w2_h[k * P:(k + 1) * P, :]))
        need = dXp[k % 2].n

        @blk.vector
        def _(v, k=k, need=need, stg=stg, tn0=tn0):
            v.wait_ge(dXp[k % 2].h, need)
            if k < 2:
                v.wait_ge(cT.h, tn0)
            v.tensor_copy(out=w2k(k), in_=stg)
            cV.one(v.drain())
        casts.append(cV.n)
    w2ld = cV.n

    @blk.sync
    def _(sy, need=w2ld):
        sy.wait_ge(cV.h, need)
        for s_ in range(SEQ):
            dX.dma(sy.dma_start(out=wselb(s_),
                                in_=h_rows[s_, 5, :].unsqueeze(0).broadcast_to([P, KSEL])))
    wselbd = dX.n
    corr_b0 = [0]

    w2vb = cV.n
    for s_ in range(SEQ):
        for e in range(DCH):
            q = s_ * DCH + e
            @blk.tensor
            def _(t, s_=s_, e=e, q=q, w2ld=w2ld):
                t.wait_ge(cV.h, w2ld)
                if q >= 4:
                    t.wait_ge(cV.h, w2vb + q - 3)
                else:
                    t.wait_ge(cS.h, h1d)
                for k in range(32):
                    _mm = t.matmul(ps1[:, 512 * (e % 4):512 * (e % 4) + 256],
                             w2k(k)[:, e * 128:(e + 1) * 128],
                             h1_r(k, s_), start=(k == 0), stop=(k == 31))
                cT.one(_mm)
            need = cT.n

            @blk.vector
            def _(v, s_=s_, e=e, need=need):
                v.wait_ge(cT.h, need)
                v.wait_ge(dX.h, wselbd)
                if s_ == 1 and e == 0:
                    v.wait_ge(dX.h, corr_b0[0])
                x2t = scr2(0)[:, 0:256]
                v.tensor_add(out=x2t, in0=ps1[:, 512 * (e % 4):512 * (e % 4) + 256],
                             in1=x1_r(e)[:, s_ * 256:(s_ + 1) * 256])
                v.drain()
                v.tensor_mul(out=x2t, in0=x2t, in1=wselb(s_))
                v.drain()
                v.tensor_sub(out=corr[:, e * 256:(e + 1) * 256],
                             in0=x2t, in1=xsel_t(e)[:, s_ * 256:(s_ + 1) * 256])
                cV.one(v.drain())
        seq_corr_v = cV.n

        @blk.sync
        def _(sy, s_=s_, need=seq_corr_v):
            sy.wait_ge(cV.h, need)
            dX.dma(sy.dma_start(out=h_corr[s_], in_=corr[:, :]))
        if s_ == 0:
            corr_b0[0] = dX.n
    corrd = cV.n

    if PHASE_LIMIT <= 5:
        _emit_finish()
        return nc

    yac = lambda c: aA[:, c * 1024:(c + 1) * 1024]
    spc6 = lambda c: aA[:, 8192 + c * 1024:8192 + (c + 1) * 1024]
    barA = [aC[:, 0:1024], aC[:, 1024:2048]]
    grep = [aC[:, 2048:3072], aC[:, 3072:4096]]
    crep = [aCb[:, 8192:9216], aCb[:, 9216:10240]]
    hbuf = [aCb[:, 10240:11264], aCb[:, 11264:12288]]
    xh = [aC[:, 6144:7168], aC[:, 7168:8192]]
    outh = [aC[:, 8192:9216], aC[:, 11264:12288]]
    scats = [aCb[:, 18432:19456], aCb[:, 26624:27648]]
    hidx = [aCi16[:, 19456 + i * 256:19456 + (i + 1) * 256] for i in range(4)]
    hstg = aC[:, 12288:13312]
    dtl6 = aCb[:, 20480:22528]

    @blk.sync
    def _(sy, need=corrd):
        sy.wait_ge(cV.h, need)
        sy.wait_ge(cT.h, cT.n)
        dX.dma(sy.dma_start(out=dtl6, in_=h_dtl))
        for s_ in range(SEQ):
            for hf in range(2):
                i = s_ * 2 + hf
                dX.dma(sy.dma_start(out=hstg[:, i * 256:(i + 1) * 256],
                                    in_=h_rows[s_, 1 + hf, :].unsqueeze(0)
                                    .broadcast_to([P, KSEL])))
    need = dX.n

    @blk.vector
    def _(v, need=need):
        v.wait_ge(dX.h, need)
        for i in range(4):
            v.tensor_copy(out=hidx[i], in_=hstg[:, i * 256:(i + 1) * 256])
        cV.one(v.drain())
    hidxd = cV.n

    @blk.gpsimd
    def _(g, need=hidxd):
        g.wait_ge(cV.h, need)
        g.load_library(libcfg.local_scatter)
    lsld = cG.n

    cVs = Ctr(nc, "cVs")
    cSb = Ctr(nc, "cSb")
    corr_rl = [0, 0]

    iglob = 0
    jglob = 0
    gdma = {}
    spd_last = [None]
    for s_ in range(SEQ):
        for hf in range(2):
            jh = s_ * 2 + hf
            tlo = s_ * S + hf * HALF

            if hf == 0:
                @blk.sync
                def _(sy, s_=s_):
                    if s_ == 0:
                        sy.wait_ge(cV.h, corrd)
                    else:
                        sy.wait_ge(cG.h, cG.n)
                    dX.dma(sy.dma_start(out=corr[:, :], in_=h_corr[s_]))
                corr_rl[s_] = dX.n

            for rnd in range(2):
                @blk.tensor
                def _(t, s_=s_, hf=hf, rnd=rnd, spd_last=spd_last[0]):
                    t.wait_ge(cS.h, cS.n)
                    if rnd == 0:
                        if s_ == 0 and hf == 0:
                            t.wait_ge(cV.h, cV.n)
                            t.wait_ge(dX.h, dX.n)
                        else:
                            t.wait_ge(cS.h, spd_last)
                    for ci in range(4):
                        c = rnd * 4 + ci
                        for sl2 in range(2):
                            _mm = t.matmul(pslice(ci * 2 + sl2),
                                     wdt_bf[64 * s_:64 * s_ + 64, c * 128:(c + 1) * 128],
                                     dtl6[64 * s_:64 * s_ + 64,
                                          hf * 1024 + sl2 * 512:hf * 1024 + (sl2 + 1) * 512],
                                     start=True, stop=True)
                    cT.one(_mm)
                need = cT.n

                @blk.scalar
                def _(s, s_=s_, hf=hf, rnd=rnd, need=need):
                    s.wait_ge(cT.h, need)
                    for ci in range(4):
                        c = rnd * 4 + ci
                        for sl2 in range(2):
                            s.activation(out=spc6(c)[:, sl2 * 512:(sl2 + 1) * 512],
                                         in_=pslice(ci * 2 + sl2), func=AF.Exp,
                                         bias=bdt_s[:, c:c + 1], scale=1.0)
                        s.activation(out=spc6(c), in_=spc6(c), func=AF.Ln, bias=1.0,
                                     scale=1.0)
                    cS.one(s.drain())
            spd = cS.n
            spd_last[0] = spd

            @blk.vector
            def _(v, s_=s_, hf=hf, tlo=tlo, spd=spd):
                v.wait_ge(cS.h, spd)
                for c in range(DCH - 1):
                    v.tensor_copy(out=yac(c), in_=u_t(c)[:, tlo:tlo + HALF])
                cV.one(v.tensor_copy(out=yac(DCH - 1), in_=u_t(DCH - 1)[:, tlo:tlo + HALF]))
            yinit = cV.n

            for n_ in range(N):
                j = jglob
                jglob += 1

                @blk.sync
                def _(sy, n_=n_, tlo=tlo, j=j, yinit=yinit):
                    if j >= 2:
                        sy.wait_ge(cVs.h, 8 * (j - 1))
                    if j == 0:
                        sy.wait_ge(cV.h, yinit)
                    dGp[j % 2].dma(sy.dma_start(out=grep[n_ % 2],
                                                in_=h_g[n_, tlo:tlo + HALF].unsqueeze(0)
                                                .broadcast_to([P, HALF])))
                    dGp[j % 2].dma(sy.dma_start(out=crep[n_ % 2],
                                                in_=h_cc[n_, tlo:tlo + HALF].unsqueeze(0)
                                                .broadcast_to([P, HALF])))
                gdma[j] = (j % 2, dGp[j % 2].n)

                for c in range(DCH):
                    i = iglob
                    iglob += 1

                    @blk.scalar
                    def _(s, c=c, n_=n_, i=i, spd=spd):
                        if i >= 2:
                            s.wait_ge(cVs.h, i - 1)
                        col = c * 16 + n_
                        cSb.one(s.activation(out=barA[i % 2], in_=spc6(c), func=AF.Exp,
                                              scale=A099[:, col:col + 1],
                                              bias=A001[:, col:col + 1]))

                    @blk.vector
                    def _(v, s_=s_, hf=hf, c=c, n_=n_, i=i, j=j,
                          sbneed=cSb.n, dneed=gdma[j], yinit=yinit):
                        v.wait_ge(cSb.h, sbneed)
                        if c == 0:
                            v.wait_ge(dGp[dneed[0]].h, dneed[1])
                            if j == 0:
                                v.wait_ge(cV.h, yinit)
                        hcol = (s_ * 8 + c) * 16 + n_
                        if hf == 0:
                            v.tensor_tensor_scan(out=hbuf[i % 2], data0=barA[i % 2],
                                                 data1=grep[n_ % 2], initial=0.0,
                                                 op0=ALU.mult, op1=ALU.add)
                            v.drain()
                            v.tensor_copy(out=hstate[:, hcol:hcol + 1],
                                          in_=hbuf[i % 2][:, HALF - 1:HALF])
                        else:
                            v.tensor_tensor_scan(out=hbuf[i % 2], data0=barA[i % 2],
                                                 data1=grep[n_ % 2],
                                                 initial=hstate[:, hcol:hcol + 1],
                                                 op0=ALU.mult, op1=ALU.add)
                        v.tensor_mul(out=hbuf[i % 2], in0=hbuf[i % 2], in1=crep[n_ % 2])
                        cVs.one(v.tensor_add(out=yac(c), in0=yac(c), in1=hbuf[i % 2]))
            scan_done = cVs.n

            finv = cV.n
            for c in range(DCH):
                @blk.sync
                def _(sy, c=c, tlo=tlo, finv=finv):
                    if c >= 2:
                        sy.wait_ge(cV.h, finv + (c - 1))
                    dHp[c % 2].dma(sy.dma_start(out=xh[c % 2],
                                                in_=xin[c * P:(c + 1) * P, tlo:tlo + HALF]))
                xneed = (c % 2, dHp[c % 2].n)

                @blk.gpsimd
                def _(g, c=c, s_=s_, jh=jh, finv=finv, scan_done=scan_done):
                    if c == 0:
                        g.wait_ge(cVs.h, scan_done)
                    else:
                        g.wait_ge(cV.h, finv + c - 1)
                    if c == 0:
                        g.wait_ge(dX.h, corr_rl[s_])
                    cG.one(g.local_scatter(
                        scats[c % 2], corr[:, c * 256:(c + 1) * 256],
                        hidx[jh], channels=P, num_elems=HALF, num_idxs=KSEL))
                gneed = cG.n

                @blk.vector
                def _(v, c=c, xneed=xneed, gneed=gneed, scan_done=scan_done):
                    v.wait_ge(dHp[xneed[0]].h, xneed[1])
                    v.wait_ge(cG.h, gneed)
                    v.wait_ge(cVs.h, scan_done)
                    if c >= 2:
                        v.wait_ge(dOp[c % 2].h, dOp[c % 2].n)
                    v.tensor_add(out=outh[c % 2], in0=xh[c % 2], in1=yac(c))
                    v.drain()
                    cV.one(v.tensor_add(out=outh[c % 2], in0=outh[c % 2],
                                        in1=scats[c % 2]))
                vneed = cV.n

                @blk.sync
                def _(sy, c=c, tlo=tlo, vneed=vneed):
                    sy.wait_ge(cV.h, vneed)
                    dOp[c % 2].dma(
                        sy.dma_start(out=out_fm[c * P:(c + 1) * P, tlo:tlo + HALF],
                                     in_=outh[c % 2]))

    @blk.sync
    def _(sy):
        sy.wait_ge(dOp[0].h, dOp[0].n)
        sy.wait_ge(dOp[1].h, dOp[1].n)

    blk.__exit__(None, None, None)
    return nc


_NC_CACHE = {}


def _get_nc():
    if "nc" not in _NC_CACHE:
        nc = build_nc()
        lower_extended_insts(nc)
        _NC_CACHE["nc"] = nc
    return _NC_CACHE["nc"]


def kernel(**inputs):
    x = np.asarray(inputs["L0_attn_x"])
    ncores = 8
    eye = np.eye(P, dtype=np.float32)
    ones = np.ones((P, P), dtype=np.float32)
    inv = (1.0 / (10000.0 ** (np.arange(0, HD, 2, dtype=np.float32) / HD))).astype(np.float32)
    cpk = np.zeros((P, 257), dtype=np.float32)
    cpk[:, 0:128] = eye
    cpk[:, 128:256] = ones
    cpk[0:32, 256] = inv
    a_log = np.asarray(inputs["L0_attn_A_log"])
    a_perm = np.ascontiguousarray(
        a_log.reshape(8, 128, 16).transpose(1, 0, 2).reshape(128, 128))

    def c8(vec):
        return np.ascontiguousarray(np.asarray(vec).reshape(8, 128).T)

    mask = np.asarray(inputs["L0_ffn_attention_mask"])[:, 0]
    posi = np.asarray(inputs["L0_ffn_position_ids"])
    base = {
        "w_in": np.asarray(inputs["L0_attn_w_in"]),
        "w_x": np.asarray(inputs["L0_attn_w_x"]),
        "w_dt": np.asarray(inputs["L0_attn_w_dt"]),
        "wq": np.asarray(inputs["L0_ffn_wq"]),
        "wk": np.asarray(inputs["L0_ffn_wk"]),
        "wv": np.asarray(inputs["L0_ffn_wv"]),
        "wo": np.asarray(inputs["L0_ffn_wo"]),
        "w1": np.asarray(inputs["L0_ffn_w1"]),
        "w2": np.asarray(inputs["L0_ffn_w2"]),
        "a_perm": a_perm,
        "bdt8": c8(inputs["L0_attn_b_dt"]),
        "dp8": c8(inputs["L0_attn_Dp"]),
        "ln18": c8(inputs["L0_ffn_ln1"]),
        "ln28": c8(inputs["L0_ffn_ln2"]),
        "rw8": c8(np.asarray(inputs["L0_ffn_router_w"])[:, 0]),
        "rb1": np.asarray(inputs["L0_ffn_router_b"]).reshape(1, 1),
        "cpack": cpk,
    }
    in_maps = []
    for core in range(ncores):
        b0 = 2 * core
        m = dict(base)
        xc = x[b0:b0 + 2]
        m["xin"] = np.ascontiguousarray(xc.transpose(2, 0, 1).reshape(D, T))
        bc0, bc1 = min(b0, 3), min(b0 + 1, 3)
        m["mask0"] = np.ascontiguousarray(mask[bc0, 0:S, 0:S])
        m["mask1"] = np.ascontiguousarray(mask[bc1, 0:S, 0:S])
        m["pos2"] = np.ascontiguousarray(
            np.stack([posi[bc0, 0:S], posi[bc1, 0:S]]).astype(np.int32))
        in_maps.append(m)

    nc = _get_nc()
    res = run_bass_kernel_spmd(nc, in_maps, core_ids=list(range(ncores)))
    outs = []
    for core in range(ncores):
        o = np.asarray(res.results[core]["out_fm"])
        outs.append(o.reshape(D, SEQ, S).transpose(1, 2, 0))
    return np.concatenate(outs, axis=0).astype(np.float32)
